# version 04
# baseline (speedup 1.0000x reference)
import math

import numpy as np

import concourse.bass as bass
import concourse.mybir as mybir
from concourse.bass_utils import run_bass_kernel_spmd

F32 = mybir.dt.float32
BF16 = mybir.dt.bfloat16
I32 = mybir.dt.int32
AF = mybir.ActivationFunctionType
OP = mybir.AluOpType
AX = mybir.AxisListType

B, S, D = 4, 2048, 4096
H, KVH, DH, G = 32, 8, 128, 4
CB = 16
NT = S // 128
TPC = NT // 2
TOK = TPC * 128
NCH = D // 128
NEG = -1.0e38
SCALE = 1.0 / math.sqrt(DH)
MAGIC = 12582912.0
EPS = 1e-5
LN1E10 = math.log(1e10)
SIM_LOCAL_CC = False


class Step:
    __slots__ = ("eng", "fn", "reads", "writes", "deps", "ord", "inc", "seq")

    def __init__(self, eng, fn, reads, writes):
        self.eng = eng
        self.fn = fn
        self.reads = reads
        self.writes = writes
        self.deps = None
        self.ord = None
        self.inc = False
        self.seq = None


VENG = {
    "pe": ("tensor", False),
    "dve": ("vector", False),
    "act": ("scalar", False),
    "pool": ("gpsimd", False),
    "qs": ("sync", True),
    "qa": ("scalar", True),
    "qp": ("gpsimd", True),
}
REAL_OF = {k: v[0] for k, v in VENG.items()}


class Sched:
    def __init__(self):
        self.steps = []
        self.fences = []

    def add(self, eng, fn, reads=(), writes=()):
        assert eng in VENG
        self.steps.append(Step(eng, fn, tuple(reads), tuple(writes)))
        return len(self.steps) - 1

    def fence(self):
        self.fences.append(len(self.steps))

    def emit(self, nc):
        steps = self.steps
        n = len(steps)
        for i, st in enumerate(steps):
            st.seq = i
        last_w = {}
        readers = {}
        last_of_eng = {}
        fi = 0
        cur_fence_deps = {}
        fdeps_per = [None] * n
        for i, st in enumerate(steps):
            while fi < len(self.fences) and self.fences[fi] == i:
                cur_fence_deps = dict(last_of_eng)
                fi += 1
            fdeps_per[i] = {e: j for e, j in cur_fence_deps.items()
                            if e != st.eng or VENG[e][1]}
            deps = {}
            for k in st.reads:
                j = last_w.get(k)
                if j is not None:
                    deps[j] = True
            for k in st.writes:
                j = last_w.get(k)
                if j is not None:
                    deps[j] = True
                for r in readers.get(k, ()):
                    deps[r] = True
            for k in st.reads:
                readers.setdefault(k, []).append(i)
            for k in st.writes:
                last_w[k] = i
                readers[k] = []
            st.deps = sorted(deps)
            last_of_eng[st.eng] = i
        self._fdeps_per = fdeps_per
        for st in steps:
            if VENG[st.eng][1]:
                st.inc = True
        for i, st in enumerate(steps):
            for j in st.deps:
                d = steps[j]
                if d.eng != st.eng:
                    d.inc = True
            for j in fdeps_per[i].values():
                steps[j].inc = True
        fi = 0
        last_of_eng = {}
        for i, st in enumerate(steps):
            while fi < len(self.fences) and self.fences[fi] == i:
                for j in last_of_eng.values():
                    steps[j].inc = True
                fi += 1
            last_of_eng[st.eng] = i
        for j in last_of_eng.values():
            steps[j].inc = True
        cnt = {e: 0 for e in VENG}
        for st in steps:
            if st.inc:
                cnt[st.eng] += 1
                st.ord = cnt[st.eng]
        self.final_counts = dict(cnt)
        NQ = 8
        SEM_LIM = 20000
        nsem = {e: (NQ if VENG[e][1] else (cnt[e] + SEM_LIM - 1) // SEM_LIM)
                for e in VENG}
        step_engseq = [0] * n
        eng_seq = {e: 0 for e in VENG}
        for i, st in enumerate(steps):
            eng_seq[st.eng] += 1
            step_engseq[i] = eng_seq[st.eng]
        waits_per = [None] * n
        drains_per = [False] * n
        hw = {}
        drain_floor = {e: 0 for e in VENG}

        def qparts(o):
            return (o - 1) % NQ, (o - 1) // NQ + 1

        def qcount(o, cls):
            return (o - 1 - cls) // NQ + 1 if o - 1 >= cls else 0

        for i, st in enumerate(steps):
            waits = {}
            need_drain = False
            creal = REAL_OF[st.eng]

            def want(peng, cls, idx):
                k = (peng, cls)
                if waits.get(k, 0) < idx:
                    waits[k] = idx

            for j in st.deps:
                d = steps[j]
                if d.eng == st.eng and not VENG[st.eng][1]:
                    if st.eng in ("dve", "act", "pool"):
                        if step_engseq[j] > drain_floor[st.eng]:
                            need_drain = True
                    continue
                assert d.ord is not None
                if VENG[d.eng][1]:
                    cls, idx = qparts(d.ord)
                    want(d.eng, cls, idx)
                else:
                    want(d.eng, None, d.ord)
            for e, j in self._fdeps_per[i].items():
                d = steps[j]
                assert d.ord is not None
                if VENG[e][1]:
                    for cls in range(NQ):
                        c = qcount(d.ord, cls)
                        if c:
                            want(e, cls, c)
                else:
                    want(e, None, d.ord)
            final = []
            for (peng, cls), idx in sorted(waits.items(),
                                           key=lambda t: (t[0][0], str(t[0][1]))):
                if hw.get((creal, peng, cls), 0) >= idx:
                    continue
                hw[(creal, peng, cls)] = idx
                final.append((peng, cls, idx))
            waits_per[i] = final
            if need_drain:
                drains_per[i] = True
                drain_floor[st.eng] = step_engseq[i] - 1
        self._waits_per = waits_per
        self._drains_per = drains_per
        sems = {}
        for e in VENG:
            sems[e] = [nc.semaphore(f"S_{e}_{k}") for k in range(max(1, nsem[e]))]
        sem_objs = {e: [cm.__enter__() for cm in sems[e]] for e in VENG}

        def wait_args(peng, cls, idx):
            if VENG[peng][1]:
                return sem_objs[peng][cls], idx * 16
            si, rem = (idx - 1) // SEM_LIM, (idx - 1) % SEM_LIM + 1
            return sem_objs[peng][si], rem

        with nc.Block() as blk:
            for real in ("sync", "tensor", "vector", "scalar", "gpsimd"):
                mysteps = [i for i, st in enumerate(steps) if REAL_OF[st.eng] == real]

                def emit_stream(E, mysteps=mysteps, real=real):
                    for i in mysteps:
                        st = steps[i]
                        for peng, cls, idx in waits_per[i]:
                            s, v = wait_args(peng, cls, idx)
                            E.wait_ge(s, v)
                        if drains_per[i]:
                            E.drain()
                        if st.inc and VENG[st.eng][1]:
                            cls, idx = qparts(st.ord)
                            if idx >= 2:
                                E.wait_ge(sem_objs[st.eng][cls], (idx - 1) * 16)
                        instr = st.fn(E)
                        if st.inc:
                            if VENG[st.eng][1]:
                                cls, _ = qparts(st.ord)
                                instr.then_inc(sem_objs[st.eng][cls], 16)
                            else:
                                si = (st.ord - 1) // SEM_LIM
                                instr.then_inc(sem_objs[st.eng][si], 1)
                    for e in VENG:
                        if REAL_OF[e] == real and VENG[e][1] and self.final_counts[e]:
                            o = self.final_counts[e]
                            for cls in range(NQ):
                                c = qcount(o, cls)
                                if c:
                                    s, v = wait_args(e, cls, c)
                                    E.wait_ge(s, v)

                getattr(blk, real)(emit_stream)
        for e in VENG:
            for cm in sems[e]:
                cm.__exit__(None, None, None)


def _rope_tables():
    pos = np.arange(S, dtype=np.float64)
    inv = 1.0 / (10000.0 ** (np.arange(0, DH, 2, dtype=np.float64) / DH))
    ang = pos[None, :] * inv[:, None]
    c, s = np.cos(ang), np.sin(ang)
    cosL0 = np.zeros((DH, S))
    sinL0 = np.zeros((DH, S))
    cosL0[0::2] = c
    cosL0[1::2] = c
    sinL0[0::2] = -s
    sinL0[1::2] = s
    cosL1 = np.zeros((DH, S))
    sinL1 = np.zeros((DH, S))
    cosL1[0:64] = c
    cosL1[64:128] = c
    sinL1[0:64] = -s
    sinL1[64:128] = s
    permL0 = np.zeros((DH, DH))
    for i in range(0, DH, 2):
        permL0[i, i + 1] = 1.0
        permL0[i + 1, i] = 1.0
    permL1 = np.zeros((DH, DH))
    for m in range(DH):
        permL1[(m + 64) % 128, m] = 1.0
    f = np.float32
    return (cosL0.astype(f), sinL0.astype(f), cosL1.astype(f), sinL1.astype(f),
            permL0.astype(f), permL1.astype(f))


def _masks(tiles_g):
    cm = np.zeros((TPC, 128, 129), dtype=np.float32)
    for j, gt in enumerate(tiles_g):
        q = gt * 128 + np.arange(128)
        blk_last = (np.arange(128) + 1) * CB - 1
        cm[j, :, 1:] = np.where(blk_last[None, :] < q[:, None], 0.0, NEG)
    i = np.arange(128)
    dm = np.where((i[:, None] // CB == i[None, :] // CB)
                  & (i[None, :] % CB <= i[:, None] % CB), 0.0, NEG).astype(np.float32)
    jj = np.arange(256)
    sb = np.where((jj[None, :] >= i[:, None] + 64)
                  & (jj[None, :] <= i[:, None] + 128), 0.0, NEG).astype(np.float32)
    jj0 = np.arange(128)
    sb0 = np.where((jj0[None, :] >= i[:, None] - 64)
                   & (jj0[None, :] <= i[:, None]), 0.0, NEG).astype(np.float32)
    return cm, dm, sb, sb0


def _tile_w(wT, nf):
    Dk, F = wT.shape
    nc_ = Dk // 128
    t = wT.reshape(nc_, 128, nf, 128).transpose(2, 1, 0, 3).reshape(nf, 128, nc_ * 128)
    return np.ascontiguousarray(t)


def _stage(inputs, core):
    b, p = core // 2, core % 2
    tiles_g = [p + 2 * j for j in range(TPC)]
    idx = np.concatenate([np.arange(gt * 128, gt * 128 + 128) for gt in tiles_g])
    f32c = lambda a: np.ascontiguousarray(np.asarray(a), dtype=np.float32)

    d = {}
    d["x_tm"] = f32c(np.asarray(inputs["L0_attn_hidden_states"])[b][idx])

    wqkvT = np.asarray(inputs["L0_attn_w_qkv"]).T.astype(np.float32)
    d["wqkvTt"] = _tile_w(wqkvT, 48)
    d["wgTt"] = f32c(np.asarray(inputs["L0_attn_w_gate"]).T
                     .reshape(NCH, 128, 96).transpose(1, 0, 2).reshape(128, NCH * 96))
    d["bg"] = f32c(np.asarray(inputs["L0_attn_b_gate"]).reshape(96, 1))
    woT = np.asarray(inputs["L0_attn_w_o"]).T.astype(np.float32)
    d["woTt"] = f32c(woT.reshape(NCH, 128, 8, 512).transpose(0, 2, 1, 3))
    d["kw1Tt"] = f32c(np.asarray(inputs["L0_attn_k_w1"]).T
                      .reshape(16, 128, 16, 128).transpose(2, 1, 0, 3).reshape(16, 128, 2048))
    d["vw1Tt"] = f32c(np.asarray(inputs["L0_attn_v_w1"]).T
                      .reshape(16, 128, 16, 128).transpose(2, 1, 0, 3).reshape(16, 128, 2048))
    d["kw2Tt"] = f32c(np.asarray(inputs["L0_attn_k_w2"]).T.reshape(16, 128, 128))
    d["vw2Tt"] = f32c(np.asarray(inputs["L0_attn_v_w2"]).T.reshape(16, 128, 128))
    d["kb1"] = f32c(np.asarray(inputs["L0_attn_k_b1"]).reshape(16, 128).T)
    d["vb1"] = f32c(np.asarray(inputs["L0_attn_v_b1"]).reshape(16, 128).T)
    d["kb2"] = f32c(np.asarray(inputs["L0_attn_k_b2"]).reshape(128, 1))
    d["vb2"] = f32c(np.asarray(inputs["L0_attn_v_b2"]).reshape(1, 128))
    d["kpos"] = f32c(np.asarray(inputs["L0_attn_k_pos"]).transpose(0, 2, 1)
                     .transpose(1, 0, 2).reshape(128, KVH * 16))
    d["vpos"] = f32c(np.asarray(inputs["L0_attn_v_pos"]).transpose(0, 2, 1)
                     .transpose(1, 0, 2).reshape(128, KVH * 16))
    mk = np.asarray(inputs["L0_attn_mem_kv"])
    d["memk"] = f32c(mk[0][:, 0, :].T)
    d["memv"] = f32c(mk[1][:, 0, :].reshape(1, KVH * 128))
    d["ffnWT"] = f32c(np.asarray(inputs["L0_ffn_W"]).T.reshape(NCH, 128, D))
    d["ffnb"] = f32c(np.asarray(inputs["L0_ffn_b"]).reshape(1, D))

    w1cat = np.concatenate([np.asarray(inputs["L1_attn_wq"]),
                            np.asarray(inputs["L1_attn_wk"]),
                            np.asarray(inputs["L1_attn_wv"])], axis=1).astype(np.float32)
    d["w1catt"] = _tile_w(w1cat, 48)
    bcat = np.concatenate([np.asarray(inputs["L1_attn_bq"]),
                           np.asarray(inputs["L1_attn_bk"]),
                           np.asarray(inputs["L1_attn_bv"])]).astype(np.float32)
    d["b1cat"] = f32c(bcat.reshape(48, 128).T)
    baseT = np.asarray(inputs["L1_ffn_base_W"]).T.astype(np.float32)
    d["baseTt"] = f32c(baseT.reshape(NCH, 128, 8, 512).transpose(0, 2, 1, 3))
    d["gwTt"] = f32c(np.asarray(inputs["L1_ffn_gate_W"]).T
                     .reshape(NCH, 128, 28).transpose(1, 0, 2).reshape(128, NCH * 28))
    d["AcatTt"] = f32c(np.asarray(inputs["L1_ffn_lora_A"]).reshape(224, D).T
                       .reshape(NCH, 128, 224).transpose(1, 0, 2).reshape(128, NCH * 224))
    d["Bcat"] = f32c(np.asarray(inputs["L1_ffn_lora_B"]).transpose(0, 2, 1).reshape(224, D))

    cosL0, sinL0, cosL1, sinL1, permL0, permL1 = _rope_tables()
    d["cosL0"], d["sinL0"], d["cosL1"], d["sinL1"] = cosL0, sinL0, cosL1, sinL1
    d["coslq0"] = f32c(cosL0[:, idx])
    d["sinlq0"] = f32c(sinL0[:, idx])
    d["coslq1"] = f32c(cosL1[:, idx])
    d["sinlq1"] = f32c(sinL1[:, idx])
    d["permL0"], d["permL1"] = permL0, permL1
    d["ident"] = np.eye(128, dtype=np.float32)
    d["onesrow"] = np.ones((1, 128), dtype=np.float32)
    d["onescol"] = np.ones((128, 1), dtype=np.float32)
    cm, dm, sbm, sbm0 = _masks(tiles_g)
    d["cmask"] = f32c(cm.transpose(1, 0, 2).reshape(128, TPC * 129))
    d["diagm"] = dm
    d["sband"] = sbm
    d["sband0"] = sbm0
    d2 = np.full((TPC, 128, 256), NEG, dtype=np.float32)
    sbm2 = np.full((TPC, 128, 384), NEG, dtype=np.float32)
    ii = np.arange(128)
    for qt in range(TPC):
        gt = p + 2 * qt
        q = gt * 128 + ii
        key2 = 2 * qt * 128 + np.arange(256)
        vis = (key2[None, :] // CB == q[:, None] // CB) \
            & (key2[None, :] % CB <= q[:, None] % CB)
        d2[qt] = np.where(vis, 0.0, NEG)
        soff = max(0, 2 * qt - 1) * 128
        key3 = soff + np.arange(384)
        visb = (key3[None, :] >= q[:, None] - 64) & (key3[None, :] <= q[:, None])
        sbm2[qt] = np.where(visb, 0.0, NEG)
    d["diag2"] = d2
    d["sbandm"] = sbm2
    return d, tiles_g


class Prog:

    def __init__(self, nc, taps=()):
        self.nc = nc
        self.sc = Sched()
        self.dram = {}
        self.taps = set(taps)
        self._ctxs = []
        self.ARENA_COLS = 50688
        self.arena = self._sbuf("ARENA", (128, self.ARENA_COLS))
        self._abase = 0
        self._amarks = []
        self.ident = self._sbuf("c_ident", (128, 128))
        self.ones1 = self._sbuf("c_ones", (1, 128))
        self.onesc = self._sbuf("c_onesc", (128, 1))
        self.swcol = self._sbuf("c_swcol", (128, 2))
        self.ps = [self._psum(f"PSB{i}", (128, 512)) for i in range(4)]
        self.psbig = self._psum("PSBIG", (128, 2048))

    def _sbuf(self, name, shape, dtype=F32):
        cm = self.nc.sbuf_tensor(name, list(shape), dtype)
        t = cm.__enter__()
        self._ctxs.append(cm)
        return t

    def _psum(self, name, shape, dtype=F32):
        cm = self.nc.psum_tensor(name, list(shape), dtype)
        t = cm.__enter__()
        self._ctxs.append(cm)
        return t

    def close(self):
        for cm in reversed(self._ctxs):
            cm.__exit__(None, None, None)

    def alloc(self, ncols):
        base = self._abase
        assert base + ncols <= self.ARENA_COLS, (base, ncols)
        self._abase += ncols
        return base

    def mark(self):
        self._amarks.append(self._abase)

    def release(self):
        self._abase = self._amarks.pop()

    def a(self, base, ncols, p0=0, np_=128):
        return self.arena[p0:p0 + np_, base:base + ncols]

    def din(self, name, shape, dtype=F32):
        ap = self.nc.dram_tensor(name, list(shape), dtype, kind="ExternalInput").ap()
        self.dram[name] = ap
        return ap

    def dout(self, name, shape, dtype=F32):
        ap = self.nc.dram_tensor(name, list(shape), dtype, kind="ExternalOutput").ap()
        self.dram[name] = ap
        return ap

    def dint(self, name, shape, dtype=F32):
        ap = self.nc.dram_tensor(name, list(shape), dtype, kind="Internal").ap()
        self.dram[name] = ap
        return ap

    def tap(self, name, src_fn, reads):
        if name in self.taps:
            dst = self.dram["tap_" + name]
            self.sc.add("qp", lambda E, s=src_fn, d=dst: E.dma_start(out=d[:], in_=s()),
                        reads=reads, writes=("tap_" + name,))


TAP_SHAPES = {
    "xn_tm": (TOK, D), "q0_fm": (NCH, 128, TOK), "gates_tm": (TOK, 96),
    "kfull": (KVH, 128, S), "vfull": (KVH, 128, S),
    "rk": (KVH, 128, S), "ck": (KVH, 128, 129), "cvtm": (KVH, 128, 128),
    "attn_acc": (TOK, D), "s0_tm": (TOK, D), "x1_tm": (TOK, D),
    "xn1_fm": (NCH, 128, TOK), "s1_tm": (TOK, D), "sw": (1, 3),
    "imp": (KVH, 128, 128), "fsel": (KVH, 128, 128),
    "kwq": (NCH, 128, D), "kx": (128, NCH * TOK),
}


def build(nc, taps=()):
    P = Prog(nc, taps)
    sc = P.sc
    d_ = {}

    def din(n, s, dt=F32):
        d_[n] = P.din(n, s, dt)
        return d_[n]

    x_tm = din("x_tm", (TOK, D))
    wqkvTt = din("wqkvTt", (48, 128, D))
    wgTt = din("wgTt", (128, NCH * 96))
    bg = din("bg", (96, 1))
    woTt = din("woTt", (NCH, 8, 128, 512))
    kw1Tt = din("kw1Tt", (16, 128, 2048))
    vw1Tt = din("vw1Tt", (16, 128, 2048))
    kw2Tt = din("kw2Tt", (16, 128, 128))
    vw2Tt = din("vw2Tt", (16, 128, 128))
    kb1 = din("kb1", (128, 16))
    vb1 = din("vb1", (128, 16))
    kb2 = din("kb2", (128, 1))
    vb2 = din("vb2", (1, 128))
    kpos = din("kpos", (128, KVH * 16))
    vpos = din("vpos", (128, KVH * 16))
    memk = din("memk", (128, KVH))
    memv = din("memv", (1, KVH * 128))
    ffnWT = din("ffnWT", (NCH, 128, D))
    ffnb = din("ffnb", (1, D))
    w1catt = din("w1catt", (48, 128, D))
    b1cat = din("b1cat", (128, 48))
    baseTt = din("baseTt", (NCH, 8, 128, 512))
    gwTt = din("gwTt", (128, NCH * 28))
    AcatTt = din("AcatTt", (128, NCH * 224))
    Bcat = din("Bcat", (224, D))
    cosL0 = din("cosL0", (128, S))
    sinL0 = din("sinL0", (128, S))
    cosL1 = din("cosL1", (128, S))
    sinL1 = din("sinL1", (128, S))
    coslq0 = din("coslq0", (128, TOK))
    sinlq0 = din("sinlq0", (128, TOK))
    coslq1 = din("coslq1", (128, TOK))
    sinlq1 = din("sinlq1", (128, TOK))
    permL0 = din("permL0", (128, 128))
    permL1 = din("permL1", (128, 128))
    identD = din("ident", (128, 128))
    onesD = din("onesrow", (1, 128))
    onescD = din("onescol", (128, 1))
    cmaskD = din("cmask", (128, TPC * 129))
    diagD = din("diagm", (128, 128))
    sbandD = din("sband", (128, 256))
    sband0D = din("sband0", (128, 128))
    diag2D = din("diag2", (TPC, 128, 256))
    sbandmD = din("sbandm", (TPC, 128, 384))

    out = P.dout("out", (TOK, D))
    for t in P.taps:
        P.dout("tap_" + t, TAP_SHAPES[t])

    xn_tm_d = P.dint("xn_tm", (TOK, D))
    q0_fm_d = P.dint("q0_fm", (NCH, 128, TOK))
    gates_d = P.dint("gates_tm", (TOK, 96))
    cc0_in = P.dint("cc0_in", (2, KVH, 128, TOK))
    cc0_out = P.dint("cc0_out", (2, 2, KVH, 128, TOK))
    acc_d = P.dint("attn_acc", (TOK, D))
    s0_d = P.dint("s0_tm", (TOK, D))
    x1_d = P.dint("x1_tm", (TOK, D))
    xn1_fm_d = P.dint("xn1_fm", (NCH, 128, TOK))
    q1_fm_d = P.dint("q1_fm", (NCH, 128, TOK))
    cc1_in = P.dint("cc1_in", (2, KVH, 128, TOK))
    cc1_out = P.dint("cc1_out", (2, 2, KVH, 128, TOK))
    s1_d = P.dint("s1_tm", (TOK, D))

    sc.add("qs", lambda E: E.dma_start(out=P.ident[:], in_=identD[:]),
           writes=("ident",))
    sc.add("qs", lambda E: E.dma_start(out=P.ones1[:], in_=onesD[:]),
           writes=("ones1",))
    sc.add("qs", lambda E: E.dma_start(out=P.onesc[:], in_=onescD[:]),
           writes=("onesc",))

    P.mark()
    w0 = P.alloc(2 * D)
    asum = P.alloc(2)
    srow = P.alloc(4)
    for i in range(NCH):
        sc.add("qs", lambda E, i=i: E.dma_start(
            out=P.a(w0 + (i % 2) * D, D), in_=ffnWT[i]),
            reads=(), writes=(f"w0.{i % 2}",))
        sc.add("dve", lambda E, i=i: E.tensor_reduce(
            P.a(asum + 1, 1), P.a(w0 + (i % 2) * D, D), axis=AX.X, op=OP.add,
            apply_absolute_value=True),
            reads=(f"w0.{i % 2}",), writes=("asum1",))
        if i == 0:
            sc.add("dve", lambda E: E.tensor_copy(P.a(asum, 1), P.a(asum + 1, 1)),
                   reads=("asum1",), writes=("asum0",))
        else:
            sc.add("dve", lambda E: E.tensor_add(
                P.a(asum, 1), P.a(asum, 1), P.a(asum + 1, 1)),
                reads=("asum0", "asum1"), writes=("asum0",))
    sc.add("pe", lambda E: E.matmul(P.ps[0][0:1, 0:1], P.onesc[:],
                                    P.a(asum, 1), start=True, stop=True),
           reads=("asum0", "onesc"), writes=("ps0",))
    sc.add("act", lambda E: E.mul(P.a(srow, 1, 0, 1), P.ps[0][0:1, 0:1],
                                  1.0 / (D * D)),
           reads=("ps0",), writes=("srow",))
    sc.add("dve", lambda E: E.tensor_scalar_max(
        P.a(srow + 1, 1, 0, 1), P.a(srow, 1, 0, 1), EPS),
        reads=("srow",), writes=("srow",))
    sc.add("dve", lambda E: E.reciprocal(P.a(srow + 2, 1, 0, 1),
                                         P.a(srow + 1, 1, 0, 1)),
           reads=("srow",), writes=("srow",))
    sc.add("pe", lambda E: E.matmul(P.ps[0][:, 0:2], P.ones1[:],
                                    P.a(srow + 1, 2, 0, 1), start=True, stop=True),
           reads=("srow", "ones1"), writes=("ps0",))
    sc.add("dve", lambda E: E.tensor_copy(P.swcol[:, 0:1], P.ps[0][:, 1:2]),
           reads=("ps0",), writes=("swcol",))
    sc.add("dve", lambda E: E.tensor_copy(P.swcol[:, 1:2], P.ps[0][:, 0:1]),
           reads=("ps0",), writes=("swcol",))
    P.tap("sw", lambda: P.a(srow, 3, 0, 1), ("srow",))
    sc.fence()
    P.release()

    _ln_proj(P, dict(
        x_src=x_tm, wt=wqkvTt, nf=48, pfx="p1",
        xn_tm_d=xn_tm_d, xn_fm_d=None, q_fm_d=q0_fm_d, q_chunks=32,
        cc_in=cc0_in, kv_chunks=(32, 40), bias=None,
        wg=wgTt, bg=bg, gates_d=gates_d))
    sc.fence()
    P.tap("xn_tm", lambda: xn_tm_d[:], ("xn_tm_d",))
    P.tap("q0_fm", lambda: q0_fm_d[:], ("q0_fm_d",))
    P.tap("gates_tm", lambda: gates_d[:], ("gates_d",))

    if SIM_LOCAL_CC:
        for r in range(2):
            sc.add("qp", lambda E, r=r: E.dma_start(
                out=cc0_out[r], in_=cc0_in[:]),
                reads=("cc0_in",), writes=("cc0_out",))
    else:
        sc.add("pool", lambda E: E.collective_compute(
            "AllGather", OP.bypass, ins=[cc0_in[:]], outs=[cc0_out[:]],
            replica_groups=[[0, 1], [2, 3], [4, 5], [6, 7]]),
            reads=("cc0_in",), writes=("cc0_out",))
    sc.fence()
    for nm, kv in (("kfull", 0), ("vfull", 1)):
        if nm not in P.taps:
            continue
        for kvh in range(KVH):
            for r in range(2):
                sc.add("qp", lambda E, nm=nm, kv=kv, kvh=kvh, r=r: E.dma_start(
                    out=P.dram["tap_" + nm][kvh]
                        .rearrange("b (j r2 c) -> b r2 j c", r2=2, c=128)[:, r, :, :],
                    in_=cc0_out[r, kv, kvh, :, :]
                        .rearrange("b (j c) -> b j c", c=128)),
                    reads=("cc0_out",), writes=("tap_" + nm,))

    _build_rest(P, d_, locals())

    sc.emit(nc)
    P.close()
    return nc, P


def _build_rest(P, d_, env):
    sc = P.sc
    cc0_out = env["cc0_out"]
    acc_d = env["acc_d"]
    gates_d = env["gates_d"]
    q0_fm_d = env["q0_fm_d"]
    s0_d = env["s0_d"]

    P.mark()
    rk = P.alloc(KVH * 2048)
    vtm = P.alloc(KVH * 2048)
    ckb = P.alloc(KVH * 129)
    cvb = P.alloc(KVH * 128)
    prm = P.alloc(128)
    kposb = P.alloc(KVH * 16)
    vposb = P.alloc(KVH * 16)
    kb1b = P.alloc(16)
    vb1b = P.alloc(16)
    kb2b = P.alloc(1)
    vb2b = P.alloc(128)
    memkb = P.alloc(KVH)
    memvb = P.alloc(KVH * 128)
    for dst, src, key in (
            (P.a(prm, 128), d_["permL0"], "prm"),
            (P.a(kposb, KVH * 16), d_["kpos"], "kposb"),
            (P.a(vposb, KVH * 16), d_["vpos"], "vposb"),
            (P.a(kb1b, 16), d_["kb1"], "kb1b"),
            (P.a(vb1b, 16), d_["vb1"], "vb1b"),
            (P.a(kb2b, 1), d_["kb2"], "kb2b"),
            (P.a(vb2b, 128, 0, 1), d_["vb2"], "vb2b"),
            (P.a(memkb, KVH), d_["memk"], "memkb"),
            (P.a(memvb, KVH * 128, 0, 1), d_["memv"], "memvb"),
    ):
        sc.add("qs", lambda E, dst=dst, src=src: E.dma_start(out=dst, in_=src[:]),
               writes=(key,))

    P.mark()
    cosf = P.alloc(2048)
    sinf = P.alloc(2048)
    vtmp = P.alloc(2048)
    w1b = P.alloc(2048)
    w2b = P.alloc(2048)
    kw1b = P.alloc(2048)
    kw2b = P.alloc(2 * 128)
    sc.add("qs", lambda E: E.dma_start(out=P.a(cosf, 2048), in_=d_["cosL0"][:]),
           writes=("cosf",))
    sc.add("qs", lambda E: E.dma_start(out=P.a(sinf, 2048), in_=d_["sinL0"][:]),
           writes=("sinf",))

    def kv_load(dst_base, dst_key, kv, kvh, cc, eng="qs"):
        for r in range(2):
            sc.add(eng, lambda E, r=r: E.dma_start(
                out=P.a(dst_base, 2048)
                    .rearrange("b (j r2 c) -> b r2 j c", r2=2, c=128)[:, r, :, :],
                in_=cc[r, kv, kvh, :, :].rearrange("b (j c) -> b j c", c=128)),
                reads=(cc is cc0_out and "cc0_out" or "cc1_out",),
                writes=(dst_key,))

    nw2 = 0
    for kvh in range(KVH):
        rkk = f"rk{kvh}"
        kv_load(rk + kvh * 2048, rkk, 0, kvh, cc0_out)
        kv_load(vtmp, "vtmp", 1, kvh, cc0_out)
        sc.add("dve", lambda E, kvh=kvh: E.tensor_tensor(
            P.a(w1b, 2048).rearrange("b (c i) -> b c i", i=16),
            P.a(rk + kvh * 2048, 2048).rearrange("b (c i) -> b c i", i=16),
            P.a(kposb + kvh * 16, 16).unsqueeze(1).to_broadcast([128, 128, 16]),
            OP.add), reads=(rkk, "kposb"), writes=("w1",))
        for hc in range(16):
            i = kvh * 16 + hc
            sc.add("qs", lambda E, hc=hc, i=i: E.dma_start(
                out=P.a(kw1b, 2048), in_=d_["kw1Tt"][hc]),
                writes=("kw1",))
            for jc in range(16):
                sc.add("pe", lambda E, hc=hc, jc=jc, i=i: E.matmul(
                    P.ps[0][:, 0:128],
                    P.a(kw1b + jc * 128, 128),
                    P.a(w1b, 2048).rearrange("b (c i2) -> b i2 c", i2=16)[:, jc, :],
                    start=(jc == 0), stop=(jc == 15)),
                    reads=("kw1", "w1"), writes=("ps0",))
            sc.add("act", lambda E, hc=hc: E.activation(
                P.a(w2b + hc * 128, 128), P.ps[0][:, 0:128], AF.Relu,
                bias=P.a(kb1b + hc, 1)),
                reads=("ps0", "kb1b"), writes=("w2",))
        for hc in range(16):
            i2 = nw2
            nw2 += 1
            sc.add("qs", lambda E, hc=hc, i2=i2: E.dma_start(
                out=P.a(kw2b + (i2 % 2) * 128, 128), in_=d_["kw2Tt"][hc]),
                writes=(f"kw2.{i2 % 2}",))
            sc.add("pe", lambda E, hc=hc, i2=i2: E.matmul(
                P.ps[1][:, 0:128], P.a(kw2b + (i2 % 2) * 128, 128),
                P.a(w2b + hc * 128, 128), start=(hc == 0), stop=(hc == 15)),
                reads=(f"kw2.{i2 % 2}", "w2"), writes=("ps1",))
        sc.add("act", lambda E, kvh=kvh: E.activation(
            P.a(ckb + kvh * 129 + 1, 128), P.ps[1][:, 0:128], AF.Identity,
            bias=P.a(kb2b, 1)), reads=("ps1", "kb2b"), writes=(f"ck{kvh}",))
        sc.add("dve", lambda E, kvh=kvh: E.tensor_copy(
            P.a(ckb + kvh * 129, 1), P.a(memkb + kvh, 1)),
            reads=("memkb",), writes=(f"ck{kvh}",))
        for j in range(4):
            sc.add("pe", lambda E, kvh=kvh, j=j: E.matmul(
                P.psbig[:, j * 512:(j + 1) * 512], P.a(prm, 128),
                P.a(rk + kvh * 2048 + j * 512, 512), start=True, stop=True),
                reads=(rkk, "prm"), writes=(f"psb{j}",))
        sc.add("dve", lambda E, kvh=kvh: E.tensor_tensor(
            P.a(w1b, 2048), P.psbig[:, 0:2048], P.a(sinf, 2048), OP.mult),
            reads=("psb0", "psb1", "psb2", "psb3", "sinf"), writes=("w1",))
        sc.add("dve", lambda E, kvh=kvh: E.tensor_tensor(
            P.a(rk + kvh * 2048, 2048), P.a(rk + kvh * 2048, 2048),
            P.a(cosf, 2048), OP.mult), reads=(rkk, "cosf"), writes=(rkk,))
        sc.add("dve", lambda E, kvh=kvh: E.tensor_tensor(
            P.a(rk + kvh * 2048, 2048), P.a(rk + kvh * 2048, 2048),
            P.a(w1b, 2048), OP.add), reads=(rkk, "w1"), writes=(rkk,))
        sc.add("dve", lambda E, kvh=kvh: E.tensor_tensor(
            P.a(w1b, 2048).rearrange("b (c i) -> b c i", i=16),
            P.a(vtmp, 2048).rearrange("b (c i) -> b c i", i=16),
            P.a(vposb + kvh * 16, 16).unsqueeze(1).to_broadcast([128, 128, 16]),
            OP.add), reads=("vtmp", "vposb"), writes=("w1",))
        for hc in range(16):
            i = KVH * 16 + kvh * 16 + hc
            sc.add("qs", lambda E, hc=hc, i=i: E.dma_start(
                out=P.a(kw1b, 2048), in_=d_["vw1Tt"][hc]),
                writes=("kw1",))
            for jc in range(16):
                sc.add("pe", lambda E, hc=hc, jc=jc, i=i: E.matmul(
                    P.ps[0][:, 0:128],
                    P.a(kw1b + jc * 128, 128),
                    P.a(w1b, 2048).rearrange("b (c i2) -> b i2 c", i2=16)[:, jc, :],
                    start=(jc == 0), stop=(jc == 15)),
                    reads=("kw1", "w1"), writes=("ps0",))
            sc.add("act", lambda E, hc=hc: E.activation(
                P.a(w2b + hc * 128, 128), P.ps[0][:, 0:128], AF.Relu,
                bias=P.a(vb1b + hc, 1)),
                reads=("ps0", "vb1b"), writes=("w2",))
        for hc in range(16):
            i2 = nw2
            nw2 += 1
            sc.add("qs", lambda E, hc=hc, i2=i2: E.dma_start(
                out=P.a(kw2b + (i2 % 2) * 128, 128), in_=d_["vw2Tt"][hc]),
                writes=(f"kw2.{i2 % 2}",))
            sc.add("pe", lambda E, hc=hc, i2=i2: E.matmul(
                P.ps[1][:, 0:128], P.a(w2b + hc * 128, 128),
                P.a(kw2b + (i2 % 2) * 128, 128), start=(hc == 0), stop=False),
                reads=(f"kw2.{i2 % 2}", "w2"), writes=("ps1",))
        sc.add("pe", lambda E, kvh=kvh: E.matmul(
            P.ps[1][:, 0:128], P.ones1[:], P.a(vb2b, 128, 0, 1),
            start=False, stop=True), reads=("vb2b", "ones1"), writes=("ps1",))
        sc.add("dve", lambda E, kvh=kvh: E.tensor_copy(
            P.a(cvb + kvh * 128, 128), P.ps[1][:, 0:128]),
            reads=("ps1",), writes=(f"cv{kvh}",))
        for gt in range(16):
            pb = 2 + (gt % 2)
            sc.add("pe", lambda E, kvh=kvh, gt=gt, pb=pb: E.transpose(
                P.ps[pb][:, 0:128], P.a(vtmp + gt * 128, 128), P.ident[:]),
                reads=("vtmp", "ident"), writes=(f"ps{pb}",))
            eng = "act" if gt % 2 == 0 else "dve"
            sc.add(eng, lambda E, kvh=kvh, gt=gt, pb=pb: (
                E.copy(P.a(vtm + kvh * 2048 + gt * 128, 128), P.ps[pb][:, 0:128])
                if gt % 2 == 0 else
                E.tensor_copy(P.a(vtm + kvh * 2048 + gt * 128, 128),
                              P.ps[pb][:, 0:128])),
                reads=(f"ps{pb}",), writes=(f"vtm{kvh}",))
    P.release()
    sc.fence()
    if "rk" in P.taps:
        for kvh in range(KVH):
            sc.add("qp", lambda E, kvh=kvh: E.dma_start(
                out=P.dram["tap_rk"][kvh], in_=P.a(rk + kvh * 2048, 2048)),
                reads=(f"rk{kvh}",), writes=("tap_rk",))
    if "ck" in P.taps:
        for kvh in range(KVH):
            sc.add("qp", lambda E, kvh=kvh: E.dma_start(
                out=P.dram["tap_ck"][kvh], in_=P.a(ckb + kvh * 129, 129)),
                reads=(f"ck{kvh}",), writes=("tap_ck",))
    if "cvtm" in P.taps:
        for kvh in range(KVH):
            sc.add("qp", lambda E, kvh=kvh: E.dma_start(
                out=P.dram["tap_cvtm"][kvh], in_=P.a(cvb + kvh * 128, 128)),
                reads=(f"cv{kvh}",), writes=("tap_cvtm",))

    P.mark()
    clq = P.alloc(128)
    slq = P.alloc(128)
    cmb = P.alloc(129)
    d2b = P.alloc(256)
    sbb = P.alloc(384)
    gtb = P.alloc(96)
    q4b = P.alloc(512)
    rqb = P.alloc(128)
    accb = P.alloc(2 * 512)
    scb = P.alloc(2048)
    pb_ = P.alloc(2048)
    ptb = P.alloc(2048)
    s2b = P.alloc(256)
    csb = P.alloc(4 * 129)
    impb = P.alloc(128)
    eb = P.alloc(128)
    fselb = P.alloc(128)
    curb = P.alloc(128)
    colb = P.alloc(32)
    mempb = P.alloc(128)
    mrep = P.alloc(8)
    sc.add("dve", lambda E: E.memset(P.a(mrep, 8), 1.0e38), writes=("mrep",))

    for qt in range(TPC):
        for dst, src, key in (
                (P.a(clq, 128), d_["coslq0"][:, qt * 128:(qt + 1) * 128], "clq"),
                (P.a(slq, 128), d_["sinlq0"][:, qt * 128:(qt + 1) * 128], "slq"),
                (P.a(cmb, 129), d_["cmask"][:, qt * 129:(qt + 1) * 129], "cmb"),
                (P.a(d2b, 256), d_["diag2"][qt], "d2b"),
                (P.a(sbb, 384), d_["sbandm"][qt], "sbb"),
                (P.a(gtb, 96), gates_d[qt * 128:(qt + 1) * 128, :], "gtb"),
        ):
            sc.add("qs", lambda E, dst=dst, src=src: E.dma_start(out=dst, in_=src),
                   writes=(key,))
        nk = (qt + 1) * 256
        nblk = nk // 16
        ntile = nk // 128
        soff = max(0, 2 * qt - 1) * 128
        for kvh in range(KVH):
            ak = f"acc{kvh % 2}"
            ao = accb + (kvh % 2) * 512
            for g in range(G):
                h = kvh * G + g
                sc.add("qs", lambda E, h=h, qt=qt, g=g: E.dma_start(
                    out=P.a(q4b + g * 128, 128),
                    in_=q0_fm_d[h, :, qt * 128:(qt + 1) * 128]),
                    writes=(f"q4.{g}",))
            for g in range(G):
                sc.add("pe", lambda E, g=g, kvh=kvh: E.matmul(
                    P.ps[0][:, 0:129], P.a(q4b + g * 128, 128),
                    P.a(ckb + kvh * 129, 129), start=True, stop=True),
                    reads=(f"q4.{g}", f"ck{kvh}"), writes=("ps0",))
                sc.add("dve", lambda E, g=g: E.scalar_tensor_tensor(
                    P.a(csb + g * 129, 129), P.ps[0][:, 0:129], SCALE,
                    P.a(cmb, 129), OP.mult, OP.add),
                    reads=("ps0", "cmb"), writes=(f"cs{g}",))
            sc.add("dve", lambda E: E.tensor_add(
                P.a(impb, 128), P.a(csb + 1, 128), P.a(csb + 129 + 1, 128)),
                reads=("cs0", "cs1"), writes=("impb",))
            sc.add("dve", lambda E: E.tensor_add(
                P.a(eb, 128), P.a(csb + 2 * 129 + 1, 128),
                P.a(csb + 3 * 129 + 1, 128)),
                reads=("cs2", "cs3"), writes=("eb",))
            sc.add("dve", lambda E: E.tensor_scalar_mul(
                P.a(impb, 128), P.a(impb, 128), 0.25),
                reads=("impb",), writes=("impb",))
            sc.add("dve", lambda E: E.tensor_scalar_mul(
                P.a(eb, 128), P.a(eb, 128), 0.25),
                reads=("eb",), writes=("eb",))
            sc.add("dve", lambda E: E.tensor_add(
                P.a(impb, 128), P.a(impb, 128), P.a(eb, 128)),
                reads=("impb", "eb"), writes=("impb",))
            sc.add("dve", lambda E: E.tensor_reduce(
                P.a(colb, 1), P.a(impb, 128), axis=AX.X, op=OP.max),
                reads=("impb",), writes=("c0",))
            sc.add("dve", lambda E: E.tensor_scalar_max(
                P.a(colb + 1, 1), P.a(colb, 1), -1000.0),
                reads=("c0",), writes=("c1",))
            sc.add("dve", lambda E: E.tensor_scalar_mul(
                P.a(colb + 2, 1), P.a(colb + 1, 1), -1.0),
                reads=("c1",), writes=("c2",))
            sc.add("act", lambda E: E.activation(
                P.a(eb, 128), P.a(impb, 128), AF.Exp,
                bias=P.a(colb + 2, 1), accum_out=P.a(colb + 3, 1)),
                reads=("impb", "c2"), writes=("eb", "c3"))
            sc.add("dve", lambda E: E.tensor_scalar_add(
                P.a(colb + 16, 1), P.a(colb + 2, 1), -1000.0),
                reads=("c2",), writes=("c16",))
            sc.add("act", lambda E: E.activation(
                P.a(colb + 4, 1), P.a(colb + 16, 1), AF.Exp),
                reads=("c16",), writes=("c4",))
            sc.add("dve", lambda E: E.tensor_add(
                P.a(colb + 5, 1), P.a(colb + 3, 1), P.a(colb + 4, 1)),
                reads=("c3", "c4"), writes=("c5",))
            sc.add("dve", lambda E: E.reciprocal(
                P.a(colb + 6, 1), P.a(colb + 5, 1)),
                reads=("c5",), writes=("c6",))
            sc.add("dve", lambda E: E.tensor_copy(P.a(curb, 128), P.a(impb, 128)),
                   reads=("impb",), writes=("curb",))
            for it in range(4):
                sc.add("dve", lambda E: E.tensor_reduce(
                    P.a(colb + 7, 1), P.a(curb, 128), axis=AX.X, op=OP.max),
                    reads=("curb",), writes=("c7",))
                if it < 3:
                    sc.add("dve", lambda E: E.tensor_copy(
                        P.a(mrep, 1), P.a(colb + 7, 1)),
                        reads=("c7",), writes=("mrep",))
                    sc.add("dve", lambda E: E.match_replace(
                        P.a(curb, 128), P.a(mrep, 8), P.a(curb, 128), NEG),
                        reads=("curb", "mrep"), writes=("curb",))
            sc.add("dve", lambda E: E.tensor_single_scalar(
                P.a(fselb, 128), P.a(impb, 128), P.a(colb + 7, 1), OP.is_ge),
                reads=("impb", "c7"), writes=("fselb",))
            sc.add("dve", lambda E: E.tensor_scalar(
                P.a(eb, 128), P.a(eb, 128), P.a(colb + 6, 1), 1e-10,
                OP.mult, OP.is_gt), reads=("eb", "c6"), writes=("eb",))
            sc.add("dve", lambda E: E.tensor_tensor(
                P.a(fselb, 128), P.a(fselb, 128), P.a(eb, 128), OP.mult),
                reads=("fselb", "eb"), writes=("fselb",))
            sc.add("dve", lambda E: E.tensor_scalar(
                P.a(fselb, 128), P.a(fselb, 128), 1.0, 1e38,
                OP.subtract, OP.mult), reads=("fselb",), writes=("fselb",))
            for g in range(G):
                h = kvh * G + g
                sc.add("dve", lambda E, g=g: E.tensor_reduce(
                    P.a(colb + 8, 1), P.a(csb + g * 129, 129), axis=AX.X,
                    op=OP.max, negate=True), reads=(f"cs{g}",), writes=("c8",))
                sc.add("act", lambda E, g=g: E.activation(
                    P.a(pb_, 129), P.a(csb + g * 129, 129), AF.Exp,
                    bias=P.a(colb + 8, 1), accum_out=P.a(colb + 9, 1)),
                    reads=(f"cs{g}", "c8"), writes=("pb0", "c9"))
                sc.add("pe", lambda E: E.transpose(
                    P.ps[1][:, 0:128], P.a(pb_ + 1, 128), P.ident[:]),
                    reads=("pb0", "ident"), writes=("ps1",))
                sc.add("pe", lambda E: E.transpose(
                    P.ps[2][0:1, 0:128], P.a(pb_, 1), P.ident[:]),
                    reads=("pb0", "ident"), writes=("ps2",))
                sc.add("act", lambda E: E.copy(P.a(ptb, 128), P.ps[1][:, 0:128]),
                       reads=("ps1",), writes=("ptb0",))
                sc.add("dve", lambda E: E.tensor_copy(
                    P.a(mempb, 128, 0, 1), P.ps[2][0:1, 0:128]),
                    reads=("ps2",), writes=("mempb",))
                sc.add("pe", lambda E, kvh=kvh: E.matmul(
                    P.ps[3][:, 0:128], P.a(ptb, 128), P.a(cvb + kvh * 128, 128),
                    start=True, stop=False),
                    reads=("ptb0", f"cv{kvh}"), writes=("ps3",))
                sc.add("pe", lambda E, kvh=kvh: E.matmul(
                    P.ps[3][:, 0:128], P.a(mempb, 128, 0, 1),
                    P.a(memvb + kvh * 128, 128, 0, 1),
                    start=False, stop=True),
                    reads=("mempb", "memvb"), writes=("ps3",))
                sc.add("dve", lambda E: E.reciprocal(
                    P.a(colb + 10, 1), P.a(colb + 9, 1)),
                    reads=("c9",), writes=("c10",))
                sc.add("dve", lambda E, h=h: E.tensor_mul(
                    P.a(colb + 11, 1), P.a(colb + 10, 1),
                    P.a(gtb + 3 * h, 1)), reads=("c10", "gtb"), writes=("c11",))
                sc.add("dve", lambda E, g=g, ao=ao: E.tensor_single_scalar(
                    P.a(ao + g * 128, 128), P.ps[3][:, 0:128],
                    P.a(colb + 11, 1), OP.mult),
                    reads=("ps3", "c11"), writes=(ak,))
                sc.add("pe", lambda E, g=g: E.matmul(
                    P.ps[2][:, 0:128], P.a(prm, 128), P.a(q4b + g * 128, 128),
                    start=True, stop=True),
                    reads=(f"q4.{g}", "prm"), writes=("ps2",))
                sc.add("dve", lambda E: E.tensor_copy(P.a(s2b, 128), P.ps[2][:, 0:128]),
                       reads=("ps2",), writes=("s2b",))
                sc.add("dve", lambda E: E.tensor_tensor(
                    P.a(s2b, 128), P.a(s2b, 128), P.a(slq, 128), OP.mult),
                    reads=("s2b", "slq"), writes=("s2b",))
                sc.add("dve", lambda E, g=g: E.tensor_tensor(
                    P.a(rqb, 128), P.a(q4b + g * 128, 128), P.a(clq, 128),
                    OP.mult), reads=(f"q4.{g}", "clq"), writes=("rqb",))
                sc.add("dve", lambda E: E.tensor_tensor(
                    P.a(rqb, 128), P.a(rqb, 128), P.a(s2b, 128), OP.add),
                    reads=("rqb", "s2b"), writes=("rqb",))
                o = 0
                while o < nk:
                    w = min(512, nk - o)
                    sc.add("pe", lambda E, kvh=kvh, o=o, w=w: E.matmul(
                        P.psbig[:, o:o + w], P.a(rqb, 128),
                        P.a(rk + kvh * 2048 + o, w),
                        start=True, stop=True),
                        reads=("rqb", f"rk{kvh}"), writes=(f"psb{o // 512}",))
                    o += w
                psbks = tuple(f"psb{t}" for t in range((nk + 511) // 512))
                sc.add("dve", lambda E, nk=nk, nblk=nblk: E.scalar_tensor_tensor(
                    P.a(scb, nk).rearrange("b (c i) -> b c i", i=16),
                    P.psbig[:, 0:nk].rearrange("b (c i) -> b c i", i=16), SCALE,
                    P.a(fselb, nblk).unsqueeze(2).to_broadcast([128, nblk, 16]),
                    OP.mult, OP.add),
                    reads=psbks + ("fselb",), writes=("scb",))
                sc.add("dve", lambda E, nk=nk: E.scalar_tensor_tensor(
                    P.a(s2b, 256), P.psbig[:, nk - 256:nk], SCALE,
                    P.a(d2b, 256), OP.mult, OP.add),
                    reads=psbks + ("d2b",), writes=("s2b",))
                sc.add("dve", lambda E, nk=nk: E.tensor_max(
                    P.a(scb + nk - 256, 256), P.a(scb + nk - 256, 256),
                    P.a(s2b, 256)), reads=("scb", "s2b"), writes=("scb",))
                sc.add("dve", lambda E, nk=nk: E.tensor_reduce(
                    P.a(colb + 12, 1), P.a(scb, nk), axis=AX.X, op=OP.max,
                    negate=True), reads=("scb",), writes=("c12",))
                sc.add("act", lambda E, nk=nk: E.activation(
                    P.a(pb_, nk), P.a(scb, nk), AF.Exp,
                    bias=P.a(colb + 12, 1), accum_out=P.a(colb + 13, 1)),
                    reads=("scb", "c12"), writes=("pb0", "c13"))
                for t in range(ntile):
                    pb2 = 1 + (t % 2)
                    sc.add("pe", lambda E, t=t, pb2=pb2: E.transpose(
                        P.ps[pb2][:, 0:128], P.a(pb_ + t * 128, 128), P.ident[:]),
                        reads=("pb0", "ident"), writes=(f"ps{pb2}",))
                    eng = "act" if t % 2 == 0 else "dve"
                    sc.add(eng, lambda E, t=t, pb2=pb2, eng=eng: (
                        E.copy(P.a(ptb + t * 128, 128), P.ps[pb2][:, 0:128])
                        if eng == "act" else
                        E.tensor_copy(P.a(ptb + t * 128, 128), P.ps[pb2][:, 0:128])),
                        reads=(f"ps{pb2}",), writes=(f"pt.{t}",))
                for t in range(ntile):
                    sc.add("pe", lambda E, t=t, kvh=kvh, ntile=ntile: E.matmul(
                        P.ps[3][:, 0:128], P.a(ptb + t * 128, 128),
                        P.a(vtm + kvh * 2048 + t * 128, 128),
                        start=(t == 0), stop=(t == ntile - 1)),
                        reads=(f"pt.{t}", f"vtm{kvh}"), writes=("ps3",))
                sc.add("dve", lambda E: E.reciprocal(
                    P.a(colb + 14, 1), P.a(colb + 13, 1)),
                    reads=("c13",), writes=("c14",))
                sc.add("dve", lambda E, h=h: E.tensor_mul(
                    P.a(colb + 15, 1), P.a(colb + 14, 1),
                    P.a(gtb + 3 * h + 1, 1)), reads=("c14", "gtb"), writes=("c15",))
                sc.add("dve", lambda E, g=g, ao=ao: E.scalar_tensor_tensor(
                    P.a(ao + g * 128, 128), P.ps[3][:, 0:128],
                    P.a(colb + 15, 1), P.a(ao + g * 128, 128),
                    OP.mult, OP.add), reads=("ps3", "c15", ak), writes=(ak,))
                sc.add("pe", lambda E, kvh=kvh, soff=soff: E.matmul(
                    P.ps[0][:, 0:384], P.a(rqb, 128),
                    P.a(rk + kvh * 2048 + soff, 384), start=True, stop=True),
                    reads=("rqb", f"rk{kvh}"), writes=("ps0",))
                sc.add("dve", lambda E: E.scalar_tensor_tensor(
                    P.a(scb, 384), P.ps[0][:, 0:384], SCALE, P.a(sbb, 384),
                    OP.mult, OP.add), reads=("ps0", "sbb"), writes=("scb",))
                sc.add("dve", lambda E: E.tensor_reduce(
                    P.a(colb + 12, 1), P.a(scb, 384), axis=AX.X, op=OP.max,
                    negate=True), reads=("scb",), writes=("c12",))
                sc.add("act", lambda E: E.activation(
                    P.a(pb_, 384), P.a(scb, 384), AF.Exp,
                    bias=P.a(colb + 12, 1), accum_out=P.a(colb + 13, 1)),
                    reads=("scb", "c12"), writes=("pb0", "c13"))
                for t in range(3):
                    pb2 = 1 + (t % 2)
                    sc.add("pe", lambda E, t=t, pb2=pb2: E.transpose(
                        P.ps[pb2][:, 0:128], P.a(pb_ + t * 128, 128), P.ident[:]),
                        reads=("pb0", "ident"), writes=(f"ps{pb2}",))
                    eng = "act" if t % 2 == 0 else "dve"
                    sc.add(eng, lambda E, t=t, pb2=pb2, eng=eng: (
                        E.copy(P.a(ptb + t * 128, 128), P.ps[pb2][:, 0:128])
                        if eng == "act" else
                        E.tensor_copy(P.a(ptb + t * 128, 128), P.ps[pb2][:, 0:128])),
                        reads=(f"ps{pb2}",), writes=(f"pt.{t}",))
                for t in range(3):
                    sc.add("pe", lambda E, t=t, kvh=kvh, soff=soff: E.matmul(
                        P.ps[3][:, 0:128], P.a(ptb + t * 128, 128),
                        P.a(vtm + kvh * 2048 + soff + t * 128, 128),
                        start=(t == 0), stop=(t == 2)),
                        reads=(f"pt.{t}", f"vtm{kvh}"), writes=("ps3",))
                sc.add("dve", lambda E: E.reciprocal(
                    P.a(colb + 14, 1), P.a(colb + 13, 1)),
                    reads=("c13",), writes=("c14",))
                sc.add("dve", lambda E, h=h: E.tensor_mul(
                    P.a(colb + 15, 1), P.a(colb + 14, 1),
                    P.a(gtb + 3 * h + 2, 1)), reads=("c14", "gtb"), writes=("c15",))
                sc.add("dve", lambda E, g=g, ao=ao: E.scalar_tensor_tensor(
                    P.a(ao + g * 128, 128), P.ps[3][:, 0:128],
                    P.a(colb + 15, 1), P.a(ao + g * 128, 128),
                    OP.mult, OP.add), reads=("ps3", "c15", ak), writes=(ak,))
            sc.add("qp", lambda E, qt=qt, kvh=kvh, ao=ao: E.dma_start(
                out=acc_d[qt * 128:(qt + 1) * 128,
                          kvh * 512:(kvh + 1) * 512], in_=P.a(ao, 512)),
                reads=(ak,), writes=("acc_d",))
    P.release()
    P.release()
    sc.fence()
    P.tap("attn_acc", lambda: acc_d[:], ("acc_d",))

    P.mark()
    atT = P.alloc(NCH * TOK)
    actl = P.alloc(D)
    wti = P.alloc(2 * 512)
    evb = P.alloc(4 * 512)
    for qt in range(TPC):
        sc.add("qs", lambda E, qt=qt: E.dma_start(
            out=P.a(actl, D), in_=acc_d[qt * 128:(qt + 1) * 128, :]),
            reads=("acc_d",), writes=("actl",))
        for c in range(NCH):
            pb = (c % 2)
            sc.add("pe", lambda E, c=c, pb=pb: E.transpose(
                P.ps[pb][:, 0:128], P.a(actl + c * 128, 128), P.ident[:]),
                reads=("actl", "ident"), writes=(f"ps{pb}",))
            eng = "act" if c % 2 == 0 else "dve"
            sc.add(eng, lambda E, c=c, qt=qt, pb=pb, eng=eng: (
                E.copy(P.a(atT + c * TOK + qt * 128, 128), P.ps[pb][:, 0:128])
                if eng == "act" else
                E.tensor_copy(P.a(atT + c * TOK + qt * 128, 128),
                              P.ps[pb][:, 0:128])),
                reads=(f"ps{pb}",), writes=(f"atT.{c}.{qt}",))
    atkeys = lambda c: tuple(f"atT.{c}.{q}" for q in range(TPC))

    def psq(qt):
        if qt < 4:
            return P.ps[qt][:, 0:512], f"ps{qt}"
        return P.psbig[:, (qt - 4) * 512:(qt - 3) * 512], f"psb{qt - 4}"

    for fc in range(8):
        for kc in range(NCH):
            i = fc * NCH + kc
            sc.add("qs", lambda E, kc=kc, fc=fc, i=i: E.dma_start(
                out=P.a(wti + (i % 2) * 512, 512), in_=d_["woTt"][kc, fc]),
                writes=(f"wt.{i % 2}",))
            for qt in range(TPC):
                ps, pk = psq(qt)
                sc.add("pe", lambda E, kc=kc, qt=qt, i=i, ps=ps: E.matmul(
                    ps, P.a(atT + kc * TOK + qt * 128, 128),
                    P.a(wti + (i % 2) * 512, 512),
                    start=(kc == 0), stop=(kc == NCH - 1)),
                    reads=(f"wt.{i % 2}", f"atT.{kc}.{qt}"),
                    writes=(pk,))
        for qt in range(TPC):
            ps, pk = psq(qt)
            ek = f"ev.{qt % 4}"
            sc.add("act", lambda E, qt=qt, ps=ps: E.copy(
                P.a(evb + (qt % 4) * 512, 512), ps),
                reads=(pk,), writes=(ek,))
            sc.add("qp", lambda E, qt=qt, fc=fc: E.dma_start(
                out=s0_d[qt * 128:(qt + 1) * 128, fc * 512:(fc + 1) * 512],
                in_=P.a(evb + (qt % 4) * 512, 512)),
                reads=(ek,), writes=("s0_d",))
    P.release()
    sc.fence()
    P.tap("s0_tm", lambda: s0_d[:], ("s0_d",))

    xn_tm_d = env["xn_tm_d"]
    x_tm = env["x_tm"]
    x1_d = env["x1_d"]
    xn1_fm_d = env["xn1_fm_d"]
    q1_fm_d = env["q1_fm_d"]
    cc1_in = env["cc1_in"]
    cc1_out = env["cc1_out"]
    s1_d = env["s1_d"]
    out_d = env["out"]
    kwq_d = P.dint("kwq", (NCH, 128, D), BF16)

    def psq(qt):
        if qt < 4:
            return P.ps[qt][:, 0:512], f"ps{qt}"
        return P.psbig[:, (qt - 4) * 512:(qt - 3) * 512], f"psb{qt - 4}"

    P.mark()
    wti6 = P.alloc(2 * D)
    b1 = P.alloc(D)
    b2 = P.alloc(D)
    b3 = P.alloc(D)
    for kc in range(NCH):
        wo = wti6 + (kc % 2) * D
        wk6 = f"w6.{kc % 2}"
        sc.add("qs", lambda E, kc=kc, wo=wo: E.dma_start(
            out=P.a(wo, D), in_=d_["ffnWT"][kc]), writes=(wk6,))
        sc.add("dve", lambda E, wo=wo: E.tensor_single_scalar(
            P.a(b1, D), P.a(wo, D), P.swcol[:, 0:1], OP.mult),
            reads=(wk6, "swcol"), writes=("b1",))
        sc.add("dve", lambda E: E.tensor_scalar(
            P.a(b2, D), P.a(b1, D), MAGIC, MAGIC, OP.add, OP.subtract),
            reads=("b1",), writes=("b2",))
        sc.add("dve", lambda E: E.tensor_copy(
            P.a(b3, D).bitcast(I32), P.a(b2, D)), reads=("b2",), writes=("b3",))
        sc.add("dve", lambda E: E.tensor_copy(
            P.a(b1, D), P.a(b3, D).bitcast(I32)), reads=("b3",), writes=("b1",))
        sc.add("dve", lambda E: E.tensor_scalar(
            P.a(b3, D // 2).bitcast(BF16), P.a(b1, D), -1.0, 1.0, OP.max, OP.min),
            reads=("b1",), writes=("b3",))
        sc.add("qp", lambda E, kc=kc: E.dma_start(
            out=kwq_d[kc], in_=P.a(b3, D // 2).bitcast(BF16)),
            reads=("b3",), writes=("kwq_d",))
    P.release()
    sc.fence()

    P.mark()
    kxT = P.alloc(NCH * TOK // 2)
    xnb = P.alloc(D)
    c1 = P.alloc(D)
    c2 = P.alloc(D)
    c3 = P.alloc(D)
    srows = P.alloc(TPC * 128)
    rcols = P.alloc(TPC)
    qcol = P.alloc(8)
    kwt = P.alloc(2 * 256)
    ffnbr = P.alloc(512)
    epi = P.alloc(4 * 512)

    def kxv(kc, qt):
        return P.a(kxT, NCH * TOK // 2).bitcast(BF16)[:, kc * TOK + qt * 128:
                                                      kc * TOK + qt * 128 + 128]

    def kwv(i):
        return P.a(kwt, 2 * 256).bitcast(BF16)[:, (i % 2) * 512:(i % 2) * 512 + 512]
    for qt in range(TPC):
        sc.add("qs", lambda E, qt=qt: E.dma_start(
            out=P.a(xnb, D), in_=xn_tm_d[qt * 128:(qt + 1) * 128, :]),
            reads=("xn_tm_d",), writes=("xnb",))
        sc.add("act", lambda E: E.activation(
            P.a(c1, D), P.a(xnb, D), AF.Square, accum_out=P.a(qcol, 1)),
            reads=("xnb",), writes=("c1", "q0"))
        sc.add("dve", lambda E: E.tensor_scalar(
            P.a(qcol + 1, 1), P.a(qcol, 1), 1.0 / D, EPS, OP.mult, OP.add),
            reads=("q0",), writes=("q1",))
        sc.add("dve", lambda E: E.reciprocal(P.a(qcol + 2, 1), P.a(qcol + 1, 1)),
               reads=("q1",), writes=("q2",))
        sc.add("act", lambda E: E.activation(
            P.a(qcol + 3, 1), P.a(qcol + 2, 1), AF.Sqrt),
            reads=("q2",), writes=("q3",))
        sc.add("dve", lambda E: E.tensor_single_scalar(
            P.a(c1, D), P.a(xnb, D), P.a(qcol + 3, 1), OP.mult),
            reads=("xnb", "q3"), writes=("c1",))
        sc.add("dve", lambda E: E.tensor_reduce(
            P.a(qcol + 4, 1), P.a(c1, D), axis=AX.X, op=OP.max,
            apply_absolute_value=True), reads=("c1",), writes=("q4",))
        sc.add("dve", lambda E: E.tensor_scalar_max(
            P.a(qcol + 5, 1), P.a(qcol + 4, 1), EPS), reads=("q4",), writes=("q5",))
        sc.add("dve", lambda E: E.reciprocal(P.a(qcol + 6, 1), P.a(qcol + 5, 1)),
               reads=("q5",), writes=("q6",))
        sc.add("dve", lambda E: E.tensor_scalar_mul(
            P.a(qcol + 6, 1), P.a(qcol + 6, 1), 127.0),
            reads=("q6",), writes=("q6",))
        sc.add("dve", lambda E: E.tensor_single_scalar(
            P.a(c2, D), P.a(c1, D), P.a(qcol + 6, 1), OP.mult),
            reads=("c1", "q6"), writes=("c2",))
        sc.add("dve", lambda E: E.tensor_scalar(
            P.a(c3, D), P.a(c2, D), MAGIC, MAGIC, OP.add, OP.subtract),
            reads=("c2",), writes=("c3",))
        sc.add("dve", lambda E: E.tensor_copy(
            P.a(c1, D).bitcast(I32), P.a(c3, D)), reads=("c3",), writes=("c1",))
        sc.add("dve", lambda E: E.tensor_copy(
            P.a(c2, D), P.a(c1, D).bitcast(I32)), reads=("c1",), writes=("c2",))
        sc.add("dve", lambda E: E.tensor_scalar(
            P.a(c3, D), P.a(c2, D), -128.0, 127.0, OP.max, OP.min),
            reads=("c2",), writes=("c3",))
        for c in range(NCH):
            pb = c % 2
            sc.add("pe", lambda E, c=c, pb=pb: E.transpose(
                P.ps[pb][:, 0:128], P.a(c3 + c * 128, 128), P.ident[:]),
                reads=("c3", "ident"), writes=(f"ps{pb}",))
            eng = "act" if c % 2 == 0 else "dve"
            sc.add(eng, lambda E, c=c, qt=qt, pb=pb, eng=eng: (
                E.copy(kxv(c, qt), P.ps[pb][:, 0:128])
                if eng == "act" else
                E.tensor_copy(kxv(c, qt), P.ps[pb][:, 0:128])),
                reads=(f"ps{pb}",), writes=(f"kxT.{c}.{qt}",))
        sc.add("dve", lambda E, qt=qt: E.tensor_tensor(
            P.a(qcol + 7, 1), P.a(qcol + 6, 1), P.swcol[:, 0:1], OP.mult),
            reads=("q6", "swcol"), writes=("q7",))
        sc.add("dve", lambda E, qt=qt: E.reciprocal(
            P.a(rcols + qt, 1), P.a(qcol + 7, 1)), reads=("q7",), writes=("rcols",))
        sc.add("pe", lambda E: E.transpose(
            P.ps[2][0:1, 0:128], P.a(qcol + 7, 1), P.ident[:]),
            reads=("q7", "ident"), writes=("ps2",))
        sc.add("dve", lambda E, qt=qt: E.tensor_copy(
            P.a(srows + qt * 128, 128, 0, 1), P.ps[2][0:1, 0:128]),
            reads=("ps2",), writes=("srows",))
    for fc in range(8):
        for kc in range(NCH):
            i = fc * NCH + kc
            sc.add("qs", lambda E, kc=kc, fc=fc, i=i: E.dma_start(
                out=kwv(i), in_=kwq_d[kc, :, fc * 512:(fc + 1) * 512]),
                reads=("kwq_d",), writes=(f"kw6.{i % 2}",))
            for qt in range(TPC):
                ps, pk = psq(qt)
                sc.add("pe", lambda E, kc=kc, qt=qt, i=i, ps=ps: E.matmul(
                    ps, kxv(kc, qt), kwv(i),
                    start=(kc == 0), stop=False),
                    reads=(f"kw6.{i % 2}", f"kxT.{kc}.{qt}"), writes=(pk,))
        sc.add("qs", lambda E, fc=fc: E.dma_start(
            out=P.a(ffnbr, 512, 0, 1),
            in_=d_["ffnb"][:, fc * 512:(fc + 1) * 512]), writes=("ffnbr",))
        for qt in range(TPC):
            ps, pk = psq(qt)
            sc.add("pe", lambda E, qt=qt, ps=ps: E.matmul(
                ps, P.a(srows + qt * 128, 128, 0, 1),
                P.a(ffnbr, 512, 0, 1), start=False, stop=True),
                reads=("srows", "ffnbr"), writes=(pk,))
            sc.add("act", lambda E, qt=qt, ps=ps: E.activation(
                P.a(epi, 512), ps, AF.Gelu, scale=P.a(rcols + qt, 1)),
                reads=(pk, "rcols"), writes=("e0",))
            sc.add("qs", lambda E, qt=qt, fc=fc: E.dma_start(
                out=P.a(epi + 512, 512),
                in_=xn_tm_d[qt * 128:(qt + 1) * 128, fc * 512:(fc + 1) * 512]),
                reads=("xn_tm_d",), writes=("e1",))
            sc.add("qs", lambda E, qt=qt, fc=fc: E.dma_start(
                out=P.a(epi + 1024, 512),
                in_=s0_d[qt * 128:(qt + 1) * 128, fc * 512:(fc + 1) * 512]),
                reads=("s0_d",), writes=("e2",))
            sc.add("qs", lambda E, qt=qt, fc=fc: E.dma_start(
                out=P.a(epi + 1536, 512),
                in_=x_tm[qt * 128:(qt + 1) * 128, fc * 512:(fc + 1) * 512]),
                writes=("e3",))
            sc.add("dve", lambda E: E.tensor_add(
                P.a(epi, 512), P.a(epi, 512), P.a(epi + 512, 512)),
                reads=("e0", "e1"), writes=("e0",))
            sc.add("act", lambda E: E.activation(
                P.a(epi + 512, 512), P.a(epi, 512), AF.Silu),
                reads=("e0",), writes=("e1",))
            sc.add("dve", lambda E: E.tensor_mul(
                P.a(epi, 512), P.a(epi + 512, 512), P.a(epi + 1024, 512)),
                reads=("e1", "e2"), writes=("e0",))
            sc.add("dve", lambda E: E.tensor_add(
                P.a(epi, 512), P.a(epi, 512), P.a(epi + 1536, 512)),
                reads=("e0", "e3"), writes=("e0",))
            sc.add("qp", lambda E, qt=qt, fc=fc: E.dma_start(
                out=x1_d[qt * 128:(qt + 1) * 128, fc * 512:(fc + 1) * 512],
                in_=P.a(epi, 512)), reads=("e0",), writes=("x1_d",))
    if "kwq" in P.taps:
        sc.add("qp", lambda E: E.dma_start(
            out=P.dram["tap_kwq"][:], in_=kwq_d[:]),
            reads=("kwq_d",), writes=("tap_kwq",))
    if "kx" in P.taps:
        sc.add("qp", lambda E: E.dma_start(
            out=P.dram["tap_kx"][:],
            in_=P.a(kxT, NCH * TOK // 2).bitcast(BF16)),
            reads=tuple(f"kxT.{c}.{q}" for c in range(NCH) for q in range(TPC)),
            writes=("tap_kx",))
    P.release()
    sc.fence()
    P.tap("x1_tm", lambda: x1_d[:], ("x1_d",))

    _ln_proj(P, dict(
        x_src=x1_d, wt=d_["w1catt"], nf=48, pfx="p7",
        xn_tm_d=None, xn_fm_d=xn1_fm_d, q_fm_d=q1_fm_d, q_chunks=32,
        cc_in=cc1_in, kv_chunks=(32, 40), bias=d_["b1cat"],
        wg=None))
    sc.fence()
    P.tap("xn1_fm", lambda: xn1_fm_d[:], ("xn1_fm_d",))

    if SIM_LOCAL_CC:
        for r in range(2):
            sc.add("qp", lambda E, r=r: E.dma_start(
                out=cc1_out[r], in_=cc1_in[:]),
                reads=("p7ccin",), writes=("cc1_out",))
    else:
        sc.add("pool", lambda E: E.collective_compute(
            "AllGather", OP.bypass, ins=[cc1_in[:]], outs=[cc1_out[:]],
            replica_groups=[[0, 1], [2, 3], [4, 5], [6, 7]]),
            reads=("p7ccin",), writes=("cc1_out",))
    sc.fence()
    P.mark()
    rk1 = P.alloc(KVH * 2048)
    v1tm = P.alloc(KVH * 2048)
    prm1 = P.alloc(128)
    cosf1 = P.alloc(2048)
    sinf1 = P.alloc(2048)
    vtmp1 = P.alloc(2048)
    w1b1 = P.alloc(2048)
    sc.add("qs", lambda E: E.dma_start(out=P.a(prm1, 128), in_=d_["permL1"][:]),
           writes=("prm1",))
    sc.add("qs", lambda E: E.dma_start(out=P.a(cosf1, 2048), in_=d_["cosL1"][:]),
           writes=("cosf1",))
    sc.add("qs", lambda E: E.dma_start(out=P.a(sinf1, 2048), in_=d_["sinL1"][:]),
           writes=("sinf1",))
    for kvh in range(KVH):
        rkk = f"rk1.{kvh}"
        for r in range(2):
            sc.add("qs", lambda E, r=r, kvh=kvh: E.dma_start(
                out=P.a(rk1 + kvh * 2048, 2048)
                    .rearrange("b (j r2 c) -> b r2 j c", r2=2, c=128)[:, r, :, :],
                in_=cc1_out[r, 0, kvh, :, :].rearrange("b (j c) -> b j c", c=128)),
                reads=("cc1_out",), writes=(rkk,))
            sc.add("qs", lambda E, r=r, kvh=kvh: E.dma_start(
                out=P.a(vtmp1, 2048)
                    .rearrange("b (j r2 c) -> b r2 j c", r2=2, c=128)[:, r, :, :],
                in_=cc1_out[r, 1, kvh, :, :].rearrange("b (j c) -> b j c", c=128)),
                reads=("cc1_out",), writes=("vtmp1",))
        for j in range(4):
            sc.add("pe", lambda E, kvh=kvh, j=j: E.matmul(
                P.psbig[:, j * 512:(j + 1) * 512], P.a(prm1, 128),
                P.a(rk1 + kvh * 2048 + j * 512, 512), start=True, stop=True),
                reads=(rkk, "prm1"), writes=(f"psb{j}",))
        sc.add("dve", lambda E: E.tensor_tensor(
            P.a(w1b1, 2048), P.psbig[:, 0:2048], P.a(sinf1, 2048), OP.mult),
            reads=("psb0", "psb1", "psb2", "psb3", "sinf1"), writes=("w1b1",))
        sc.add("dve", lambda E, kvh=kvh: E.tensor_tensor(
            P.a(rk1 + kvh * 2048, 2048), P.a(rk1 + kvh * 2048, 2048),
            P.a(cosf1, 2048), OP.mult), reads=(rkk, "cosf1"), writes=(rkk,))
        sc.add("dve", lambda E, kvh=kvh: E.tensor_tensor(
            P.a(rk1 + kvh * 2048, 2048), P.a(rk1 + kvh * 2048, 2048),
            P.a(w1b1, 2048), OP.add), reads=(rkk, "w1b1"), writes=(rkk,))
        for gt in range(16):
            pb = 2 + (gt % 2)
            sc.add("pe", lambda E, gt=gt, pb=pb: E.transpose(
                P.ps[pb][:, 0:128], P.a(vtmp1 + gt * 128, 128), P.ident[:]),
                reads=("vtmp1", "ident"), writes=(f"ps{pb}",))
            eng = "act" if gt % 2 == 0 else "dve"
            sc.add(eng, lambda E, kvh=kvh, gt=gt, pb=pb, eng=eng: (
                E.copy(P.a(v1tm + kvh * 2048 + gt * 128, 128), P.ps[pb][:, 0:128])
                if eng == "act" else
                E.tensor_copy(P.a(v1tm + kvh * 2048 + gt * 128, 128),
                              P.ps[pb][:, 0:128])),
                reads=(f"ps{pb}",), writes=(f"v1tm{kvh}",))

    clq1 = P.alloc(128)
    slq1 = P.alloc(128)
    qb9 = P.alloc(128)
    rq9 = P.alloc(128)
    pq9 = P.alloc(128)
    pb9 = P.alloc(2048)
    pt9 = P.alloc(2048)
    ob9 = P.alloc(128)
    cb9 = P.alloc(8)
    for qt in range(TPC):
        sc.add("qs", lambda E, qt=qt: E.dma_start(
            out=P.a(clq1, 128), in_=d_["coslq1"][:, qt * 128:(qt + 1) * 128]),
            writes=("clq1",))
        sc.add("qs", lambda E, qt=qt: E.dma_start(
            out=P.a(slq1, 128), in_=d_["sinlq1"][:, qt * 128:(qt + 1) * 128]),
            writes=("slq1",))
        for kvh in range(KVH):
            for g in range(G):
                h = kvh * G + g
                sc.add("qs", lambda E, h=h, qt=qt: E.dma_start(
                    out=P.a(qb9, 128), in_=q1_fm_d[h, :, qt * 128:(qt + 1) * 128]),
                    reads=("p7qfm",), writes=("qb9",))
                sc.add("pe", lambda E: E.matmul(
                    P.ps[0][:, 0:128], P.a(prm1, 128), P.a(qb9, 128),
                    start=True, stop=True), reads=("qb9", "prm1"), writes=("ps0",))
                sc.add("dve", lambda E: E.tensor_copy(P.a(pq9, 128), P.ps[0][:, 0:128]),
                       reads=("ps0",), writes=("pq9",))
                sc.add("dve", lambda E: E.tensor_tensor(
                    P.a(pq9, 128), P.a(pq9, 128), P.a(slq1, 128), OP.mult),
                    reads=("pq9", "slq1"), writes=("pq9",))
                sc.add("dve", lambda E: E.tensor_tensor(
                    P.a(rq9, 128), P.a(qb9, 128), P.a(clq1, 128), OP.mult),
                    reads=("qb9", "clq1"), writes=("rq9",))
                sc.add("dve", lambda E: E.tensor_tensor(
                    P.a(rq9, 128), P.a(rq9, 128), P.a(pq9, 128), OP.add),
                    reads=("rq9", "pq9"), writes=("rq9",))
                for t in range(4):
                    sc.add("pe", lambda E, kvh=kvh, t=t: E.matmul(
                        P.psbig[:, t * 512:(t + 1) * 512], P.a(rq9, 128),
                        P.a(rk1 + kvh * 2048 + t * 512, 512),
                        start=True, stop=True),
                        reads=("rq9", f"rk1.{kvh}"), writes=(f"psb{t}",))
                psbks = ("psb0", "psb1", "psb2", "psb3")
                sc.add("dve", lambda E: E.tensor_reduce(
                    P.a(cb9, 1), P.psbig[:, 0:2048], axis=AX.X, op=OP.max,
                    negate=True), reads=psbks, writes=("cb0",))
                sc.add("dve", lambda E: E.tensor_scalar_mul(
                    P.a(cb9 + 1, 1), P.a(cb9, 1), SCALE),
                    reads=("cb0",), writes=("cb1",))
                sc.add("act", lambda E: E.activation(
                    P.a(pb9, 2048), P.psbig[:, 0:2048], AF.Exp,
                    bias=P.a(cb9 + 1, 1), scale=SCALE,
                    accum_out=P.a(cb9 + 2, 1)),
                    reads=psbks + ("cb1",), writes=("pb9", "cb2"))
                for t in range(16):
                    pb = 1 + (t % 2)
                    sc.add("pe", lambda E, t=t, pb=pb: E.transpose(
                        P.ps[pb][:, 0:128], P.a(pb9 + t * 128, 128), P.ident[:]),
                        reads=("pb9", "ident"), writes=(f"ps{pb}",))
                    eng = "act" if t % 2 == 0 else "dve"
                    sc.add(eng, lambda E, t=t, pb=pb, eng=eng: (
                        E.copy(P.a(pt9 + t * 128, 128), P.ps[pb][:, 0:128])
                        if eng == "act" else
                        E.tensor_copy(P.a(pt9 + t * 128, 128), P.ps[pb][:, 0:128])),
                        reads=(f"ps{pb}",), writes=(f"pt9.{t}",))
                for t in range(16):
                    sc.add("pe", lambda E, t=t, kvh=kvh: E.matmul(
                        P.ps[3][:, 0:128], P.a(pt9 + t * 128, 128),
                        P.a(v1tm + kvh * 2048 + t * 128, 128),
                        start=(t == 0), stop=(t == 15)),
                        reads=(f"pt9.{t}", f"v1tm{kvh}"), writes=("ps3",))
                sc.add("dve", lambda E: E.reciprocal(P.a(cb9 + 3, 1),
                                                     P.a(cb9 + 2, 1)),
                       reads=("cb2",), writes=("cb3",))
                sc.add("dve", lambda E: E.tensor_single_scalar(
                    P.a(ob9, 128), P.ps[3][:, 0:128], P.a(cb9 + 3, 1), OP.mult),
                    reads=("ps3", "cb3"), writes=("ob9",))
                sc.add("qp", lambda E, qt=qt, h=h: E.dma_start(
                    out=s1_d[qt * 128:(qt + 1) * 128, h * 128:(h + 1) * 128],
                    in_=P.a(ob9, 128)), reads=("ob9",), writes=("s1_d",))
    P.release()
    sc.fence()
    P.tap("s1_tm", lambda: s1_d[:], ("s1_d",))

    P.mark()
    xn1r = P.alloc(NCH * TOK)
    gwb = P.alloc(NCH * 28)
    acb = P.alloc(NCH * 224)
    hTb = P.alloc(TPC * 256)
    lgb = P.alloc(28)
    eb10 = P.alloc(28)
    selb = P.alloc(28)
    gtb10 = P.alloc(28)
    hb10 = P.alloc(224)
    mrep10 = P.alloc(8)
    cb10 = P.alloc(8)
    bct = P.alloc(1024)
    bwt = P.alloc(2 * 512)
    ep10 = P.alloc(4 * 512)
    sc.add("dve", lambda E: E.memset(P.a(mrep10, 8), 1.0e38), writes=("mrep10",))
    sc.add("qs", lambda E: E.dma_start(out=P.a(gwb, NCH * 28), in_=d_["gwTt"][:]),
           writes=("gwb",))
    sc.add("qs", lambda E: E.dma_start(out=P.a(acb, NCH * 224), in_=d_["AcatTt"][:]),
           writes=("acb",))
    for c in range(NCH):
        sc.add("qs", lambda E, c=c: E.dma_start(
            out=P.a(xn1r + c * TOK, TOK), in_=xn1_fm_d[c]),
            reads=("xn1_fm_d",), writes=(f"xn1.{c}",))
    for qt in range(TPC):
        for kc in range(NCH):
            sc.add("pe", lambda E, kc=kc, qt=qt: E.matmul(
                P.ps[0][:, 0:28], P.a(xn1r + kc * TOK + qt * 128, 128),
                P.a(gwb + kc * 28, 28), start=(kc == 0), stop=(kc == NCH - 1)),
                reads=(f"xn1.{kc}", "gwb"), writes=("ps0",))
        sc.add("dve", lambda E: E.tensor_copy(P.a(lgb, 28), P.ps[0][:, 0:28]),
               reads=("ps0",), writes=("lgb",))
        sc.add("dve", lambda E: E.tensor_reduce(
            P.a(cb10, 1), P.a(lgb, 28), axis=AX.X, op=OP.max, negate=True),
            reads=("lgb",), writes=("d0",))
        sc.add("dve", lambda E: E.tensor_scalar_mul(
            P.a(cb10 + 1, 1), P.a(cb10, 1), -1.0), reads=("d0",), writes=("d1",))
        sc.add("dve", lambda E: E.tensor_copy(P.a(eb10, 28), P.a(lgb, 28)),
               reads=("lgb",), writes=("eb10",))
        sc.add("dve", lambda E: E.tensor_copy(P.a(mrep10, 1), P.a(cb10 + 1, 1)),
               reads=("d1",), writes=("mrep10",))
        sc.add("dve", lambda E: E.match_replace(
            P.a(eb10, 28), P.a(mrep10, 8), P.a(eb10, 28), NEG),
            reads=("eb10", "mrep10"), writes=("eb10",))
        sc.add("dve", lambda E: E.tensor_reduce(
            P.a(cb10 + 2, 1), P.a(eb10, 28), axis=AX.X, op=OP.max),
            reads=("eb10",), writes=("d2",))
        sc.add("dve", lambda E: E.tensor_single_scalar(
            P.a(selb, 28), P.a(lgb, 28), P.a(cb10 + 2, 1), OP.is_ge),
            reads=("lgb", "d2"), writes=("selb",))
        sc.add("act", lambda E: E.activation(
            P.a(eb10, 28), P.a(lgb, 28), AF.Exp, bias=P.a(cb10, 1)),
            reads=("lgb", "d0"), writes=("eb10",))
        sc.add("dve", lambda E: E.tensor_tensor(
            P.a(eb10, 28), P.a(eb10, 28), P.a(selb, 28), OP.mult),
            reads=("eb10", "selb"), writes=("eb10",))
        sc.add("dve", lambda E: E.tensor_reduce(
            P.a(cb10 + 3, 1), P.a(eb10, 28), axis=AX.X, op=OP.add),
            reads=("eb10",), writes=("d3",))
        sc.add("dve", lambda E: E.reciprocal(P.a(cb10 + 4, 1), P.a(cb10 + 3, 1)),
               reads=("d3",), writes=("d4",))
        sc.add("dve", lambda E: E.tensor_scalar_mul(
            P.a(cb10 + 5, 1), P.a(cb10 + 4, 1), 2.0), reads=("d4",), writes=("d5",))
        sc.add("dve", lambda E: E.tensor_single_scalar(
            P.a(gtb10, 28), P.a(eb10, 28), P.a(cb10 + 5, 1), OP.mult),
            reads=("eb10", "d5"), writes=("gtb10",))
        for kc in range(NCH):
            sc.add("pe", lambda E, kc=kc, qt=qt: E.matmul(
                P.ps[1][:, 0:224], P.a(xn1r + kc * TOK + qt * 128, 128),
                P.a(acb + kc * 224, 224), start=(kc == 0), stop=(kc == NCH - 1)),
                reads=(f"xn1.{kc}", "acb"), writes=("ps1",))
        sc.add("dve", lambda E: E.tensor_copy(P.a(hb10, 224), P.ps[1][:, 0:224]),
               reads=("ps1",), writes=("hb10",))
        sc.add("dve", lambda E: E.tensor_tensor(
            P.a(hb10, 224).rearrange("b (e r) -> b e r", r=8),
            P.a(hb10, 224).rearrange("b (e r) -> b e r", r=8),
            P.a(gtb10, 28).unsqueeze(2).to_broadcast([128, 28, 8]), OP.mult),
            reads=("hb10", "gtb10"), writes=("hb10",))
        sc.add("pe", lambda E: E.transpose(
            P.ps[2][:, 0:128], P.a(hb10, 128), P.ident[:]),
            reads=("hb10", "ident"), writes=("ps2",))
        sc.add("dve", lambda E, qt=qt: E.tensor_copy(
            P.a(hTb + qt * 256, 128), P.ps[2][:, 0:128]),
            reads=("ps2",), writes=(f"hT.{qt}",))
        sc.add("pe", lambda E: E.transpose(
            P.ps[2][0:96, 0:128], P.a(hb10 + 128, 96), P.ident[:]),
            reads=("hb10", "ident"), writes=("ps2",))
        sc.add("dve", lambda E, qt=qt: E.tensor_copy(
            P.a(hTb + qt * 256 + 128, 128, 0, 96), P.ps[2][0:96, 0:128]),
            reads=("ps2",), writes=(f"hT.{qt}",))
    for fc in range(8):
        sc.add("qs", lambda E, fc=fc: E.dma_start(
            out=P.a(bct, 512), in_=d_["Bcat"][0:128, fc * 512:(fc + 1) * 512]),
            writes=("bct0",))
        sc.add("qs", lambda E, fc=fc: E.dma_start(
            out=P.a(bct + 512, 512, 0, 96),
            in_=d_["Bcat"][128:224, fc * 512:(fc + 1) * 512]),
            writes=("bct1",))
        for kc in range(NCH):
            i = fc * NCH + kc
            sc.add("qs", lambda E, kc=kc, fc=fc, i=i: E.dma_start(
                out=P.a(bwt + (i % 2) * 512, 512), in_=d_["baseTt"][kc, fc]),
                writes=(f"bw.{i % 2}",))
            for qt in range(TPC):
                ps, pk = psq(qt)
                sc.add("pe", lambda E, kc=kc, qt=qt, i=i, ps=ps: E.matmul(
                    ps, P.a(xn1r + kc * TOK + qt * 128, 128),
                    P.a(bwt + (i % 2) * 512, 512),
                    start=(kc == 0), stop=False),
                    reads=(f"bw.{i % 2}", f"xn1.{kc}"), writes=(pk,))
        for qt in range(TPC):
            ps, pk = psq(qt)
            sc.add("pe", lambda E, qt=qt, ps=ps: E.matmul(
                ps, P.a(hTb + qt * 256, 128), P.a(bct, 512),
                start=False, stop=False),
                reads=(f"hT.{qt}", "bct0"), writes=(pk,))
            sc.add("pe", lambda E, qt=qt, ps=ps: E.matmul(
                ps, P.a(hTb + qt * 256 + 128, 128, 0, 96),
                P.a(bct + 512, 512, 0, 96), start=False, stop=True),
                reads=(f"hT.{qt}", "bct1"), writes=(pk,))
            sc.add("act", lambda E, ps=ps: E.activation(
                P.a(ep10, 512), ps, AF.Silu), reads=(pk,), writes=("f0",))
            sc.add("qs", lambda E, qt=qt, fc=fc: E.dma_start(
                out=P.a(ep10 + 512, 512),
                in_=s1_d[qt * 128:(qt + 1) * 128, fc * 512:(fc + 1) * 512]),
                reads=("s1_d",), writes=("f1",))
            sc.add("qs", lambda E, qt=qt, fc=fc: E.dma_start(
                out=P.a(ep10 + 1024, 512),
                in_=x1_d[qt * 128:(qt + 1) * 128, fc * 512:(fc + 1) * 512]),
                reads=("x1_d",), writes=("f2",))
            sc.add("dve", lambda E: E.tensor_mul(
                P.a(ep10, 512), P.a(ep10, 512), P.a(ep10 + 512, 512)),
                reads=("f0", "f1"), writes=("f0",))
            sc.add("dve", lambda E: E.tensor_add(
                P.a(ep10, 512), P.a(ep10, 512), P.a(ep10 + 1024, 512)),
                reads=("f0", "f2"), writes=("f0",))
            sc.add("qp", lambda E, qt=qt, fc=fc: E.dma_start(
                out=out_d[qt * 128:(qt + 1) * 128, fc * 512:(fc + 1) * 512],
                in_=P.a(ep10, 512)), reads=("f0",), writes=("out",))
    P.release()


def _ln_proj(P, cfg):
    sc, nc = P.sc, P.nc
    pfx = cfg["pfx"]
    nf = cfg["nf"]
    qch = cfg["q_chunks"]
    kv0, v0 = cfg["kv_chunks"]
    wt = cfg["wt"]

    P.mark()
    xg = P.alloc(2 * D)
    xnt = P.alloc(D)
    sq = P.alloc(D)
    cols = P.alloc(2 * 8)
    xnfm = P.alloc(NCH * 512)
    wcol = P.alloc(2 * D)
    ev = P.alloc(4 * 512)
    has_g = cfg.get("wg") is not None
    if has_g:
        wgb = P.alloc(NCH * 96)
        gev = P.alloc(512)
        gtm = P.alloc(4 * 96)
        bgc = P.alloc(1)
        sc.add("qs", lambda E: E.dma_start(out=P.a(wgb, NCH * 96), in_=cfg["wg"][:]),
               writes=(pfx + "wg",))
        sc.add("qs", lambda E: E.dma_start(out=P.a(bgc, 1, 0, 96), in_=cfg["bg"][:]),
               writes=(pfx + "bg",))
    has_b = cfg.get("bias") is not None
    if has_b:
        bc = P.alloc(48)
        sc.add("qs", lambda E: E.dma_start(out=P.a(bc, 48), in_=cfg["bias"][:]),
               writes=(pfx + "bias",))

    for grp in range(2):
        for j2 in range(4):
            t = grp * 4 + j2
            xk = f"{pfx}xg.{t % 2}"
            ck = f"{pfx}cols.{t % 2}"
            o = xg + (t % 2) * D
            co = cols + (t % 2) * 8
            sc.add("qs", lambda E, t=t, o=o: E.dma_start(
                out=P.a(o, D), in_=cfg["x_src"][t * 128:(t + 1) * 128, :]),
                writes=(xk,))
            sc.add("act", lambda E, o=o, co=co: E.activation(
                P.a(sq, D), P.a(o, D), AF.Square, accum_out=P.a(co + 1, 1)),
                reads=(xk,), writes=(pfx + "sq", ck + "s"))
            sc.add("dve", lambda E, o=o, co=co: E.tensor_reduce(
                P.a(co, 1), P.a(o, D), axis=AX.X, op=OP.add),
                reads=(xk,), writes=(ck,))
            sc.add("dve", lambda E, co=co: E.tensor_scalar_mul(
                P.a(co + 2, 1), P.a(co, 1), 1.0 / D), reads=(ck,), writes=(ck,))
            sc.add("dve", lambda E, co=co: E.tensor_mul(
                P.a(co + 3, 1), P.a(co + 2, 1), P.a(co + 2, 1)),
                reads=(ck,), writes=(ck,))
            sc.add("dve", lambda E, co=co: E.scalar_tensor_tensor(
                P.a(co + 4, 1), P.a(co + 1, 1), 1.0 / D, P.a(co + 3, 1),
                OP.mult, OP.subtract), reads=(ck, ck + "s"), writes=(ck,))
            sc.add("dve", lambda E, co=co: E.tensor_scalar_add(
                P.a(co + 5, 1), P.a(co + 4, 1), EPS), reads=(ck,), writes=(ck,))
            sc.add("dve", lambda E, co=co: E.reciprocal(
                P.a(co + 6, 1), P.a(co + 5, 1)), reads=(ck,), writes=(ck,))
            sc.add("act", lambda E, co=co: E.activation(
                P.a(co + 7, 1), P.a(co + 6, 1), AF.Sqrt), reads=(ck,), writes=(ck + "r",))
            sc.add("dve", lambda E, o=o, co=co: E.scalar_tensor_tensor(
                P.a(xnt, D), P.a(o, D), P.a(co + 2, 1),
                P.a(co + 7, 1).to_broadcast([128, D]), OP.subtract, OP.mult),
                reads=(xk, ck, ck + "r"), writes=(pfx + "xnt",))
            if cfg.get("xn_tm_d") is not None:
                sc.add("qp", lambda E, t=t: E.dma_start(
                    out=cfg["xn_tm_d"][t * 128:(t + 1) * 128, :], in_=P.a(xnt, D)),
                    reads=(pfx + "xnt",), writes=("xn_tm_d",))
            for c in range(NCH):
                i = t * NCH + c
                pb = 2 + (i % 2)
                sc.add("pe", lambda E, c=c, pb=pb: E.transpose(
                    P.ps[pb][:, 0:128], P.a(xnt + c * 128, 128), P.ident[:]),
                    reads=(pfx + "xnt", "ident"), writes=(f"ps{pb}",))
                sc.add("act", lambda E, c=c, j2=j2, pb=pb: E.copy(
                    P.a(xnfm + c * 512 + j2 * 128, 128), P.ps[pb][:, 0:128]),
                    reads=(f"ps{pb}",), writes=(f"{pfx}fm.{c}.{j2}",))
            if cfg.get("xn_fm_d") is not None:
                sc.add("qp", lambda E, t=t, j2=j2: E.dma_start(
                    out=cfg["xn_fm_d"][:, :, t * 128:(t + 1) * 128]
                        .transpose([1, 0, 2]),
                    in_=P.a(xnfm, NCH * 512)
                        .rearrange("b (c n) -> b c n", n=512)
                        [:, :, j2 * 128:j2 * 128 + 128]),
                    reads=tuple(f"{pfx}fm.{c}.{j2}" for c in range(NCH)),
                    writes=("xn1_fm_d",))
        fmkeys = tuple(f"{pfx}fm.{c}.{j2}" for c in range(NCH) for j2 in range(4))
        for fc in range(nf):
            i = grp * nf + fc
            wk = f"{pfx}w.{i % 2}"
            ek = f"{pfx}ev.{i % 4}"
            pk = f"ps{i % 2}"
            wo = wcol + (i % 2) * D
            eo = ev + (i % 4) * 512
            sc.add("qs", lambda E, fc=fc, wo=wo: E.dma_start(
                out=P.a(wo, D), in_=wt[fc]), writes=(wk,))
            for kc in range(NCH):
                sc.add("pe", lambda E, kc=kc, wo=wo, pk=pk: E.matmul(
                    P.ps[int(pk[2])][:, 0:512], P.a(wo + kc * 128, 128),
                    P.a(xnfm + kc * 512, 512), start=(kc == 0), stop=(kc == NCH - 1)),
                    reads=(wk,) + (fmkeys if kc == 0 else
                                   (f"{pfx}fm.{kc}.0", f"{pfx}fm.{kc}.1",
                                    f"{pfx}fm.{kc}.2", f"{pfx}fm.{kc}.3")),
                    writes=(pk,))
            if has_b:
                sc.add("act", lambda E, fc=fc, eo=eo, pk=pk: E.activation(
                    P.a(eo, 512), P.ps[int(pk[2])][:, 0:512], AF.Identity,
                    bias=P.a(bc + fc, 1)),
                    reads=(pk, pfx + "bias"), writes=(ek,))
            else:
                sc.add("act", lambda E, eo=eo, pk=pk: E.copy(
                    P.a(eo, 512), P.ps[int(pk[2])][:, 0:512]),
                    reads=(pk,), writes=(ek,))
            if fc < qch:
                sc.add("qp", lambda E, fc=fc, grp=grp, eo=eo, dst=cfg["q_fm_d"]:
                       E.dma_start(out=dst[fc, :, grp * 512:(grp + 1) * 512],
                                   in_=P.a(eo, 512)),
                       reads=(ek,), writes=(pfx + "qfm",))
            else:
                kvsel = 0 if fc < v0 else 1
                kvh = fc - (kv0 if kvsel == 0 else v0)
                sc.add("qp", lambda E, kvsel=kvsel, kvh=kvh, grp=grp, eo=eo:
                       E.dma_start(out=cfg["cc_in"][kvsel, kvh, :,
                                                    grp * 512:(grp + 1) * 512],
                                   in_=P.a(eo, 512)),
                       reads=(ek,), writes=(pfx + "ccin",))
        if has_g:
            for kc in range(NCH):
                sc.add("pe", lambda E, kc=kc: E.matmul(
                    P.psbig[0:96, 0:512], P.a(wgb + kc * 96, 96, 0, 128),
                    P.a(xnfm + kc * 512, 512), start=(kc == 0), stop=(kc == NCH - 1)),
                    reads=(pfx + "wg",) + fmkeys, writes=("psg",))
            sc.add("act", lambda E: E.activation(
                P.a(gev, 512, 0, 96), P.psbig[0:96, 0:512], AF.Sigmoid,
                bias=P.a(bgc, 1, 0, 96)),
                reads=("psg", pfx + "bg"), writes=(pfx + "gev",))
            for j2 in range(4):
                sc.add("pe", lambda E, j2=j2: E.transpose(
                    P.psbig[0:128, 512:512 + 96],
                    P.a(gev + j2 * 128, 128, 0, 96), P.ident[0:96, 0:96]),
                    reads=(pfx + "gev", "ident"), writes=("psgt",))
                sc.add("dve", lambda E, j2=j2: E.tensor_copy(
                    P.a(gtm + j2 * 96, 96), P.psbig[0:128, 512:512 + 96]),
                    reads=("psgt",), writes=(f"{pfx}gtm.{j2}",))
                t = grp * 4 + j2
                sc.add("qp", lambda E, t=t, j2=j2: E.dma_start(
                    out=cfg["gates_d"][t * 128:(t + 1) * 128, :],
                    in_=P.a(gtm + j2 * 96, 96)),
                    reads=(f"{pfx}gtm.{j2}",), writes=("gates_d",))
    P.release()


def _run_raw(inputs, taps=(), trace=False):
    staged, tiles = [], []
    for c in range(8):
        d, tg = _stage(inputs, c)
        staged.append(d)
        tiles.append(tg)
    nc = bass.Bass("TRN2", target_bir_lowering=False)
    nc, P = build(nc, taps=taps)
    res = run_bass_kernel_spmd(nc, staged, core_ids=list(range(8)), trace=trace)
    return res, tiles


def kernel(**inputs):
    staged, tiles = [], []
    for c in range(8):
        d, tg = _stage(inputs, c)
        staged.append(d)
        tiles.append(tg)
    nc = bass.Bass("TRN2", target_bir_lowering=False)
    nc, P = build(nc, taps=())
    res = run_bass_kernel_spmd(nc, staged, core_ids=list(range(8)))
    out = np.zeros((B, S, D), dtype=np.float32)
    for c in range(8):
        b = c // 2
        o = np.asarray(res.results[c]["out"]).reshape(TPC, 128, D)
        for j, gt in enumerate(tiles[c]):
            out[b, gt * 128:(gt + 1) * 128, :] = o[j]
    return out


# version 05
# speedup vs baseline: 1.0138x; 1.0138x over previous
import math

import numpy as np

import concourse.bass as bass
import concourse.mybir as mybir
from concourse.bass_utils import run_bass_kernel_spmd

F32 = mybir.dt.float32
BF16 = mybir.dt.bfloat16
I32 = mybir.dt.int32
AF = mybir.ActivationFunctionType
OP = mybir.AluOpType
AX = mybir.AxisListType

B, S, D = 4, 2048, 4096
H, KVH, DH, G = 32, 8, 128, 4
CB = 16
NT = S // 128
TPC = NT // 2
TOK = TPC * 128
NCH = D // 128
NEG = -1.0e38
SCALE = 1.0 / math.sqrt(DH)
MAGIC = 12582912.0
EPS = 1e-5
LN1E10 = math.log(1e10)
SIM_LOCAL_CC = False


class Step:
    __slots__ = ("eng", "fn", "reads", "writes", "deps", "ord", "inc", "seq")

    def __init__(self, eng, fn, reads, writes):
        self.eng = eng
        self.fn = fn
        self.reads = reads
        self.writes = writes
        self.deps = None
        self.ord = None
        self.inc = False
        self.seq = None


VENG = {
    "pe": ("tensor", False),
    "dve": ("vector", False),
    "act": ("scalar", False),
    "pool": ("gpsimd", False),
    "qs": ("sync", True),
    "qa": ("scalar", True),
    "qp": ("gpsimd", True),
}
REAL_OF = {k: v[0] for k, v in VENG.items()}


class Sched:
    def __init__(self):
        self.steps = []
        self.fences = []

    def add(self, eng, fn, reads=(), writes=()):
        assert eng in VENG
        self.steps.append(Step(eng, fn, tuple(reads), tuple(writes)))
        return len(self.steps) - 1

    def fence(self):
        self.fences.append(len(self.steps))

    def emit(self, nc):
        steps = self.steps
        n = len(steps)
        for i, st in enumerate(steps):
            st.seq = i
        last_w = {}
        readers = {}
        last_of_eng = {}
        fi = 0
        cur_fence_deps = {}
        fdeps_per = [None] * n
        for i, st in enumerate(steps):
            while fi < len(self.fences) and self.fences[fi] == i:
                cur_fence_deps = dict(last_of_eng)
                fi += 1
            fdeps_per[i] = {e: j for e, j in cur_fence_deps.items()
                            if e != st.eng or VENG[e][1]}
            deps = {}
            for k in st.reads:
                j = last_w.get(k)
                if j is not None:
                    deps[j] = True
            for k in st.writes:
                j = last_w.get(k)
                if j is not None:
                    deps[j] = True
                for r in readers.get(k, ()):
                    deps[r] = True
            for k in st.reads:
                readers.setdefault(k, []).append(i)
            for k in st.writes:
                last_w[k] = i
                readers[k] = []
            st.deps = sorted(deps)
            last_of_eng[st.eng] = i
        self._fdeps_per = fdeps_per
        for st in steps:
            if VENG[st.eng][1]:
                st.inc = True
        for i, st in enumerate(steps):
            for j in st.deps:
                d = steps[j]
                if d.eng != st.eng:
                    d.inc = True
            for j in fdeps_per[i].values():
                steps[j].inc = True
        fi = 0
        last_of_eng = {}
        for i, st in enumerate(steps):
            while fi < len(self.fences) and self.fences[fi] == i:
                for j in last_of_eng.values():
                    steps[j].inc = True
                fi += 1
            last_of_eng[st.eng] = i
        for j in last_of_eng.values():
            steps[j].inc = True
        cnt = {e: 0 for e in VENG}
        for st in steps:
            if st.inc:
                cnt[st.eng] += 1
                st.ord = cnt[st.eng]
        self.final_counts = dict(cnt)
        NQ = 8
        SEM_LIM = 20000
        nsem = {e: (NQ if VENG[e][1] else (cnt[e] + SEM_LIM - 1) // SEM_LIM)
                for e in VENG}
        step_engseq = [0] * n
        eng_seq = {e: 0 for e in VENG}
        for i, st in enumerate(steps):
            eng_seq[st.eng] += 1
            step_engseq[i] = eng_seq[st.eng]
        waits_per = [None] * n
        drains_per = [False] * n
        hw = {}
        drain_floor = {e: 0 for e in VENG}

        def qparts(o):
            return (o - 1) % NQ, (o - 1) // NQ + 1

        def qcount(o, cls):
            return (o - 1 - cls) // NQ + 1 if o - 1 >= cls else 0

        for i, st in enumerate(steps):
            waits = {}
            need_drain = False
            creal = REAL_OF[st.eng]

            def want(peng, cls, idx):
                k = (peng, cls)
                if waits.get(k, 0) < idx:
                    waits[k] = idx

            for j in st.deps:
                d = steps[j]
                if d.eng == st.eng and not VENG[st.eng][1]:
                    if st.eng in ("dve", "act", "pool"):
                        if step_engseq[j] > drain_floor[st.eng]:
                            need_drain = True
                    continue
                assert d.ord is not None
                if VENG[d.eng][1]:
                    cls, idx = qparts(d.ord)
                    want(d.eng, cls, idx)
                else:
                    want(d.eng, None, d.ord)
            for e, j in self._fdeps_per[i].items():
                d = steps[j]
                assert d.ord is not None
                if VENG[e][1]:
                    for cls in range(NQ):
                        c = qcount(d.ord, cls)
                        if c:
                            want(e, cls, c)
                else:
                    want(e, None, d.ord)
            final = []
            for (peng, cls), idx in sorted(waits.items(),
                                           key=lambda t: (t[0][0], str(t[0][1]))):
                if hw.get((creal, peng, cls), 0) >= idx:
                    continue
                hw[(creal, peng, cls)] = idx
                final.append((peng, cls, idx))
            waits_per[i] = final
            if need_drain:
                drains_per[i] = True
                drain_floor[st.eng] = step_engseq[i] - 1
        self._waits_per = waits_per
        self._drains_per = drains_per
        sems = {}
        for e in VENG:
            sems[e] = [nc.semaphore(f"S_{e}_{k}") for k in range(max(1, nsem[e]))]
        sem_objs = {e: [cm.__enter__() for cm in sems[e]] for e in VENG}

        def wait_args(peng, cls, idx):
            if VENG[peng][1]:
                return sem_objs[peng][cls], idx * 16
            si, rem = (idx - 1) // SEM_LIM, (idx - 1) % SEM_LIM + 1
            return sem_objs[peng][si], rem

        with nc.Block() as blk:
            for real in ("sync", "tensor", "vector", "scalar", "gpsimd"):
                mysteps = [i for i, st in enumerate(steps) if REAL_OF[st.eng] == real]

                def emit_stream(E, mysteps=mysteps, real=real):
                    for i in mysteps:
                        st = steps[i]
                        for peng, cls, idx in waits_per[i]:
                            s, v = wait_args(peng, cls, idx)
                            E.wait_ge(s, v)
                        if drains_per[i]:
                            E.drain()
                        if st.inc and VENG[st.eng][1]:
                            cls, idx = qparts(st.ord)
                            if idx >= 2:
                                E.wait_ge(sem_objs[st.eng][cls], (idx - 1) * 16)
                        instr = st.fn(E)
                        if st.inc:
                            if VENG[st.eng][1]:
                                cls, _ = qparts(st.ord)
                                instr.then_inc(sem_objs[st.eng][cls], 16)
                            else:
                                si = (st.ord - 1) // SEM_LIM
                                instr.then_inc(sem_objs[st.eng][si], 1)
                    for e in VENG:
                        if REAL_OF[e] == real and VENG[e][1] and self.final_counts[e]:
                            o = self.final_counts[e]
                            for cls in range(NQ):
                                c = qcount(o, cls)
                                if c:
                                    s, v = wait_args(e, cls, c)
                                    E.wait_ge(s, v)

                getattr(blk, real)(emit_stream)
        for e in VENG:
            for cm in sems[e]:
                cm.__exit__(None, None, None)


def _rope_tables():
    pos = np.arange(S, dtype=np.float64)
    inv = 1.0 / (10000.0 ** (np.arange(0, DH, 2, dtype=np.float64) / DH))
    ang = pos[None, :] * inv[:, None]
    c, s = np.cos(ang), np.sin(ang)
    cosL0 = np.zeros((DH, S))
    sinL0 = np.zeros((DH, S))
    cosL0[0::2] = c
    cosL0[1::2] = c
    sinL0[0::2] = -s
    sinL0[1::2] = s
    cosL1 = np.zeros((DH, S))
    sinL1 = np.zeros((DH, S))
    cosL1[0:64] = c
    cosL1[64:128] = c
    sinL1[0:64] = -s
    sinL1[64:128] = s
    permL0 = np.zeros((DH, DH))
    for i in range(0, DH, 2):
        permL0[i, i + 1] = 1.0
        permL0[i + 1, i] = 1.0
    permL1 = np.zeros((DH, DH))
    for m in range(DH):
        permL1[(m + 64) % 128, m] = 1.0
    f = np.float32
    return (cosL0.astype(f), sinL0.astype(f), cosL1.astype(f), sinL1.astype(f),
            permL0.astype(f), permL1.astype(f))


def _masks(tiles_g):
    cm = np.zeros((TPC, 128, 129), dtype=np.float32)
    for j, gt in enumerate(tiles_g):
        q = gt * 128 + np.arange(128)
        blk_last = (np.arange(128) + 1) * CB - 1
        cm[j, :, 1:] = np.where(blk_last[None, :] < q[:, None], 0.0, NEG)
    i = np.arange(128)
    dm = np.where((i[:, None] // CB == i[None, :] // CB)
                  & (i[None, :] % CB <= i[:, None] % CB), 0.0, NEG).astype(np.float32)
    jj = np.arange(256)
    sb = np.where((jj[None, :] >= i[:, None] + 64)
                  & (jj[None, :] <= i[:, None] + 128), 0.0, NEG).astype(np.float32)
    jj0 = np.arange(128)
    sb0 = np.where((jj0[None, :] >= i[:, None] - 64)
                   & (jj0[None, :] <= i[:, None]), 0.0, NEG).astype(np.float32)
    return cm, dm, sb, sb0


def _tile_w(wT, nf):
    Dk, F = wT.shape
    nc_ = Dk // 128
    t = wT.reshape(nc_, 128, nf, 128).transpose(2, 1, 0, 3).reshape(nf, 128, nc_ * 128)
    return np.ascontiguousarray(t)


def _stage(inputs, core):
    b, p = core // 2, core % 2
    tiles_g = [p + 2 * j for j in range(TPC)]
    idx = np.concatenate([np.arange(gt * 128, gt * 128 + 128) for gt in tiles_g])
    f32c = lambda a: np.ascontiguousarray(np.asarray(a), dtype=np.float32)

    d = {}
    d["x_tm"] = f32c(np.asarray(inputs["L0_attn_hidden_states"])[b][idx])

    wqkvT = np.asarray(inputs["L0_attn_w_qkv"]).T.astype(np.float32)
    d["wqkvTt"] = _tile_w(wqkvT, 48)
    d["wgTt"] = f32c(np.asarray(inputs["L0_attn_w_gate"]).T
                     .reshape(NCH, 128, 96).transpose(1, 0, 2).reshape(128, NCH * 96))
    d["bg"] = f32c(np.asarray(inputs["L0_attn_b_gate"]).reshape(96, 1))
    woT = np.asarray(inputs["L0_attn_w_o"]).T.astype(np.float32)
    d["woTt"] = f32c(woT.reshape(NCH, 128, 8, 512).transpose(0, 2, 1, 3))
    d["kw1Tt"] = f32c(np.asarray(inputs["L0_attn_k_w1"]).T
                      .reshape(16, 128, 16, 128).transpose(2, 1, 0, 3).reshape(16, 128, 2048))
    d["vw1Tt"] = f32c(np.asarray(inputs["L0_attn_v_w1"]).T
                      .reshape(16, 128, 16, 128).transpose(2, 1, 0, 3).reshape(16, 128, 2048))
    d["kw2Tt"] = f32c(np.asarray(inputs["L0_attn_k_w2"]).T.reshape(16, 128, 128))
    d["vw2Tt"] = f32c(np.asarray(inputs["L0_attn_v_w2"]).T.reshape(16, 128, 128))
    d["kb1"] = f32c(np.asarray(inputs["L0_attn_k_b1"]).reshape(16, 128).T)
    d["vb1"] = f32c(np.asarray(inputs["L0_attn_v_b1"]).reshape(16, 128).T)
    d["kb2"] = f32c(np.asarray(inputs["L0_attn_k_b2"]).reshape(128, 1))
    d["vb2"] = f32c(np.asarray(inputs["L0_attn_v_b2"]).reshape(1, 128))
    d["kpos"] = f32c(np.asarray(inputs["L0_attn_k_pos"]).transpose(0, 2, 1)
                     .transpose(1, 0, 2).reshape(128, KVH * 16))
    d["vpos"] = f32c(np.asarray(inputs["L0_attn_v_pos"]).transpose(0, 2, 1)
                     .transpose(1, 0, 2).reshape(128, KVH * 16))
    mk = np.asarray(inputs["L0_attn_mem_kv"])
    d["memk"] = f32c(mk[0][:, 0, :].T)
    d["memv"] = f32c(mk[1][:, 0, :].reshape(1, KVH * 128))
    d["ffnWT"] = f32c(np.asarray(inputs["L0_ffn_W"]).T.reshape(NCH, 128, D))
    d["ffnb"] = f32c(np.asarray(inputs["L0_ffn_b"]).reshape(1, D))

    w1cat = np.concatenate([np.asarray(inputs["L1_attn_wq"]),
                            np.asarray(inputs["L1_attn_wk"]),
                            np.asarray(inputs["L1_attn_wv"])], axis=1).astype(np.float32)
    d["w1catt"] = _tile_w(w1cat, 48)
    bcat = np.concatenate([np.asarray(inputs["L1_attn_bq"]),
                           np.asarray(inputs["L1_attn_bk"]),
                           np.asarray(inputs["L1_attn_bv"])]).astype(np.float32)
    d["b1cat"] = f32c(bcat.reshape(48, 128).T)
    baseT = np.asarray(inputs["L1_ffn_base_W"]).T.astype(np.float32)
    d["baseTt"] = f32c(baseT.reshape(NCH, 128, 8, 512).transpose(0, 2, 1, 3))
    d["gwTt"] = f32c(np.asarray(inputs["L1_ffn_gate_W"]).T
                     .reshape(NCH, 128, 28).transpose(1, 0, 2).reshape(128, NCH * 28))
    d["AcatTt"] = f32c(np.asarray(inputs["L1_ffn_lora_A"]).reshape(224, D).T
                       .reshape(NCH, 128, 224).transpose(1, 0, 2).reshape(128, NCH * 224))
    d["Bcat"] = f32c(np.asarray(inputs["L1_ffn_lora_B"]).transpose(0, 2, 1).reshape(224, D))

    cosL0, sinL0, cosL1, sinL1, permL0, permL1 = _rope_tables()
    d["cosL0"], d["sinL0"], d["cosL1"], d["sinL1"] = cosL0, sinL0, cosL1, sinL1
    d["coslq0"] = f32c(cosL0[:, idx])
    d["sinlq0"] = f32c(sinL0[:, idx])
    d["coslq1"] = f32c(cosL1[:, idx])
    d["sinlq1"] = f32c(sinL1[:, idx])
    d["permL0"], d["permL1"] = permL0, permL1
    d["ident"] = np.eye(128, dtype=np.float32)
    d["onesrow"] = np.ones((1, 128), dtype=np.float32)
    d["onescol"] = np.ones((128, 1), dtype=np.float32)
    cm, dm, sbm, sbm0 = _masks(tiles_g)
    d["cmask"] = f32c(cm.transpose(1, 0, 2).reshape(128, TPC * 129))
    d["diagm"] = dm
    d["sband"] = sbm
    d["sband0"] = sbm0
    d2 = np.full((TPC, 128, 256), NEG, dtype=np.float32)
    sbm2 = np.full((TPC, 128, 384), NEG, dtype=np.float32)
    ii = np.arange(128)
    for qt in range(TPC):
        gt = p + 2 * qt
        q = gt * 128 + ii
        key2 = 2 * qt * 128 + np.arange(256)
        vis = (key2[None, :] // CB == q[:, None] // CB) \
            & (key2[None, :] % CB <= q[:, None] % CB)
        d2[qt] = np.where(vis, 0.0, NEG)
        soff = max(0, 2 * qt - 1) * 128
        key3 = soff + np.arange(384)
        visb = (key3[None, :] >= q[:, None] - 64) & (key3[None, :] <= q[:, None])
        sbm2[qt] = np.where(visb, 0.0, NEG)
    d["diag2"] = d2
    d["sbandm"] = sbm2
    return d, tiles_g


class Prog:

    def __init__(self, nc, taps=()):
        self.nc = nc
        self.sc = Sched()
        self.dram = {}
        self.taps = set(taps)
        self._ctxs = []
        self.ARENA_COLS = 51712
        self.arena = self._sbuf("ARENA", (128, self.ARENA_COLS))
        self._abase = 0
        self._amarks = []
        self.ident = self._sbuf("c_ident", (128, 128))
        self.ones1 = self._sbuf("c_ones", (1, 128))
        self.onesc = self._sbuf("c_onesc", (128, 1))
        self.swcol = self._sbuf("c_swcol", (128, 2))
        self.ps = [self._psum(f"PSB{i}", (128, 512)) for i in range(4)]
        self.psbig = self._psum("PSBIG", (128, 2048))

    def _sbuf(self, name, shape, dtype=F32):
        cm = self.nc.sbuf_tensor(name, list(shape), dtype)
        t = cm.__enter__()
        self._ctxs.append(cm)
        return t

    def _psum(self, name, shape, dtype=F32):
        cm = self.nc.psum_tensor(name, list(shape), dtype)
        t = cm.__enter__()
        self._ctxs.append(cm)
        return t

    def close(self):
        for cm in reversed(self._ctxs):
            cm.__exit__(None, None, None)

    def alloc(self, ncols):
        base = self._abase
        assert base + ncols <= self.ARENA_COLS, (base, ncols)
        self._abase += ncols
        return base

    def mark(self):
        self._amarks.append(self._abase)

    def release(self):
        self._abase = self._amarks.pop()

    def a(self, base, ncols, p0=0, np_=128):
        return self.arena[p0:p0 + np_, base:base + ncols]

    def din(self, name, shape, dtype=F32):
        ap = self.nc.dram_tensor(name, list(shape), dtype, kind="ExternalInput").ap()
        self.dram[name] = ap
        return ap

    def dout(self, name, shape, dtype=F32):
        ap = self.nc.dram_tensor(name, list(shape), dtype, kind="ExternalOutput").ap()
        self.dram[name] = ap
        return ap

    def dint(self, name, shape, dtype=F32):
        ap = self.nc.dram_tensor(name, list(shape), dtype, kind="Internal").ap()
        self.dram[name] = ap
        return ap

    def tap(self, name, src_fn, reads):
        if name in self.taps:
            dst = self.dram["tap_" + name]
            self.sc.add("qp", lambda E, s=src_fn, d=dst: E.dma_start(out=d[:], in_=s()),
                        reads=reads, writes=("tap_" + name,))


TAP_SHAPES = {
    "xn_tm": (TOK, D), "q0_fm": (NCH, 128, TOK), "gates_tm": (TOK, 96),
    "kfull": (KVH, 128, S), "vfull": (KVH, 128, S),
    "rk": (KVH, 128, S), "ck": (KVH, 128, 129), "cvtm": (KVH, 128, 128),
    "attn_acc": (TOK, D), "s0_tm": (TOK, D), "x1_tm": (TOK, D),
    "xn1_fm": (NCH, 128, TOK), "s1_tm": (TOK, D), "sw": (1, 3),
    "imp": (KVH, 128, 128), "fsel": (KVH, 128, 128),
    "kwq": (NCH, 128, D), "kx": (128, NCH * TOK),
}


def build(nc, taps=()):
    P = Prog(nc, taps)
    sc = P.sc
    d_ = {}

    def din(n, s, dt=F32):
        d_[n] = P.din(n, s, dt)
        return d_[n]

    x_tm = din("x_tm", (TOK, D))
    wqkvTt = din("wqkvTt", (48, 128, D))
    wgTt = din("wgTt", (128, NCH * 96))
    bg = din("bg", (96, 1))
    woTt = din("woTt", (NCH, 8, 128, 512))
    kw1Tt = din("kw1Tt", (16, 128, 2048))
    vw1Tt = din("vw1Tt", (16, 128, 2048))
    kw2Tt = din("kw2Tt", (16, 128, 128))
    vw2Tt = din("vw2Tt", (16, 128, 128))
    kb1 = din("kb1", (128, 16))
    vb1 = din("vb1", (128, 16))
    kb2 = din("kb2", (128, 1))
    vb2 = din("vb2", (1, 128))
    kpos = din("kpos", (128, KVH * 16))
    vpos = din("vpos", (128, KVH * 16))
    memk = din("memk", (128, KVH))
    memv = din("memv", (1, KVH * 128))
    ffnWT = din("ffnWT", (NCH, 128, D))
    ffnb = din("ffnb", (1, D))
    w1catt = din("w1catt", (48, 128, D))
    b1cat = din("b1cat", (128, 48))
    baseTt = din("baseTt", (NCH, 8, 128, 512))
    gwTt = din("gwTt", (128, NCH * 28))
    AcatTt = din("AcatTt", (128, NCH * 224))
    Bcat = din("Bcat", (224, D))
    cosL0 = din("cosL0", (128, S))
    sinL0 = din("sinL0", (128, S))
    cosL1 = din("cosL1", (128, S))
    sinL1 = din("sinL1", (128, S))
    coslq0 = din("coslq0", (128, TOK))
    sinlq0 = din("sinlq0", (128, TOK))
    coslq1 = din("coslq1", (128, TOK))
    sinlq1 = din("sinlq1", (128, TOK))
    permL0 = din("permL0", (128, 128))
    permL1 = din("permL1", (128, 128))
    identD = din("ident", (128, 128))
    onesD = din("onesrow", (1, 128))
    onescD = din("onescol", (128, 1))
    cmaskD = din("cmask", (128, TPC * 129))
    diagD = din("diagm", (128, 128))
    sbandD = din("sband", (128, 256))
    sband0D = din("sband0", (128, 128))
    diag2D = din("diag2", (TPC, 128, 256))
    sbandmD = din("sbandm", (TPC, 128, 384))

    out = P.dout("out", (TOK, D))
    for t in P.taps:
        P.dout("tap_" + t, TAP_SHAPES[t])

    xn_tm_d = P.dint("xn_tm", (TOK, D))
    q0_fm_d = P.dint("q0_fm", (NCH, 128, TOK))
    gates_d = P.dint("gates_tm", (TOK, 96))
    cc0_in = P.dint("cc0_in", (2, KVH, 128, TOK))
    cc0_out = P.dint("cc0_out", (2, 2, KVH, 128, TOK))
    acc_d = P.dint("attn_acc", (TOK, D))
    s0_d = P.dint("s0_tm", (TOK, D))
    x1_d = P.dint("x1_tm", (TOK, D))
    xn1_fm_d = P.dint("xn1_fm", (NCH, 128, TOK))
    q1_fm_d = P.dint("q1_fm", (NCH, 128, TOK))
    cc1_in = P.dint("cc1_in", (2, KVH, 128, TOK))
    cc1_out = P.dint("cc1_out", (2, 2, KVH, 128, TOK))
    s1_d = P.dint("s1_tm", (TOK, D))

    sc.add("qs", lambda E: E.dma_start(out=P.ident[:], in_=identD[:]),
           writes=("ident",))
    sc.add("qs", lambda E: E.dma_start(out=P.ones1[:], in_=onesD[:]),
           writes=("ones1",))
    sc.add("qs", lambda E: E.dma_start(out=P.onesc[:], in_=onescD[:]),
           writes=("onesc",))

    P.mark()
    w0 = P.alloc(2 * D)
    asum = P.alloc(2)
    srow = P.alloc(4)
    for i in range(NCH):
        sc.add("qs", lambda E, i=i: E.dma_start(
            out=P.a(w0 + (i % 2) * D, D), in_=ffnWT[i]),
            reads=(), writes=(f"w0.{i % 2}",))
        sc.add("dve", lambda E, i=i: E.tensor_reduce(
            P.a(asum + 1, 1), P.a(w0 + (i % 2) * D, D), axis=AX.X, op=OP.add,
            apply_absolute_value=True),
            reads=(f"w0.{i % 2}",), writes=("asum1",))
        if i == 0:
            sc.add("dve", lambda E: E.tensor_copy(P.a(asum, 1), P.a(asum + 1, 1)),
                   reads=("asum1",), writes=("asum0",))
        else:
            sc.add("dve", lambda E: E.tensor_add(
                P.a(asum, 1), P.a(asum, 1), P.a(asum + 1, 1)),
                reads=("asum0", "asum1"), writes=("asum0",))
    sc.add("pe", lambda E: E.matmul(P.ps[0][0:1, 0:1], P.onesc[:],
                                    P.a(asum, 1), start=True, stop=True),
           reads=("asum0", "onesc"), writes=("ps0",))
    sc.add("act", lambda E: E.mul(P.a(srow, 1, 0, 1), P.ps[0][0:1, 0:1],
                                  1.0 / (D * D)),
           reads=("ps0",), writes=("srow",))
    sc.add("dve", lambda E: E.tensor_scalar_max(
        P.a(srow + 1, 1, 0, 1), P.a(srow, 1, 0, 1), EPS),
        reads=("srow",), writes=("srow",))
    sc.add("dve", lambda E: E.reciprocal(P.a(srow + 2, 1, 0, 1),
                                         P.a(srow + 1, 1, 0, 1)),
           reads=("srow",), writes=("srow",))
    sc.add("pe", lambda E: E.matmul(P.ps[0][:, 0:2], P.ones1[:],
                                    P.a(srow + 1, 2, 0, 1), start=True, stop=True),
           reads=("srow", "ones1"), writes=("ps0",))
    sc.add("dve", lambda E: E.tensor_copy(P.swcol[:, 0:1], P.ps[0][:, 1:2]),
           reads=("ps0",), writes=("swcol",))
    sc.add("dve", lambda E: E.tensor_copy(P.swcol[:, 1:2], P.ps[0][:, 0:1]),
           reads=("ps0",), writes=("swcol",))
    P.tap("sw", lambda: P.a(srow, 3, 0, 1), ("srow",))
    sc.fence()
    P.release()

    _ln_proj(P, dict(
        x_src=x_tm, wt=wqkvTt, nf=48, pfx="p1",
        xn_tm_d=xn_tm_d, xn_fm_d=None, q_fm_d=q0_fm_d, q_chunks=32,
        cc_in=cc0_in, kv_chunks=(32, 40), bias=None,
        wg=wgTt, bg=bg, gates_d=gates_d))
    sc.fence()
    P.tap("xn_tm", lambda: xn_tm_d[:], ("xn_tm_d",))
    P.tap("q0_fm", lambda: q0_fm_d[:], ("q0_fm_d",))
    P.tap("gates_tm", lambda: gates_d[:], ("gates_d",))

    if SIM_LOCAL_CC:
        for r in range(2):
            sc.add("qp", lambda E, r=r: E.dma_start(
                out=cc0_out[r], in_=cc0_in[:]),
                reads=("cc0_in",), writes=("cc0_out",))
    else:
        sc.add("pool", lambda E: E.collective_compute(
            "AllGather", OP.bypass, ins=[cc0_in[:]], outs=[cc0_out[:]],
            replica_groups=[[0, 1], [2, 3], [4, 5], [6, 7]]),
            reads=("cc0_in",), writes=("cc0_out",))
    sc.fence()
    for nm, kv in (("kfull", 0), ("vfull", 1)):
        if nm not in P.taps:
            continue
        for kvh in range(KVH):
            for r in range(2):
                sc.add("qp", lambda E, nm=nm, kv=kv, kvh=kvh, r=r: E.dma_start(
                    out=P.dram["tap_" + nm][kvh]
                        .rearrange("b (j r2 c) -> b r2 j c", r2=2, c=128)[:, r, :, :],
                    in_=cc0_out[r, kv, kvh, :, :]
                        .rearrange("b (j c) -> b j c", c=128)),
                    reads=("cc0_out",), writes=("tap_" + nm,))

    _build_rest(P, d_, locals())

    sc.emit(nc)
    P.close()
    return nc, P


def _build_rest(P, d_, env):
    sc = P.sc
    cc0_out = env["cc0_out"]
    acc_d = env["acc_d"]
    gates_d = env["gates_d"]
    q0_fm_d = env["q0_fm_d"]
    s0_d = env["s0_d"]

    P.mark()
    rk = P.alloc(KVH * 2048)
    vtm = P.alloc(KVH * 2048)
    ckb = P.alloc(KVH * 129)
    cvb = P.alloc(KVH * 128)
    prm = P.alloc(128)
    kposb = P.alloc(KVH * 16)
    vposb = P.alloc(KVH * 16)
    kb1b = P.alloc(16)
    vb1b = P.alloc(16)
    kb2b = P.alloc(1)
    vb2b = P.alloc(128)
    memkb = P.alloc(KVH)
    memvb = P.alloc(KVH * 128)
    for dst, src, key in (
            (P.a(prm, 128), d_["permL0"], "prm"),
            (P.a(kposb, KVH * 16), d_["kpos"], "kposb"),
            (P.a(vposb, KVH * 16), d_["vpos"], "vposb"),
            (P.a(kb1b, 16), d_["kb1"], "kb1b"),
            (P.a(vb1b, 16), d_["vb1"], "vb1b"),
            (P.a(kb2b, 1), d_["kb2"], "kb2b"),
            (P.a(vb2b, 128, 0, 1), d_["vb2"], "vb2b"),
            (P.a(memkb, KVH), d_["memk"], "memkb"),
            (P.a(memvb, KVH * 128, 0, 1), d_["memv"], "memvb"),
    ):
        sc.add("qs", lambda E, dst=dst, src=src: E.dma_start(out=dst, in_=src[:]),
               writes=(key,))

    P.mark()
    cosf = P.alloc(2048)
    sinf = P.alloc(2048)
    vtmp = P.alloc(2048)
    w1b = P.alloc(2048)
    w2b = P.alloc(2048)
    kw1b = P.alloc(2048)
    kw2b = P.alloc(2 * 128)
    sc.add("qs", lambda E: E.dma_start(out=P.a(cosf, 2048), in_=d_["cosL0"][:]),
           writes=("cosf",))
    sc.add("qs", lambda E: E.dma_start(out=P.a(sinf, 2048), in_=d_["sinL0"][:]),
           writes=("sinf",))

    def kv_load(dst_base, dst_key, kv, kvh, cc, eng="qs"):
        for r in range(2):
            sc.add(eng, lambda E, r=r: E.dma_start(
                out=P.a(dst_base, 2048)
                    .rearrange("b (j r2 c) -> b r2 j c", r2=2, c=128)[:, r, :, :],
                in_=cc[r, kv, kvh, :, :].rearrange("b (j c) -> b j c", c=128)),
                reads=(cc is cc0_out and "cc0_out" or "cc1_out",),
                writes=(dst_key,))

    nw2 = 0
    for kvh in range(KVH):
        rkk = f"rk{kvh}"
        kv_load(rk + kvh * 2048, rkk, 0, kvh, cc0_out)
        kv_load(vtmp, "vtmp", 1, kvh, cc0_out)
        sc.add("dve", lambda E, kvh=kvh: E.tensor_tensor(
            P.a(w1b, 2048).rearrange("b (c i) -> b c i", i=16),
            P.a(rk + kvh * 2048, 2048).rearrange("b (c i) -> b c i", i=16),
            P.a(kposb + kvh * 16, 16).unsqueeze(1).to_broadcast([128, 128, 16]),
            OP.add), reads=(rkk, "kposb"), writes=("w1",))
        for hc in range(16):
            i = kvh * 16 + hc
            sc.add("qs", lambda E, hc=hc, i=i: E.dma_start(
                out=P.a(kw1b, 2048), in_=d_["kw1Tt"][hc]),
                writes=("kw1",))
            for jc in range(16):
                sc.add("pe", lambda E, hc=hc, jc=jc, i=i: E.matmul(
                    P.ps[0][:, 0:128],
                    P.a(kw1b + jc * 128, 128),
                    P.a(w1b, 2048).rearrange("b (c i2) -> b i2 c", i2=16)[:, jc, :],
                    start=(jc == 0), stop=(jc == 15)),
                    reads=("kw1", "w1"), writes=("ps0",))
            sc.add("act", lambda E, hc=hc: E.activation(
                P.a(w2b + hc * 128, 128), P.ps[0][:, 0:128], AF.Relu,
                bias=P.a(kb1b + hc, 1)),
                reads=("ps0", "kb1b"), writes=("w2",))
        for hc in range(16):
            i2 = nw2
            nw2 += 1
            sc.add("qs", lambda E, hc=hc, i2=i2: E.dma_start(
                out=P.a(kw2b + (i2 % 2) * 128, 128), in_=d_["kw2Tt"][hc]),
                writes=(f"kw2.{i2 % 2}",))
            sc.add("pe", lambda E, hc=hc, i2=i2: E.matmul(
                P.ps[1][:, 0:128], P.a(kw2b + (i2 % 2) * 128, 128),
                P.a(w2b + hc * 128, 128), start=(hc == 0), stop=(hc == 15)),
                reads=(f"kw2.{i2 % 2}", "w2"), writes=("ps1",))
        sc.add("act", lambda E, kvh=kvh: E.activation(
            P.a(ckb + kvh * 129 + 1, 128), P.ps[1][:, 0:128], AF.Identity,
            bias=P.a(kb2b, 1)), reads=("ps1", "kb2b"), writes=(f"ck{kvh}",))
        sc.add("dve", lambda E, kvh=kvh: E.tensor_copy(
            P.a(ckb + kvh * 129, 1), P.a(memkb + kvh, 1)),
            reads=("memkb",), writes=(f"ck{kvh}",))
        for j in range(4):
            sc.add("pe", lambda E, kvh=kvh, j=j: E.matmul(
                P.psbig[:, j * 512:(j + 1) * 512], P.a(prm, 128),
                P.a(rk + kvh * 2048 + j * 512, 512), start=True, stop=True),
                reads=(rkk, "prm"), writes=(f"psb{j}",))
        sc.add("dve", lambda E, kvh=kvh: E.tensor_tensor(
            P.a(w1b, 2048), P.psbig[:, 0:2048], P.a(sinf, 2048), OP.mult),
            reads=("psb0", "psb1", "psb2", "psb3", "sinf"), writes=("w1",))
        sc.add("dve", lambda E, kvh=kvh: E.tensor_tensor(
            P.a(rk + kvh * 2048, 2048), P.a(rk + kvh * 2048, 2048),
            P.a(cosf, 2048), OP.mult), reads=(rkk, "cosf"), writes=(rkk,))
        sc.add("dve", lambda E, kvh=kvh: E.tensor_tensor(
            P.a(rk + kvh * 2048, 2048), P.a(rk + kvh * 2048, 2048),
            P.a(w1b, 2048), OP.add), reads=(rkk, "w1"), writes=(rkk,))
        sc.add("dve", lambda E, kvh=kvh: E.tensor_tensor(
            P.a(w1b, 2048).rearrange("b (c i) -> b c i", i=16),
            P.a(vtmp, 2048).rearrange("b (c i) -> b c i", i=16),
            P.a(vposb + kvh * 16, 16).unsqueeze(1).to_broadcast([128, 128, 16]),
            OP.add), reads=("vtmp", "vposb"), writes=("w1",))
        for hc in range(16):
            i = KVH * 16 + kvh * 16 + hc
            sc.add("qs", lambda E, hc=hc, i=i: E.dma_start(
                out=P.a(kw1b, 2048), in_=d_["vw1Tt"][hc]),
                writes=("kw1",))
            for jc in range(16):
                sc.add("pe", lambda E, hc=hc, jc=jc, i=i: E.matmul(
                    P.ps[0][:, 0:128],
                    P.a(kw1b + jc * 128, 128),
                    P.a(w1b, 2048).rearrange("b (c i2) -> b i2 c", i2=16)[:, jc, :],
                    start=(jc == 0), stop=(jc == 15)),
                    reads=("kw1", "w1"), writes=("ps0",))
            sc.add("act", lambda E, hc=hc: E.activation(
                P.a(w2b + hc * 128, 128), P.ps[0][:, 0:128], AF.Relu,
                bias=P.a(vb1b + hc, 1)),
                reads=("ps0", "vb1b"), writes=("w2",))
        for hc in range(16):
            i2 = nw2
            nw2 += 1
            sc.add("qs", lambda E, hc=hc, i2=i2: E.dma_start(
                out=P.a(kw2b + (i2 % 2) * 128, 128), in_=d_["vw2Tt"][hc]),
                writes=(f"kw2.{i2 % 2}",))
            sc.add("pe", lambda E, hc=hc, i2=i2: E.matmul(
                P.ps[1][:, 0:128], P.a(w2b + hc * 128, 128),
                P.a(kw2b + (i2 % 2) * 128, 128), start=(hc == 0), stop=False),
                reads=(f"kw2.{i2 % 2}", "w2"), writes=("ps1",))
        sc.add("pe", lambda E, kvh=kvh: E.matmul(
            P.ps[1][:, 0:128], P.ones1[:], P.a(vb2b, 128, 0, 1),
            start=False, stop=True), reads=("vb2b", "ones1"), writes=("ps1",))
        sc.add("dve", lambda E, kvh=kvh: E.tensor_copy(
            P.a(cvb + kvh * 128, 128), P.ps[1][:, 0:128]),
            reads=("ps1",), writes=(f"cv{kvh}",))
        for gt in range(16):
            pb = 2 + (gt % 2)
            sc.add("pe", lambda E, kvh=kvh, gt=gt, pb=pb: E.transpose(
                P.ps[pb][:, 0:128], P.a(vtmp + gt * 128, 128), P.ident[:]),
                reads=("vtmp", "ident"), writes=(f"ps{pb}",))
            eng = "act" if gt % 2 == 0 else "dve"
            sc.add(eng, lambda E, kvh=kvh, gt=gt, pb=pb: (
                E.copy(P.a(vtm + kvh * 2048 + gt * 128, 128), P.ps[pb][:, 0:128])
                if gt % 2 == 0 else
                E.tensor_copy(P.a(vtm + kvh * 2048 + gt * 128, 128),
                              P.ps[pb][:, 0:128])),
                reads=(f"ps{pb}",), writes=(f"vtm{kvh}",))
    P.release()
    sc.fence()
    if "rk" in P.taps:
        for kvh in range(KVH):
            sc.add("qp", lambda E, kvh=kvh: E.dma_start(
                out=P.dram["tap_rk"][kvh], in_=P.a(rk + kvh * 2048, 2048)),
                reads=(f"rk{kvh}",), writes=("tap_rk",))
    if "ck" in P.taps:
        for kvh in range(KVH):
            sc.add("qp", lambda E, kvh=kvh: E.dma_start(
                out=P.dram["tap_ck"][kvh], in_=P.a(ckb + kvh * 129, 129)),
                reads=(f"ck{kvh}",), writes=("tap_ck",))
    if "cvtm" in P.taps:
        for kvh in range(KVH):
            sc.add("qp", lambda E, kvh=kvh: E.dma_start(
                out=P.dram["tap_cvtm"][kvh], in_=P.a(cvb + kvh * 128, 128)),
                reads=(f"cv{kvh}",), writes=("tap_cvtm",))

    P.mark()
    clq = P.alloc(128)
    slq = P.alloc(128)
    cmb = P.alloc(129)
    d2b = P.alloc(256)
    sbb = P.alloc(384)
    gtb = P.alloc(96)
    q4b = P.alloc(512)
    rqb = P.alloc(2 * 128)
    accb = P.alloc(2 * 512)
    scb = P.alloc(2 * 2048)
    pb_ = P.alloc(2 * 2048)
    ptb = P.alloc(2048)
    s2b = P.alloc(256)
    csb = P.alloc(4 * 129)
    impb = P.alloc(128)
    eb = P.alloc(128)
    fselb = P.alloc(128)
    curb = P.alloc(128)
    colb = P.alloc(2 * 32)
    mempb = P.alloc(128)
    it4 = [0]
    mrep = P.alloc(8)
    sc.add("dve", lambda E: E.memset(P.a(mrep, 8), 1.0e38), writes=("mrep",))

    for qt in range(TPC):
        for dst, src, key in (
                (P.a(clq, 128), d_["coslq0"][:, qt * 128:(qt + 1) * 128], "clq"),
                (P.a(slq, 128), d_["sinlq0"][:, qt * 128:(qt + 1) * 128], "slq"),
                (P.a(cmb, 129), d_["cmask"][:, qt * 129:(qt + 1) * 129], "cmb"),
                (P.a(d2b, 256), d_["diag2"][qt], "d2b"),
                (P.a(sbb, 384), d_["sbandm"][qt], "sbb"),
                (P.a(gtb, 96), gates_d[qt * 128:(qt + 1) * 128, :], "gtb"),
        ):
            sc.add("qs", lambda E, dst=dst, src=src: E.dma_start(out=dst, in_=src),
                   writes=(key,))
        nk = (qt + 1) * 256
        nblk = nk // 16
        ntile = nk // 128
        soff = max(0, 2 * qt - 1) * 128
        for kvh in range(KVH):
            ak = f"acc{kvh % 2}"
            ao = accb + (kvh % 2) * 512
            for g in range(G):
                h = kvh * G + g
                sc.add("qs", lambda E, h=h, qt=qt, g=g: E.dma_start(
                    out=P.a(q4b + g * 128, 128),
                    in_=q0_fm_d[h, :, qt * 128:(qt + 1) * 128]),
                    writes=(f"q4.{g}",))
            for g in range(G):
                sc.add("pe", lambda E, g=g, kvh=kvh: E.matmul(
                    P.ps[0][:, 0:129], P.a(q4b + g * 128, 128),
                    P.a(ckb + kvh * 129, 129), start=True, stop=True),
                    reads=(f"q4.{g}", f"ck{kvh}"), writes=("ps0",))
                sc.add("dve", lambda E, g=g: E.scalar_tensor_tensor(
                    P.a(csb + g * 129, 129), P.ps[0][:, 0:129], SCALE,
                    P.a(cmb, 129), OP.mult, OP.add),
                    reads=("ps0", "cmb"), writes=(f"cs{g}",))
            sc.add("dve", lambda E: E.tensor_add(
                P.a(impb, 128), P.a(csb + 1, 128), P.a(csb + 129 + 1, 128)),
                reads=("cs0", "cs1"), writes=("impb",))
            sc.add("dve", lambda E: E.tensor_add(
                P.a(eb, 128), P.a(csb + 2 * 129 + 1, 128),
                P.a(csb + 3 * 129 + 1, 128)),
                reads=("cs2", "cs3"), writes=("eb",))
            sc.add("dve", lambda E: E.tensor_scalar_mul(
                P.a(impb, 128), P.a(impb, 128), 0.25),
                reads=("impb",), writes=("impb",))
            sc.add("dve", lambda E: E.tensor_scalar_mul(
                P.a(eb, 128), P.a(eb, 128), 0.25),
                reads=("eb",), writes=("eb",))
            sc.add("dve", lambda E: E.tensor_add(
                P.a(impb, 128), P.a(impb, 128), P.a(eb, 128)),
                reads=("impb", "eb"), writes=("impb",))
            sc.add("dve", lambda E: E.tensor_reduce(
                P.a(colb, 1), P.a(impb, 128), axis=AX.X, op=OP.max),
                reads=("impb",), writes=("c0",))
            sc.add("dve", lambda E: E.tensor_scalar_max(
                P.a(colb + 1, 1), P.a(colb, 1), -1000.0),
                reads=("c0",), writes=("c1",))
            sc.add("dve", lambda E: E.tensor_scalar_mul(
                P.a(colb + 2, 1), P.a(colb + 1, 1), -1.0),
                reads=("c1",), writes=("c2",))
            sc.add("act", lambda E: E.activation(
                P.a(eb, 128), P.a(impb, 128), AF.Exp,
                bias=P.a(colb + 2, 1), accum_out=P.a(colb + 3, 1)),
                reads=("impb", "c2"), writes=("eb", "c3"))
            sc.add("dve", lambda E: E.tensor_scalar_add(
                P.a(colb + 16, 1), P.a(colb + 2, 1), -1000.0),
                reads=("c2",), writes=("c16",))
            sc.add("act", lambda E: E.activation(
                P.a(colb + 4, 1), P.a(colb + 16, 1), AF.Exp),
                reads=("c16",), writes=("c4",))
            sc.add("dve", lambda E: E.tensor_add(
                P.a(colb + 5, 1), P.a(colb + 3, 1), P.a(colb + 4, 1)),
                reads=("c3", "c4"), writes=("c5",))
            sc.add("dve", lambda E: E.reciprocal(
                P.a(colb + 6, 1), P.a(colb + 5, 1)),
                reads=("c5",), writes=("c6",))
            sc.add("dve", lambda E: E.tensor_copy(P.a(curb, 128), P.a(impb, 128)),
                   reads=("impb",), writes=("curb",))
            for it in range(4):
                sc.add("dve", lambda E: E.tensor_reduce(
                    P.a(colb + 7, 1), P.a(curb, 128), axis=AX.X, op=OP.max),
                    reads=("curb",), writes=("c7",))
                if it < 3:
                    sc.add("dve", lambda E: E.tensor_copy(
                        P.a(mrep, 1), P.a(colb + 7, 1)),
                        reads=("c7",), writes=("mrep",))
                    sc.add("dve", lambda E: E.match_replace(
                        P.a(curb, 128), P.a(mrep, 8), P.a(curb, 128), NEG),
                        reads=("curb", "mrep"), writes=("curb",))
            sc.add("dve", lambda E: E.tensor_single_scalar(
                P.a(fselb, 128), P.a(impb, 128), P.a(colb + 7, 1), OP.is_ge),
                reads=("impb", "c7"), writes=("fselb",))
            sc.add("dve", lambda E: E.tensor_scalar(
                P.a(eb, 128), P.a(eb, 128), P.a(colb + 6, 1), 1e-10,
                OP.mult, OP.is_gt), reads=("eb", "c6"), writes=("eb",))
            sc.add("dve", lambda E: E.tensor_tensor(
                P.a(fselb, 128), P.a(fselb, 128), P.a(eb, 128), OP.mult),
                reads=("fselb", "eb"), writes=("fselb",))
            sc.add("dve", lambda E: E.tensor_scalar(
                P.a(fselb, 128), P.a(fselb, 128), 1.0, 1e38,
                OP.subtract, OP.mult), reads=("fselb",), writes=("fselb",))
            for g in range(G):
                h = kvh * G + g
                pr4 = it4[0] % 2
                it4[0] += 1
                RQB = rqb + pr4 * 128
                SCB = scb + pr4 * 2048
                PBB = pb_ + pr4 * 2048
                PTB = ptb
                S2B = s2b
                CLB = colb + pr4 * 32
                MPB = mempb
                KP = lambda nm: f"{nm}{pr4}"
                sc.add("dve", lambda E, g=g: E.tensor_reduce(
                    P.a(CLB + 0, 1), P.a(csb + g * 129, 129), axis=AX.X,
                    op=OP.max, negate=True), reads=(f"cs{g}",), writes=(KP("c8"),))
                sc.add("act", lambda E, g=g: E.activation(
                    P.a(PBB, 129), P.a(csb + g * 129, 129), AF.Exp,
                    bias=P.a(CLB + 0, 1), accum_out=P.a(CLB + 1, 1)),
                    reads=(f"cs{g}", KP("c8")), writes=(KP("pb0"), KP("c9")))
                sc.add("pe", lambda E: E.transpose(
                    P.ps[1][:, 0:128], P.a(PBB + 1, 128), P.ident[:]),
                    reads=(KP("pb0"), "ident"), writes=("ps1",))
                sc.add("pe", lambda E: E.transpose(
                    P.ps[2][0:1, 0:128], P.a(PBB, 1), P.ident[:]),
                    reads=(KP("pb0"), "ident"), writes=("ps2",))
                sc.add("act", lambda E: E.copy(P.a(PTB, 128), P.ps[1][:, 0:128]),
                       reads=("ps1",), writes=("ptb0",))
                sc.add("dve", lambda E: E.tensor_copy(
                    P.a(MPB, 128, 0, 1), P.ps[2][0:1, 0:128]),
                    reads=("ps2",), writes=("mempb",))
                sc.add("pe", lambda E, kvh=kvh: E.matmul(
                    P.ps[3][:, 0:128], P.a(PTB, 128), P.a(cvb + kvh * 128, 128),
                    start=True, stop=False),
                    reads=("ptb0", f"cv{kvh}"), writes=("ps3",))
                sc.add("pe", lambda E, kvh=kvh: E.matmul(
                    P.ps[3][:, 0:128], P.a(MPB, 128, 0, 1),
                    P.a(memvb + kvh * 128, 128, 0, 1),
                    start=False, stop=True),
                    reads=("mempb", "memvb"), writes=("ps3",))
                sc.add("dve", lambda E: E.reciprocal(
                    P.a(CLB + 2, 1), P.a(CLB + 1, 1)),
                    reads=(KP("c9"),), writes=(KP("c10"),))
                sc.add("dve", lambda E, h=h: E.tensor_mul(
                    P.a(CLB + 3, 1), P.a(CLB + 2, 1),
                    P.a(gtb + 3 * h, 1)), reads=(KP("c10"), "gtb"), writes=(KP("c11"),))
                sc.add("dve", lambda E, g=g, ao=ao: E.tensor_single_scalar(
                    P.a(ao + g * 128, 128), P.ps[3][:, 0:128],
                    P.a(CLB + 3, 1), OP.mult),
                    reads=("ps3", KP("c11")), writes=(ak,))
                sc.add("pe", lambda E, g=g: E.matmul(
                    P.ps[2][:, 0:128], P.a(prm, 128), P.a(q4b + g * 128, 128),
                    start=True, stop=True),
                    reads=(f"q4.{g}", "prm"), writes=("ps2",))
                sc.add("dve", lambda E: E.tensor_copy(P.a(S2B, 128), P.ps[2][:, 0:128]),
                       reads=("ps2",), writes=("s2b",))
                sc.add("dve", lambda E: E.tensor_tensor(
                    P.a(S2B, 128), P.a(S2B, 128), P.a(slq, 128), OP.mult),
                    reads=("s2b", "slq"), writes=("s2b",))
                sc.add("dve", lambda E, g=g: E.tensor_tensor(
                    P.a(RQB, 128), P.a(q4b + g * 128, 128), P.a(clq, 128),
                    OP.mult), reads=(f"q4.{g}", "clq"), writes=(KP("rqb"),))
                sc.add("dve", lambda E: E.tensor_tensor(
                    P.a(RQB, 128), P.a(RQB, 128), P.a(S2B, 128), OP.add),
                    reads=(KP("rqb"), "s2b"), writes=(KP("rqb"),))
                o = 0
                while o < nk:
                    w = min(512, nk - o)
                    sc.add("pe", lambda E, kvh=kvh, o=o, w=w: E.matmul(
                        P.psbig[:, o:o + w], P.a(RQB, 128),
                        P.a(rk + kvh * 2048 + o, w),
                        start=True, stop=True),
                        reads=(KP("rqb"), f"rk{kvh}"), writes=(f"psb{o // 512}",))
                    o += w
                psbks = tuple(f"psb{t}" for t in range((nk + 511) // 512))
                sc.add("dve", lambda E, nk=nk, nblk=nblk: E.scalar_tensor_tensor(
                    P.a(SCB, nk).rearrange("b (c i) -> b c i", i=16),
                    P.psbig[:, 0:nk].rearrange("b (c i) -> b c i", i=16), SCALE,
                    P.a(fselb, nblk).unsqueeze(2).to_broadcast([128, nblk, 16]),
                    OP.mult, OP.add),
                    reads=psbks + ("fselb",), writes=(KP("scb"),))
                sc.add("dve", lambda E, nk=nk: E.scalar_tensor_tensor(
                    P.a(S2B, 256), P.psbig[:, nk - 256:nk], SCALE,
                    P.a(d2b, 256), OP.mult, OP.add),
                    reads=psbks + ("d2b",), writes=("s2b",))
                sc.add("dve", lambda E, nk=nk: E.tensor_max(
                    P.a(SCB + nk - 256, 256), P.a(SCB + nk - 256, 256),
                    P.a(S2B, 256)), reads=(KP("scb"), "s2b"), writes=(KP("scb"),))
                sc.add("dve", lambda E, nk=nk: E.tensor_reduce(
                    P.a(CLB + 4, 1), P.a(SCB, nk), axis=AX.X, op=OP.max,
                    negate=True), reads=(KP("scb"),), writes=(KP("c12"),))
                sc.add("act", lambda E, nk=nk: E.activation(
                    P.a(PBB, nk), P.a(SCB, nk), AF.Exp,
                    bias=P.a(CLB + 4, 1), accum_out=P.a(CLB + 5, 1)),
                    reads=(KP("scb"), KP("c12")), writes=(KP("pb0"), KP("c13")))
                for t in range(ntile):
                    pb2 = 1 + (t % 2)
                    sc.add("pe", lambda E, t=t, pb2=pb2: E.transpose(
                        P.ps[pb2][:, 0:128], P.a(PBB + t * 128, 128), P.ident[:]),
                        reads=(KP("pb0"), "ident"), writes=(f"ps{pb2}",))
                    eng = "act" if t % 2 == 0 else "dve"
                    sc.add(eng, lambda E, t=t, pb2=pb2, eng=eng: (
                        E.copy(P.a(PTB + t * 128, 128), P.ps[pb2][:, 0:128])
                        if eng == "act" else
                        E.tensor_copy(P.a(PTB + t * 128, 128), P.ps[pb2][:, 0:128])),
                        reads=(f"ps{pb2}",), writes=(f"pt.{t}",))
                for t in range(ntile):
                    sc.add("pe", lambda E, t=t, kvh=kvh, ntile=ntile: E.matmul(
                        P.ps[3][:, 0:128], P.a(PTB + t * 128, 128),
                        P.a(vtm + kvh * 2048 + t * 128, 128),
                        start=(t == 0), stop=(t == ntile - 1)),
                        reads=(f"pt.{t}", f"vtm{kvh}"), writes=("ps3",))
                sc.add("dve", lambda E: E.reciprocal(
                    P.a(CLB + 6, 1), P.a(CLB + 5, 1)),
                    reads=(KP("c13"),), writes=(KP("c14"),))
                sc.add("dve", lambda E, h=h: E.tensor_mul(
                    P.a(CLB + 7, 1), P.a(CLB + 6, 1),
                    P.a(gtb + 3 * h + 1, 1)), reads=(KP("c14"), "gtb"), writes=(KP("c15"),))
                sc.add("dve", lambda E, g=g, ao=ao: E.scalar_tensor_tensor(
                    P.a(ao + g * 128, 128), P.ps[3][:, 0:128],
                    P.a(CLB + 7, 1), P.a(ao + g * 128, 128),
                    OP.mult, OP.add), reads=("ps3", KP("c15"), ak), writes=(ak,))
                sc.add("pe", lambda E, kvh=kvh, soff=soff: E.matmul(
                    P.ps[0][:, 0:384], P.a(RQB, 128),
                    P.a(rk + kvh * 2048 + soff, 384), start=True, stop=True),
                    reads=(KP("rqb"), f"rk{kvh}"), writes=("ps0",))
                sc.add("dve", lambda E: E.scalar_tensor_tensor(
                    P.a(SCB, 384), P.ps[0][:, 0:384], SCALE, P.a(sbb, 384),
                    OP.mult, OP.add), reads=("ps0", "sbb"), writes=(KP("scb"),))
                sc.add("dve", lambda E: E.tensor_reduce(
                    P.a(CLB + 4, 1), P.a(SCB, 384), axis=AX.X, op=OP.max,
                    negate=True), reads=(KP("scb"),), writes=(KP("c12"),))
                sc.add("act", lambda E: E.activation(
                    P.a(PBB, 384), P.a(SCB, 384), AF.Exp,
                    bias=P.a(CLB + 4, 1), accum_out=P.a(CLB + 5, 1)),
                    reads=(KP("scb"), KP("c12")), writes=(KP("pb0"), KP("c13")))
                for t in range(3):
                    pb2 = 1 + (t % 2)
                    sc.add("pe", lambda E, t=t, pb2=pb2: E.transpose(
                        P.ps[pb2][:, 0:128], P.a(PBB + t * 128, 128), P.ident[:]),
                        reads=(KP("pb0"), "ident"), writes=(f"ps{pb2}",))
                    eng = "act" if t % 2 == 0 else "dve"
                    sc.add(eng, lambda E, t=t, pb2=pb2, eng=eng: (
                        E.copy(P.a(PTB + t * 128, 128), P.ps[pb2][:, 0:128])
                        if eng == "act" else
                        E.tensor_copy(P.a(PTB + t * 128, 128), P.ps[pb2][:, 0:128])),
                        reads=(f"ps{pb2}",), writes=(f"pt.{t}",))
                for t in range(3):
                    sc.add("pe", lambda E, t=t, kvh=kvh, soff=soff: E.matmul(
                        P.ps[3][:, 0:128], P.a(PTB + t * 128, 128),
                        P.a(vtm + kvh * 2048 + soff + t * 128, 128),
                        start=(t == 0), stop=(t == 2)),
                        reads=(f"pt.{t}", f"vtm{kvh}"), writes=("ps3",))
                sc.add("dve", lambda E: E.reciprocal(
                    P.a(CLB + 6, 1), P.a(CLB + 5, 1)),
                    reads=(KP("c13"),), writes=(KP("c14"),))
                sc.add("dve", lambda E, h=h: E.tensor_mul(
                    P.a(CLB + 7, 1), P.a(CLB + 6, 1),
                    P.a(gtb + 3 * h + 2, 1)), reads=(KP("c14"), "gtb"), writes=(KP("c15"),))
                sc.add("dve", lambda E, g=g, ao=ao: E.scalar_tensor_tensor(
                    P.a(ao + g * 128, 128), P.ps[3][:, 0:128],
                    P.a(CLB + 7, 1), P.a(ao + g * 128, 128),
                    OP.mult, OP.add), reads=("ps3", KP("c15"), ak), writes=(ak,))
            sc.add("qp", lambda E, qt=qt, kvh=kvh, ao=ao: E.dma_start(
                out=acc_d[qt * 128:(qt + 1) * 128,
                          kvh * 512:(kvh + 1) * 512], in_=P.a(ao, 512)),
                reads=(ak,), writes=("acc_d",))
    P.release()
    P.release()
    sc.fence()
    P.tap("attn_acc", lambda: acc_d[:], ("acc_d",))

    P.mark()
    atT = P.alloc(NCH * TOK)
    actl = P.alloc(D)
    wti = P.alloc(2 * 512)
    evb = P.alloc(4 * 512)
    for qt in range(TPC):
        sc.add("qs", lambda E, qt=qt: E.dma_start(
            out=P.a(actl, D), in_=acc_d[qt * 128:(qt + 1) * 128, :]),
            reads=("acc_d",), writes=("actl",))
        for c in range(NCH):
            pb = (c % 2)
            sc.add("pe", lambda E, c=c, pb=pb: E.transpose(
                P.ps[pb][:, 0:128], P.a(actl + c * 128, 128), P.ident[:]),
                reads=("actl", "ident"), writes=(f"ps{pb}",))
            eng = "act" if c % 2 == 0 else "dve"
            sc.add(eng, lambda E, c=c, qt=qt, pb=pb, eng=eng: (
                E.copy(P.a(atT + c * TOK + qt * 128, 128), P.ps[pb][:, 0:128])
                if eng == "act" else
                E.tensor_copy(P.a(atT + c * TOK + qt * 128, 128),
                              P.ps[pb][:, 0:128])),
                reads=(f"ps{pb}",), writes=(f"atT.{c}.{qt}",))
    atkeys = lambda c: tuple(f"atT.{c}.{q}" for q in range(TPC))

    def psq(qt):
        if qt < 4:
            return P.ps[qt][:, 0:512], f"ps{qt}"
        return P.psbig[:, (qt - 4) * 512:(qt - 3) * 512], f"psb{qt - 4}"

    for fc in range(8):
        for kc in range(NCH):
            i = fc * NCH + kc
            sc.add("qs", lambda E, kc=kc, fc=fc, i=i: E.dma_start(
                out=P.a(wti + (i % 2) * 512, 512), in_=d_["woTt"][kc, fc]),
                writes=(f"wt.{i % 2}",))
            for qt in range(TPC):
                ps, pk = psq(qt)
                sc.add("pe", lambda E, kc=kc, qt=qt, i=i, ps=ps: E.matmul(
                    ps, P.a(atT + kc * TOK + qt * 128, 128),
                    P.a(wti + (i % 2) * 512, 512),
                    start=(kc == 0), stop=(kc == NCH - 1)),
                    reads=(f"wt.{i % 2}", f"atT.{kc}.{qt}"),
                    writes=(pk,))
        for qt in range(TPC):
            ps, pk = psq(qt)
            ek = f"ev.{qt % 4}"
            sc.add("act", lambda E, qt=qt, ps=ps: E.copy(
                P.a(evb + (qt % 4) * 512, 512), ps),
                reads=(pk,), writes=(ek,))
            sc.add("qp", lambda E, qt=qt, fc=fc: E.dma_start(
                out=s0_d[qt * 128:(qt + 1) * 128, fc * 512:(fc + 1) * 512],
                in_=P.a(evb + (qt % 4) * 512, 512)),
                reads=(ek,), writes=("s0_d",))
    P.release()
    sc.fence()
    P.tap("s0_tm", lambda: s0_d[:], ("s0_d",))

    xn_tm_d = env["xn_tm_d"]
    x_tm = env["x_tm"]
    x1_d = env["x1_d"]
    xn1_fm_d = env["xn1_fm_d"]
    q1_fm_d = env["q1_fm_d"]
    cc1_in = env["cc1_in"]
    cc1_out = env["cc1_out"]
    s1_d = env["s1_d"]
    out_d = env["out"]
    kwq_d = P.dint("kwq", (NCH, 128, D), BF16)

    def psq(qt):
        if qt < 4:
            return P.ps[qt][:, 0:512], f"ps{qt}"
        return P.psbig[:, (qt - 4) * 512:(qt - 3) * 512], f"psb{qt - 4}"

    P.mark()
    wti6 = P.alloc(2 * D)
    b1 = P.alloc(D)
    b2 = P.alloc(D)
    b3 = P.alloc(D)
    for kc in range(NCH):
        wo = wti6 + (kc % 2) * D
        wk6 = f"w6.{kc % 2}"
        sc.add("qs", lambda E, kc=kc, wo=wo: E.dma_start(
            out=P.a(wo, D), in_=d_["ffnWT"][kc]), writes=(wk6,))
        sc.add("dve", lambda E, wo=wo: E.tensor_single_scalar(
            P.a(b1, D), P.a(wo, D), P.swcol[:, 0:1], OP.mult),
            reads=(wk6, "swcol"), writes=("b1",))
        sc.add("dve", lambda E: E.tensor_scalar(
            P.a(b2, D), P.a(b1, D), MAGIC, MAGIC, OP.add, OP.subtract),
            reads=("b1",), writes=("b2",))
        sc.add("dve", lambda E: E.tensor_copy(
            P.a(b3, D).bitcast(I32), P.a(b2, D)), reads=("b2",), writes=("b3",))
        sc.add("dve", lambda E: E.tensor_copy(
            P.a(b1, D), P.a(b3, D).bitcast(I32)), reads=("b3",), writes=("b1",))
        sc.add("dve", lambda E: E.tensor_scalar(
            P.a(b3, D // 2).bitcast(BF16), P.a(b1, D), -1.0, 1.0, OP.max, OP.min),
            reads=("b1",), writes=("b3",))
        sc.add("qp", lambda E, kc=kc: E.dma_start(
            out=kwq_d[kc], in_=P.a(b3, D // 2).bitcast(BF16)),
            reads=("b3",), writes=("kwq_d",))
    P.release()
    sc.fence()

    P.mark()
    kxT = P.alloc(NCH * TOK // 2)
    xnb = P.alloc(D)
    c1 = P.alloc(D)
    c2 = P.alloc(D)
    c3 = P.alloc(D)
    srows = P.alloc(TPC * 128)
    rcols = P.alloc(TPC)
    qcol = P.alloc(8)
    kwt = P.alloc(2 * 256)
    ffnbr = P.alloc(512)
    epi = P.alloc(4 * 512)

    def kxv(kc, qt):
        return P.a(kxT, NCH * TOK // 2).bitcast(BF16)[:, kc * TOK + qt * 128:
                                                      kc * TOK + qt * 128 + 128]

    def kwv(i):
        return P.a(kwt, 2 * 256).bitcast(BF16)[:, (i % 2) * 512:(i % 2) * 512 + 512]
    for qt in range(TPC):
        sc.add("qs", lambda E, qt=qt: E.dma_start(
            out=P.a(xnb, D), in_=xn_tm_d[qt * 128:(qt + 1) * 128, :]),
            reads=("xn_tm_d",), writes=("xnb",))
        sc.add("act", lambda E: E.activation(
            P.a(c1, D), P.a(xnb, D), AF.Square, accum_out=P.a(qcol, 1)),
            reads=("xnb",), writes=("c1", "q0"))
        sc.add("dve", lambda E: E.tensor_scalar(
            P.a(qcol + 1, 1), P.a(qcol, 1), 1.0 / D, EPS, OP.mult, OP.add),
            reads=("q0",), writes=("q1",))
        sc.add("dve", lambda E: E.reciprocal(P.a(qcol + 2, 1), P.a(qcol + 1, 1)),
               reads=("q1",), writes=("q2",))
        sc.add("act", lambda E: E.activation(
            P.a(qcol + 3, 1), P.a(qcol + 2, 1), AF.Sqrt),
            reads=("q2",), writes=("q3",))
        sc.add("dve", lambda E: E.tensor_single_scalar(
            P.a(c1, D), P.a(xnb, D), P.a(qcol + 3, 1), OP.mult),
            reads=("xnb", "q3"), writes=("c1",))
        sc.add("dve", lambda E: E.tensor_reduce(
            P.a(qcol + 4, 1), P.a(c1, D), axis=AX.X, op=OP.max,
            apply_absolute_value=True), reads=("c1",), writes=("q4",))
        sc.add("dve", lambda E: E.tensor_scalar_max(
            P.a(qcol + 5, 1), P.a(qcol + 4, 1), EPS), reads=("q4",), writes=("q5",))
        sc.add("dve", lambda E: E.reciprocal(P.a(qcol + 6, 1), P.a(qcol + 5, 1)),
               reads=("q5",), writes=("q6",))
        sc.add("dve", lambda E: E.tensor_scalar_mul(
            P.a(qcol + 6, 1), P.a(qcol + 6, 1), 127.0),
            reads=("q6",), writes=("q6",))
        sc.add("dve", lambda E: E.tensor_single_scalar(
            P.a(c2, D), P.a(c1, D), P.a(qcol + 6, 1), OP.mult),
            reads=("c1", "q6"), writes=("c2",))
        sc.add("dve", lambda E: E.tensor_scalar(
            P.a(c3, D), P.a(c2, D), MAGIC, MAGIC, OP.add, OP.subtract),
            reads=("c2",), writes=("c3",))
        sc.add("dve", lambda E: E.tensor_copy(
            P.a(c1, D).bitcast(I32), P.a(c3, D)), reads=("c3",), writes=("c1",))
        sc.add("dve", lambda E: E.tensor_copy(
            P.a(c2, D), P.a(c1, D).bitcast(I32)), reads=("c1",), writes=("c2",))
        sc.add("dve", lambda E: E.tensor_scalar(
            P.a(c3, D), P.a(c2, D), -128.0, 127.0, OP.max, OP.min),
            reads=("c2",), writes=("c3",))
        for c in range(NCH):
            pb = c % 2
            sc.add("pe", lambda E, c=c, pb=pb: E.transpose(
                P.ps[pb][:, 0:128], P.a(c3 + c * 128, 128), P.ident[:]),
                reads=("c3", "ident"), writes=(f"ps{pb}",))
            eng = "act" if c % 2 == 0 else "dve"
            sc.add(eng, lambda E, c=c, qt=qt, pb=pb, eng=eng: (
                E.copy(kxv(c, qt), P.ps[pb][:, 0:128])
                if eng == "act" else
                E.tensor_copy(kxv(c, qt), P.ps[pb][:, 0:128])),
                reads=(f"ps{pb}",), writes=(f"kxT.{c}.{qt}",))
        sc.add("dve", lambda E, qt=qt: E.tensor_tensor(
            P.a(qcol + 7, 1), P.a(qcol + 6, 1), P.swcol[:, 0:1], OP.mult),
            reads=("q6", "swcol"), writes=("q7",))
        sc.add("dve", lambda E, qt=qt: E.reciprocal(
            P.a(rcols + qt, 1), P.a(qcol + 7, 1)), reads=("q7",), writes=("rcols",))
        sc.add("pe", lambda E: E.transpose(
            P.ps[2][0:1, 0:128], P.a(qcol + 7, 1), P.ident[:]),
            reads=("q7", "ident"), writes=("ps2",))
        sc.add("dve", lambda E, qt=qt: E.tensor_copy(
            P.a(srows + qt * 128, 128, 0, 1), P.ps[2][0:1, 0:128]),
            reads=("ps2",), writes=("srows",))
    for fc in range(8):
        for kc in range(NCH):
            i = fc * NCH + kc
            sc.add("qs", lambda E, kc=kc, fc=fc, i=i: E.dma_start(
                out=kwv(i), in_=kwq_d[kc, :, fc * 512:(fc + 1) * 512]),
                reads=("kwq_d",), writes=(f"kw6.{i % 2}",))
            for qt in range(TPC):
                ps, pk = psq(qt)
                sc.add("pe", lambda E, kc=kc, qt=qt, i=i, ps=ps: E.matmul(
                    ps, kxv(kc, qt), kwv(i),
                    start=(kc == 0), stop=False),
                    reads=(f"kw6.{i % 2}", f"kxT.{kc}.{qt}"), writes=(pk,))
        sc.add("qs", lambda E, fc=fc: E.dma_start(
            out=P.a(ffnbr, 512, 0, 1),
            in_=d_["ffnb"][:, fc * 512:(fc + 1) * 512]), writes=("ffnbr",))
        for qt in range(TPC):
            ps, pk = psq(qt)
            sc.add("pe", lambda E, qt=qt, ps=ps: E.matmul(
                ps, P.a(srows + qt * 128, 128, 0, 1),
                P.a(ffnbr, 512, 0, 1), start=False, stop=True),
                reads=("srows", "ffnbr"), writes=(pk,))
            sc.add("act", lambda E, qt=qt, ps=ps: E.activation(
                P.a(epi, 512), ps, AF.Gelu, scale=P.a(rcols + qt, 1)),
                reads=(pk, "rcols"), writes=("e0",))
            sc.add("qs", lambda E, qt=qt, fc=fc: E.dma_start(
                out=P.a(epi + 512, 512),
                in_=xn_tm_d[qt * 128:(qt + 1) * 128, fc * 512:(fc + 1) * 512]),
                reads=("xn_tm_d",), writes=("e1",))
            sc.add("qs", lambda E, qt=qt, fc=fc: E.dma_start(
                out=P.a(epi + 1024, 512),
                in_=s0_d[qt * 128:(qt + 1) * 128, fc * 512:(fc + 1) * 512]),
                reads=("s0_d",), writes=("e2",))
            sc.add("qs", lambda E, qt=qt, fc=fc: E.dma_start(
                out=P.a(epi + 1536, 512),
                in_=x_tm[qt * 128:(qt + 1) * 128, fc * 512:(fc + 1) * 512]),
                writes=("e3",))
            sc.add("dve", lambda E: E.tensor_add(
                P.a(epi, 512), P.a(epi, 512), P.a(epi + 512, 512)),
                reads=("e0", "e1"), writes=("e0",))
            sc.add("act", lambda E: E.activation(
                P.a(epi + 512, 512), P.a(epi, 512), AF.Silu),
                reads=("e0",), writes=("e1",))
            sc.add("dve", lambda E: E.tensor_mul(
                P.a(epi, 512), P.a(epi + 512, 512), P.a(epi + 1024, 512)),
                reads=("e1", "e2"), writes=("e0",))
            sc.add("dve", lambda E: E.tensor_add(
                P.a(epi, 512), P.a(epi, 512), P.a(epi + 1536, 512)),
                reads=("e0", "e3"), writes=("e0",))
            sc.add("qp", lambda E, qt=qt, fc=fc: E.dma_start(
                out=x1_d[qt * 128:(qt + 1) * 128, fc * 512:(fc + 1) * 512],
                in_=P.a(epi, 512)), reads=("e0",), writes=("x1_d",))
    if "kwq" in P.taps:
        sc.add("qp", lambda E: E.dma_start(
            out=P.dram["tap_kwq"][:], in_=kwq_d[:]),
            reads=("kwq_d",), writes=("tap_kwq",))
    if "kx" in P.taps:
        sc.add("qp", lambda E: E.dma_start(
            out=P.dram["tap_kx"][:],
            in_=P.a(kxT, NCH * TOK // 2).bitcast(BF16)),
            reads=tuple(f"kxT.{c}.{q}" for c in range(NCH) for q in range(TPC)),
            writes=("tap_kx",))
    P.release()
    sc.fence()
    P.tap("x1_tm", lambda: x1_d[:], ("x1_d",))

    _ln_proj(P, dict(
        x_src=x1_d, wt=d_["w1catt"], nf=48, pfx="p7",
        xn_tm_d=None, xn_fm_d=xn1_fm_d, q_fm_d=q1_fm_d, q_chunks=32,
        cc_in=cc1_in, kv_chunks=(32, 40), bias=d_["b1cat"],
        wg=None))
    sc.fence()
    P.tap("xn1_fm", lambda: xn1_fm_d[:], ("xn1_fm_d",))

    if SIM_LOCAL_CC:
        for r in range(2):
            sc.add("qp", lambda E, r=r: E.dma_start(
                out=cc1_out[r], in_=cc1_in[:]),
                reads=("p7ccin",), writes=("cc1_out",))
    else:
        sc.add("pool", lambda E: E.collective_compute(
            "AllGather", OP.bypass, ins=[cc1_in[:]], outs=[cc1_out[:]],
            replica_groups=[[0, 1], [2, 3], [4, 5], [6, 7]]),
            reads=("p7ccin",), writes=("cc1_out",))
    sc.fence()
    P.mark()
    rk1 = P.alloc(KVH * 2048)
    v1tm = P.alloc(KVH * 2048)
    prm1 = P.alloc(128)
    cosf1 = P.alloc(2048)
    sinf1 = P.alloc(2048)
    vtmp1 = P.alloc(2048)
    w1b1 = P.alloc(2048)
    sc.add("qs", lambda E: E.dma_start(out=P.a(prm1, 128), in_=d_["permL1"][:]),
           writes=("prm1",))
    sc.add("qs", lambda E: E.dma_start(out=P.a(cosf1, 2048), in_=d_["cosL1"][:]),
           writes=("cosf1",))
    sc.add("qs", lambda E: E.dma_start(out=P.a(sinf1, 2048), in_=d_["sinL1"][:]),
           writes=("sinf1",))
    for kvh in range(KVH):
        rkk = f"rk1.{kvh}"
        for r in range(2):
            sc.add("qs", lambda E, r=r, kvh=kvh: E.dma_start(
                out=P.a(rk1 + kvh * 2048, 2048)
                    .rearrange("b (j r2 c) -> b r2 j c", r2=2, c=128)[:, r, :, :],
                in_=cc1_out[r, 0, kvh, :, :].rearrange("b (j c) -> b j c", c=128)),
                reads=("cc1_out",), writes=(rkk,))
            sc.add("qs", lambda E, r=r, kvh=kvh: E.dma_start(
                out=P.a(vtmp1, 2048)
                    .rearrange("b (j r2 c) -> b r2 j c", r2=2, c=128)[:, r, :, :],
                in_=cc1_out[r, 1, kvh, :, :].rearrange("b (j c) -> b j c", c=128)),
                reads=("cc1_out",), writes=("vtmp1",))
        for j in range(4):
            sc.add("pe", lambda E, kvh=kvh, j=j: E.matmul(
                P.psbig[:, j * 512:(j + 1) * 512], P.a(prm1, 128),
                P.a(rk1 + kvh * 2048 + j * 512, 512), start=True, stop=True),
                reads=(rkk, "prm1"), writes=(f"psb{j}",))
        sc.add("dve", lambda E: E.tensor_tensor(
            P.a(w1b1, 2048), P.psbig[:, 0:2048], P.a(sinf1, 2048), OP.mult),
            reads=("psb0", "psb1", "psb2", "psb3", "sinf1"), writes=("w1b1",))
        sc.add("dve", lambda E, kvh=kvh: E.tensor_tensor(
            P.a(rk1 + kvh * 2048, 2048), P.a(rk1 + kvh * 2048, 2048),
            P.a(cosf1, 2048), OP.mult), reads=(rkk, "cosf1"), writes=(rkk,))
        sc.add("dve", lambda E, kvh=kvh: E.tensor_tensor(
            P.a(rk1 + kvh * 2048, 2048), P.a(rk1 + kvh * 2048, 2048),
            P.a(w1b1, 2048), OP.add), reads=(rkk, "w1b1"), writes=(rkk,))
        for gt in range(16):
            pb = 2 + (gt % 2)
            sc.add("pe", lambda E, gt=gt, pb=pb: E.transpose(
                P.ps[pb][:, 0:128], P.a(vtmp1 + gt * 128, 128), P.ident[:]),
                reads=("vtmp1", "ident"), writes=(f"ps{pb}",))
            eng = "act" if gt % 2 == 0 else "dve"
            sc.add(eng, lambda E, kvh=kvh, gt=gt, pb=pb, eng=eng: (
                E.copy(P.a(v1tm + kvh * 2048 + gt * 128, 128), P.ps[pb][:, 0:128])
                if eng == "act" else
                E.tensor_copy(P.a(v1tm + kvh * 2048 + gt * 128, 128),
                              P.ps[pb][:, 0:128])),
                reads=(f"ps{pb}",), writes=(f"v1tm{kvh}",))

    clq1 = P.alloc(128)
    slq1 = P.alloc(128)
    qb9 = P.alloc(2 * 128)
    rq9 = P.alloc(2 * 128)
    pq9 = P.alloc(2 * 128)
    pb9 = P.alloc(2 * 2048)
    pt9 = P.alloc(2 * 2048)
    ob9 = P.alloc(2 * 128)
    cb9 = P.alloc(2 * 8)
    it9 = 0
    for qt in range(TPC):
        sc.add("qs", lambda E, qt=qt: E.dma_start(
            out=P.a(clq1, 128), in_=d_["coslq1"][:, qt * 128:(qt + 1) * 128]),
            writes=("clq1",))
        sc.add("qs", lambda E, qt=qt: E.dma_start(
            out=P.a(slq1, 128), in_=d_["sinlq1"][:, qt * 128:(qt + 1) * 128]),
            writes=("slq1",))
        for kvh in range(KVH):
            for g in range(G):
                h = kvh * G + g
                pr = it9 % 2
                it9 += 1
                QB, RQ, PQ, PB, PT, OB, CB_ = (qb9 + pr * 128, rq9 + pr * 128,
                                               pq9 + pr * 128, pb9 + pr * 2048,
                                               pt9 + pr * 2048, ob9 + pr * 128,
                                               cb9 + pr * 8)
                K_ = lambda nm: f"{nm}{pr}"
                PSP, PSA = 0, 3
                sc.add("qs", lambda E, h=h, qt=qt, QB=QB: E.dma_start(
                    out=P.a(QB, 128), in_=q1_fm_d[h, :, qt * 128:(qt + 1) * 128]),
                    reads=("p7qfm",), writes=(K_("qb9"),))
                sc.add("pe", lambda E, QB=QB, PSP=PSP: E.matmul(
                    P.ps[PSP][:, 128:256], P.a(prm1, 128), P.a(QB, 128),
                    start=True, stop=True), reads=(K_("qb9"), "prm1"),
                    writes=(f"ps{PSP}",))
                sc.add("dve", lambda E, PQ=PQ, PSP=PSP: E.tensor_copy(
                    P.a(PQ, 128), P.ps[PSP][:, 128:256]),
                    reads=(f"ps{PSP}",), writes=(K_("pq9"),))
                sc.add("dve", lambda E, PQ=PQ: E.tensor_tensor(
                    P.a(PQ, 128), P.a(PQ, 128), P.a(slq1, 128), OP.mult),
                    reads=(K_("pq9"), "slq1"), writes=(K_("pq9"),))
                sc.add("dve", lambda E, QB=QB, RQ=RQ: E.tensor_tensor(
                    P.a(RQ, 128), P.a(QB, 128), P.a(clq1, 128), OP.mult),
                    reads=(K_("qb9"), "clq1"), writes=(K_("rq9"),))
                sc.add("dve", lambda E, RQ=RQ, PQ=PQ: E.tensor_tensor(
                    P.a(RQ, 128), P.a(RQ, 128), P.a(PQ, 128), OP.add),
                    reads=(K_("rq9"), K_("pq9")), writes=(K_("rq9"),))
                for t in range(4):
                    sc.add("pe", lambda E, kvh=kvh, t=t, RQ=RQ: E.matmul(
                        P.psbig[:, t * 512:(t + 1) * 512], P.a(RQ, 128),
                        P.a(rk1 + kvh * 2048 + t * 512, 512),
                        start=True, stop=True),
                        reads=(K_("rq9"), f"rk1.{kvh}"), writes=(f"psb{t}",))
                psbks = ("psb0", "psb1", "psb2", "psb3")
                sc.add("dve", lambda E, CB_=CB_: E.tensor_reduce(
                    P.a(CB_, 1), P.psbig[:, 0:2048], axis=AX.X, op=OP.max,
                    negate=True), reads=psbks, writes=(K_("cb0"),))
                sc.add("dve", lambda E, CB_=CB_: E.tensor_scalar_mul(
                    P.a(CB_ + 1, 1), P.a(CB_, 1), SCALE),
                    reads=(K_("cb0"),), writes=(K_("cb1"),))
                sc.add("act", lambda E, CB_=CB_, PB=PB: E.activation(
                    P.a(PB, 2048), P.psbig[:, 0:2048], AF.Exp,
                    bias=P.a(CB_ + 1, 1), scale=SCALE,
                    accum_out=P.a(CB_ + 2, 1)),
                    reads=psbks + (K_("cb1"),), writes=(K_("pb9"), K_("cb2")))
                for t in range(16):
                    pb = 1 + (t % 2)
                    sc.add("pe", lambda E, t=t, pb=pb, PB=PB: E.transpose(
                        P.ps[pb][:, 0:128], P.a(PB + t * 128, 128), P.ident[:]),
                        reads=(K_("pb9"), "ident"), writes=(f"ps{pb}",))
                    eng = "act" if t % 2 == 0 else "dve"
                    sc.add(eng, lambda E, t=t, pb=pb, eng=eng, PT=PT: (
                        E.copy(P.a(PT + t * 128, 128), P.ps[pb][:, 0:128])
                        if eng == "act" else
                        E.tensor_copy(P.a(PT + t * 128, 128), P.ps[pb][:, 0:128])),
                        reads=(f"ps{pb}",), writes=(K_(f"pt9.{t}"),))
                for t in range(16):
                    sc.add("pe", lambda E, t=t, kvh=kvh, PT=PT, PSA=PSA: E.matmul(
                        P.ps[PSA][:, 0:128], P.a(PT + t * 128, 128),
                        P.a(v1tm + kvh * 2048 + t * 128, 128),
                        start=(t == 0), stop=(t == 15)),
                        reads=(K_(f"pt9.{t}"), f"v1tm{kvh}"), writes=(f"ps{PSA}",))
                sc.add("dve", lambda E, CB_=CB_: E.reciprocal(
                    P.a(CB_ + 3, 1), P.a(CB_ + 2, 1)),
                    reads=(K_("cb2"),), writes=(K_("cb3"),))
                sc.add("dve", lambda E, CB_=CB_, OB=OB, PSA=PSA: E.tensor_single_scalar(
                    P.a(OB, 128), P.ps[PSA][:, 0:128], P.a(CB_ + 3, 1), OP.mult),
                    reads=(f"ps{PSA}", K_("cb3")), writes=(K_("ob9"),))
                sc.add("qp", lambda E, qt=qt, h=h, OB=OB: E.dma_start(
                    out=s1_d[qt * 128:(qt + 1) * 128, h * 128:(h + 1) * 128],
                    in_=P.a(OB, 128)), reads=(K_("ob9"),), writes=("s1_d",))
    P.release()
    sc.fence()
    P.tap("s1_tm", lambda: s1_d[:], ("s1_d",))

    P.mark()
    xn1r = P.alloc(NCH * TOK)
    gwb = P.alloc(NCH * 28)
    acb = P.alloc(NCH * 224)
    hTb = P.alloc(TPC * 256)
    lgb = P.alloc(28)
    eb10 = P.alloc(28)
    selb = P.alloc(28)
    gtb10 = P.alloc(28)
    hb10 = P.alloc(224)
    mrep10 = P.alloc(8)
    cb10 = P.alloc(8)
    bct = P.alloc(1024)
    bwt = P.alloc(2 * 512)
    ep10 = P.alloc(4 * 512)
    sc.add("dve", lambda E: E.memset(P.a(mrep10, 8), 1.0e38), writes=("mrep10",))
    sc.add("qs", lambda E: E.dma_start(out=P.a(gwb, NCH * 28), in_=d_["gwTt"][:]),
           writes=("gwb",))
    sc.add("qs", lambda E: E.dma_start(out=P.a(acb, NCH * 224), in_=d_["AcatTt"][:]),
           writes=("acb",))
    for c in range(NCH):
        sc.add("qs", lambda E, c=c: E.dma_start(
            out=P.a(xn1r + c * TOK, TOK), in_=xn1_fm_d[c]),
            reads=("xn1_fm_d",), writes=(f"xn1.{c}",))
    for qt in range(TPC):
        for kc in range(NCH):
            sc.add("pe", lambda E, kc=kc, qt=qt: E.matmul(
                P.ps[0][:, 0:28], P.a(xn1r + kc * TOK + qt * 128, 128),
                P.a(gwb + kc * 28, 28), start=(kc == 0), stop=(kc == NCH - 1)),
                reads=(f"xn1.{kc}", "gwb"), writes=("ps0",))
        sc.add("dve", lambda E: E.tensor_copy(P.a(lgb, 28), P.ps[0][:, 0:28]),
               reads=("ps0",), writes=("lgb",))
        sc.add("dve", lambda E: E.tensor_reduce(
            P.a(cb10, 1), P.a(lgb, 28), axis=AX.X, op=OP.max, negate=True),
            reads=("lgb",), writes=("d0",))
        sc.add("dve", lambda E: E.tensor_scalar_mul(
            P.a(cb10 + 1, 1), P.a(cb10, 1), -1.0), reads=("d0",), writes=("d1",))
        sc.add("dve", lambda E: E.tensor_copy(P.a(eb10, 28), P.a(lgb, 28)),
               reads=("lgb",), writes=("eb10",))
        sc.add("dve", lambda E: E.tensor_copy(P.a(mrep10, 1), P.a(cb10 + 1, 1)),
               reads=("d1",), writes=("mrep10",))
        sc.add("dve", lambda E: E.match_replace(
            P.a(eb10, 28), P.a(mrep10, 8), P.a(eb10, 28), NEG),
            reads=("eb10", "mrep10"), writes=("eb10",))
        sc.add("dve", lambda E: E.tensor_reduce(
            P.a(cb10 + 2, 1), P.a(eb10, 28), axis=AX.X, op=OP.max),
            reads=("eb10",), writes=("d2",))
        sc.add("dve", lambda E: E.tensor_single_scalar(
            P.a(selb, 28), P.a(lgb, 28), P.a(cb10 + 2, 1), OP.is_ge),
            reads=("lgb", "d2"), writes=("selb",))
        sc.add("act", lambda E: E.activation(
            P.a(eb10, 28), P.a(lgb, 28), AF.Exp, bias=P.a(cb10, 1)),
            reads=("lgb", "d0"), writes=("eb10",))
        sc.add("dve", lambda E: E.tensor_tensor(
            P.a(eb10, 28), P.a(eb10, 28), P.a(selb, 28), OP.mult),
            reads=("eb10", "selb"), writes=("eb10",))
        sc.add("dve", lambda E: E.tensor_reduce(
            P.a(cb10 + 3, 1), P.a(eb10, 28), axis=AX.X, op=OP.add),
            reads=("eb10",), writes=("d3",))
        sc.add("dve", lambda E: E.reciprocal(P.a(cb10 + 4, 1), P.a(cb10 + 3, 1)),
               reads=("d3",), writes=("d4",))
        sc.add("dve", lambda E: E.tensor_scalar_mul(
            P.a(cb10 + 5, 1), P.a(cb10 + 4, 1), 2.0), reads=("d4",), writes=("d5",))
        sc.add("dve", lambda E: E.tensor_single_scalar(
            P.a(gtb10, 28), P.a(eb10, 28), P.a(cb10 + 5, 1), OP.mult),
            reads=("eb10", "d5"), writes=("gtb10",))
        for kc in range(NCH):
            sc.add("pe", lambda E, kc=kc, qt=qt: E.matmul(
                P.ps[1][:, 0:224], P.a(xn1r + kc * TOK + qt * 128, 128),
                P.a(acb + kc * 224, 224), start=(kc == 0), stop=(kc == NCH - 1)),
                reads=(f"xn1.{kc}", "acb"), writes=("ps1",))
        sc.add("dve", lambda E: E.tensor_copy(P.a(hb10, 224), P.ps[1][:, 0:224]),
               reads=("ps1",), writes=("hb10",))
        sc.add("dve", lambda E: E.tensor_tensor(
            P.a(hb10, 224).rearrange("b (e r) -> b e r", r=8),
            P.a(hb10, 224).rearrange("b (e r) -> b e r", r=8),
            P.a(gtb10, 28).unsqueeze(2).to_broadcast([128, 28, 8]), OP.mult),
            reads=("hb10", "gtb10"), writes=("hb10",))
        sc.add("pe", lambda E: E.transpose(
            P.ps[2][:, 0:128], P.a(hb10, 128), P.ident[:]),
            reads=("hb10", "ident"), writes=("ps2",))
        sc.add("dve", lambda E, qt=qt: E.tensor_copy(
            P.a(hTb + qt * 256, 128), P.ps[2][:, 0:128]),
            reads=("ps2",), writes=(f"hT.{qt}",))
        sc.add("pe", lambda E: E.transpose(
            P.ps[2][0:96, 0:128], P.a(hb10 + 128, 96), P.ident[:]),
            reads=("hb10", "ident"), writes=("ps2",))
        sc.add("dve", lambda E, qt=qt: E.tensor_copy(
            P.a(hTb + qt * 256 + 128, 128, 0, 96), P.ps[2][0:96, 0:128]),
            reads=("ps2",), writes=(f"hT.{qt}",))
    for fc in range(8):
        sc.add("qs", lambda E, fc=fc: E.dma_start(
            out=P.a(bct, 512), in_=d_["Bcat"][0:128, fc * 512:(fc + 1) * 512]),
            writes=("bct0",))
        sc.add("qs", lambda E, fc=fc: E.dma_start(
            out=P.a(bct + 512, 512, 0, 96),
            in_=d_["Bcat"][128:224, fc * 512:(fc + 1) * 512]),
            writes=("bct1",))
        for kc in range(NCH):
            i = fc * NCH + kc
            sc.add("qs", lambda E, kc=kc, fc=fc, i=i: E.dma_start(
                out=P.a(bwt + (i % 2) * 512, 512), in_=d_["baseTt"][kc, fc]),
                writes=(f"bw.{i % 2}",))
            for qt in range(TPC):
                ps, pk = psq(qt)
                sc.add("pe", lambda E, kc=kc, qt=qt, i=i, ps=ps: E.matmul(
                    ps, P.a(xn1r + kc * TOK + qt * 128, 128),
                    P.a(bwt + (i % 2) * 512, 512),
                    start=(kc == 0), stop=False),
                    reads=(f"bw.{i % 2}", f"xn1.{kc}"), writes=(pk,))
        for qt in range(TPC):
            ps, pk = psq(qt)
            sc.add("pe", lambda E, qt=qt, ps=ps: E.matmul(
                ps, P.a(hTb + qt * 256, 128), P.a(bct, 512),
                start=False, stop=False),
                reads=(f"hT.{qt}", "bct0"), writes=(pk,))
            sc.add("pe", lambda E, qt=qt, ps=ps: E.matmul(
                ps, P.a(hTb + qt * 256 + 128, 128, 0, 96),
                P.a(bct + 512, 512, 0, 96), start=False, stop=True),
                reads=(f"hT.{qt}", "bct1"), writes=(pk,))
            sc.add("act", lambda E, ps=ps: E.activation(
                P.a(ep10, 512), ps, AF.Silu), reads=(pk,), writes=("f0",))
            sc.add("qs", lambda E, qt=qt, fc=fc: E.dma_start(
                out=P.a(ep10 + 512, 512),
                in_=s1_d[qt * 128:(qt + 1) * 128, fc * 512:(fc + 1) * 512]),
                reads=("s1_d",), writes=("f1",))
            sc.add("qs", lambda E, qt=qt, fc=fc: E.dma_start(
                out=P.a(ep10 + 1024, 512),
                in_=x1_d[qt * 128:(qt + 1) * 128, fc * 512:(fc + 1) * 512]),
                reads=("x1_d",), writes=("f2",))
            sc.add("dve", lambda E: E.tensor_mul(
                P.a(ep10, 512), P.a(ep10, 512), P.a(ep10 + 512, 512)),
                reads=("f0", "f1"), writes=("f0",))
            sc.add("dve", lambda E: E.tensor_add(
                P.a(ep10, 512), P.a(ep10, 512), P.a(ep10 + 1024, 512)),
                reads=("f0", "f2"), writes=("f0",))
            sc.add("qp", lambda E, qt=qt, fc=fc: E.dma_start(
                out=out_d[qt * 128:(qt + 1) * 128, fc * 512:(fc + 1) * 512],
                in_=P.a(ep10, 512)), reads=("f0",), writes=("out",))
    P.release()


def _ln_proj(P, cfg):
    sc, nc = P.sc, P.nc
    pfx = cfg["pfx"]
    nf = cfg["nf"]
    qch = cfg["q_chunks"]
    kv0, v0 = cfg["kv_chunks"]
    wt = cfg["wt"]

    P.mark()
    xg = P.alloc(2 * D)
    xnt = P.alloc(D)
    sq = P.alloc(D)
    cols = P.alloc(2 * 8)
    xnfm = P.alloc(NCH * 512)
    wcol = P.alloc(2 * D)
    ev = P.alloc(4 * 512)
    has_g = cfg.get("wg") is not None
    if has_g:
        wgb = P.alloc(NCH * 96)
        gev = P.alloc(512)
        gtm = P.alloc(4 * 96)
        bgc = P.alloc(1)
        sc.add("qs", lambda E: E.dma_start(out=P.a(wgb, NCH * 96), in_=cfg["wg"][:]),
               writes=(pfx + "wg",))
        sc.add("qs", lambda E: E.dma_start(out=P.a(bgc, 1, 0, 96), in_=cfg["bg"][:]),
               writes=(pfx + "bg",))
    has_b = cfg.get("bias") is not None
    if has_b:
        bc = P.alloc(48)
        sc.add("qs", lambda E: E.dma_start(out=P.a(bc, 48), in_=cfg["bias"][:]),
               writes=(pfx + "bias",))

    for grp in range(2):
        for j2 in range(4):
            t = grp * 4 + j2
            xk = f"{pfx}xg.{t % 2}"
            ck = f"{pfx}cols.{t % 2}"
            o = xg + (t % 2) * D
            co = cols + (t % 2) * 8
            sc.add("qs", lambda E, t=t, o=o: E.dma_start(
                out=P.a(o, D), in_=cfg["x_src"][t * 128:(t + 1) * 128, :]),
                writes=(xk,))
            sc.add("act", lambda E, o=o, co=co: E.activation(
                P.a(sq, D), P.a(o, D), AF.Square, accum_out=P.a(co + 1, 1)),
                reads=(xk,), writes=(pfx + "sq", ck + "s"))
            sc.add("dve", lambda E, o=o, co=co: E.tensor_reduce(
                P.a(co, 1), P.a(o, D), axis=AX.X, op=OP.add),
                reads=(xk,), writes=(ck,))
            sc.add("dve", lambda E, co=co: E.tensor_scalar_mul(
                P.a(co + 2, 1), P.a(co, 1), 1.0 / D), reads=(ck,), writes=(ck,))
            sc.add("dve", lambda E, co=co: E.tensor_mul(
                P.a(co + 3, 1), P.a(co + 2, 1), P.a(co + 2, 1)),
                reads=(ck,), writes=(ck,))
            sc.add("dve", lambda E, co=co: E.scalar_tensor_tensor(
                P.a(co + 4, 1), P.a(co + 1, 1), 1.0 / D, P.a(co + 3, 1),
                OP.mult, OP.subtract), reads=(ck, ck + "s"), writes=(ck,))
            sc.add("dve", lambda E, co=co: E.tensor_scalar_add(
                P.a(co + 5, 1), P.a(co + 4, 1), EPS), reads=(ck,), writes=(ck,))
            sc.add("dve", lambda E, co=co: E.reciprocal(
                P.a(co + 6, 1), P.a(co + 5, 1)), reads=(ck,), writes=(ck,))
            sc.add("act", lambda E, co=co: E.activation(
                P.a(co + 7, 1), P.a(co + 6, 1), AF.Sqrt), reads=(ck,), writes=(ck + "r",))
            sc.add("dve", lambda E, o=o, co=co: E.scalar_tensor_tensor(
                P.a(xnt, D), P.a(o, D), P.a(co + 2, 1),
                P.a(co + 7, 1).to_broadcast([128, D]), OP.subtract, OP.mult),
                reads=(xk, ck, ck + "r"), writes=(pfx + "xnt",))
            if cfg.get("xn_tm_d") is not None:
                sc.add("qp", lambda E, t=t: E.dma_start(
                    out=cfg["xn_tm_d"][t * 128:(t + 1) * 128, :], in_=P.a(xnt, D)),
                    reads=(pfx + "xnt",), writes=("xn_tm_d",))
            for c in range(NCH):
                i = t * NCH + c
                pb = 2 + (i % 2)
                sc.add("pe", lambda E, c=c, pb=pb: E.transpose(
                    P.ps[pb][:, 0:128], P.a(xnt + c * 128, 128), P.ident[:]),
                    reads=(pfx + "xnt", "ident"), writes=(f"ps{pb}",))
                sc.add("act", lambda E, c=c, j2=j2, pb=pb: E.copy(
                    P.a(xnfm + c * 512 + j2 * 128, 128), P.ps[pb][:, 0:128]),
                    reads=(f"ps{pb}",), writes=(f"{pfx}fm.{c}.{j2}",))
            if cfg.get("xn_fm_d") is not None:
                sc.add("qp", lambda E, t=t, j2=j2: E.dma_start(
                    out=cfg["xn_fm_d"][:, :, t * 128:(t + 1) * 128]
                        .transpose([1, 0, 2]),
                    in_=P.a(xnfm, NCH * 512)
                        .rearrange("b (c n) -> b c n", n=512)
                        [:, :, j2 * 128:j2 * 128 + 128]),
                    reads=tuple(f"{pfx}fm.{c}.{j2}" for c in range(NCH)),
                    writes=("xn1_fm_d",))
        fmkeys = tuple(f"{pfx}fm.{c}.{j2}" for c in range(NCH) for j2 in range(4))
        for fc in range(nf):
            i = grp * nf + fc
            wk = f"{pfx}w.{i % 2}"
            ek = f"{pfx}ev.{i % 4}"
            pk = f"ps{i % 2}"
            wo = wcol + (i % 2) * D
            eo = ev + (i % 4) * 512
            sc.add("qs", lambda E, fc=fc, wo=wo: E.dma_start(
                out=P.a(wo, D), in_=wt[fc]), writes=(wk,))
            for kc in range(NCH):
                sc.add("pe", lambda E, kc=kc, wo=wo, pk=pk: E.matmul(
                    P.ps[int(pk[2])][:, 0:512], P.a(wo + kc * 128, 128),
                    P.a(xnfm + kc * 512, 512), start=(kc == 0), stop=(kc == NCH - 1)),
                    reads=(wk,) + (fmkeys if kc == 0 else
                                   (f"{pfx}fm.{kc}.0", f"{pfx}fm.{kc}.1",
                                    f"{pfx}fm.{kc}.2", f"{pfx}fm.{kc}.3")),
                    writes=(pk,))
            if has_b:
                sc.add("act", lambda E, fc=fc, eo=eo, pk=pk: E.activation(
                    P.a(eo, 512), P.ps[int(pk[2])][:, 0:512], AF.Identity,
                    bias=P.a(bc + fc, 1)),
                    reads=(pk, pfx + "bias"), writes=(ek,))
            else:
                sc.add("act", lambda E, eo=eo, pk=pk: E.copy(
                    P.a(eo, 512), P.ps[int(pk[2])][:, 0:512]),
                    reads=(pk,), writes=(ek,))
            if fc < qch:
                sc.add("qp", lambda E, fc=fc, grp=grp, eo=eo, dst=cfg["q_fm_d"]:
                       E.dma_start(out=dst[fc, :, grp * 512:(grp + 1) * 512],
                                   in_=P.a(eo, 512)),
                       reads=(ek,), writes=(pfx + "qfm",))
            else:
                kvsel = 0 if fc < v0 else 1
                kvh = fc - (kv0 if kvsel == 0 else v0)
                sc.add("qp", lambda E, kvsel=kvsel, kvh=kvh, grp=grp, eo=eo:
                       E.dma_start(out=cfg["cc_in"][kvsel, kvh, :,
                                                    grp * 512:(grp + 1) * 512],
                                   in_=P.a(eo, 512)),
                       reads=(ek,), writes=(pfx + "ccin",))
        if has_g:
            for kc in range(NCH):
                sc.add("pe", lambda E, kc=kc: E.matmul(
                    P.psbig[0:96, 0:512], P.a(wgb + kc * 96, 96, 0, 128),
                    P.a(xnfm + kc * 512, 512), start=(kc == 0), stop=(kc == NCH - 1)),
                    reads=(pfx + "wg",) + fmkeys, writes=("psg",))
            sc.add("act", lambda E: E.activation(
                P.a(gev, 512, 0, 96), P.psbig[0:96, 0:512], AF.Sigmoid,
                bias=P.a(bgc, 1, 0, 96)),
                reads=("psg", pfx + "bg"), writes=(pfx + "gev",))
            for j2 in range(4):
                sc.add("pe", lambda E, j2=j2: E.transpose(
                    P.psbig[0:128, 512:512 + 96],
                    P.a(gev + j2 * 128, 128, 0, 96), P.ident[0:96, 0:96]),
                    reads=(pfx + "gev", "ident"), writes=("psgt",))
                sc.add("dve", lambda E, j2=j2: E.tensor_copy(
                    P.a(gtm + j2 * 96, 96), P.psbig[0:128, 512:512 + 96]),
                    reads=("psgt",), writes=(f"{pfx}gtm.{j2}",))
                t = grp * 4 + j2
                sc.add("qp", lambda E, t=t, j2=j2: E.dma_start(
                    out=cfg["gates_d"][t * 128:(t + 1) * 128, :],
                    in_=P.a(gtm + j2 * 96, 96)),
                    reads=(f"{pfx}gtm.{j2}",), writes=("gates_d",))
    P.release()


def _run_raw(inputs, taps=(), trace=False):
    staged, tiles = [], []
    for c in range(8):
        d, tg = _stage(inputs, c)
        staged.append(d)
        tiles.append(tg)
    nc = bass.Bass("TRN2", target_bir_lowering=False)
    nc, P = build(nc, taps=taps)
    res = run_bass_kernel_spmd(nc, staged, core_ids=list(range(8)), trace=trace)
    return res, tiles


def kernel(**inputs):
    staged, tiles = [], []
    for c in range(8):
        d, tg = _stage(inputs, c)
        staged.append(d)
        tiles.append(tg)
    nc = bass.Bass("TRN2", target_bir_lowering=False)
    nc, P = build(nc, taps=())
    res = run_bass_kernel_spmd(nc, staged, core_ids=list(range(8)))
    out = np.zeros((B, S, D), dtype=np.float32)
    for c in range(8):
        b = c // 2
        o = np.asarray(res.results[c]["out"]).reshape(TPC, 128, D)
        for j, gt in enumerate(tiles[c]):
            out[b, gt * 128:(gt + 1) * 128, :] = o[j]
    return out


# version 06
# speedup vs baseline: 1.0217x; 1.0079x over previous
import math

import numpy as np

import concourse.bass as bass
import concourse.mybir as mybir
from concourse.bass_utils import run_bass_kernel_spmd

F32 = mybir.dt.float32
BF16 = mybir.dt.bfloat16
I32 = mybir.dt.int32
AF = mybir.ActivationFunctionType
OP = mybir.AluOpType
AX = mybir.AxisListType

B, S, D = 4, 2048, 4096
H, KVH, DH, G = 32, 8, 128, 4
CB = 16
NT = S // 128
TPC = NT // 2
TOK = TPC * 128
NCH = D // 128
NEG = -1.0e38
SCALE = 1.0 / math.sqrt(DH)
MAGIC = 12582912.0
EPS = 1e-5
LN1E10 = math.log(1e10)
SIM_LOCAL_CC = False


class Step:
    __slots__ = ("eng", "fn", "reads", "writes", "deps", "ord", "inc", "seq")

    def __init__(self, eng, fn, reads, writes):
        self.eng = eng
        self.fn = fn
        self.reads = reads
        self.writes = writes
        self.deps = None
        self.ord = None
        self.inc = False
        self.seq = None


VENG = {
    "pe": ("tensor", False),
    "dve": ("vector", False),
    "act": ("scalar", False),
    "pool": ("gpsimd", False),
    "qs": ("sync", True),
    "qa": ("scalar", True),
    "qp": ("gpsimd", True),
}
REAL_OF = {k: v[0] for k, v in VENG.items()}


class Sched:
    def __init__(self):
        self.steps = []
        self.fences = []

    def add(self, eng, fn, reads=(), writes=()):
        assert eng in VENG
        self.steps.append(Step(eng, fn, tuple(reads), tuple(writes)))
        return len(self.steps) - 1

    def fence(self):
        self.fences.append(len(self.steps))

    def emit(self, nc):
        steps = self.steps
        n = len(steps)
        for i, st in enumerate(steps):
            st.seq = i
        last_w = {}
        readers = {}
        last_of_eng = {}
        fi = 0
        cur_fence_deps = {}
        fdeps_per = [None] * n
        for i, st in enumerate(steps):
            while fi < len(self.fences) and self.fences[fi] == i:
                cur_fence_deps = dict(last_of_eng)
                fi += 1
            fdeps_per[i] = {e: j for e, j in cur_fence_deps.items()
                            if e != st.eng or VENG[e][1]}
            deps = {}
            for k in st.reads:
                j = last_w.get(k)
                if j is not None:
                    deps[j] = True
            for k in st.writes:
                j = last_w.get(k)
                if j is not None:
                    deps[j] = True
                for r in readers.get(k, ()):
                    deps[r] = True
            for k in st.reads:
                readers.setdefault(k, []).append(i)
            for k in st.writes:
                last_w[k] = i
                readers[k] = []
            st.deps = sorted(deps)
            last_of_eng[st.eng] = i
        self._fdeps_per = fdeps_per
        for st in steps:
            if VENG[st.eng][1]:
                st.inc = True
        for i, st in enumerate(steps):
            for j in st.deps:
                d = steps[j]
                if d.eng != st.eng:
                    d.inc = True
            for j in fdeps_per[i].values():
                steps[j].inc = True
        fi = 0
        last_of_eng = {}
        for i, st in enumerate(steps):
            while fi < len(self.fences) and self.fences[fi] == i:
                for j in last_of_eng.values():
                    steps[j].inc = True
                fi += 1
            last_of_eng[st.eng] = i
        for j in last_of_eng.values():
            steps[j].inc = True
        cnt = {e: 0 for e in VENG}
        for st in steps:
            if st.inc:
                cnt[st.eng] += 1
                st.ord = cnt[st.eng]
        self.final_counts = dict(cnt)
        NQ = 8
        SEM_LIM = 20000
        nsem = {e: (NQ if VENG[e][1] else (cnt[e] + SEM_LIM - 1) // SEM_LIM)
                for e in VENG}
        step_engseq = [0] * n
        eng_seq = {e: 0 for e in VENG}
        for i, st in enumerate(steps):
            eng_seq[st.eng] += 1
            step_engseq[i] = eng_seq[st.eng]
        waits_per = [None] * n
        drains_per = [False] * n
        hw = {}
        drain_floor = {e: 0 for e in VENG}

        def qparts(o):
            return (o - 1) % NQ, (o - 1) // NQ + 1

        def qcount(o, cls):
            return (o - 1 - cls) // NQ + 1 if o - 1 >= cls else 0

        for i, st in enumerate(steps):
            waits = {}
            need_drain = False
            creal = REAL_OF[st.eng]

            def want(peng, cls, idx):
                k = (peng, cls)
                if waits.get(k, 0) < idx:
                    waits[k] = idx

            for j in st.deps:
                d = steps[j]
                if d.eng == st.eng and not VENG[st.eng][1]:
                    if st.eng in ("dve", "act", "pool"):
                        if step_engseq[j] > drain_floor[st.eng]:
                            need_drain = True
                    continue
                assert d.ord is not None
                if VENG[d.eng][1]:
                    cls, idx = qparts(d.ord)
                    want(d.eng, cls, idx)
                else:
                    want(d.eng, None, d.ord)
            for e, j in self._fdeps_per[i].items():
                d = steps[j]
                assert d.ord is not None
                if VENG[e][1]:
                    for cls in range(NQ):
                        c = qcount(d.ord, cls)
                        if c:
                            want(e, cls, c)
                else:
                    want(e, None, d.ord)
            final = []
            for (peng, cls), idx in sorted(waits.items(),
                                           key=lambda t: (t[0][0], str(t[0][1]))):
                if hw.get((creal, peng, cls), 0) >= idx:
                    continue
                hw[(creal, peng, cls)] = idx
                final.append((peng, cls, idx))
            waits_per[i] = final
            if need_drain:
                drains_per[i] = True
                drain_floor[st.eng] = step_engseq[i] - 1
        self._waits_per = waits_per
        self._drains_per = drains_per
        sems = {}
        for e in VENG:
            sems[e] = [nc.semaphore(f"S_{e}_{k}") for k in range(max(1, nsem[e]))]
        sem_objs = {e: [cm.__enter__() for cm in sems[e]] for e in VENG}

        def wait_args(peng, cls, idx):
            if VENG[peng][1]:
                return sem_objs[peng][cls], idx * 16
            si, rem = (idx - 1) // SEM_LIM, (idx - 1) % SEM_LIM + 1
            return sem_objs[peng][si], rem

        with nc.Block() as blk:
            for real in ("sync", "tensor", "vector", "scalar", "gpsimd"):
                mysteps = [i for i, st in enumerate(steps) if REAL_OF[st.eng] == real]

                def emit_stream(E, mysteps=mysteps, real=real):
                    for i in mysteps:
                        st = steps[i]
                        for peng, cls, idx in waits_per[i]:
                            s, v = wait_args(peng, cls, idx)
                            E.wait_ge(s, v)
                        if drains_per[i]:
                            E.drain()
                        if st.inc and VENG[st.eng][1]:
                            cls, idx = qparts(st.ord)
                            if idx >= 2:
                                E.wait_ge(sem_objs[st.eng][cls], (idx - 1) * 16)
                        instr = st.fn(E)
                        if st.inc:
                            if VENG[st.eng][1]:
                                cls, _ = qparts(st.ord)
                                instr.then_inc(sem_objs[st.eng][cls], 16)
                            else:
                                si = (st.ord - 1) // SEM_LIM
                                instr.then_inc(sem_objs[st.eng][si], 1)
                    for e in VENG:
                        if REAL_OF[e] == real and VENG[e][1] and self.final_counts[e]:
                            o = self.final_counts[e]
                            for cls in range(NQ):
                                c = qcount(o, cls)
                                if c:
                                    s, v = wait_args(e, cls, c)
                                    E.wait_ge(s, v)

                getattr(blk, real)(emit_stream)
        for e in VENG:
            for cm in sems[e]:
                cm.__exit__(None, None, None)


def _rope_tables():
    pos = np.arange(S, dtype=np.float64)
    inv = 1.0 / (10000.0 ** (np.arange(0, DH, 2, dtype=np.float64) / DH))
    ang = pos[None, :] * inv[:, None]
    c, s = np.cos(ang), np.sin(ang)
    cosL0 = np.zeros((DH, S))
    sinL0 = np.zeros((DH, S))
    cosL0[0::2] = c
    cosL0[1::2] = c
    sinL0[0::2] = -s
    sinL0[1::2] = s
    cosL1 = np.zeros((DH, S))
    sinL1 = np.zeros((DH, S))
    cosL1[0:64] = c
    cosL1[64:128] = c
    sinL1[0:64] = -s
    sinL1[64:128] = s
    permL0 = np.zeros((DH, DH))
    for i in range(0, DH, 2):
        permL0[i, i + 1] = 1.0
        permL0[i + 1, i] = 1.0
    permL1 = np.zeros((DH, DH))
    for m in range(DH):
        permL1[(m + 64) % 128, m] = 1.0
    f = np.float32
    return (cosL0.astype(f), sinL0.astype(f), cosL1.astype(f), sinL1.astype(f),
            permL0.astype(f), permL1.astype(f))


def _masks(tiles_g):
    cm = np.zeros((TPC, 128, 129), dtype=np.float32)
    for j, gt in enumerate(tiles_g):
        q = gt * 128 + np.arange(128)
        blk_last = (np.arange(128) + 1) * CB - 1
        cm[j, :, 1:] = np.where(blk_last[None, :] < q[:, None], 0.0, NEG)
    i = np.arange(128)
    dm = np.where((i[:, None] // CB == i[None, :] // CB)
                  & (i[None, :] % CB <= i[:, None] % CB), 0.0, NEG).astype(np.float32)
    jj = np.arange(256)
    sb = np.where((jj[None, :] >= i[:, None] + 64)
                  & (jj[None, :] <= i[:, None] + 128), 0.0, NEG).astype(np.float32)
    jj0 = np.arange(128)
    sb0 = np.where((jj0[None, :] >= i[:, None] - 64)
                   & (jj0[None, :] <= i[:, None]), 0.0, NEG).astype(np.float32)
    return cm, dm, sb, sb0


def _tile_w(wT, nf):
    Dk, F = wT.shape
    nc_ = Dk // 128
    t = wT.reshape(nc_, 128, nf, 128).transpose(2, 1, 0, 3).reshape(nf, 128, nc_ * 128)
    return np.ascontiguousarray(t)


def _stage(inputs, core):
    b, p = core // 2, core % 2
    tiles_g = [p + 2 * j for j in range(TPC)]
    idx = np.concatenate([np.arange(gt * 128, gt * 128 + 128) for gt in tiles_g])
    f32c = lambda a: np.ascontiguousarray(np.asarray(a), dtype=np.float32)

    d = {}
    d["x_tm"] = f32c(np.asarray(inputs["L0_attn_hidden_states"])[b][idx])

    wqkvT = np.asarray(inputs["L0_attn_w_qkv"]).T.astype(np.float32)
    d["wqkvTt"] = _tile_w(wqkvT, 48)
    d["wgTt"] = f32c(np.asarray(inputs["L0_attn_w_gate"]).T
                     .reshape(NCH, 128, 96).transpose(1, 0, 2).reshape(128, NCH * 96))
    d["bg"] = f32c(np.asarray(inputs["L0_attn_b_gate"]).reshape(96, 1))
    woT = np.asarray(inputs["L0_attn_w_o"]).T.astype(np.float32)
    d["woTt"] = f32c(woT.reshape(NCH, 128, 8, 512).transpose(0, 2, 1, 3))
    d["kw1Tt"] = f32c(np.asarray(inputs["L0_attn_k_w1"]).T
                      .reshape(16, 128, 16, 128).transpose(2, 1, 0, 3).reshape(16, 128, 2048))
    d["vw1Tt"] = f32c(np.asarray(inputs["L0_attn_v_w1"]).T
                      .reshape(16, 128, 16, 128).transpose(2, 1, 0, 3).reshape(16, 128, 2048))
    d["kw2Tt"] = f32c(np.asarray(inputs["L0_attn_k_w2"]).T.reshape(16, 128, 128))
    d["vw2Tt"] = f32c(np.asarray(inputs["L0_attn_v_w2"]).T.reshape(16, 128, 128))
    d["kb1"] = f32c(np.asarray(inputs["L0_attn_k_b1"]).reshape(16, 128).T)
    d["vb1"] = f32c(np.asarray(inputs["L0_attn_v_b1"]).reshape(16, 128).T)
    d["kb2"] = f32c(np.asarray(inputs["L0_attn_k_b2"]).reshape(128, 1))
    d["vb2"] = f32c(np.asarray(inputs["L0_attn_v_b2"]).reshape(1, 128))
    d["kpos"] = f32c(np.asarray(inputs["L0_attn_k_pos"]).transpose(0, 2, 1)
                     .transpose(1, 0, 2).reshape(128, KVH * 16))
    d["vpos"] = f32c(np.asarray(inputs["L0_attn_v_pos"]).transpose(0, 2, 1)
                     .transpose(1, 0, 2).reshape(128, KVH * 16))
    mk = np.asarray(inputs["L0_attn_mem_kv"])
    d["memk"] = f32c(mk[0][:, 0, :].T)
    d["memv"] = f32c(mk[1][:, 0, :].reshape(1, KVH * 128))
    d["ffnWT"] = f32c(np.asarray(inputs["L0_ffn_W"]).T.reshape(NCH, 128, D))
    d["ffnb"] = f32c(np.asarray(inputs["L0_ffn_b"]).reshape(1, D))

    w1cat = np.concatenate([np.asarray(inputs["L1_attn_wq"]),
                            np.asarray(inputs["L1_attn_wk"]),
                            np.asarray(inputs["L1_attn_wv"])], axis=1).astype(np.float32)
    d["w1catt"] = _tile_w(w1cat, 48)
    bcat = np.concatenate([np.asarray(inputs["L1_attn_bq"]),
                           np.asarray(inputs["L1_attn_bk"]),
                           np.asarray(inputs["L1_attn_bv"])]).astype(np.float32)
    d["b1cat"] = f32c(bcat.reshape(48, 128).T)
    baseT = np.asarray(inputs["L1_ffn_base_W"]).T.astype(np.float32)
    d["baseTt"] = f32c(baseT.reshape(NCH, 128, 8, 512).transpose(0, 2, 1, 3))
    d["gwTt"] = f32c(np.asarray(inputs["L1_ffn_gate_W"]).T
                     .reshape(NCH, 128, 28).transpose(1, 0, 2).reshape(128, NCH * 28))
    d["AcatTt"] = f32c(np.asarray(inputs["L1_ffn_lora_A"]).reshape(224, D).T
                       .reshape(NCH, 128, 224).transpose(1, 0, 2).reshape(128, NCH * 224))
    d["Bcat"] = f32c(np.asarray(inputs["L1_ffn_lora_B"]).transpose(0, 2, 1).reshape(224, D))

    cosL0, sinL0, cosL1, sinL1, permL0, permL1 = _rope_tables()
    d["cosL0"], d["sinL0"], d["cosL1"], d["sinL1"] = cosL0, sinL0, cosL1, sinL1
    d["coslq0"] = f32c(cosL0[:, idx])
    d["sinlq0"] = f32c(sinL0[:, idx])
    d["coslq1"] = f32c(cosL1[:, idx])
    d["sinlq1"] = f32c(sinL1[:, idx])
    d["permL0"], d["permL1"] = permL0, permL1
    d["ident"] = np.eye(128, dtype=np.float32)
    d["onesrow"] = np.ones((1, 128), dtype=np.float32)
    d["onescol"] = np.ones((128, 1), dtype=np.float32)
    cm, dm, sbm, sbm0 = _masks(tiles_g)
    d["cmask"] = f32c(cm.transpose(1, 0, 2).reshape(128, TPC * 129))
    d["diagm"] = dm
    d["sband"] = sbm
    d["sband0"] = sbm0
    d2 = np.full((TPC, 128, 256), NEG, dtype=np.float32)
    sbm2 = np.full((TPC, 128, 384), NEG, dtype=np.float32)
    ii = np.arange(128)
    for qt in range(TPC):
        gt = p + 2 * qt
        q = gt * 128 + ii
        key2 = 2 * qt * 128 + np.arange(256)
        vis = (key2[None, :] // CB == q[:, None] // CB) \
            & (key2[None, :] % CB <= q[:, None] % CB)
        d2[qt] = np.where(vis, 0.0, NEG)
        soff = max(0, 2 * qt - 1) * 128
        key3 = soff + np.arange(384)
        visb = (key3[None, :] >= q[:, None] - 64) & (key3[None, :] <= q[:, None])
        sbm2[qt] = np.where(visb, 0.0, NEG)
    d["diag2"] = d2
    d["sbandm"] = sbm2
    return d, tiles_g


class Prog:

    def __init__(self, nc, taps=()):
        self.nc = nc
        self.sc = Sched()
        self.dram = {}
        self.taps = set(taps)
        self._ctxs = []
        self.ARENA_COLS = 51712
        self.arena = self._sbuf("ARENA", (128, self.ARENA_COLS))
        self._abase = 0
        self._amarks = []
        self.ident = self._sbuf("c_ident", (128, 128))
        self.ones1 = self._sbuf("c_ones", (1, 128))
        self.onesc = self._sbuf("c_onesc", (128, 1))
        self.swcol = self._sbuf("c_swcol", (128, 2))
        self.ps = [self._psum(f"PSB{i}", (128, 512)) for i in range(4)]
        self.psbig = self._psum("PSBIG", (128, 2048))

    def _sbuf(self, name, shape, dtype=F32):
        cm = self.nc.sbuf_tensor(name, list(shape), dtype)
        t = cm.__enter__()
        self._ctxs.append(cm)
        return t

    def _psum(self, name, shape, dtype=F32):
        cm = self.nc.psum_tensor(name, list(shape), dtype)
        t = cm.__enter__()
        self._ctxs.append(cm)
        return t

    def close(self):
        for cm in reversed(self._ctxs):
            cm.__exit__(None, None, None)

    def alloc(self, ncols):
        base = self._abase
        assert base + ncols <= self.ARENA_COLS, (base, ncols)
        self._abase += ncols
        return base

    def mark(self):
        self._amarks.append(self._abase)

    def release(self):
        self._abase = self._amarks.pop()

    def a(self, base, ncols, p0=0, np_=128):
        return self.arena[p0:p0 + np_, base:base + ncols]

    def din(self, name, shape, dtype=F32):
        ap = self.nc.dram_tensor(name, list(shape), dtype, kind="ExternalInput").ap()
        self.dram[name] = ap
        return ap

    def dout(self, name, shape, dtype=F32):
        ap = self.nc.dram_tensor(name, list(shape), dtype, kind="ExternalOutput").ap()
        self.dram[name] = ap
        return ap

    def dint(self, name, shape, dtype=F32):
        ap = self.nc.dram_tensor(name, list(shape), dtype, kind="Internal").ap()
        self.dram[name] = ap
        return ap

    def tap(self, name, src_fn, reads):
        if name in self.taps:
            dst = self.dram["tap_" + name]
            self.sc.add("qp", lambda E, s=src_fn, d=dst: E.dma_start(out=d[:], in_=s()),
                        reads=reads, writes=("tap_" + name,))


TAP_SHAPES = {
    "xn_tm": (TOK, D), "q0_fm": (NCH, 128, TOK), "gates_tm": (TOK, 96),
    "kfull": (KVH, 128, S), "vfull": (KVH, 128, S),
    "rk": (KVH, 128, S), "ck": (KVH, 128, 129), "cvtm": (KVH, 128, 128),
    "attn_acc": (TOK, D), "s0_tm": (TOK, D), "x1_tm": (TOK, D),
    "xn1_fm": (NCH, 128, TOK), "s1_tm": (TOK, D), "sw": (1, 3),
    "imp": (KVH, 128, 128), "fsel": (KVH, 128, 128),
    "kwq": (NCH, 128, D), "kx": (128, NCH * TOK),
}


def build(nc, taps=()):
    P = Prog(nc, taps)
    sc = P.sc
    d_ = {}

    def din(n, s, dt=F32):
        d_[n] = P.din(n, s, dt)
        return d_[n]

    x_tm = din("x_tm", (TOK, D))
    wqkvTt = din("wqkvTt", (48, 128, D))
    wgTt = din("wgTt", (128, NCH * 96))
    bg = din("bg", (96, 1))
    woTt = din("woTt", (NCH, 8, 128, 512))
    kw1Tt = din("kw1Tt", (16, 128, 2048))
    vw1Tt = din("vw1Tt", (16, 128, 2048))
    kw2Tt = din("kw2Tt", (16, 128, 128))
    vw2Tt = din("vw2Tt", (16, 128, 128))
    kb1 = din("kb1", (128, 16))
    vb1 = din("vb1", (128, 16))
    kb2 = din("kb2", (128, 1))
    vb2 = din("vb2", (1, 128))
    kpos = din("kpos", (128, KVH * 16))
    vpos = din("vpos", (128, KVH * 16))
    memk = din("memk", (128, KVH))
    memv = din("memv", (1, KVH * 128))
    ffnWT = din("ffnWT", (NCH, 128, D))
    ffnb = din("ffnb", (1, D))
    w1catt = din("w1catt", (48, 128, D))
    b1cat = din("b1cat", (128, 48))
    baseTt = din("baseTt", (NCH, 8, 128, 512))
    gwTt = din("gwTt", (128, NCH * 28))
    AcatTt = din("AcatTt", (128, NCH * 224))
    Bcat = din("Bcat", (224, D))
    cosL0 = din("cosL0", (128, S))
    sinL0 = din("sinL0", (128, S))
    cosL1 = din("cosL1", (128, S))
    sinL1 = din("sinL1", (128, S))
    coslq0 = din("coslq0", (128, TOK))
    sinlq0 = din("sinlq0", (128, TOK))
    coslq1 = din("coslq1", (128, TOK))
    sinlq1 = din("sinlq1", (128, TOK))
    permL0 = din("permL0", (128, 128))
    permL1 = din("permL1", (128, 128))
    identD = din("ident", (128, 128))
    onesD = din("onesrow", (1, 128))
    onescD = din("onescol", (128, 1))
    cmaskD = din("cmask", (128, TPC * 129))
    diagD = din("diagm", (128, 128))
    sbandD = din("sband", (128, 256))
    sband0D = din("sband0", (128, 128))
    diag2D = din("diag2", (TPC, 128, 256))
    sbandmD = din("sbandm", (TPC, 128, 384))

    out = P.dout("out", (TOK, D))
    for t in P.taps:
        P.dout("tap_" + t, TAP_SHAPES[t])

    xn_tm_d = P.dint("xn_tm", (TOK, D))
    q0_fm_d = P.dint("q0_fm", (NCH, 128, TOK))
    gates_d = P.dint("gates_tm", (TOK, 96))
    cc0_in = P.dint("cc0_in", (2, KVH, 128, TOK))
    cc0_out = P.dint("cc0_out", (2, 2, KVH, 128, TOK))
    acc_d = P.dint("attn_acc", (TOK, D))
    s0_d = P.dint("s0_tm", (TOK, D))
    x1_d = P.dint("x1_tm", (TOK, D))
    xn1_fm_d = P.dint("xn1_fm", (NCH, 128, TOK))
    q1_fm_d = P.dint("q1_fm", (NCH, 128, TOK))
    cc1_in = P.dint("cc1_in", (2, KVH, 128, TOK))
    cc1_out = P.dint("cc1_out", (2, 2, KVH, 128, TOK))
    s1_d = P.dint("s1_tm", (TOK, D))

    sc.add("qs", lambda E: E.dma_start(out=P.ident[:], in_=identD[:]),
           writes=("ident",))
    sc.add("qs", lambda E: E.dma_start(out=P.ones1[:], in_=onesD[:]),
           writes=("ones1",))
    sc.add("qs", lambda E: E.dma_start(out=P.onesc[:], in_=onescD[:]),
           writes=("onesc",))

    P.mark()
    w0 = P.alloc(2 * D)
    asum = P.alloc(2)
    srow = P.alloc(4)
    for i in range(NCH):
        sc.add("qs", lambda E, i=i: E.dma_start(
            out=P.a(w0 + (i % 2) * D, D), in_=ffnWT[i]),
            reads=(), writes=(f"w0.{i % 2}",))
        sc.add("dve", lambda E, i=i: E.tensor_reduce(
            P.a(asum + 1, 1), P.a(w0 + (i % 2) * D, D), axis=AX.X, op=OP.add,
            apply_absolute_value=True),
            reads=(f"w0.{i % 2}",), writes=("asum1",))
        if i == 0:
            sc.add("dve", lambda E: E.tensor_copy(P.a(asum, 1), P.a(asum + 1, 1)),
                   reads=("asum1",), writes=("asum0",))
        else:
            sc.add("dve", lambda E: E.tensor_add(
                P.a(asum, 1), P.a(asum, 1), P.a(asum + 1, 1)),
                reads=("asum0", "asum1"), writes=("asum0",))
    sc.add("pe", lambda E: E.matmul(P.ps[0][0:1, 0:1], P.onesc[:],
                                    P.a(asum, 1), start=True, stop=True),
           reads=("asum0", "onesc"), writes=("ps0",))
    sc.add("act", lambda E: E.mul(P.a(srow, 1, 0, 1), P.ps[0][0:1, 0:1],
                                  1.0 / (D * D)),
           reads=("ps0",), writes=("srow",))
    sc.add("dve", lambda E: E.tensor_scalar_max(
        P.a(srow + 1, 1, 0, 1), P.a(srow, 1, 0, 1), EPS),
        reads=("srow",), writes=("srow",))
    sc.add("dve", lambda E: E.reciprocal(P.a(srow + 2, 1, 0, 1),
                                         P.a(srow + 1, 1, 0, 1)),
           reads=("srow",), writes=("srow",))
    sc.add("pe", lambda E: E.matmul(P.ps[0][:, 0:2], P.ones1[:],
                                    P.a(srow + 1, 2, 0, 1), start=True, stop=True),
           reads=("srow", "ones1"), writes=("ps0",))
    sc.add("dve", lambda E: E.tensor_copy(P.swcol[:, 0:1], P.ps[0][:, 1:2]),
           reads=("ps0",), writes=("swcol",))
    sc.add("dve", lambda E: E.tensor_copy(P.swcol[:, 1:2], P.ps[0][:, 0:1]),
           reads=("ps0",), writes=("swcol",))
    P.tap("sw", lambda: P.a(srow, 3, 0, 1), ("srow",))
    sc.fence()
    P.release()

    _ln_proj(P, dict(
        x_src=x_tm, wt=wqkvTt, nf=48, pfx="p1",
        xn_tm_d=xn_tm_d, xn_fm_d=None, q_fm_d=q0_fm_d, q_chunks=32,
        cc_in=cc0_in, kv_chunks=(32, 40), bias=None,
        wg=wgTt, bg=bg, gates_d=gates_d))
    sc.fence()
    P.tap("xn_tm", lambda: xn_tm_d[:], ("xn_tm_d",))
    P.tap("q0_fm", lambda: q0_fm_d[:], ("q0_fm_d",))
    P.tap("gates_tm", lambda: gates_d[:], ("gates_d",))

    if SIM_LOCAL_CC:
        for r in range(2):
            sc.add("qp", lambda E, r=r: E.dma_start(
                out=cc0_out[r], in_=cc0_in[:]),
                reads=("cc0_in",), writes=("cc0_out",))
    else:
        sc.add("pool", lambda E: E.collective_compute(
            "AllGather", OP.bypass, ins=[cc0_in[:]], outs=[cc0_out[:]],
            replica_groups=[[0, 1], [2, 3], [4, 5], [6, 7]]),
            reads=("cc0_in",), writes=("cc0_out",))
    sc.fence()
    for nm, kv in (("kfull", 0), ("vfull", 1)):
        if nm not in P.taps:
            continue
        for kvh in range(KVH):
            for r in range(2):
                sc.add("qp", lambda E, nm=nm, kv=kv, kvh=kvh, r=r: E.dma_start(
                    out=P.dram["tap_" + nm][kvh]
                        .rearrange("b (j r2 c) -> b r2 j c", r2=2, c=128)[:, r, :, :],
                    in_=cc0_out[r, kv, kvh, :, :]
                        .rearrange("b (j c) -> b j c", c=128)),
                    reads=("cc0_out",), writes=("tap_" + nm,))

    _build_rest(P, d_, locals())

    sc.emit(nc)
    P.close()
    return nc, P


def _build_rest(P, d_, env):
    sc = P.sc
    cc0_out = env["cc0_out"]
    acc_d = env["acc_d"]
    gates_d = env["gates_d"]
    q0_fm_d = env["q0_fm_d"]
    s0_d = env["s0_d"]
    kwq_d = P.dint("kwq", (NCH, 128, D), BF16)

    P.mark()
    rk = P.alloc(KVH * 2048)
    vtm = P.alloc(KVH * 2048)
    ckb = P.alloc(KVH * 129)
    cvb = P.alloc(KVH * 128)
    prm = P.alloc(128)
    kposb = P.alloc(KVH * 16)
    vposb = P.alloc(KVH * 16)
    kb1b = P.alloc(16)
    vb1b = P.alloc(16)
    kb2b = P.alloc(1)
    vb2b = P.alloc(128)
    memkb = P.alloc(KVH)
    memvb = P.alloc(KVH * 128)
    for dst, src, key in (
            (P.a(prm, 128), d_["permL0"], "prm"),
            (P.a(kposb, KVH * 16), d_["kpos"], "kposb"),
            (P.a(vposb, KVH * 16), d_["vpos"], "vposb"),
            (P.a(kb1b, 16), d_["kb1"], "kb1b"),
            (P.a(vb1b, 16), d_["vb1"], "vb1b"),
            (P.a(kb2b, 1), d_["kb2"], "kb2b"),
            (P.a(vb2b, 128, 0, 1), d_["vb2"], "vb2b"),
            (P.a(memkb, KVH), d_["memk"], "memkb"),
            (P.a(memvb, KVH * 128, 0, 1), d_["memv"], "memvb"),
    ):
        sc.add("qs", lambda E, dst=dst, src=src: E.dma_start(out=dst, in_=src[:]),
               writes=(key,))

    P.mark()
    cosf = P.alloc(2048)
    sinf = P.alloc(2048)
    vtmp = P.alloc(2048)
    w1b = P.alloc(2048)
    w2b = P.alloc(2048)
    kw1b = P.alloc(2048)
    kw2b = P.alloc(2 * 128)
    sc.add("qs", lambda E: E.dma_start(out=P.a(cosf, 2048), in_=d_["cosL0"][:]),
           writes=("cosf",))
    sc.add("qs", lambda E: E.dma_start(out=P.a(sinf, 2048), in_=d_["sinL0"][:]),
           writes=("sinf",))

    def kv_load(dst_base, dst_key, kv, kvh, cc, eng="qs"):
        for r in range(2):
            sc.add(eng, lambda E, r=r: E.dma_start(
                out=P.a(dst_base, 2048)
                    .rearrange("b (j r2 c) -> b r2 j c", r2=2, c=128)[:, r, :, :],
                in_=cc[r, kv, kvh, :, :].rearrange("b (j c) -> b j c", c=128)),
                reads=(cc is cc0_out and "cc0_out" or "cc1_out",),
                writes=(dst_key,))

    nw2 = 0
    for kvh in range(KVH):
        rkk = f"rk{kvh}"
        kv_load(rk + kvh * 2048, rkk, 0, kvh, cc0_out)
        kv_load(vtmp, "vtmp", 1, kvh, cc0_out)
        sc.add("dve", lambda E, kvh=kvh: E.tensor_tensor(
            P.a(w1b, 2048).rearrange("b (c i) -> b c i", i=16),
            P.a(rk + kvh * 2048, 2048).rearrange("b (c i) -> b c i", i=16),
            P.a(kposb + kvh * 16, 16).unsqueeze(1).to_broadcast([128, 128, 16]),
            OP.add), reads=(rkk, "kposb"), writes=("w1",))
        for hc in range(16):
            i = kvh * 16 + hc
            sc.add("qs", lambda E, hc=hc, i=i: E.dma_start(
                out=P.a(kw1b, 2048), in_=d_["kw1Tt"][hc]),
                writes=("kw1",))
            for jc in range(16):
                sc.add("pe", lambda E, hc=hc, jc=jc, i=i: E.matmul(
                    P.ps[0][:, 0:128],
                    P.a(kw1b + jc * 128, 128),
                    P.a(w1b, 2048).rearrange("b (c i2) -> b i2 c", i2=16)[:, jc, :],
                    start=(jc == 0), stop=(jc == 15)),
                    reads=("kw1", "w1"), writes=("ps0",))
            sc.add("act", lambda E, hc=hc: E.activation(
                P.a(w2b + hc * 128, 128), P.ps[0][:, 0:128], AF.Relu,
                bias=P.a(kb1b + hc, 1)),
                reads=("ps0", "kb1b"), writes=("w2",))
        for hc in range(16):
            i2 = nw2
            nw2 += 1
            sc.add("qs", lambda E, hc=hc, i2=i2: E.dma_start(
                out=P.a(kw2b + (i2 % 2) * 128, 128), in_=d_["kw2Tt"][hc]),
                writes=(f"kw2.{i2 % 2}",))
            sc.add("pe", lambda E, hc=hc, i2=i2: E.matmul(
                P.ps[1][:, 0:128], P.a(kw2b + (i2 % 2) * 128, 128),
                P.a(w2b + hc * 128, 128), start=(hc == 0), stop=(hc == 15)),
                reads=(f"kw2.{i2 % 2}", "w2"), writes=("ps1",))
        sc.add("act", lambda E, kvh=kvh: E.activation(
            P.a(ckb + kvh * 129 + 1, 128), P.ps[1][:, 0:128], AF.Identity,
            bias=P.a(kb2b, 1)), reads=("ps1", "kb2b"), writes=(f"ck{kvh}",))
        sc.add("dve", lambda E, kvh=kvh: E.tensor_copy(
            P.a(ckb + kvh * 129, 1), P.a(memkb + kvh, 1)),
            reads=("memkb",), writes=(f"ck{kvh}",))
        for j in range(4):
            sc.add("pe", lambda E, kvh=kvh, j=j: E.matmul(
                P.psbig[:, j * 512:(j + 1) * 512], P.a(prm, 128),
                P.a(rk + kvh * 2048 + j * 512, 512), start=True, stop=True),
                reads=(rkk, "prm"), writes=(f"psb{j}",))
        sc.add("dve", lambda E, kvh=kvh: E.tensor_tensor(
            P.a(w1b, 2048), P.psbig[:, 0:2048], P.a(sinf, 2048), OP.mult),
            reads=("psb0", "psb1", "psb2", "psb3", "sinf"), writes=("w1",))
        sc.add("dve", lambda E, kvh=kvh: E.tensor_tensor(
            P.a(rk + kvh * 2048, 2048), P.a(rk + kvh * 2048, 2048),
            P.a(cosf, 2048), OP.mult), reads=(rkk, "cosf"), writes=(rkk,))
        sc.add("dve", lambda E, kvh=kvh: E.tensor_tensor(
            P.a(rk + kvh * 2048, 2048), P.a(rk + kvh * 2048, 2048),
            P.a(w1b, 2048), OP.add), reads=(rkk, "w1"), writes=(rkk,))
        sc.add("dve", lambda E, kvh=kvh: E.tensor_tensor(
            P.a(w1b, 2048).rearrange("b (c i) -> b c i", i=16),
            P.a(vtmp, 2048).rearrange("b (c i) -> b c i", i=16),
            P.a(vposb + kvh * 16, 16).unsqueeze(1).to_broadcast([128, 128, 16]),
            OP.add), reads=("vtmp", "vposb"), writes=("w1",))
        for hc in range(16):
            i = KVH * 16 + kvh * 16 + hc
            sc.add("qs", lambda E, hc=hc, i=i: E.dma_start(
                out=P.a(kw1b, 2048), in_=d_["vw1Tt"][hc]),
                writes=("kw1",))
            for jc in range(16):
                sc.add("pe", lambda E, hc=hc, jc=jc, i=i: E.matmul(
                    P.ps[0][:, 0:128],
                    P.a(kw1b + jc * 128, 128),
                    P.a(w1b, 2048).rearrange("b (c i2) -> b i2 c", i2=16)[:, jc, :],
                    start=(jc == 0), stop=(jc == 15)),
                    reads=("kw1", "w1"), writes=("ps0",))
            sc.add("act", lambda E, hc=hc: E.activation(
                P.a(w2b + hc * 128, 128), P.ps[0][:, 0:128], AF.Relu,
                bias=P.a(vb1b + hc, 1)),
                reads=("ps0", "vb1b"), writes=("w2",))
        for hc in range(16):
            i2 = nw2
            nw2 += 1
            sc.add("qs", lambda E, hc=hc, i2=i2: E.dma_start(
                out=P.a(kw2b + (i2 % 2) * 128, 128), in_=d_["vw2Tt"][hc]),
                writes=(f"kw2.{i2 % 2}",))
            sc.add("pe", lambda E, hc=hc, i2=i2: E.matmul(
                P.ps[1][:, 0:128], P.a(w2b + hc * 128, 128),
                P.a(kw2b + (i2 % 2) * 128, 128), start=(hc == 0), stop=False),
                reads=(f"kw2.{i2 % 2}", "w2"), writes=("ps1",))
        sc.add("pe", lambda E, kvh=kvh: E.matmul(
            P.ps[1][:, 0:128], P.ones1[:], P.a(vb2b, 128, 0, 1),
            start=False, stop=True), reads=("vb2b", "ones1"), writes=("ps1",))
        sc.add("dve", lambda E, kvh=kvh: E.tensor_copy(
            P.a(cvb + kvh * 128, 128), P.ps[1][:, 0:128]),
            reads=("ps1",), writes=(f"cv{kvh}",))
        for gt in range(16):
            pb = 2 + (gt % 2)
            sc.add("pe", lambda E, kvh=kvh, gt=gt, pb=pb: E.transpose(
                P.ps[pb][:, 0:128], P.a(vtmp + gt * 128, 128), P.ident[:]),
                reads=("vtmp", "ident"), writes=(f"ps{pb}",))
            eng = "act" if gt % 2 == 0 else "dve"
            sc.add(eng, lambda E, kvh=kvh, gt=gt, pb=pb: (
                E.copy(P.a(vtm + kvh * 2048 + gt * 128, 128), P.ps[pb][:, 0:128])
                if gt % 2 == 0 else
                E.tensor_copy(P.a(vtm + kvh * 2048 + gt * 128, 128),
                              P.ps[pb][:, 0:128])),
                reads=(f"ps{pb}",), writes=(f"vtm{kvh}",))
    P.release()
    sc.fence()
    if "rk" in P.taps:
        for kvh in range(KVH):
            sc.add("qp", lambda E, kvh=kvh: E.dma_start(
                out=P.dram["tap_rk"][kvh], in_=P.a(rk + kvh * 2048, 2048)),
                reads=(f"rk{kvh}",), writes=("tap_rk",))
    if "ck" in P.taps:
        for kvh in range(KVH):
            sc.add("qp", lambda E, kvh=kvh: E.dma_start(
                out=P.dram["tap_ck"][kvh], in_=P.a(ckb + kvh * 129, 129)),
                reads=(f"ck{kvh}",), writes=("tap_ck",))
    if "cvtm" in P.taps:
        for kvh in range(KVH):
            sc.add("qp", lambda E, kvh=kvh: E.dma_start(
                out=P.dram["tap_cvtm"][kvh], in_=P.a(cvb + kvh * 128, 128)),
                reads=(f"cv{kvh}",), writes=("tap_cvtm",))

    P.mark()
    clq = P.alloc(128)
    slq = P.alloc(128)
    cmb = P.alloc(129)
    d2b = P.alloc(256)
    sbb = P.alloc(384)
    gtb = P.alloc(96)
    q4b = P.alloc(512)
    rqb = P.alloc(2 * 128)
    accb = P.alloc(2 * 512)
    scb = P.alloc(2 * 2048)
    pb_ = P.alloc(2 * 2048)
    ptb = P.alloc(2048)
    s2b = P.alloc(256)
    csb = P.alloc(4 * 129)
    impb = P.alloc(128)
    eb = P.alloc(128)
    fselb = P.alloc(128)
    curb = P.alloc(128)
    colb = P.alloc(2 * 32)
    mempb = P.alloc(128)
    cpb = P.alloc(2 * 132)
    it4 = [0]
    pend4 = [None]
    mrep = P.alloc(8)
    sc.add("dve", lambda E: E.memset(P.a(mrep, 8), 1.0e38), writes=("mrep",))

    for qt in range(TPC):
        if pend4[0] is not None:
            pend4[0]()
            pend4[0] = None
        for dst, src, key in (
                (P.a(clq, 128), d_["coslq0"][:, qt * 128:(qt + 1) * 128], "clq"),
                (P.a(slq, 128), d_["sinlq0"][:, qt * 128:(qt + 1) * 128], "slq"),
                (P.a(cmb, 129), d_["cmask"][:, qt * 129:(qt + 1) * 129], "cmb"),
                (P.a(d2b, 256), d_["diag2"][qt], "d2b"),
                (P.a(sbb, 384), d_["sbandm"][qt], "sbb"),
                (P.a(gtb, 96), gates_d[qt * 128:(qt + 1) * 128, :], "gtb"),
        ):
            sc.add("qs", lambda E, dst=dst, src=src: E.dma_start(out=dst, in_=src),
                   writes=(key,))
        nk = (qt + 1) * 256
        nblk = nk // 16
        ntile = nk // 128
        soff = max(0, 2 * qt - 1) * 128
        for kvh in range(KVH):
            ak = f"acc{kvh % 2}"
            ao = accb + (kvh % 2) * 512
            for g in range(G):
                h = kvh * G + g
                sc.add("qs", lambda E, h=h, qt=qt, g=g: E.dma_start(
                    out=P.a(q4b + g * 128, 128),
                    in_=q0_fm_d[h, :, qt * 128:(qt + 1) * 128]),
                    writes=(f"q4.{g}",))
            for g in range(G):
                sc.add("pe", lambda E, g=g, kvh=kvh: E.matmul(
                    P.ps[0][:, 0:129], P.a(q4b + g * 128, 128),
                    P.a(ckb + kvh * 129, 129), start=True, stop=True),
                    reads=(f"q4.{g}", f"ck{kvh}"), writes=("ps0",))
                sc.add("dve", lambda E, g=g: E.scalar_tensor_tensor(
                    P.a(csb + g * 129, 129), P.ps[0][:, 0:129], SCALE,
                    P.a(cmb, 129), OP.mult, OP.add),
                    reads=("ps0", "cmb"), writes=(f"cs{g}",))
            sc.add("dve", lambda E: E.tensor_add(
                P.a(impb, 128), P.a(csb + 1, 128), P.a(csb + 129 + 1, 128)),
                reads=("cs0", "cs1"), writes=("impb",))
            sc.add("dve", lambda E: E.tensor_add(
                P.a(eb, 128), P.a(csb + 2 * 129 + 1, 128),
                P.a(csb + 3 * 129 + 1, 128)),
                reads=("cs2", "cs3"), writes=("eb",))
            sc.add("dve", lambda E: E.tensor_scalar_mul(
                P.a(impb, 128), P.a(impb, 128), 0.25),
                reads=("impb",), writes=("impb",))
            sc.add("dve", lambda E: E.tensor_scalar_mul(
                P.a(eb, 128), P.a(eb, 128), 0.25),
                reads=("eb",), writes=("eb",))
            sc.add("dve", lambda E: E.tensor_add(
                P.a(impb, 128), P.a(impb, 128), P.a(eb, 128)),
                reads=("impb", "eb"), writes=("impb",))
            sc.add("dve", lambda E: E.tensor_reduce(
                P.a(colb, 1), P.a(impb, 128), axis=AX.X, op=OP.max),
                reads=("impb",), writes=("c0",))
            sc.add("dve", lambda E: E.tensor_scalar_max(
                P.a(colb + 1, 1), P.a(colb, 1), -1000.0),
                reads=("c0",), writes=("c1",))
            sc.add("dve", lambda E: E.tensor_scalar_mul(
                P.a(colb + 2, 1), P.a(colb + 1, 1), -1.0),
                reads=("c1",), writes=("c2",))
            sc.add("act", lambda E: E.activation(
                P.a(eb, 128), P.a(impb, 128), AF.Exp,
                bias=P.a(colb + 2, 1), accum_out=P.a(colb + 3, 1)),
                reads=("impb", "c2"), writes=("eb", "c3"))
            sc.add("dve", lambda E: E.tensor_scalar_add(
                P.a(colb + 16, 1), P.a(colb + 2, 1), -1000.0),
                reads=("c2",), writes=("c16",))
            sc.add("act", lambda E: E.activation(
                P.a(colb + 4, 1), P.a(colb + 16, 1), AF.Exp),
                reads=("c16",), writes=("c4",))
            sc.add("dve", lambda E: E.tensor_add(
                P.a(colb + 5, 1), P.a(colb + 3, 1), P.a(colb + 4, 1)),
                reads=("c3", "c4"), writes=("c5",))
            sc.add("dve", lambda E: E.reciprocal(
                P.a(colb + 6, 1), P.a(colb + 5, 1)),
                reads=("c5",), writes=("c6",))
            sc.add("dve", lambda E: E.tensor_copy(P.a(curb, 128), P.a(impb, 128)),
                   reads=("impb",), writes=("curb",))
            for it in range(4):
                sc.add("dve", lambda E: E.tensor_reduce(
                    P.a(colb + 7, 1), P.a(curb, 128), axis=AX.X, op=OP.max),
                    reads=("curb",), writes=("c7",))
                if it < 3:
                    sc.add("dve", lambda E: E.tensor_copy(
                        P.a(mrep, 1), P.a(colb + 7, 1)),
                        reads=("c7",), writes=("mrep",))
                    sc.add("dve", lambda E: E.match_replace(
                        P.a(curb, 128), P.a(mrep, 8), P.a(curb, 128), NEG),
                        reads=("curb", "mrep"), writes=("curb",))
            sc.add("dve", lambda E: E.tensor_single_scalar(
                P.a(fselb, 128), P.a(impb, 128), P.a(colb + 7, 1), OP.is_ge),
                reads=("impb", "c7"), writes=("fselb",))
            sc.add("dve", lambda E: E.tensor_scalar(
                P.a(eb, 128), P.a(eb, 128), P.a(colb + 6, 1), 1e-10,
                OP.mult, OP.is_gt), reads=("eb", "c6"), writes=("eb",))
            sc.add("dve", lambda E: E.tensor_tensor(
                P.a(fselb, 128), P.a(fselb, 128), P.a(eb, 128), OP.mult),
                reads=("fselb", "eb"), writes=("fselb",))
            sc.add("dve", lambda E: E.tensor_scalar(
                P.a(fselb, 128), P.a(fselb, 128), 1.0, 1e38,
                OP.subtract, OP.mult), reads=("fselb",), writes=("fselb",))
            for g in range(G):
                h = kvh * G + g
                pr4 = it4[0] % 2
                it4[0] += 1
                RQB = rqb + pr4 * 128
                SCB = scb + pr4 * 2048
                PBB = pb_ + pr4 * 2048
                PTB = ptb
                S2B = s2b
                CLB = colb + pr4 * 32
                MPB = mempb
                CPB = cpb + pr4 * 132
                KP = lambda nm, pr4=pr4: f"{nm}{pr4}"
                sc.add("pe", lambda E, g=g: E.matmul(
                    P.ps[2][:, 0:128], P.a(prm, 128), P.a(q4b + g * 128, 128),
                    start=True, stop=True),
                    reads=(f"q4.{g}", "prm"), writes=("ps2",))
                sc.add("dve", lambda E, S2B=S2B: E.tensor_copy(
                    P.a(S2B, 128), P.ps[2][:, 0:128]),
                    reads=("ps2",), writes=("s2b",))
                sc.add("dve", lambda E, S2B=S2B: E.tensor_tensor(
                    P.a(S2B, 128), P.a(S2B, 128), P.a(slq, 128), OP.mult),
                    reads=("s2b", "slq"), writes=("s2b",))
                sc.add("dve", lambda E, g=g, RQB=RQB: E.tensor_tensor(
                    P.a(RQB, 128), P.a(q4b + g * 128, 128), P.a(clq, 128),
                    OP.mult), reads=(f"q4.{g}", "clq"), writes=(KP("rqb"),))
                sc.add("dve", lambda E, RQB=RQB, S2B=S2B: E.tensor_tensor(
                    P.a(RQB, 128), P.a(RQB, 128), P.a(S2B, 128), OP.add),
                    reads=(KP("rqb"), "s2b"), writes=(KP("rqb"),))
                o = 0
                while o < nk:
                    w = min(512, nk - o)
                    sc.add("pe", lambda E, kvh=kvh, o=o, w=w, RQB=RQB: E.matmul(
                        P.psbig[:, o:o + w], P.a(RQB, 128),
                        P.a(rk + kvh * 2048 + o, w),
                        start=True, stop=True),
                        reads=(KP("rqb"), f"rk{kvh}"), writes=(f"psb{o // 512}",))
                    o += w
                psbks = tuple(f"psb{t}" for t in range((nk + 511) // 512))
                sc.add("dve", lambda E, nk=nk, nblk=nblk, SCB=SCB: E.scalar_tensor_tensor(
                    P.a(SCB, nk).rearrange("b (c i) -> b c i", i=16),
                    P.psbig[:, 0:nk].rearrange("b (c i) -> b c i", i=16), SCALE,
                    P.a(fselb, nblk).unsqueeze(2).to_broadcast([128, nblk, 16]),
                    OP.mult, OP.add),
                    reads=psbks + ("fselb",), writes=(KP("scb"),))
                sc.add("dve", lambda E, nk=nk, S2B=S2B: E.scalar_tensor_tensor(
                    P.a(S2B, 256), P.psbig[:, nk - 256:nk], SCALE,
                    P.a(d2b, 256), OP.mult, OP.add),
                    reads=psbks + ("d2b",), writes=("s2b",))
                sc.add("dve", lambda E, nk=nk, SCB=SCB, S2B=S2B: E.tensor_max(
                    P.a(SCB + nk - 256, 256), P.a(SCB + nk - 256, 256),
                    P.a(S2B, 256)), reads=(KP("scb"), "s2b"), writes=(KP("scb"),))
                sc.add("dve", lambda E, nk=nk, SCB=SCB, CLB=CLB: E.tensor_reduce(
                    P.a(CLB + 4, 1), P.a(SCB, nk), axis=AX.X, op=OP.max,
                    negate=True), reads=(KP("scb"),), writes=(KP("c12"),))
                sc.add("act", lambda E, nk=nk, SCB=SCB, PBB=PBB, CLB=CLB: E.activation(
                    P.a(PBB, nk), P.a(SCB, nk), AF.Exp,
                    bias=P.a(CLB + 4, 1), accum_out=P.a(CLB + 5, 1)),
                    reads=(KP("scb"), KP("c12")), writes=(KP("pb0"), KP("c13")))
                sc.add("dve", lambda E, g=g, CLB=CLB: E.tensor_reduce(
                    P.a(CLB + 0, 1), P.a(csb + g * 129, 129), axis=AX.X,
                    op=OP.max, negate=True), reads=(f"cs{g}",), writes=(KP("c8"),))
                sc.add("act", lambda E, g=g, CPB=CPB, CLB=CLB: E.activation(
                    P.a(CPB, 129), P.a(csb + g * 129, 129), AF.Exp,
                    bias=P.a(CLB + 0, 1), accum_out=P.a(CLB + 1, 1)),
                    reads=(f"cs{g}", KP("c8")), writes=(KP("cpb"), KP("c9")))
                sc.add("pe", lambda E, CPB=CPB: E.transpose(
                    P.ps[1][:, 0:128], P.a(CPB + 1, 128), P.ident[:]),
                    reads=(KP("cpb"), "ident"), writes=("ps1",))
                sc.add("pe", lambda E, CPB=CPB: E.transpose(
                    P.ps[2][0:1, 0:128], P.a(CPB, 1), P.ident[:]),
                    reads=(KP("cpb"), "ident"), writes=("ps2",))
                sc.add("act", lambda E, PTB=PTB: E.copy(
                    P.a(PTB, 128), P.ps[1][:, 0:128]),
                    reads=("ps1",), writes=("ptb0",))
                sc.add("dve", lambda E, MPB=MPB: E.tensor_copy(
                    P.a(MPB, 128, 0, 1), P.ps[2][0:1, 0:128]),
                    reads=("ps2",), writes=("mempb",))
                sc.add("pe", lambda E, kvh=kvh, PTB=PTB: E.matmul(
                    P.ps[3][:, 0:128], P.a(PTB, 128), P.a(cvb + kvh * 128, 128),
                    start=True, stop=False),
                    reads=("ptb0", f"cv{kvh}"), writes=("ps3",))
                sc.add("pe", lambda E, kvh=kvh, MPB=MPB: E.matmul(
                    P.ps[3][:, 0:128], P.a(MPB, 128, 0, 1),
                    P.a(memvb + kvh * 128, 128, 0, 1),
                    start=False, stop=True),
                    reads=("mempb", "memvb"), writes=("ps3",))
                sc.add("dve", lambda E, CLB=CLB: E.reciprocal(
                    P.a(CLB + 2, 1), P.a(CLB + 1, 1)),
                    reads=(KP("c9"),), writes=(KP("c10"),))
                sc.add("dve", lambda E, h=h, CLB=CLB: E.tensor_mul(
                    P.a(CLB + 3, 1), P.a(CLB + 2, 1),
                    P.a(gtb + 3 * h, 1)), reads=(KP("c10"), "gtb"), writes=(KP("c11"),))
                sc.add("dve", lambda E, g=g, ao=ao, CLB=CLB: E.tensor_single_scalar(
                    P.a(ao + g * 128, 128), P.ps[3][:, 0:128],
                    P.a(CLB + 3, 1), OP.mult),
                    reads=("ps3", KP("c11")), writes=(ak,))

                def stage2(g=g, h=h, kvh=kvh, qt=qt, ao=ao, ak=ak, nk=nk,
                           ntile=ntile, soff=soff, RQB=RQB, SCB=SCB, PBB=PBB,
                           PTB=PTB, S2B=S2B, CLB=CLB, KP=KP, last=(g == G - 1)):
                    for t in range(ntile):
                        pb2 = 1 + (t % 2)
                        sc.add("pe", lambda E, t=t, pb2=pb2: E.transpose(
                            P.ps[pb2][:, 0:128], P.a(PBB + t * 128, 128), P.ident[:]),
                            reads=(KP("pb0"), "ident"), writes=(f"ps{pb2}",))
                        eng = "act" if t % 2 == 0 else "dve"
                        sc.add(eng, lambda E, t=t, pb2=pb2, eng=eng: (
                            E.copy(P.a(PTB + t * 128, 128), P.ps[pb2][:, 0:128])
                            if eng == "act" else
                            E.tensor_copy(P.a(PTB + t * 128, 128), P.ps[pb2][:, 0:128])),
                            reads=(f"ps{pb2}",), writes=(f"pt.{t}",))
                    for t in range(ntile):
                        sc.add("pe", lambda E, t=t: E.matmul(
                            P.ps[3][:, 0:128], P.a(PTB + t * 128, 128),
                            P.a(vtm + kvh * 2048 + t * 128, 128),
                            start=(t == 0), stop=(t == ntile - 1)),
                            reads=(f"pt.{t}", f"vtm{kvh}"), writes=("ps3",))
                    sc.add("dve", lambda E: E.reciprocal(
                        P.a(CLB + 6, 1), P.a(CLB + 5, 1)),
                        reads=(KP("c13"),), writes=(KP("c14"),))
                    sc.add("dve", lambda E: E.tensor_mul(
                        P.a(CLB + 7, 1), P.a(CLB + 6, 1),
                        P.a(gtb + 3 * h + 1, 1)), reads=(KP("c14"), "gtb"),
                        writes=(KP("c15"),))
                    sc.add("dve", lambda E: E.scalar_tensor_tensor(
                        P.a(ao + g * 128, 128), P.ps[3][:, 0:128],
                        P.a(CLB + 7, 1), P.a(ao + g * 128, 128),
                        OP.mult, OP.add), reads=("ps3", KP("c15"), ak), writes=(ak,))
                    sc.add("pe", lambda E: E.matmul(
                        P.ps[0][:, 0:384], P.a(RQB, 128),
                        P.a(rk + kvh * 2048 + soff, 384), start=True, stop=True),
                        reads=(KP("rqb"), f"rk{kvh}"), writes=("ps0",))
                    sc.add("dve", lambda E: E.scalar_tensor_tensor(
                        P.a(SCB, 384), P.ps[0][:, 0:384], SCALE, P.a(sbb, 384),
                        OP.mult, OP.add), reads=("ps0", "sbb"), writes=(KP("scb"),))
                    sc.add("dve", lambda E: E.tensor_reduce(
                        P.a(CLB + 4, 1), P.a(SCB, 384), axis=AX.X, op=OP.max,
                        negate=True), reads=(KP("scb"),), writes=(KP("c12"),))
                    sc.add("act", lambda E: E.activation(
                        P.a(PBB, 384), P.a(SCB, 384), AF.Exp,
                        bias=P.a(CLB + 4, 1), accum_out=P.a(CLB + 5, 1)),
                        reads=(KP("scb"), KP("c12")), writes=(KP("pb0"), KP("c13")))
                    for t in range(3):
                        pb2 = 1 + (t % 2)
                        sc.add("pe", lambda E, t=t, pb2=pb2: E.transpose(
                            P.ps[pb2][:, 0:128], P.a(PBB + t * 128, 128), P.ident[:]),
                            reads=(KP("pb0"), "ident"), writes=(f"ps{pb2}",))
                        eng = "act" if t % 2 == 0 else "dve"
                        sc.add(eng, lambda E, t=t, pb2=pb2, eng=eng: (
                            E.copy(P.a(PTB + t * 128, 128), P.ps[pb2][:, 0:128])
                            if eng == "act" else
                            E.tensor_copy(P.a(PTB + t * 128, 128), P.ps[pb2][:, 0:128])),
                            reads=(f"ps{pb2}",), writes=(f"pt.{t}",))
                    for t in range(3):
                        sc.add("pe", lambda E, t=t: E.matmul(
                            P.ps[3][:, 0:128], P.a(PTB + t * 128, 128),
                            P.a(vtm + kvh * 2048 + soff + t * 128, 128),
                            start=(t == 0), stop=(t == 2)),
                            reads=(f"pt.{t}", f"vtm{kvh}"), writes=("ps3",))
                    sc.add("dve", lambda E: E.reciprocal(
                        P.a(CLB + 6, 1), P.a(CLB + 5, 1)),
                        reads=(KP("c13"),), writes=(KP("c14"),))
                    sc.add("dve", lambda E: E.tensor_mul(
                        P.a(CLB + 7, 1), P.a(CLB + 6, 1),
                        P.a(gtb + 3 * h + 2, 1)), reads=(KP("c14"), "gtb"),
                        writes=(KP("c15"),))
                    sc.add("dve", lambda E: E.scalar_tensor_tensor(
                        P.a(ao + g * 128, 128), P.ps[3][:, 0:128],
                        P.a(CLB + 7, 1), P.a(ao + g * 128, 128),
                        OP.mult, OP.add), reads=("ps3", KP("c15"), ak), writes=(ak,))
                    if last:
                        sc.add("qp", lambda E: E.dma_start(
                            out=acc_d[qt * 128:(qt + 1) * 128,
                                      kvh * 512:(kvh + 1) * 512], in_=P.a(ao, 512)),
                            reads=(ak,), writes=("acc_d",))

                if pend4[0] is not None:
                    pend4[0]()
                pend4[0] = stage2
    if pend4[0] is not None:
        pend4[0]()
        pend4[0] = None
    P.release()
    P.release()
    sc.fence()
    P.tap("attn_acc", lambda: acc_d[:], ("acc_d",))

    P.mark()
    atT = P.alloc(NCH * TOK)
    actl = P.alloc(D)
    wti = P.alloc(2 * 512)
    evb = P.alloc(4 * 512)
    for qt in range(TPC):
        sc.add("qs", lambda E, qt=qt: E.dma_start(
            out=P.a(actl, D), in_=acc_d[qt * 128:(qt + 1) * 128, :]),
            reads=("acc_d",), writes=("actl",))
        for c in range(NCH):
            pb = (c % 2)
            sc.add("pe", lambda E, c=c, pb=pb: E.transpose(
                P.ps[pb][:, 0:128], P.a(actl + c * 128, 128), P.ident[:]),
                reads=("actl", "ident"), writes=(f"ps{pb}",))
            eng = "act" if c % 2 == 0 else "dve"
            sc.add(eng, lambda E, c=c, qt=qt, pb=pb, eng=eng: (
                E.copy(P.a(atT + c * TOK + qt * 128, 128), P.ps[pb][:, 0:128])
                if eng == "act" else
                E.tensor_copy(P.a(atT + c * TOK + qt * 128, 128),
                              P.ps[pb][:, 0:128])),
                reads=(f"ps{pb}",), writes=(f"atT.{c}.{qt}",))
    atkeys = lambda c: tuple(f"atT.{c}.{q}" for q in range(TPC))

    def psq(qt):
        if qt < 4:
            return P.ps[qt][:, 0:512], f"ps{qt}"
        return P.psbig[:, (qt - 4) * 512:(qt - 3) * 512], f"psb{qt - 4}"

    for fc in range(8):
        for kc in range(NCH):
            i = fc * NCH + kc
            sc.add("qs", lambda E, kc=kc, fc=fc, i=i: E.dma_start(
                out=P.a(wti + (i % 2) * 512, 512), in_=d_["woTt"][kc, fc]),
                writes=(f"wt.{i % 2}",))
            for qt in range(TPC):
                ps, pk = psq(qt)
                sc.add("pe", lambda E, kc=kc, qt=qt, i=i, ps=ps: E.matmul(
                    ps, P.a(atT + kc * TOK + qt * 128, 128),
                    P.a(wti + (i % 2) * 512, 512),
                    start=(kc == 0), stop=(kc == NCH - 1)),
                    reads=(f"wt.{i % 2}", f"atT.{kc}.{qt}"),
                    writes=(pk,))
        for qt in range(TPC):
            ps, pk = psq(qt)
            ek = f"ev.{qt % 4}"
            sc.add("act", lambda E, qt=qt, ps=ps: E.copy(
                P.a(evb + (qt % 4) * 512, 512), ps),
                reads=(pk,), writes=(ek,))
            sc.add("qp", lambda E, qt=qt, fc=fc: E.dma_start(
                out=s0_d[qt * 128:(qt + 1) * 128, fc * 512:(fc + 1) * 512],
                in_=P.a(evb + (qt % 4) * 512, 512)),
                reads=(ek,), writes=("s0_d",))
    wti6 = P.alloc(D)
    b2 = P.alloc(D)
    kwb = P.alloc(D // 2)
    b1 = actl
    for kc in range(NCH):
        sc.add("qs", lambda E, kc=kc: E.dma_start(
            out=P.a(wti6, D), in_=d_["ffnWT"][kc]), writes=("w6",))
        sc.add("dve", lambda E: E.tensor_single_scalar(
            P.a(b1, D), P.a(wti6, D), P.swcol[:, 0:1], OP.mult),
            reads=("w6", "swcol"), writes=("actl",))
        sc.add("dve", lambda E: E.tensor_scalar(
            P.a(b2, D), P.a(b1, D), MAGIC, MAGIC, OP.add, OP.subtract),
            reads=("actl",), writes=("b2",))
        sc.add("dve", lambda E: E.tensor_copy(
            P.a(b1, D).bitcast(I32), P.a(b2, D)), reads=("b2",), writes=("actl",))
        sc.add("dve", lambda E: E.tensor_scalar(
            P.a(kwb, D // 2).bitcast(BF16), P.a(b1, D).bitcast(I32),
            -1.0, 1.0, OP.max, OP.min),
            reads=("actl",), writes=("kwb",))
        sc.add("qp", lambda E, kc=kc: E.dma_start(
            out=kwq_d[kc], in_=P.a(kwb, D // 2).bitcast(BF16)),
            reads=("kwb",), writes=("kwq_d",))
    P.release()
    sc.fence()
    P.tap("s0_tm", lambda: s0_d[:], ("s0_d",))

    xn_tm_d = env["xn_tm_d"]
    x_tm = env["x_tm"]
    x1_d = env["x1_d"]
    xn1_fm_d = env["xn1_fm_d"]
    q1_fm_d = env["q1_fm_d"]
    cc1_in = env["cc1_in"]
    cc1_out = env["cc1_out"]
    s1_d = env["s1_d"]
    out_d = env["out"]

    def psq(qt):
        if qt < 4:
            return P.ps[qt][:, 0:512], f"ps{qt}"
        return P.psbig[:, (qt - 4) * 512:(qt - 3) * 512], f"psb{qt - 4}"

    P.mark()
    kxT = P.alloc(NCH * TOK // 2)
    xnb = P.alloc(D)
    c1 = P.alloc(D)
    c2 = P.alloc(D)
    c3 = P.alloc(D)
    srows = P.alloc(TPC * 128)
    rcols = P.alloc(TPC)
    qcol = P.alloc(8)
    kwt = P.alloc(2 * 256)
    ffnbr = P.alloc(512)
    epi = P.alloc(4 * 512)

    def kxv(kc, qt):
        return P.a(kxT, NCH * TOK // 2).bitcast(BF16)[:, kc * TOK + qt * 128:
                                                      kc * TOK + qt * 128 + 128]

    def kwv(i):
        return P.a(kwt, 2 * 256).bitcast(BF16)[:, (i % 2) * 512:(i % 2) * 512 + 512]
    for qt in range(TPC):
        sc.add("qs", lambda E, qt=qt: E.dma_start(
            out=P.a(xnb, D), in_=xn_tm_d[qt * 128:(qt + 1) * 128, :]),
            reads=("xn_tm_d",), writes=("xnb",))
        sc.add("act", lambda E: E.activation(
            P.a(c1, D), P.a(xnb, D), AF.Square, accum_out=P.a(qcol, 1)),
            reads=("xnb",), writes=("c1", "q0"))
        sc.add("dve", lambda E: E.tensor_scalar(
            P.a(qcol + 1, 1), P.a(qcol, 1), 1.0 / D, EPS, OP.mult, OP.add),
            reads=("q0",), writes=("q1",))
        sc.add("dve", lambda E: E.reciprocal(P.a(qcol + 2, 1), P.a(qcol + 1, 1)),
               reads=("q1",), writes=("q2",))
        sc.add("act", lambda E: E.activation(
            P.a(qcol + 3, 1), P.a(qcol + 2, 1), AF.Sqrt),
            reads=("q2",), writes=("q3",))
        sc.add("dve", lambda E: E.tensor_single_scalar(
            P.a(c1, D), P.a(xnb, D), P.a(qcol + 3, 1), OP.mult),
            reads=("xnb", "q3"), writes=("c1",))
        sc.add("dve", lambda E: E.tensor_reduce(
            P.a(qcol + 4, 1), P.a(c1, D), axis=AX.X, op=OP.max,
            apply_absolute_value=True), reads=("c1",), writes=("q4",))
        sc.add("dve", lambda E: E.tensor_scalar_max(
            P.a(qcol + 5, 1), P.a(qcol + 4, 1), EPS), reads=("q4",), writes=("q5",))
        sc.add("dve", lambda E: E.reciprocal(P.a(qcol + 6, 1), P.a(qcol + 5, 1)),
               reads=("q5",), writes=("q6",))
        sc.add("dve", lambda E: E.tensor_scalar_mul(
            P.a(qcol + 6, 1), P.a(qcol + 6, 1), 127.0),
            reads=("q6",), writes=("q6",))
        sc.add("dve", lambda E: E.tensor_single_scalar(
            P.a(c2, D), P.a(c1, D), P.a(qcol + 6, 1), OP.mult),
            reads=("c1", "q6"), writes=("c2",))
        sc.add("dve", lambda E: E.tensor_scalar(
            P.a(c3, D), P.a(c2, D), MAGIC, MAGIC, OP.add, OP.subtract),
            reads=("c2",), writes=("c3",))
        sc.add("dve", lambda E: E.tensor_copy(
            P.a(c1, D).bitcast(I32), P.a(c3, D)), reads=("c3",), writes=("c1",))
        sc.add("dve", lambda E: E.tensor_copy(
            P.a(c2, D), P.a(c1, D).bitcast(I32)), reads=("c1",), writes=("c2",))
        sc.add("dve", lambda E: E.tensor_scalar(
            P.a(c3, D), P.a(c2, D), -128.0, 127.0, OP.max, OP.min),
            reads=("c2",), writes=("c3",))
        for c in range(NCH):
            pb = c % 2
            sc.add("pe", lambda E, c=c, pb=pb: E.transpose(
                P.ps[pb][:, 0:128], P.a(c3 + c * 128, 128), P.ident[:]),
                reads=("c3", "ident"), writes=(f"ps{pb}",))
            eng = "act" if c % 2 == 0 else "dve"
            sc.add(eng, lambda E, c=c, qt=qt, pb=pb, eng=eng: (
                E.copy(kxv(c, qt), P.ps[pb][:, 0:128])
                if eng == "act" else
                E.tensor_copy(kxv(c, qt), P.ps[pb][:, 0:128])),
                reads=(f"ps{pb}",), writes=(f"kxT.{c}.{qt}",))
        sc.add("dve", lambda E, qt=qt: E.tensor_tensor(
            P.a(qcol + 7, 1), P.a(qcol + 6, 1), P.swcol[:, 0:1], OP.mult),
            reads=("q6", "swcol"), writes=("q7",))
        sc.add("dve", lambda E, qt=qt: E.reciprocal(
            P.a(rcols + qt, 1), P.a(qcol + 7, 1)), reads=("q7",), writes=("rcols",))
        sc.add("pe", lambda E: E.transpose(
            P.ps[2][0:1, 0:128], P.a(qcol + 7, 1), P.ident[:]),
            reads=("q7", "ident"), writes=("ps2",))
        sc.add("dve", lambda E, qt=qt: E.tensor_copy(
            P.a(srows + qt * 128, 128, 0, 1), P.ps[2][0:1, 0:128]),
            reads=("ps2",), writes=("srows",))
    for fc in range(8):
        for kc in range(NCH):
            i = fc * NCH + kc
            sc.add("qs", lambda E, kc=kc, fc=fc, i=i: E.dma_start(
                out=kwv(i), in_=kwq_d[kc, :, fc * 512:(fc + 1) * 512]),
                reads=("kwq_d",), writes=(f"kw6.{i % 2}",))
            for qt in range(TPC):
                ps, pk = psq(qt)
                sc.add("pe", lambda E, kc=kc, qt=qt, i=i, ps=ps: E.matmul(
                    ps, kxv(kc, qt), kwv(i),
                    start=(kc == 0), stop=False),
                    reads=(f"kw6.{i % 2}", f"kxT.{kc}.{qt}"), writes=(pk,))
        sc.add("qs", lambda E, fc=fc: E.dma_start(
            out=P.a(ffnbr, 512, 0, 1),
            in_=d_["ffnb"][:, fc * 512:(fc + 1) * 512]), writes=("ffnbr",))
        for qt in range(TPC):
            ps, pk = psq(qt)
            sc.add("pe", lambda E, qt=qt, ps=ps: E.matmul(
                ps, P.a(srows + qt * 128, 128, 0, 1),
                P.a(ffnbr, 512, 0, 1), start=False, stop=True),
                reads=("srows", "ffnbr"), writes=(pk,))
            sc.add("act", lambda E, qt=qt, ps=ps: E.activation(
                P.a(epi, 512), ps, AF.Gelu, scale=P.a(rcols + qt, 1)),
                reads=(pk, "rcols"), writes=("e0",))
            sc.add("qs", lambda E, qt=qt, fc=fc: E.dma_start(
                out=P.a(epi + 512, 512),
                in_=xn_tm_d[qt * 128:(qt + 1) * 128, fc * 512:(fc + 1) * 512]),
                reads=("xn_tm_d",), writes=("e1",))
            sc.add("qs", lambda E, qt=qt, fc=fc: E.dma_start(
                out=P.a(epi + 1024, 512),
                in_=s0_d[qt * 128:(qt + 1) * 128, fc * 512:(fc + 1) * 512]),
                reads=("s0_d",), writes=("e2",))
            sc.add("qs", lambda E, qt=qt, fc=fc: E.dma_start(
                out=P.a(epi + 1536, 512),
                in_=x_tm[qt * 128:(qt + 1) * 128, fc * 512:(fc + 1) * 512]),
                writes=("e3",))
            sc.add("dve", lambda E: E.tensor_add(
                P.a(epi, 512), P.a(epi, 512), P.a(epi + 512, 512)),
                reads=("e0", "e1"), writes=("e0",))
            sc.add("act", lambda E: E.activation(
                P.a(epi + 512, 512), P.a(epi, 512), AF.Silu),
                reads=("e0",), writes=("e1",))
            sc.add("dve", lambda E: E.tensor_mul(
                P.a(epi, 512), P.a(epi + 512, 512), P.a(epi + 1024, 512)),
                reads=("e1", "e2"), writes=("e0",))
            sc.add("dve", lambda E: E.tensor_add(
                P.a(epi, 512), P.a(epi, 512), P.a(epi + 1536, 512)),
                reads=("e0", "e3"), writes=("e0",))
            sc.add("qp", lambda E, qt=qt, fc=fc: E.dma_start(
                out=x1_d[qt * 128:(qt + 1) * 128, fc * 512:(fc + 1) * 512],
                in_=P.a(epi, 512)), reads=("e0",), writes=("x1_d",))
    if "kwq" in P.taps:
        sc.add("qp", lambda E: E.dma_start(
            out=P.dram["tap_kwq"][:], in_=kwq_d[:]),
            reads=("kwq_d",), writes=("tap_kwq",))
    if "kx" in P.taps:
        sc.add("qp", lambda E: E.dma_start(
            out=P.dram["tap_kx"][:],
            in_=P.a(kxT, NCH * TOK // 2).bitcast(BF16)),
            reads=tuple(f"kxT.{c}.{q}" for c in range(NCH) for q in range(TPC)),
            writes=("tap_kx",))
    P.release()
    sc.fence()
    P.tap("x1_tm", lambda: x1_d[:], ("x1_d",))

    _ln_proj(P, dict(
        x_src=x1_d, wt=d_["w1catt"], nf=48, pfx="p7",
        xn_tm_d=None, xn_fm_d=xn1_fm_d, q_fm_d=q1_fm_d, q_chunks=32,
        cc_in=cc1_in, kv_chunks=(32, 40), bias=d_["b1cat"],
        wg=None))
    sc.fence()
    P.tap("xn1_fm", lambda: xn1_fm_d[:], ("xn1_fm_d",))

    if SIM_LOCAL_CC:
        for r in range(2):
            sc.add("qp", lambda E, r=r: E.dma_start(
                out=cc1_out[r], in_=cc1_in[:]),
                reads=("p7ccin",), writes=("cc1_out",))
    else:
        sc.add("pool", lambda E: E.collective_compute(
            "AllGather", OP.bypass, ins=[cc1_in[:]], outs=[cc1_out[:]],
            replica_groups=[[0, 1], [2, 3], [4, 5], [6, 7]]),
            reads=("p7ccin",), writes=("cc1_out",))
    sc.fence()
    P.mark()
    rk1 = P.alloc(KVH * 2048)
    v1tm = P.alloc(KVH * 2048)
    prm1 = P.alloc(128)
    cosf1 = P.alloc(2048)
    sinf1 = P.alloc(2048)
    vtmp1 = P.alloc(2048)
    w1b1 = P.alloc(2048)
    sc.add("qs", lambda E: E.dma_start(out=P.a(prm1, 128), in_=d_["permL1"][:]),
           writes=("prm1",))
    sc.add("qs", lambda E: E.dma_start(out=P.a(cosf1, 2048), in_=d_["cosL1"][:]),
           writes=("cosf1",))
    sc.add("qs", lambda E: E.dma_start(out=P.a(sinf1, 2048), in_=d_["sinL1"][:]),
           writes=("sinf1",))
    for kvh in range(KVH):
        rkk = f"rk1.{kvh}"
        for r in range(2):
            sc.add("qs", lambda E, r=r, kvh=kvh: E.dma_start(
                out=P.a(rk1 + kvh * 2048, 2048)
                    .rearrange("b (j r2 c) -> b r2 j c", r2=2, c=128)[:, r, :, :],
                in_=cc1_out[r, 0, kvh, :, :].rearrange("b (j c) -> b j c", c=128)),
                reads=("cc1_out",), writes=(rkk,))
            sc.add("qs", lambda E, r=r, kvh=kvh: E.dma_start(
                out=P.a(vtmp1, 2048)
                    .rearrange("b (j r2 c) -> b r2 j c", r2=2, c=128)[:, r, :, :],
                in_=cc1_out[r, 1, kvh, :, :].rearrange("b (j c) -> b j c", c=128)),
                reads=("cc1_out",), writes=("vtmp1",))
        for j in range(4):
            sc.add("pe", lambda E, kvh=kvh, j=j: E.matmul(
                P.psbig[:, j * 512:(j + 1) * 512], P.a(prm1, 128),
                P.a(rk1 + kvh * 2048 + j * 512, 512), start=True, stop=True),
                reads=(rkk, "prm1"), writes=(f"psb{j}",))
        sc.add("dve", lambda E: E.tensor_tensor(
            P.a(w1b1, 2048), P.psbig[:, 0:2048], P.a(sinf1, 2048), OP.mult),
            reads=("psb0", "psb1", "psb2", "psb3", "sinf1"), writes=("w1b1",))
        sc.add("dve", lambda E, kvh=kvh: E.tensor_tensor(
            P.a(rk1 + kvh * 2048, 2048), P.a(rk1 + kvh * 2048, 2048),
            P.a(cosf1, 2048), OP.mult), reads=(rkk, "cosf1"), writes=(rkk,))
        sc.add("dve", lambda E, kvh=kvh: E.tensor_tensor(
            P.a(rk1 + kvh * 2048, 2048), P.a(rk1 + kvh * 2048, 2048),
            P.a(w1b1, 2048), OP.add), reads=(rkk, "w1b1"), writes=(rkk,))
        for gt in range(16):
            pb = 2 + (gt % 2)
            sc.add("pe", lambda E, gt=gt, pb=pb: E.transpose(
                P.ps[pb][:, 0:128], P.a(vtmp1 + gt * 128, 128), P.ident[:]),
                reads=("vtmp1", "ident"), writes=(f"ps{pb}",))
            eng = "act" if gt % 2 == 0 else "dve"
            sc.add(eng, lambda E, kvh=kvh, gt=gt, pb=pb, eng=eng: (
                E.copy(P.a(v1tm + kvh * 2048 + gt * 128, 128), P.ps[pb][:, 0:128])
                if eng == "act" else
                E.tensor_copy(P.a(v1tm + kvh * 2048 + gt * 128, 128),
                              P.ps[pb][:, 0:128])),
                reads=(f"ps{pb}",), writes=(f"v1tm{kvh}",))

    clq1 = P.alloc(128)
    slq1 = P.alloc(128)
    qb9 = P.alloc(2 * 128)
    rq9 = P.alloc(2 * 128)
    pq9 = P.alloc(2 * 128)
    pb9 = P.alloc(2 * 2048)
    pt9 = P.alloc(2 * 2048)
    ob9 = P.alloc(2 * 128)
    cb9 = P.alloc(2 * 8)
    it9 = 0
    pend9 = [None]
    for qt in range(TPC):
        sc.add("qs", lambda E, qt=qt: E.dma_start(
            out=P.a(clq1, 128), in_=d_["coslq1"][:, qt * 128:(qt + 1) * 128]),
            writes=("clq1",))
        sc.add("qs", lambda E, qt=qt: E.dma_start(
            out=P.a(slq1, 128), in_=d_["sinlq1"][:, qt * 128:(qt + 1) * 128]),
            writes=("slq1",))
        for kvh in range(KVH):
            for g in range(G):
                h = kvh * G + g
                pr = it9 % 2
                it9 += 1
                QB, RQ, PQ, PB, PT, OB, CB_ = (qb9 + pr * 128, rq9 + pr * 128,
                                               pq9 + pr * 128, pb9 + pr * 2048,
                                               pt9 + pr * 2048, ob9 + pr * 128,
                                               cb9 + pr * 8)
                K_ = lambda nm: f"{nm}{pr}"
                PSP, PSA = 0, 3
                sc.add("qs", lambda E, h=h, qt=qt, QB=QB: E.dma_start(
                    out=P.a(QB, 128), in_=q1_fm_d[h, :, qt * 128:(qt + 1) * 128]),
                    reads=("p7qfm",), writes=(K_("qb9"),))
                sc.add("pe", lambda E, QB=QB, PSP=PSP: E.matmul(
                    P.ps[PSP][:, 128:256], P.a(prm1, 128), P.a(QB, 128),
                    start=True, stop=True), reads=(K_("qb9"), "prm1"),
                    writes=(f"ps{PSP}",))
                sc.add("dve", lambda E, PQ=PQ, PSP=PSP: E.tensor_copy(
                    P.a(PQ, 128), P.ps[PSP][:, 128:256]),
                    reads=(f"ps{PSP}",), writes=(K_("pq9"),))
                sc.add("dve", lambda E, PQ=PQ: E.tensor_tensor(
                    P.a(PQ, 128), P.a(PQ, 128), P.a(slq1, 128), OP.mult),
                    reads=(K_("pq9"), "slq1"), writes=(K_("pq9"),))
                sc.add("dve", lambda E, QB=QB, RQ=RQ: E.tensor_tensor(
                    P.a(RQ, 128), P.a(QB, 128), P.a(clq1, 128), OP.mult),
                    reads=(K_("qb9"), "clq1"), writes=(K_("rq9"),))
                sc.add("dve", lambda E, RQ=RQ, PQ=PQ: E.tensor_tensor(
                    P.a(RQ, 128), P.a(RQ, 128), P.a(PQ, 128), OP.add),
                    reads=(K_("rq9"), K_("pq9")), writes=(K_("rq9"),))
                for t in range(4):
                    sc.add("pe", lambda E, kvh=kvh, t=t, RQ=RQ: E.matmul(
                        P.psbig[:, t * 512:(t + 1) * 512], P.a(RQ, 128),
                        P.a(rk1 + kvh * 2048 + t * 512, 512),
                        start=True, stop=True),
                        reads=(K_("rq9"), f"rk1.{kvh}"), writes=(f"psb{t}",))
                psbks = ("psb0", "psb1", "psb2", "psb3")
                sc.add("dve", lambda E, CB_=CB_: E.tensor_reduce(
                    P.a(CB_, 1), P.psbig[:, 0:2048], axis=AX.X, op=OP.max,
                    negate=True), reads=psbks, writes=(K_("cb0"),))
                sc.add("dve", lambda E, CB_=CB_: E.tensor_scalar_mul(
                    P.a(CB_ + 1, 1), P.a(CB_, 1), SCALE),
                    reads=(K_("cb0"),), writes=(K_("cb1"),))
                sc.add("act", lambda E, CB_=CB_, PB=PB: E.activation(
                    P.a(PB, 2048), P.psbig[:, 0:2048], AF.Exp,
                    bias=P.a(CB_ + 1, 1), scale=SCALE,
                    accum_out=P.a(CB_ + 2, 1)),
                    reads=psbks + (K_("cb1"),), writes=(K_("pb9"), K_("cb2")))

                def stageB(kvh=kvh, qt=qt, h=h, PB=PB, PT=PT, OB=OB, CB_=CB_,
                           K_=K_, PSA=PSA):
                    for t in range(16):
                        pb = 1 + (t % 2)
                        sc.add("pe", lambda E, t=t, pb=pb, PB=PB: E.transpose(
                            P.ps[pb][:, 0:128], P.a(PB + t * 128, 128), P.ident[:]),
                            reads=(K_("pb9"), "ident"), writes=(f"ps{pb}",))
                        eng = "act" if t % 2 == 0 else "dve"
                        sc.add(eng, lambda E, t=t, pb=pb, eng=eng, PT=PT: (
                            E.copy(P.a(PT + t * 128, 128), P.ps[pb][:, 0:128])
                            if eng == "act" else
                            E.tensor_copy(P.a(PT + t * 128, 128), P.ps[pb][:, 0:128])),
                            reads=(f"ps{pb}",), writes=(K_(f"pt9.{t}"),))
                    for t in range(16):
                        sc.add("pe", lambda E, t=t, kvh=kvh, PT=PT, PSA=PSA: E.matmul(
                            P.ps[PSA][:, 0:128], P.a(PT + t * 128, 128),
                            P.a(v1tm + kvh * 2048 + t * 128, 128),
                            start=(t == 0), stop=(t == 15)),
                            reads=(K_(f"pt9.{t}"), f"v1tm{kvh}"), writes=(f"ps{PSA}",))
                    sc.add("dve", lambda E, CB_=CB_: E.reciprocal(
                        P.a(CB_ + 3, 1), P.a(CB_ + 2, 1)),
                        reads=(K_("cb2"),), writes=(K_("cb3"),))
                    sc.add("dve", lambda E, CB_=CB_, OB=OB, PSA=PSA: E.tensor_single_scalar(
                        P.a(OB, 128), P.ps[PSA][:, 0:128], P.a(CB_ + 3, 1), OP.mult),
                        reads=(f"ps{PSA}", K_("cb3")), writes=(K_("ob9"),))
                    sc.add("qp", lambda E, qt=qt, h=h, OB=OB: E.dma_start(
                        out=s1_d[qt * 128:(qt + 1) * 128, h * 128:(h + 1) * 128],
                        in_=P.a(OB, 128)), reads=(K_("ob9"),), writes=("s1_d",))

                if pend9[0] is not None:
                    pend9[0]()
                pend9[0] = stageB
    if pend9[0] is not None:
        pend9[0]()
        pend9[0] = None
    P.release()
    sc.fence()
    P.tap("s1_tm", lambda: s1_d[:], ("s1_d",))

    P.mark()
    xn1r = P.alloc(NCH * TOK)
    gwb = P.alloc(NCH * 28)
    acb = P.alloc(NCH * 224)
    hTb = P.alloc(TPC * 256)
    lgb = P.alloc(28)
    eb10 = P.alloc(28)
    selb = P.alloc(28)
    gtb10 = P.alloc(28)
    hb10 = P.alloc(224)
    mrep10 = P.alloc(8)
    cb10 = P.alloc(8)
    bct = P.alloc(1024)
    bwt = P.alloc(2 * 512)
    ep10 = P.alloc(4 * 512)
    sc.add("dve", lambda E: E.memset(P.a(mrep10, 8), 1.0e38), writes=("mrep10",))
    sc.add("qs", lambda E: E.dma_start(out=P.a(gwb, NCH * 28), in_=d_["gwTt"][:]),
           writes=("gwb",))
    sc.add("qs", lambda E: E.dma_start(out=P.a(acb, NCH * 224), in_=d_["AcatTt"][:]),
           writes=("acb",))
    for c in range(NCH):
        sc.add("qs", lambda E, c=c: E.dma_start(
            out=P.a(xn1r + c * TOK, TOK), in_=xn1_fm_d[c]),
            reads=("xn1_fm_d",), writes=(f"xn1.{c}",))
    for qt in range(TPC):
        for kc in range(NCH):
            sc.add("pe", lambda E, kc=kc, qt=qt: E.matmul(
                P.ps[0][:, 0:28], P.a(xn1r + kc * TOK + qt * 128, 128),
                P.a(gwb + kc * 28, 28), start=(kc == 0), stop=(kc == NCH - 1)),
                reads=(f"xn1.{kc}", "gwb"), writes=("ps0",))
        sc.add("dve", lambda E: E.tensor_copy(P.a(lgb, 28), P.ps[0][:, 0:28]),
               reads=("ps0",), writes=("lgb",))
        sc.add("dve", lambda E: E.tensor_reduce(
            P.a(cb10, 1), P.a(lgb, 28), axis=AX.X, op=OP.max, negate=True),
            reads=("lgb",), writes=("d0",))
        sc.add("dve", lambda E: E.tensor_scalar_mul(
            P.a(cb10 + 1, 1), P.a(cb10, 1), -1.0), reads=("d0",), writes=("d1",))
        sc.add("dve", lambda E: E.tensor_copy(P.a(eb10, 28), P.a(lgb, 28)),
               reads=("lgb",), writes=("eb10",))
        sc.add("dve", lambda E: E.tensor_copy(P.a(mrep10, 1), P.a(cb10 + 1, 1)),
               reads=("d1",), writes=("mrep10",))
        sc.add("dve", lambda E: E.match_replace(
            P.a(eb10, 28), P.a(mrep10, 8), P.a(eb10, 28), NEG),
            reads=("eb10", "mrep10"), writes=("eb10",))
        sc.add("dve", lambda E: E.tensor_reduce(
            P.a(cb10 + 2, 1), P.a(eb10, 28), axis=AX.X, op=OP.max),
            reads=("eb10",), writes=("d2",))
        sc.add("dve", lambda E: E.tensor_single_scalar(
            P.a(selb, 28), P.a(lgb, 28), P.a(cb10 + 2, 1), OP.is_ge),
            reads=("lgb", "d2"), writes=("selb",))
        sc.add("act", lambda E: E.activation(
            P.a(eb10, 28), P.a(lgb, 28), AF.Exp, bias=P.a(cb10, 1)),
            reads=("lgb", "d0"), writes=("eb10",))
        sc.add("dve", lambda E: E.tensor_tensor(
            P.a(eb10, 28), P.a(eb10, 28), P.a(selb, 28), OP.mult),
            reads=("eb10", "selb"), writes=("eb10",))
        sc.add("dve", lambda E: E.tensor_reduce(
            P.a(cb10 + 3, 1), P.a(eb10, 28), axis=AX.X, op=OP.add),
            reads=("eb10",), writes=("d3",))
        sc.add("dve", lambda E: E.reciprocal(P.a(cb10 + 4, 1), P.a(cb10 + 3, 1)),
               reads=("d3",), writes=("d4",))
        sc.add("dve", lambda E: E.tensor_scalar_mul(
            P.a(cb10 + 5, 1), P.a(cb10 + 4, 1), 2.0), reads=("d4",), writes=("d5",))
        sc.add("dve", lambda E: E.tensor_single_scalar(
            P.a(gtb10, 28), P.a(eb10, 28), P.a(cb10 + 5, 1), OP.mult),
            reads=("eb10", "d5"), writes=("gtb10",))
        for kc in range(NCH):
            sc.add("pe", lambda E, kc=kc, qt=qt: E.matmul(
                P.ps[1][:, 0:224], P.a(xn1r + kc * TOK + qt * 128, 128),
                P.a(acb + kc * 224, 224), start=(kc == 0), stop=(kc == NCH - 1)),
                reads=(f"xn1.{kc}", "acb"), writes=("ps1",))
        sc.add("dve", lambda E: E.tensor_copy(P.a(hb10, 224), P.ps[1][:, 0:224]),
               reads=("ps1",), writes=("hb10",))
        sc.add("dve", lambda E: E.tensor_tensor(
            P.a(hb10, 224).rearrange("b (e r) -> b e r", r=8),
            P.a(hb10, 224).rearrange("b (e r) -> b e r", r=8),
            P.a(gtb10, 28).unsqueeze(2).to_broadcast([128, 28, 8]), OP.mult),
            reads=("hb10", "gtb10"), writes=("hb10",))
        sc.add("pe", lambda E: E.transpose(
            P.ps[2][:, 0:128], P.a(hb10, 128), P.ident[:]),
            reads=("hb10", "ident"), writes=("ps2",))
        sc.add("dve", lambda E, qt=qt: E.tensor_copy(
            P.a(hTb + qt * 256, 128), P.ps[2][:, 0:128]),
            reads=("ps2",), writes=(f"hT.{qt}",))
        sc.add("pe", lambda E: E.transpose(
            P.ps[2][0:96, 0:128], P.a(hb10 + 128, 96), P.ident[:]),
            reads=("hb10", "ident"), writes=("ps2",))
        sc.add("dve", lambda E, qt=qt: E.tensor_copy(
            P.a(hTb + qt * 256 + 128, 128, 0, 96), P.ps[2][0:96, 0:128]),
            reads=("ps2",), writes=(f"hT.{qt}",))
    for fc in range(8):
        sc.add("qs", lambda E, fc=fc: E.dma_start(
            out=P.a(bct, 512), in_=d_["Bcat"][0:128, fc * 512:(fc + 1) * 512]),
            writes=("bct0",))
        sc.add("qs", lambda E, fc=fc: E.dma_start(
            out=P.a(bct + 512, 512, 0, 96),
            in_=d_["Bcat"][128:224, fc * 512:(fc + 1) * 512]),
            writes=("bct1",))
        for kc in range(NCH):
            i = fc * NCH + kc
            sc.add("qs", lambda E, kc=kc, fc=fc, i=i: E.dma_start(
                out=P.a(bwt + (i % 2) * 512, 512), in_=d_["baseTt"][kc, fc]),
                writes=(f"bw.{i % 2}",))
            for qt in range(TPC):
                ps, pk = psq(qt)
                sc.add("pe", lambda E, kc=kc, qt=qt, i=i, ps=ps: E.matmul(
                    ps, P.a(xn1r + kc * TOK + qt * 128, 128),
                    P.a(bwt + (i % 2) * 512, 512),
                    start=(kc == 0), stop=False),
                    reads=(f"bw.{i % 2}", f"xn1.{kc}"), writes=(pk,))
        for qt in range(TPC):
            ps, pk = psq(qt)
            sc.add("pe", lambda E, qt=qt, ps=ps: E.matmul(
                ps, P.a(hTb + qt * 256, 128), P.a(bct, 512),
                start=False, stop=False),
                reads=(f"hT.{qt}", "bct0"), writes=(pk,))
            sc.add("pe", lambda E, qt=qt, ps=ps: E.matmul(
                ps, P.a(hTb + qt * 256 + 128, 128, 0, 96),
                P.a(bct + 512, 512, 0, 96), start=False, stop=True),
                reads=(f"hT.{qt}", "bct1"), writes=(pk,))
            sc.add("act", lambda E, ps=ps: E.activation(
                P.a(ep10, 512), ps, AF.Silu), reads=(pk,), writes=("f0",))
            sc.add("qs", lambda E, qt=qt, fc=fc: E.dma_start(
                out=P.a(ep10 + 512, 512),
                in_=s1_d[qt * 128:(qt + 1) * 128, fc * 512:(fc + 1) * 512]),
                reads=("s1_d",), writes=("f1",))
            sc.add("qs", lambda E, qt=qt, fc=fc: E.dma_start(
                out=P.a(ep10 + 1024, 512),
                in_=x1_d[qt * 128:(qt + 1) * 128, fc * 512:(fc + 1) * 512]),
                reads=("x1_d",), writes=("f2",))
            sc.add("dve", lambda E: E.tensor_mul(
                P.a(ep10, 512), P.a(ep10, 512), P.a(ep10 + 512, 512)),
                reads=("f0", "f1"), writes=("f0",))
            sc.add("dve", lambda E: E.tensor_add(
                P.a(ep10, 512), P.a(ep10, 512), P.a(ep10 + 1024, 512)),
                reads=("f0", "f2"), writes=("f0",))
            sc.add("qp", lambda E, qt=qt, fc=fc: E.dma_start(
                out=out_d[qt * 128:(qt + 1) * 128, fc * 512:(fc + 1) * 512],
                in_=P.a(ep10, 512)), reads=("f0",), writes=("out",))
    P.release()


def _ln_proj(P, cfg):
    sc, nc = P.sc, P.nc
    pfx = cfg["pfx"]
    nf = cfg["nf"]
    qch = cfg["q_chunks"]
    kv0, v0 = cfg["kv_chunks"]
    wt = cfg["wt"]

    P.mark()
    xg = P.alloc(2 * D)
    xnt = P.alloc(D)
    sq = P.alloc(D)
    cols = P.alloc(2 * 8)
    xnfm = P.alloc(NCH * 512)
    wcol = P.alloc(2 * D)
    ev = P.alloc(4 * 512)
    has_g = cfg.get("wg") is not None
    if has_g:
        wgb = P.alloc(NCH * 96)
        gev = P.alloc(512)
        gtm = P.alloc(4 * 96)
        bgc = P.alloc(1)
        sc.add("qs", lambda E: E.dma_start(out=P.a(wgb, NCH * 96), in_=cfg["wg"][:]),
               writes=(pfx + "wg",))
        sc.add("qs", lambda E: E.dma_start(out=P.a(bgc, 1, 0, 96), in_=cfg["bg"][:]),
               writes=(pfx + "bg",))
    has_b = cfg.get("bias") is not None
    if has_b:
        bc = P.alloc(48)
        sc.add("qs", lambda E: E.dma_start(out=P.a(bc, 48), in_=cfg["bias"][:]),
               writes=(pfx + "bias",))

    for grp in range(2):
        for j2 in range(4):
            t = grp * 4 + j2
            xk = f"{pfx}xg.{t % 2}"
            ck = f"{pfx}cols.{t % 2}"
            o = xg + (t % 2) * D
            co = cols + (t % 2) * 8
            sc.add("qs", lambda E, t=t, o=o: E.dma_start(
                out=P.a(o, D), in_=cfg["x_src"][t * 128:(t + 1) * 128, :]),
                writes=(xk,))
            sc.add("act", lambda E, o=o, co=co: E.activation(
                P.a(sq, D), P.a(o, D), AF.Square, accum_out=P.a(co + 1, 1)),
                reads=(xk,), writes=(pfx + "sq", ck + "s"))
            sc.add("dve", lambda E, o=o, co=co: E.tensor_reduce(
                P.a(co, 1), P.a(o, D), axis=AX.X, op=OP.add),
                reads=(xk,), writes=(ck,))
            sc.add("dve", lambda E, co=co: E.tensor_scalar_mul(
                P.a(co + 2, 1), P.a(co, 1), 1.0 / D), reads=(ck,), writes=(ck,))
            sc.add("dve", lambda E, co=co: E.tensor_mul(
                P.a(co + 3, 1), P.a(co + 2, 1), P.a(co + 2, 1)),
                reads=(ck,), writes=(ck,))
            sc.add("dve", lambda E, co=co: E.scalar_tensor_tensor(
                P.a(co + 4, 1), P.a(co + 1, 1), 1.0 / D, P.a(co + 3, 1),
                OP.mult, OP.subtract), reads=(ck, ck + "s"), writes=(ck,))
            sc.add("dve", lambda E, co=co: E.tensor_scalar_add(
                P.a(co + 5, 1), P.a(co + 4, 1), EPS), reads=(ck,), writes=(ck,))
            sc.add("dve", lambda E, co=co: E.reciprocal(
                P.a(co + 6, 1), P.a(co + 5, 1)), reads=(ck,), writes=(ck,))
            sc.add("act", lambda E, co=co: E.activation(
                P.a(co + 7, 1), P.a(co + 6, 1), AF.Sqrt), reads=(ck,), writes=(ck + "r",))
            sc.add("dve", lambda E, o=o, co=co: E.scalar_tensor_tensor(
                P.a(xnt, D), P.a(o, D), P.a(co + 2, 1),
                P.a(co + 7, 1).to_broadcast([128, D]), OP.subtract, OP.mult),
                reads=(xk, ck, ck + "r"), writes=(pfx + "xnt",))
            if cfg.get("xn_tm_d") is not None:
                sc.add("qp", lambda E, t=t: E.dma_start(
                    out=cfg["xn_tm_d"][t * 128:(t + 1) * 128, :], in_=P.a(xnt, D)),
                    reads=(pfx + "xnt",), writes=("xn_tm_d",))
            for c in range(NCH):
                i = t * NCH + c
                pb = 2 + (i % 2)
                sc.add("pe", lambda E, c=c, pb=pb: E.transpose(
                    P.ps[pb][:, 0:128], P.a(xnt + c * 128, 128), P.ident[:]),
                    reads=(pfx + "xnt", "ident"), writes=(f"ps{pb}",))
                sc.add("act", lambda E, c=c, j2=j2, pb=pb: E.copy(
                    P.a(xnfm + c * 512 + j2 * 128, 128), P.ps[pb][:, 0:128]),
                    reads=(f"ps{pb}",), writes=(f"{pfx}fm.{c}.{j2}",))
            if cfg.get("xn_fm_d") is not None:
                sc.add("qp", lambda E, t=t, j2=j2: E.dma_start(
                    out=cfg["xn_fm_d"][:, :, t * 128:(t + 1) * 128]
                        .transpose([1, 0, 2]),
                    in_=P.a(xnfm, NCH * 512)
                        .rearrange("b (c n) -> b c n", n=512)
                        [:, :, j2 * 128:j2 * 128 + 128]),
                    reads=tuple(f"{pfx}fm.{c}.{j2}" for c in range(NCH)),
                    writes=("xn1_fm_d",))
        fmkeys = tuple(f"{pfx}fm.{c}.{j2}" for c in range(NCH) for j2 in range(4))
        for fc in range(nf):
            i = grp * nf + fc
            wk = f"{pfx}w.{i % 2}"
            ek = f"{pfx}ev.{i % 4}"
            pk = f"ps{i % 2}"
            wo = wcol + (i % 2) * D
            eo = ev + (i % 4) * 512
            sc.add("qs", lambda E, fc=fc, wo=wo: E.dma_start(
                out=P.a(wo, D), in_=wt[fc]), writes=(wk,))
            for kc in range(NCH):
                sc.add("pe", lambda E, kc=kc, wo=wo, pk=pk: E.matmul(
                    P.ps[int(pk[2])][:, 0:512], P.a(wo + kc * 128, 128),
                    P.a(xnfm + kc * 512, 512), start=(kc == 0), stop=(kc == NCH - 1)),
                    reads=(wk,) + (fmkeys if kc == 0 else
                                   (f"{pfx}fm.{kc}.0", f"{pfx}fm.{kc}.1",
                                    f"{pfx}fm.{kc}.2", f"{pfx}fm.{kc}.3")),
                    writes=(pk,))
            if has_b:
                sc.add("act", lambda E, fc=fc, eo=eo, pk=pk: E.activation(
                    P.a(eo, 512), P.ps[int(pk[2])][:, 0:512], AF.Identity,
                    bias=P.a(bc + fc, 1)),
                    reads=(pk, pfx + "bias"), writes=(ek,))
            else:
                sc.add("act", lambda E, eo=eo, pk=pk: E.copy(
                    P.a(eo, 512), P.ps[int(pk[2])][:, 0:512]),
                    reads=(pk,), writes=(ek,))
            if fc < qch:
                sc.add("qp", lambda E, fc=fc, grp=grp, eo=eo, dst=cfg["q_fm_d"]:
                       E.dma_start(out=dst[fc, :, grp * 512:(grp + 1) * 512],
                                   in_=P.a(eo, 512)),
                       reads=(ek,), writes=(pfx + "qfm",))
            else:
                kvsel = 0 if fc < v0 else 1
                kvh = fc - (kv0 if kvsel == 0 else v0)
                sc.add("qp", lambda E, kvsel=kvsel, kvh=kvh, grp=grp, eo=eo:
                       E.dma_start(out=cfg["cc_in"][kvsel, kvh, :,
                                                    grp * 512:(grp + 1) * 512],
                                   in_=P.a(eo, 512)),
                       reads=(ek,), writes=(pfx + "ccin",))
        if has_g:
            for kc in range(NCH):
                sc.add("pe", lambda E, kc=kc: E.matmul(
                    P.psbig[0:96, 0:512], P.a(wgb + kc * 96, 96, 0, 128),
                    P.a(xnfm + kc * 512, 512), start=(kc == 0), stop=(kc == NCH - 1)),
                    reads=(pfx + "wg",) + fmkeys, writes=("psg",))
            sc.add("act", lambda E: E.activation(
                P.a(gev, 512, 0, 96), P.psbig[0:96, 0:512], AF.Sigmoid,
                bias=P.a(bgc, 1, 0, 96)),
                reads=("psg", pfx + "bg"), writes=(pfx + "gev",))
            for j2 in range(4):
                sc.add("pe", lambda E, j2=j2: E.transpose(
                    P.psbig[0:128, 512:512 + 96],
                    P.a(gev + j2 * 128, 128, 0, 96), P.ident[0:96, 0:96]),
                    reads=(pfx + "gev", "ident"), writes=("psgt",))
                sc.add("dve", lambda E, j2=j2: E.tensor_copy(
                    P.a(gtm + j2 * 96, 96), P.psbig[0:128, 512:512 + 96]),
                    reads=("psgt",), writes=(f"{pfx}gtm.{j2}",))
                t = grp * 4 + j2
                sc.add("qp", lambda E, t=t, j2=j2: E.dma_start(
                    out=cfg["gates_d"][t * 128:(t + 1) * 128, :],
                    in_=P.a(gtm + j2 * 96, 96)),
                    reads=(f"{pfx}gtm.{j2}",), writes=("gates_d",))
    P.release()


def _run_raw(inputs, taps=(), trace=False):
    staged, tiles = [], []
    for c in range(8):
        d, tg = _stage(inputs, c)
        staged.append(d)
        tiles.append(tg)
    nc = bass.Bass("TRN2", target_bir_lowering=False)
    nc, P = build(nc, taps=taps)
    res = run_bass_kernel_spmd(nc, staged, core_ids=list(range(8)), trace=trace)
    return res, tiles


def kernel(**inputs):
    staged, tiles = [], []
    for c in range(8):
        d, tg = _stage(inputs, c)
        staged.append(d)
        tiles.append(tg)
    nc = bass.Bass("TRN2", target_bir_lowering=False)
    nc, P = build(nc, taps=())
    res = run_bass_kernel_spmd(nc, staged, core_ids=list(range(8)))
    out = np.zeros((B, S, D), dtype=np.float32)
    for c in range(8):
        b = c // 2
        o = np.asarray(res.results[c]["out"]).reshape(TPC, 128, D)
        for j, gt in enumerate(tiles[c]):
            out[b, gt * 128:(gt + 1) * 128, :] = o[j]
    return out


# version 07
# speedup vs baseline: 1.0276x; 1.0058x over previous
import math

import numpy as np

import concourse.bass as bass
import concourse.mybir as mybir
from concourse.bass_utils import run_bass_kernel_spmd

F32 = mybir.dt.float32
BF16 = mybir.dt.bfloat16
I32 = mybir.dt.int32
AF = mybir.ActivationFunctionType
OP = mybir.AluOpType
AX = mybir.AxisListType

B, S, D = 4, 2048, 4096
H, KVH, DH, G = 32, 8, 128, 4
CB = 16
NT = S // 128
TPC = NT // 2
TOK = TPC * 128
NCH = D // 128
NEG = -1.0e38
SCALE = 1.0 / math.sqrt(DH)
MAGIC = 12582912.0
EPS = 1e-5
LN1E10 = math.log(1e10)
SIM_LOCAL_CC = False


class Step:
    __slots__ = ("eng", "fn", "reads", "writes", "deps", "ord", "inc", "seq")

    def __init__(self, eng, fn, reads, writes):
        self.eng = eng
        self.fn = fn
        self.reads = reads
        self.writes = writes
        self.deps = None
        self.ord = None
        self.inc = False
        self.seq = None


VENG = {
    "pe": ("tensor", False),
    "dve": ("vector", False),
    "act": ("scalar", False),
    "pool": ("gpsimd", False),
    "qs": ("sync", True),
    "qa": ("scalar", True),
    "qp": ("gpsimd", True),
}
REAL_OF = {k: v[0] for k, v in VENG.items()}


class Sched:
    def __init__(self):
        self.steps = []
        self.fences = []

    def add(self, eng, fn, reads=(), writes=()):
        assert eng in VENG
        self.steps.append(Step(eng, fn, tuple(reads), tuple(writes)))
        return len(self.steps) - 1

    def fence(self):
        self.fences.append(len(self.steps))

    def emit(self, nc):
        steps = self.steps
        n = len(steps)
        for i, st in enumerate(steps):
            st.seq = i
        last_w = {}
        readers = {}
        last_of_eng = {}
        fi = 0
        cur_fence_deps = {}
        fdeps_per = [None] * n
        for i, st in enumerate(steps):
            while fi < len(self.fences) and self.fences[fi] == i:
                cur_fence_deps = dict(last_of_eng)
                fi += 1
            fdeps_per[i] = {e: j for e, j in cur_fence_deps.items()
                            if e != st.eng or VENG[e][1]}
            deps = {}
            for k in st.reads:
                j = last_w.get(k)
                if j is not None:
                    deps[j] = True
            for k in st.writes:
                j = last_w.get(k)
                if j is not None:
                    deps[j] = True
                for r in readers.get(k, ()):
                    deps[r] = True
            for k in st.reads:
                readers.setdefault(k, []).append(i)
            for k in st.writes:
                last_w[k] = i
                readers[k] = []
            st.deps = sorted(deps)
            last_of_eng[st.eng] = i
        self._fdeps_per = fdeps_per
        for st in steps:
            if VENG[st.eng][1]:
                st.inc = True
        for i, st in enumerate(steps):
            for j in st.deps:
                d = steps[j]
                if d.eng != st.eng:
                    d.inc = True
            for j in fdeps_per[i].values():
                steps[j].inc = True
        fi = 0
        last_of_eng = {}
        for i, st in enumerate(steps):
            while fi < len(self.fences) and self.fences[fi] == i:
                for j in last_of_eng.values():
                    steps[j].inc = True
                fi += 1
            last_of_eng[st.eng] = i
        for j in last_of_eng.values():
            steps[j].inc = True
        cnt = {e: 0 for e in VENG}
        for st in steps:
            if st.inc:
                cnt[st.eng] += 1
                st.ord = cnt[st.eng]
        self.final_counts = dict(cnt)
        NQ = 8
        SEM_LIM = 20000
        nsem = {e: (NQ if VENG[e][1] else (cnt[e] + SEM_LIM - 1) // SEM_LIM)
                for e in VENG}
        step_engseq = [0] * n
        eng_seq = {e: 0 for e in VENG}
        for i, st in enumerate(steps):
            eng_seq[st.eng] += 1
            step_engseq[i] = eng_seq[st.eng]
        waits_per = [None] * n
        drains_per = [False] * n
        hw = {}
        drain_floor = {e: 0 for e in VENG}

        def qparts(o):
            return (o - 1) % NQ, (o - 1) // NQ + 1

        def qcount(o, cls):
            return (o - 1 - cls) // NQ + 1 if o - 1 >= cls else 0

        for i, st in enumerate(steps):
            waits = {}
            need_drain = False
            creal = REAL_OF[st.eng]

            def want(peng, cls, idx):
                k = (peng, cls)
                if waits.get(k, 0) < idx:
                    waits[k] = idx

            for j in st.deps:
                d = steps[j]
                if d.eng == st.eng and not VENG[st.eng][1]:
                    if st.eng in ("dve", "act", "pool"):
                        if step_engseq[j] > drain_floor[st.eng]:
                            need_drain = True
                    continue
                assert d.ord is not None
                if VENG[d.eng][1]:
                    cls, idx = qparts(d.ord)
                    want(d.eng, cls, idx)
                else:
                    want(d.eng, None, d.ord)
            for e, j in self._fdeps_per[i].items():
                d = steps[j]
                assert d.ord is not None
                if VENG[e][1]:
                    for cls in range(NQ):
                        c = qcount(d.ord, cls)
                        if c:
                            want(e, cls, c)
                else:
                    want(e, None, d.ord)
            final = []
            for (peng, cls), idx in sorted(waits.items(),
                                           key=lambda t: (t[0][0], str(t[0][1]))):
                if hw.get((creal, peng, cls), 0) >= idx:
                    continue
                hw[(creal, peng, cls)] = idx
                final.append((peng, cls, idx))
            waits_per[i] = final
            if need_drain:
                drains_per[i] = True
                drain_floor[st.eng] = step_engseq[i] - 1
        self._waits_per = waits_per
        self._drains_per = drains_per
        sems = {}
        for e in VENG:
            sems[e] = [nc.semaphore(f"S_{e}_{k}") for k in range(max(1, nsem[e]))]
        sem_objs = {e: [cm.__enter__() for cm in sems[e]] for e in VENG}

        def wait_args(peng, cls, idx):
            if VENG[peng][1]:
                return sem_objs[peng][cls], idx * 16
            si, rem = (idx - 1) // SEM_LIM, (idx - 1) % SEM_LIM + 1
            return sem_objs[peng][si], rem

        with nc.Block() as blk:
            for real in ("sync", "tensor", "vector", "scalar", "gpsimd"):
                mysteps = [i for i, st in enumerate(steps) if REAL_OF[st.eng] == real]

                def emit_stream(E, mysteps=mysteps, real=real):
                    for i in mysteps:
                        st = steps[i]
                        for peng, cls, idx in waits_per[i]:
                            s, v = wait_args(peng, cls, idx)
                            E.wait_ge(s, v)
                        if drains_per[i]:
                            E.drain()
                        if st.inc and VENG[st.eng][1]:
                            cls, idx = qparts(st.ord)
                            if idx >= 2:
                                E.wait_ge(sem_objs[st.eng][cls], (idx - 1) * 16)
                        instr = st.fn(E)
                        if st.inc:
                            if VENG[st.eng][1]:
                                cls, _ = qparts(st.ord)
                                instr.then_inc(sem_objs[st.eng][cls], 16)
                            else:
                                si = (st.ord - 1) // SEM_LIM
                                instr.then_inc(sem_objs[st.eng][si], 1)
                    for e in VENG:
                        if REAL_OF[e] == real and VENG[e][1] and self.final_counts[e]:
                            o = self.final_counts[e]
                            for cls in range(NQ):
                                c = qcount(o, cls)
                                if c:
                                    s, v = wait_args(e, cls, c)
                                    E.wait_ge(s, v)

                getattr(blk, real)(emit_stream)
        for e in VENG:
            for cm in sems[e]:
                cm.__exit__(None, None, None)


def _rope_tables():
    pos = np.arange(S, dtype=np.float64)
    inv = 1.0 / (10000.0 ** (np.arange(0, DH, 2, dtype=np.float64) / DH))
    ang = pos[None, :] * inv[:, None]
    c, s = np.cos(ang), np.sin(ang)
    cosL0 = np.zeros((DH, S))
    sinL0 = np.zeros((DH, S))
    cosL0[0::2] = c
    cosL0[1::2] = c
    sinL0[0::2] = -s
    sinL0[1::2] = s
    cosL1 = np.zeros((DH, S))
    sinL1 = np.zeros((DH, S))
    cosL1[0:64] = c
    cosL1[64:128] = c
    sinL1[0:64] = -s
    sinL1[64:128] = s
    permL0 = np.zeros((DH, DH))
    for i in range(0, DH, 2):
        permL0[i, i + 1] = 1.0
        permL0[i + 1, i] = 1.0
    permL1 = np.zeros((DH, DH))
    for m in range(DH):
        permL1[(m + 64) % 128, m] = 1.0
    f = np.float32
    return (cosL0.astype(f), sinL0.astype(f), cosL1.astype(f), sinL1.astype(f),
            permL0.astype(f), permL1.astype(f))


def _masks(tiles_g):
    cm = np.zeros((TPC, 128, 129), dtype=np.float32)
    for j, gt in enumerate(tiles_g):
        q = gt * 128 + np.arange(128)
        blk_last = (np.arange(128) + 1) * CB - 1
        cm[j, :, 1:] = np.where(blk_last[None, :] < q[:, None], 0.0, NEG)
    i = np.arange(128)
    dm = np.where((i[:, None] // CB == i[None, :] // CB)
                  & (i[None, :] % CB <= i[:, None] % CB), 0.0, NEG).astype(np.float32)
    jj = np.arange(256)
    sb = np.where((jj[None, :] >= i[:, None] + 64)
                  & (jj[None, :] <= i[:, None] + 128), 0.0, NEG).astype(np.float32)
    jj0 = np.arange(128)
    sb0 = np.where((jj0[None, :] >= i[:, None] - 64)
                   & (jj0[None, :] <= i[:, None]), 0.0, NEG).astype(np.float32)
    return cm, dm, sb, sb0


def _tile_w(wT, nf):
    Dk, F = wT.shape
    nc_ = Dk // 128
    t = wT.reshape(nc_, 128, nf, 128).transpose(2, 1, 0, 3).reshape(nf, 128, nc_ * 128)
    return np.ascontiguousarray(t)


def _stage(inputs, core):
    b, p = core // 2, core % 2
    tiles_g = [p + 2 * j for j in range(TPC)]
    idx = np.concatenate([np.arange(gt * 128, gt * 128 + 128) for gt in tiles_g])
    f32c = lambda a: np.ascontiguousarray(np.asarray(a), dtype=np.float32)

    d = {}
    d["x_tm"] = f32c(np.asarray(inputs["L0_attn_hidden_states"])[b][idx])

    wqkvT = np.asarray(inputs["L0_attn_w_qkv"]).T.astype(np.float32)
    d["wqkvTt"] = _tile_w(wqkvT, 48)
    d["wgTt"] = f32c(np.asarray(inputs["L0_attn_w_gate"]).T
                     .reshape(NCH, 128, 96).transpose(1, 0, 2).reshape(128, NCH * 96))
    d["bg"] = f32c(np.asarray(inputs["L0_attn_b_gate"]).reshape(96, 1))
    woT = np.asarray(inputs["L0_attn_w_o"]).T.astype(np.float32)
    d["woTt"] = f32c(woT.reshape(NCH, 128, 8, 512).transpose(0, 2, 1, 3))
    d["kw1Tt"] = f32c(np.asarray(inputs["L0_attn_k_w1"]).T
                      .reshape(16, 128, 16, 128).transpose(2, 1, 0, 3).reshape(16, 128, 2048))
    d["vw1Tt"] = f32c(np.asarray(inputs["L0_attn_v_w1"]).T
                      .reshape(16, 128, 16, 128).transpose(2, 1, 0, 3).reshape(16, 128, 2048))
    d["kw2Tt"] = f32c(np.asarray(inputs["L0_attn_k_w2"]).T.reshape(16, 128, 128))
    d["vw2Tt"] = f32c(np.asarray(inputs["L0_attn_v_w2"]).T.reshape(16, 128, 128))
    d["kb1"] = f32c(np.asarray(inputs["L0_attn_k_b1"]).reshape(16, 128).T)
    d["vb1"] = f32c(np.asarray(inputs["L0_attn_v_b1"]).reshape(16, 128).T)
    d["kb2"] = f32c(np.asarray(inputs["L0_attn_k_b2"]).reshape(128, 1))
    d["vb2"] = f32c(np.asarray(inputs["L0_attn_v_b2"]).reshape(1, 128))
    d["kpos"] = f32c(np.asarray(inputs["L0_attn_k_pos"]).transpose(0, 2, 1)
                     .transpose(1, 0, 2).reshape(128, KVH * 16))
    d["vpos"] = f32c(np.asarray(inputs["L0_attn_v_pos"]).transpose(0, 2, 1)
                     .transpose(1, 0, 2).reshape(128, KVH * 16))
    mk = np.asarray(inputs["L0_attn_mem_kv"])
    d["memk"] = f32c(mk[0][:, 0, :].T)
    d["memv"] = f32c(mk[1][:, 0, :].reshape(1, KVH * 128))
    d["ffnWT"] = f32c(np.asarray(inputs["L0_ffn_W"]).T.reshape(NCH, 128, D))
    d["ffnb"] = f32c(np.asarray(inputs["L0_ffn_b"]).reshape(1, D))

    w1cat = np.concatenate([np.asarray(inputs["L1_attn_wq"]),
                            np.asarray(inputs["L1_attn_wk"]),
                            np.asarray(inputs["L1_attn_wv"])], axis=1).astype(np.float32)
    d["w1catt"] = _tile_w(w1cat, 48)
    bcat = np.concatenate([np.asarray(inputs["L1_attn_bq"]),
                           np.asarray(inputs["L1_attn_bk"]),
                           np.asarray(inputs["L1_attn_bv"])]).astype(np.float32)
    d["b1cat"] = f32c(bcat.reshape(48, 128).T)
    baseT = np.asarray(inputs["L1_ffn_base_W"]).T.astype(np.float32)
    d["baseTt"] = f32c(baseT.reshape(NCH, 128, 8, 512).transpose(0, 2, 1, 3))
    d["gwTt"] = f32c(np.asarray(inputs["L1_ffn_gate_W"]).T
                     .reshape(NCH, 128, 28).transpose(1, 0, 2).reshape(128, NCH * 28))
    d["AcatTt"] = f32c(np.asarray(inputs["L1_ffn_lora_A"]).reshape(224, D).T
                       .reshape(NCH, 128, 224).transpose(1, 0, 2).reshape(128, NCH * 224))
    d["Bcat"] = f32c(np.asarray(inputs["L1_ffn_lora_B"]).transpose(0, 2, 1).reshape(224, D))

    cosL0, sinL0, cosL1, sinL1, permL0, permL1 = _rope_tables()
    d["cosL0"], d["sinL0"], d["cosL1"], d["sinL1"] = cosL0, sinL0, cosL1, sinL1
    d["coslq0"] = f32c(cosL0[:, idx])
    d["sinlq0"] = f32c(sinL0[:, idx])
    d["coslq1"] = f32c(cosL1[:, idx])
    d["sinlq1"] = f32c(sinL1[:, idx])
    d["permL0"], d["permL1"] = permL0, permL1
    d["ident"] = np.eye(128, dtype=np.float32)
    d["onesrow"] = np.ones((1, 128), dtype=np.float32)
    d["onescol"] = np.ones((128, 1), dtype=np.float32)
    cm, dm, sbm, sbm0 = _masks(tiles_g)
    d["cmask"] = f32c(cm.transpose(1, 0, 2).reshape(128, TPC * 129))
    d["diagm"] = dm
    d["sband"] = sbm
    d["sband0"] = sbm0
    d2 = np.full((TPC, 128, 256), NEG, dtype=np.float32)
    sbm2 = np.full((TPC, 128, 384), NEG, dtype=np.float32)
    ii = np.arange(128)
    for qt in range(TPC):
        gt = p + 2 * qt
        q = gt * 128 + ii
        key2 = 2 * qt * 128 + np.arange(256)
        vis = (key2[None, :] // CB == q[:, None] // CB) \
            & (key2[None, :] % CB <= q[:, None] % CB)
        d2[qt] = np.where(vis, 0.0, NEG)
        soff = max(0, 2 * qt - 1) * 128
        key3 = soff + np.arange(384)
        visb = (key3[None, :] >= q[:, None] - 64) & (key3[None, :] <= q[:, None])
        sbm2[qt] = np.where(visb, 0.0, NEG)
    d["diag2"] = d2
    d["sbandm"] = sbm2
    return d, tiles_g


class Prog:

    def __init__(self, nc, taps=()):
        self.nc = nc
        self.sc = Sched()
        self.dram = {}
        self.taps = set(taps)
        self._ctxs = []
        self.ARENA_COLS = 51712
        self.arena = self._sbuf("ARENA", (128, self.ARENA_COLS))
        self._abase = 0
        self._amarks = []
        self.ident = self._sbuf("c_ident", (128, 128))
        self.ones1 = self._sbuf("c_ones", (1, 128))
        self.onesc = self._sbuf("c_onesc", (128, 1))
        self.swcol = self._sbuf("c_swcol", (128, 2))
        self.ps = [self._psum(f"PSB{i}", (128, 512)) for i in range(4)]
        self.psbig = self._psum("PSBIG", (128, 2048))

    def _sbuf(self, name, shape, dtype=F32):
        cm = self.nc.sbuf_tensor(name, list(shape), dtype)
        t = cm.__enter__()
        self._ctxs.append(cm)
        return t

    def _psum(self, name, shape, dtype=F32):
        cm = self.nc.psum_tensor(name, list(shape), dtype)
        t = cm.__enter__()
        self._ctxs.append(cm)
        return t

    def close(self):
        for cm in reversed(self._ctxs):
            cm.__exit__(None, None, None)

    def alloc(self, ncols):
        base = self._abase
        assert base + ncols <= self.ARENA_COLS, (base, ncols)
        self._abase += ncols
        return base

    def mark(self):
        self._amarks.append(self._abase)

    def release(self):
        self._abase = self._amarks.pop()

    def a(self, base, ncols, p0=0, np_=128):
        return self.arena[p0:p0 + np_, base:base + ncols]

    def din(self, name, shape, dtype=F32):
        ap = self.nc.dram_tensor(name, list(shape), dtype, kind="ExternalInput").ap()
        self.dram[name] = ap
        return ap

    def dout(self, name, shape, dtype=F32):
        ap = self.nc.dram_tensor(name, list(shape), dtype, kind="ExternalOutput").ap()
        self.dram[name] = ap
        return ap

    def dint(self, name, shape, dtype=F32):
        ap = self.nc.dram_tensor(name, list(shape), dtype, kind="Internal").ap()
        self.dram[name] = ap
        return ap

    def tap(self, name, src_fn, reads):
        if name in self.taps:
            dst = self.dram["tap_" + name]
            self.sc.add("qp", lambda E, s=src_fn, d=dst: E.dma_start(out=d[:], in_=s()),
                        reads=reads, writes=("tap_" + name,))


TAP_SHAPES = {
    "xn_tm": (TOK, D), "q0_fm": (NCH, 128, TOK), "gates_tm": (TOK, 96),
    "kfull": (KVH, 128, S), "vfull": (KVH, 128, S),
    "rk": (KVH, 128, S), "ck": (KVH, 128, 129), "cvtm": (KVH, 128, 128),
    "attn_acc": (TOK, D), "s0_tm": (TOK, D), "x1_tm": (TOK, D),
    "xn1_fm": (NCH, 128, TOK), "s1_tm": (TOK, D), "sw": (1, 3),
    "imp": (KVH, 128, 128), "fsel": (KVH, 128, 128),
    "kwq": (NCH, 128, D), "kx": (128, NCH * TOK),
}


def build(nc, taps=()):
    P = Prog(nc, taps)
    sc = P.sc
    d_ = {}

    def din(n, s, dt=F32):
        d_[n] = P.din(n, s, dt)
        return d_[n]

    x_tm = din("x_tm", (TOK, D))
    wqkvTt = din("wqkvTt", (48, 128, D))
    wgTt = din("wgTt", (128, NCH * 96))
    bg = din("bg", (96, 1))
    woTt = din("woTt", (NCH, 8, 128, 512))
    kw1Tt = din("kw1Tt", (16, 128, 2048))
    vw1Tt = din("vw1Tt", (16, 128, 2048))
    kw2Tt = din("kw2Tt", (16, 128, 128))
    vw2Tt = din("vw2Tt", (16, 128, 128))
    kb1 = din("kb1", (128, 16))
    vb1 = din("vb1", (128, 16))
    kb2 = din("kb2", (128, 1))
    vb2 = din("vb2", (1, 128))
    kpos = din("kpos", (128, KVH * 16))
    vpos = din("vpos", (128, KVH * 16))
    memk = din("memk", (128, KVH))
    memv = din("memv", (1, KVH * 128))
    ffnWT = din("ffnWT", (NCH, 128, D))
    ffnb = din("ffnb", (1, D))
    w1catt = din("w1catt", (48, 128, D))
    b1cat = din("b1cat", (128, 48))
    baseTt = din("baseTt", (NCH, 8, 128, 512))
    gwTt = din("gwTt", (128, NCH * 28))
    AcatTt = din("AcatTt", (128, NCH * 224))
    Bcat = din("Bcat", (224, D))
    cosL0 = din("cosL0", (128, S))
    sinL0 = din("sinL0", (128, S))
    cosL1 = din("cosL1", (128, S))
    sinL1 = din("sinL1", (128, S))
    coslq0 = din("coslq0", (128, TOK))
    sinlq0 = din("sinlq0", (128, TOK))
    coslq1 = din("coslq1", (128, TOK))
    sinlq1 = din("sinlq1", (128, TOK))
    permL0 = din("permL0", (128, 128))
    permL1 = din("permL1", (128, 128))
    identD = din("ident", (128, 128))
    onesD = din("onesrow", (1, 128))
    onescD = din("onescol", (128, 1))
    cmaskD = din("cmask", (128, TPC * 129))
    diagD = din("diagm", (128, 128))
    sbandD = din("sband", (128, 256))
    sband0D = din("sband0", (128, 128))
    diag2D = din("diag2", (TPC, 128, 256))
    sbandmD = din("sbandm", (TPC, 128, 384))

    out = P.dout("out", (TOK, D))
    for t in P.taps:
        P.dout("tap_" + t, TAP_SHAPES[t])

    xn_tm_d = P.dint("xn_tm", (TOK, D))
    q0_fm_d = P.dint("q0_fm", (NCH, 128, TOK))
    gates_d = P.dint("gates_tm", (TOK, 96))
    cc0_in = P.dint("cc0_in", (2, KVH, 128, TOK))
    cc0_out = P.dint("cc0_out", (2, 2, KVH, 128, TOK))
    acc_d = P.dint("attn_acc", (TOK, D))
    s0_d = P.dint("s0_tm", (TOK, D))
    x1_d = P.dint("x1_tm", (TOK, D))
    xn1_fm_d = P.dint("xn1_fm", (NCH, 128, TOK))
    q1_fm_d = P.dint("q1_fm", (NCH, 128, TOK))
    cc1_in = P.dint("cc1_in", (2, KVH, 128, TOK))
    cc1_out = P.dint("cc1_out", (2, 2, KVH, 128, TOK))
    s1_d = P.dint("s1_tm", (TOK, D))

    sc.add("qs", lambda E: E.dma_start(out=P.ident[:], in_=identD[:]),
           writes=("ident",))
    sc.add("qs", lambda E: E.dma_start(out=P.ones1[:], in_=onesD[:]),
           writes=("ones1",))
    sc.add("qs", lambda E: E.dma_start(out=P.onesc[:], in_=onescD[:]),
           writes=("onesc",))

    P.mark()
    w0 = P.alloc(2 * D)
    asum = P.alloc(2)
    srow = P.alloc(4)
    for i in range(NCH):
        sc.add("qs", lambda E, i=i: E.dma_start(
            out=P.a(w0 + (i % 2) * D, D), in_=ffnWT[i]),
            reads=(), writes=(f"w0.{i % 2}",))
        sc.add("dve", lambda E, i=i: E.tensor_reduce(
            P.a(asum + 1, 1), P.a(w0 + (i % 2) * D, D), axis=AX.X, op=OP.add,
            apply_absolute_value=True),
            reads=(f"w0.{i % 2}",), writes=("asum1",))
        if i == 0:
            sc.add("dve", lambda E: E.tensor_copy(P.a(asum, 1), P.a(asum + 1, 1)),
                   reads=("asum1",), writes=("asum0",))
        else:
            sc.add("dve", lambda E: E.tensor_add(
                P.a(asum, 1), P.a(asum, 1), P.a(asum + 1, 1)),
                reads=("asum0", "asum1"), writes=("asum0",))
    sc.add("pe", lambda E: E.matmul(P.ps[0][0:1, 0:1], P.onesc[:],
                                    P.a(asum, 1), start=True, stop=True),
           reads=("asum0", "onesc"), writes=("ps0",))
    sc.add("act", lambda E: E.mul(P.a(srow, 1, 0, 1), P.ps[0][0:1, 0:1],
                                  1.0 / (D * D)),
           reads=("ps0",), writes=("srow",))
    sc.add("dve", lambda E: E.tensor_scalar_max(
        P.a(srow + 1, 1, 0, 1), P.a(srow, 1, 0, 1), EPS),
        reads=("srow",), writes=("srow",))
    sc.add("dve", lambda E: E.reciprocal(P.a(srow + 2, 1, 0, 1),
                                         P.a(srow + 1, 1, 0, 1)),
           reads=("srow",), writes=("srow",))
    sc.add("pe", lambda E: E.matmul(P.ps[0][:, 0:2], P.ones1[:],
                                    P.a(srow + 1, 2, 0, 1), start=True, stop=True),
           reads=("srow", "ones1"), writes=("ps0",))
    sc.add("dve", lambda E: E.tensor_copy(P.swcol[:, 0:1], P.ps[0][:, 1:2]),
           reads=("ps0",), writes=("swcol",))
    sc.add("dve", lambda E: E.tensor_copy(P.swcol[:, 1:2], P.ps[0][:, 0:1]),
           reads=("ps0",), writes=("swcol",))
    P.tap("sw", lambda: P.a(srow, 3, 0, 1), ("srow",))
    sc.fence()
    P.release()

    _ln_proj(P, dict(
        x_src=x_tm, wt=wqkvTt, nf=48, pfx="p1",
        xn_tm_d=xn_tm_d, xn_fm_d=None, q_fm_d=q0_fm_d, q_chunks=32,
        cc_in=cc0_in, kv_chunks=(32, 40), bias=None,
        wg=wgTt, bg=bg, gates_d=gates_d))
    sc.fence()
    P.tap("xn_tm", lambda: xn_tm_d[:], ("xn_tm_d",))
    P.tap("q0_fm", lambda: q0_fm_d[:], ("q0_fm_d",))
    P.tap("gates_tm", lambda: gates_d[:], ("gates_d",))

    if SIM_LOCAL_CC:
        for r in range(2):
            sc.add("qp", lambda E, r=r: E.dma_start(
                out=cc0_out[r], in_=cc0_in[:]),
                reads=("cc0_in",), writes=("cc0_out",))
    else:
        sc.add("pool", lambda E: E.collective_compute(
            "AllGather", OP.bypass, ins=[cc0_in[:]], outs=[cc0_out[:]],
            replica_groups=[[0, 1], [2, 3], [4, 5], [6, 7]]),
            reads=("cc0_in",), writes=("cc0_out",))
    sc.fence()
    for nm, kv in (("kfull", 0), ("vfull", 1)):
        if nm not in P.taps:
            continue
        for kvh in range(KVH):
            for r in range(2):
                sc.add("qp", lambda E, nm=nm, kv=kv, kvh=kvh, r=r: E.dma_start(
                    out=P.dram["tap_" + nm][kvh]
                        .rearrange("b (j r2 c) -> b r2 j c", r2=2, c=128)[:, r, :, :],
                    in_=cc0_out[r, kv, kvh, :, :]
                        .rearrange("b (j c) -> b j c", c=128)),
                    reads=("cc0_out",), writes=("tap_" + nm,))

    _build_rest(P, d_, locals())

    sc.emit(nc)
    P.close()
    return nc, P


def _build_rest(P, d_, env):
    sc = P.sc
    cc0_out = env["cc0_out"]
    acc_d = env["acc_d"]
    gates_d = env["gates_d"]
    q0_fm_d = env["q0_fm_d"]
    s0_d = env["s0_d"]
    kwq_d = P.dint("kwq", (NCH, 128, D), BF16)

    P.mark()
    rk = P.alloc(KVH * 2048)
    vtm = P.alloc(KVH * 2048)
    ckb = P.alloc(KVH * 129)
    cvb = P.alloc(KVH * 128)
    prm = P.alloc(128)
    kposb = P.alloc(KVH * 16)
    vposb = P.alloc(KVH * 16)
    kb1b = P.alloc(16)
    vb1b = P.alloc(16)
    kb2b = P.alloc(1)
    vb2b = P.alloc(128)
    memkb = P.alloc(KVH)
    memvb = P.alloc(KVH * 128)
    for dst, src, key in (
            (P.a(prm, 128), d_["permL0"], "prm"),
            (P.a(kposb, KVH * 16), d_["kpos"], "kposb"),
            (P.a(vposb, KVH * 16), d_["vpos"], "vposb"),
            (P.a(kb1b, 16), d_["kb1"], "kb1b"),
            (P.a(vb1b, 16), d_["vb1"], "vb1b"),
            (P.a(kb2b, 1), d_["kb2"], "kb2b"),
            (P.a(vb2b, 128, 0, 1), d_["vb2"], "vb2b"),
            (P.a(memkb, KVH), d_["memk"], "memkb"),
            (P.a(memvb, KVH * 128, 0, 1), d_["memv"], "memvb"),
    ):
        sc.add("qs", lambda E, dst=dst, src=src: E.dma_start(out=dst, in_=src[:]),
               writes=(key,))

    P.mark()
    cosf = P.alloc(2048)
    sinf = P.alloc(2048)
    vtmp = P.alloc(2048)
    w1b = P.alloc(2048)
    w2b = P.alloc(2048)
    kw1b = P.alloc(2048)
    kw2b = P.alloc(2 * 128)
    sc.add("qs", lambda E: E.dma_start(out=P.a(cosf, 2048), in_=d_["cosL0"][:]),
           writes=("cosf",))
    sc.add("qs", lambda E: E.dma_start(out=P.a(sinf, 2048), in_=d_["sinL0"][:]),
           writes=("sinf",))

    def kv_load(dst_base, dst_key, kv, kvh, cc, eng="qs"):
        for r in range(2):
            sc.add(eng, lambda E, r=r: E.dma_start(
                out=P.a(dst_base, 2048)
                    .rearrange("b (j r2 c) -> b r2 j c", r2=2, c=128)[:, r, :, :],
                in_=cc[r, kv, kvh, :, :].rearrange("b (j c) -> b j c", c=128)),
                reads=(cc is cc0_out and "cc0_out" or "cc1_out",),
                writes=(dst_key,))

    nw2 = 0
    for kvh in range(KVH):
        rkk = f"rk{kvh}"
        kv_load(rk + kvh * 2048, rkk, 0, kvh, cc0_out)
        kv_load(vtmp, "vtmp", 1, kvh, cc0_out)
        sc.add("dve", lambda E, kvh=kvh: E.tensor_tensor(
            P.a(w1b, 2048).rearrange("b (c i) -> b c i", i=16),
            P.a(rk + kvh * 2048, 2048).rearrange("b (c i) -> b c i", i=16),
            P.a(kposb + kvh * 16, 16).unsqueeze(1).to_broadcast([128, 128, 16]),
            OP.add), reads=(rkk, "kposb"), writes=("w1",))
        for hc in range(16):
            i = kvh * 16 + hc
            sc.add("qs", lambda E, hc=hc, i=i: E.dma_start(
                out=P.a(kw1b, 2048), in_=d_["kw1Tt"][hc]),
                writes=("kw1",))
            for jc in range(16):
                sc.add("pe", lambda E, hc=hc, jc=jc, i=i: E.matmul(
                    P.ps[0][:, 0:128],
                    P.a(kw1b + jc * 128, 128),
                    P.a(w1b, 2048).rearrange("b (c i2) -> b i2 c", i2=16)[:, jc, :],
                    start=(jc == 0), stop=(jc == 15)),
                    reads=("kw1", "w1"), writes=("ps0",))
            sc.add("act", lambda E, hc=hc: E.activation(
                P.a(w2b + hc * 128, 128), P.ps[0][:, 0:128], AF.Relu,
                bias=P.a(kb1b + hc, 1)),
                reads=("ps0", "kb1b"), writes=("w2",))
        for hc in range(16):
            i2 = nw2
            nw2 += 1
            sc.add("qs", lambda E, hc=hc, i2=i2: E.dma_start(
                out=P.a(kw2b + (i2 % 2) * 128, 128), in_=d_["kw2Tt"][hc]),
                writes=(f"kw2.{i2 % 2}",))
            sc.add("pe", lambda E, hc=hc, i2=i2: E.matmul(
                P.ps[1][:, 0:128], P.a(kw2b + (i2 % 2) * 128, 128),
                P.a(w2b + hc * 128, 128), start=(hc == 0), stop=(hc == 15)),
                reads=(f"kw2.{i2 % 2}", "w2"), writes=("ps1",))
        sc.add("act", lambda E, kvh=kvh: E.activation(
            P.a(ckb + kvh * 129 + 1, 128), P.ps[1][:, 0:128], AF.Identity,
            bias=P.a(kb2b, 1)), reads=("ps1", "kb2b"), writes=(f"ck{kvh}",))
        sc.add("dve", lambda E, kvh=kvh: E.tensor_copy(
            P.a(ckb + kvh * 129, 1), P.a(memkb + kvh, 1)),
            reads=("memkb",), writes=(f"ck{kvh}",))
        for j in range(4):
            sc.add("pe", lambda E, kvh=kvh, j=j: E.matmul(
                P.psbig[:, j * 512:(j + 1) * 512], P.a(prm, 128),
                P.a(rk + kvh * 2048 + j * 512, 512), start=True, stop=True),
                reads=(rkk, "prm"), writes=(f"psb{j}",))
        sc.add("dve", lambda E, kvh=kvh: E.tensor_tensor(
            P.a(w1b, 2048), P.psbig[:, 0:2048], P.a(sinf, 2048), OP.mult),
            reads=("psb0", "psb1", "psb2", "psb3", "sinf"), writes=("w1",))
        sc.add("dve", lambda E, kvh=kvh: E.tensor_tensor(
            P.a(rk + kvh * 2048, 2048), P.a(rk + kvh * 2048, 2048),
            P.a(cosf, 2048), OP.mult), reads=(rkk, "cosf"), writes=(rkk,))
        sc.add("dve", lambda E, kvh=kvh: E.tensor_tensor(
            P.a(rk + kvh * 2048, 2048), P.a(rk + kvh * 2048, 2048),
            P.a(w1b, 2048), OP.add), reads=(rkk, "w1"), writes=(rkk,))
        sc.add("dve", lambda E, kvh=kvh: E.tensor_tensor(
            P.a(w1b, 2048).rearrange("b (c i) -> b c i", i=16),
            P.a(vtmp, 2048).rearrange("b (c i) -> b c i", i=16),
            P.a(vposb + kvh * 16, 16).unsqueeze(1).to_broadcast([128, 128, 16]),
            OP.add), reads=("vtmp", "vposb"), writes=("w1",))
        for hc in range(16):
            i = KVH * 16 + kvh * 16 + hc
            sc.add("qs", lambda E, hc=hc, i=i: E.dma_start(
                out=P.a(kw1b, 2048), in_=d_["vw1Tt"][hc]),
                writes=("kw1",))
            for jc in range(16):
                sc.add("pe", lambda E, hc=hc, jc=jc, i=i: E.matmul(
                    P.ps[0][:, 0:128],
                    P.a(kw1b + jc * 128, 128),
                    P.a(w1b, 2048).rearrange("b (c i2) -> b i2 c", i2=16)[:, jc, :],
                    start=(jc == 0), stop=(jc == 15)),
                    reads=("kw1", "w1"), writes=("ps0",))
            sc.add("act", lambda E, hc=hc: E.activation(
                P.a(w2b + hc * 128, 128), P.ps[0][:, 0:128], AF.Relu,
                bias=P.a(vb1b + hc, 1)),
                reads=("ps0", "vb1b"), writes=("w2",))
        for hc in range(16):
            i2 = nw2
            nw2 += 1
            sc.add("qs", lambda E, hc=hc, i2=i2: E.dma_start(
                out=P.a(kw2b + (i2 % 2) * 128, 128), in_=d_["vw2Tt"][hc]),
                writes=(f"kw2.{i2 % 2}",))
            sc.add("pe", lambda E, hc=hc, i2=i2: E.matmul(
                P.ps[1][:, 0:128], P.a(w2b + hc * 128, 128),
                P.a(kw2b + (i2 % 2) * 128, 128), start=(hc == 0), stop=False),
                reads=(f"kw2.{i2 % 2}", "w2"), writes=("ps1",))
        sc.add("pe", lambda E, kvh=kvh: E.matmul(
            P.ps[1][:, 0:128], P.ones1[:], P.a(vb2b, 128, 0, 1),
            start=False, stop=True), reads=("vb2b", "ones1"), writes=("ps1",))
        sc.add("dve", lambda E, kvh=kvh: E.tensor_copy(
            P.a(cvb + kvh * 128, 128), P.ps[1][:, 0:128]),
            reads=("ps1",), writes=(f"cv{kvh}",))
        for gt in range(16):
            pb = 2 + (gt % 2)
            sc.add("pe", lambda E, kvh=kvh, gt=gt, pb=pb: E.transpose(
                P.ps[pb][:, 0:128], P.a(vtmp + gt * 128, 128), P.ident[:]),
                reads=("vtmp", "ident"), writes=(f"ps{pb}",))
            eng = "act" if gt % 2 == 0 else "dve"
            sc.add(eng, lambda E, kvh=kvh, gt=gt, pb=pb: (
                E.copy(P.a(vtm + kvh * 2048 + gt * 128, 128), P.ps[pb][:, 0:128])
                if gt % 2 == 0 else
                E.tensor_copy(P.a(vtm + kvh * 2048 + gt * 128, 128),
                              P.ps[pb][:, 0:128])),
                reads=(f"ps{pb}",), writes=(f"vtm{kvh}",))
    P.release()
    sc.fence()
    if "rk" in P.taps:
        for kvh in range(KVH):
            sc.add("qp", lambda E, kvh=kvh: E.dma_start(
                out=P.dram["tap_rk"][kvh], in_=P.a(rk + kvh * 2048, 2048)),
                reads=(f"rk{kvh}",), writes=("tap_rk",))
    if "ck" in P.taps:
        for kvh in range(KVH):
            sc.add("qp", lambda E, kvh=kvh: E.dma_start(
                out=P.dram["tap_ck"][kvh], in_=P.a(ckb + kvh * 129, 129)),
                reads=(f"ck{kvh}",), writes=("tap_ck",))
    if "cvtm" in P.taps:
        for kvh in range(KVH):
            sc.add("qp", lambda E, kvh=kvh: E.dma_start(
                out=P.dram["tap_cvtm"][kvh], in_=P.a(cvb + kvh * 128, 128)),
                reads=(f"cv{kvh}",), writes=("tap_cvtm",))

    P.mark()
    clq = P.alloc(128)
    slq = P.alloc(128)
    cmb = P.alloc(129)
    d2b = P.alloc(256)
    sbb = P.alloc(384)
    gtb = P.alloc(96)
    q4b = P.alloc(512)
    rqb = P.alloc(2 * 128)
    accb = P.alloc(2 * 512)
    scb = P.alloc(2 * 2048)
    pb_ = P.alloc(2 * 2048)
    ptb = P.alloc(2048)
    s2b = P.alloc(256)
    csb = P.alloc(4 * 129)
    impb = P.alloc(128)
    eb = P.alloc(128)
    fselb = P.alloc(128)
    curb = P.alloc(128)
    colb = P.alloc(2 * 32)
    mempb = P.alloc(128)
    cpb = P.alloc(2 * 132)
    spb = P.alloc(384)
    it4 = [0]
    pend4 = [None]
    mrep = P.alloc(8)
    sc.add("dve", lambda E: E.memset(P.a(mrep, 8), 1.0e38), writes=("mrep",))

    for qt in range(TPC):
        if pend4[0] is not None:
            pend4[0]()
            pend4[0] = None
        for dst, src, key in (
                (P.a(clq, 128), d_["coslq0"][:, qt * 128:(qt + 1) * 128], "clq"),
                (P.a(slq, 128), d_["sinlq0"][:, qt * 128:(qt + 1) * 128], "slq"),
                (P.a(cmb, 129), d_["cmask"][:, qt * 129:(qt + 1) * 129], "cmb"),
                (P.a(d2b, 256), d_["diag2"][qt], "d2b"),
                (P.a(sbb, 384), d_["sbandm"][qt], "sbb"),
                (P.a(gtb, 96), gates_d[qt * 128:(qt + 1) * 128, :], "gtb"),
        ):
            sc.add("qs", lambda E, dst=dst, src=src: E.dma_start(out=dst, in_=src),
                   writes=(key,))
        nk = (qt + 1) * 256
        nblk = nk // 16
        ntile = nk // 128
        soff = max(0, 2 * qt - 1) * 128
        for kvh in range(KVH):
            ak = f"acc{kvh % 2}"
            ao = accb + (kvh % 2) * 512
            for g in range(G):
                h = kvh * G + g
                sc.add("qs", lambda E, h=h, qt=qt, g=g: E.dma_start(
                    out=P.a(q4b + g * 128, 128),
                    in_=q0_fm_d[h, :, qt * 128:(qt + 1) * 128]),
                    writes=(f"q4.{g}",))
            for g in range(G):
                sc.add("pe", lambda E, g=g, kvh=kvh: E.matmul(
                    P.ps[0][:, 0:129], P.a(q4b + g * 128, 128),
                    P.a(ckb + kvh * 129, 129), start=True, stop=True),
                    reads=(f"q4.{g}", f"ck{kvh}"), writes=("ps0",))
                sc.add("dve", lambda E, g=g: E.scalar_tensor_tensor(
                    P.a(csb + g * 129, 129), P.ps[0][:, 0:129], SCALE,
                    P.a(cmb, 129), OP.mult, OP.add),
                    reads=("ps0", "cmb"), writes=(f"cs{g}",))
            sc.add("dve", lambda E: E.tensor_add(
                P.a(impb, 128), P.a(csb + 1, 128), P.a(csb + 129 + 1, 128)),
                reads=("cs0", "cs1"), writes=("impb",))
            sc.add("dve", lambda E: E.tensor_add(
                P.a(eb, 128), P.a(csb + 2 * 129 + 1, 128),
                P.a(csb + 3 * 129 + 1, 128)),
                reads=("cs2", "cs3"), writes=("eb",))
            sc.add("dve", lambda E: E.tensor_scalar_mul(
                P.a(impb, 128), P.a(impb, 128), 0.25),
                reads=("impb",), writes=("impb",))
            sc.add("dve", lambda E: E.tensor_scalar_mul(
                P.a(eb, 128), P.a(eb, 128), 0.25),
                reads=("eb",), writes=("eb",))
            sc.add("dve", lambda E: E.tensor_add(
                P.a(impb, 128), P.a(impb, 128), P.a(eb, 128)),
                reads=("impb", "eb"), writes=("impb",))
            sc.add("dve", lambda E: E.tensor_reduce(
                P.a(colb, 1), P.a(impb, 128), axis=AX.X, op=OP.max),
                reads=("impb",), writes=("c0",))
            sc.add("dve", lambda E: E.tensor_scalar_max(
                P.a(colb + 1, 1), P.a(colb, 1), -1000.0),
                reads=("c0",), writes=("c1",))
            sc.add("dve", lambda E: E.tensor_scalar_mul(
                P.a(colb + 2, 1), P.a(colb + 1, 1), -1.0),
                reads=("c1",), writes=("c2",))
            sc.add("act", lambda E: E.activation(
                P.a(eb, 128), P.a(impb, 128), AF.Exp,
                bias=P.a(colb + 2, 1), accum_out=P.a(colb + 3, 1)),
                reads=("impb", "c2"), writes=("eb", "c3"))
            sc.add("dve", lambda E: E.tensor_scalar_add(
                P.a(colb + 16, 1), P.a(colb + 2, 1), -1000.0),
                reads=("c2",), writes=("c16",))
            sc.add("act", lambda E: E.activation(
                P.a(colb + 4, 1), P.a(colb + 16, 1), AF.Exp),
                reads=("c16",), writes=("c4",))
            sc.add("dve", lambda E: E.tensor_add(
                P.a(colb + 5, 1), P.a(colb + 3, 1), P.a(colb + 4, 1)),
                reads=("c3", "c4"), writes=("c5",))
            sc.add("dve", lambda E: E.reciprocal(
                P.a(colb + 6, 1), P.a(colb + 5, 1)),
                reads=("c5",), writes=("c6",))
            sc.add("dve", lambda E: E.tensor_copy(P.a(curb, 128), P.a(impb, 128)),
                   reads=("impb",), writes=("curb",))
            for it in range(4):
                sc.add("dve", lambda E: E.tensor_reduce(
                    P.a(colb + 7, 1), P.a(curb, 128), axis=AX.X, op=OP.max),
                    reads=("curb",), writes=("c7",))
                if it < 3:
                    sc.add("dve", lambda E: E.tensor_copy(
                        P.a(mrep, 1), P.a(colb + 7, 1)),
                        reads=("c7",), writes=("mrep",))
                    sc.add("dve", lambda E: E.match_replace(
                        P.a(curb, 128), P.a(mrep, 8), P.a(curb, 128), NEG),
                        reads=("curb", "mrep"), writes=("curb",))
            sc.add("dve", lambda E: E.tensor_single_scalar(
                P.a(fselb, 128), P.a(impb, 128), P.a(colb + 7, 1), OP.is_ge),
                reads=("impb", "c7"), writes=("fselb",))
            sc.add("dve", lambda E: E.tensor_scalar(
                P.a(eb, 128), P.a(eb, 128), P.a(colb + 6, 1), 1e-10,
                OP.mult, OP.is_gt), reads=("eb", "c6"), writes=("eb",))
            sc.add("dve", lambda E: E.tensor_tensor(
                P.a(fselb, 128), P.a(fselb, 128), P.a(eb, 128), OP.mult),
                reads=("fselb", "eb"), writes=("fselb",))
            sc.add("dve", lambda E: E.tensor_scalar(
                P.a(fselb, 128), P.a(fselb, 128), 1.0, 1e38,
                OP.subtract, OP.mult), reads=("fselb",), writes=("fselb",))
            for g in range(G):
                h = kvh * G + g
                pr4 = it4[0] % 2
                it4[0] += 1
                RQB = rqb + pr4 * 128
                SCB = scb + pr4 * 2048
                PBB = pb_ + pr4 * 2048
                PTB = ptb
                S2B = s2b
                CLB = colb + pr4 * 32
                MPB = mempb
                CPB = cpb + pr4 * 132
                KP = lambda nm, pr4=pr4: f"{nm}{pr4}"
                sc.add("dve", lambda E, g=g, CLB=CLB: E.tensor_reduce(
                    P.a(CLB + 0, 1), P.a(csb + g * 129, 129), axis=AX.X,
                    op=OP.max, negate=True), reads=(f"cs{g}",), writes=(KP("c8"),))
                sc.add("act", lambda E, g=g, CPB=CPB, CLB=CLB: E.activation(
                    P.a(CPB, 129), P.a(csb + g * 129, 129), AF.Exp,
                    bias=P.a(CLB + 0, 1), accum_out=P.a(CLB + 1, 1)),
                    reads=(f"cs{g}", KP("c8")), writes=(KP("cpb"), KP("c9")))
                sc.add("pe", lambda E, g=g: E.matmul(
                    P.ps[2][:, 0:128], P.a(prm, 128), P.a(q4b + g * 128, 128),
                    start=True, stop=True),
                    reads=(f"q4.{g}", "prm"), writes=("ps2",))
                sc.add("dve", lambda E, S2B=S2B: E.tensor_copy(
                    P.a(S2B, 128), P.ps[2][:, 0:128]),
                    reads=("ps2",), writes=("s2b",))
                sc.add("dve", lambda E, S2B=S2B: E.tensor_tensor(
                    P.a(S2B, 128), P.a(S2B, 128), P.a(slq, 128), OP.mult),
                    reads=("s2b", "slq"), writes=("s2b",))
                sc.add("dve", lambda E, g=g, RQB=RQB: E.tensor_tensor(
                    P.a(RQB, 128), P.a(q4b + g * 128, 128), P.a(clq, 128),
                    OP.mult), reads=(f"q4.{g}", "clq"), writes=(KP("rqb"),))
                sc.add("dve", lambda E, RQB=RQB, S2B=S2B: E.tensor_tensor(
                    P.a(RQB, 128), P.a(RQB, 128), P.a(S2B, 128), OP.add),
                    reads=(KP("rqb"), "s2b"), writes=(KP("rqb"),))
                o = 0
                while o < nk:
                    w = min(512, nk - o)
                    sc.add("pe", lambda E, kvh=kvh, o=o, w=w, RQB=RQB: E.matmul(
                        P.psbig[:, o:o + w], P.a(RQB, 128),
                        P.a(rk + kvh * 2048 + o, w),
                        start=True, stop=True),
                        reads=(KP("rqb"), f"rk{kvh}"), writes=(f"psb{o // 512}",))
                    o += w
                psbks = tuple(f"psb{t}" for t in range((nk + 511) // 512))
                sc.add("dve", lambda E, nk=nk, nblk=nblk, SCB=SCB: E.scalar_tensor_tensor(
                    P.a(SCB, nk).rearrange("b (c i) -> b c i", i=16),
                    P.psbig[:, 0:nk].rearrange("b (c i) -> b c i", i=16), SCALE,
                    P.a(fselb, nblk).unsqueeze(2).to_broadcast([128, nblk, 16]),
                    OP.mult, OP.add),
                    reads=psbks + ("fselb",), writes=(KP("scb"),))
                sc.add("dve", lambda E, nk=nk, S2B=S2B: E.scalar_tensor_tensor(
                    P.a(S2B, 256), P.psbig[:, nk - 256:nk], SCALE,
                    P.a(d2b, 256), OP.mult, OP.add),
                    reads=psbks + ("d2b",), writes=("s2b",))
                sc.add("dve", lambda E, nk=nk, SCB=SCB, S2B=S2B: E.tensor_max(
                    P.a(SCB + nk - 256, 256), P.a(SCB + nk - 256, 256),
                    P.a(S2B, 256)), reads=(KP("scb"), "s2b"), writes=(KP("scb"),))
                sc.add("dve", lambda E, nk=nk, SCB=SCB, CLB=CLB: E.tensor_reduce(
                    P.a(CLB + 4, 1), P.a(SCB, nk), axis=AX.X, op=OP.max,
                    negate=True), reads=(KP("scb"),), writes=(KP("c12"),))
                sc.add("act", lambda E, nk=nk, SCB=SCB, PBB=PBB, CLB=CLB: E.activation(
                    P.a(PBB, nk), P.a(SCB, nk), AF.Exp,
                    bias=P.a(CLB + 4, 1), accum_out=P.a(CLB + 5, 1)),
                    reads=(KP("scb"), KP("c12")), writes=(KP("pb0"), KP("c13")))
                sc.add("pe", lambda E, CPB=CPB: E.transpose(
                    P.ps[1][:, 0:128], P.a(CPB + 1, 128), P.ident[:]),
                    reads=(KP("cpb"), "ident"), writes=("ps1",))
                sc.add("pe", lambda E, CPB=CPB: E.transpose(
                    P.ps[2][0:1, 0:128], P.a(CPB, 1), P.ident[:]),
                    reads=(KP("cpb"), "ident"), writes=("ps2",))
                sc.add("act", lambda E, PTB=PTB: E.copy(
                    P.a(PTB, 128), P.ps[1][:, 0:128]),
                    reads=("ps1",), writes=("ptb0",))
                sc.add("dve", lambda E, MPB=MPB: E.tensor_copy(
                    P.a(MPB, 128, 0, 1), P.ps[2][0:1, 0:128]),
                    reads=("ps2",), writes=("mempb",))
                sc.add("pe", lambda E, kvh=kvh, PTB=PTB: E.matmul(
                    P.ps[3][:, 0:128], P.a(PTB, 128), P.a(cvb + kvh * 128, 128),
                    start=True, stop=False),
                    reads=("ptb0", f"cv{kvh}"), writes=("ps3",))
                sc.add("pe", lambda E, kvh=kvh, MPB=MPB: E.matmul(
                    P.ps[3][:, 0:128], P.a(MPB, 128, 0, 1),
                    P.a(memvb + kvh * 128, 128, 0, 1),
                    start=False, stop=True),
                    reads=("mempb", "memvb"), writes=("ps3",))
                sc.add("dve", lambda E, CLB=CLB: E.reciprocal(
                    P.a(CLB + 2, 1), P.a(CLB + 1, 1)),
                    reads=(KP("c9"),), writes=(KP("c10"),))
                sc.add("dve", lambda E, h=h, CLB=CLB: E.tensor_mul(
                    P.a(CLB + 3, 1), P.a(CLB + 2, 1),
                    P.a(gtb + 3 * h, 1)), reads=(KP("c10"), "gtb"), writes=(KP("c11"),))
                sc.add("dve", lambda E, g=g, ao=ao, CLB=CLB: E.tensor_single_scalar(
                    P.a(ao + g * 128, 128), P.ps[3][:, 0:128],
                    P.a(CLB + 3, 1), OP.mult),
                    reads=("ps3", KP("c11")), writes=(ak,))

                def stage2(g=g, h=h, kvh=kvh, qt=qt, ao=ao, ak=ak, nk=nk,
                           ntile=ntile, soff=soff, RQB=RQB, SCB=SCB, PBB=PBB,
                           PTB=PTB, S2B=S2B, CLB=CLB, KP=KP, last=(g == G - 1)):
                    sc.add("pe", lambda E: E.matmul(
                        P.ps[0][:, 0:384], P.a(RQB, 128),
                        P.a(rk + kvh * 2048 + soff, 384), start=True, stop=True),
                        reads=(KP("rqb"), f"rk{kvh}"), writes=("ps0",))
                    sc.add("dve", lambda E: E.scalar_tensor_tensor(
                        P.a(SCB, 384), P.ps[0][:, 0:384], SCALE, P.a(sbb, 384),
                        OP.mult, OP.add), reads=("ps0", "sbb"), writes=(KP("scb"),))
                    sc.add("dve", lambda E: E.tensor_reduce(
                        P.a(CLB + 8, 1), P.a(SCB, 384), axis=AX.X, op=OP.max,
                        negate=True), reads=(KP("scb"),), writes=(KP("c16"),))
                    sc.add("act", lambda E: E.activation(
                        P.a(spb, 384), P.a(SCB, 384), AF.Exp,
                        bias=P.a(CLB + 8, 1), accum_out=P.a(CLB + 9, 1)),
                        reads=(KP("scb"), KP("c16")), writes=("spb", KP("c17")))
                    for t in range(ntile):
                        pb2 = 1 + (t % 2)
                        sc.add("pe", lambda E, t=t, pb2=pb2: E.transpose(
                            P.ps[pb2][:, 0:128], P.a(PBB + t * 128, 128), P.ident[:]),
                            reads=(KP("pb0"), "ident"), writes=(f"ps{pb2}",))
                        eng = "act" if t % 2 == 0 else "dve"
                        sc.add(eng, lambda E, t=t, pb2=pb2, eng=eng: (
                            E.copy(P.a(PTB + t * 128, 128), P.ps[pb2][:, 0:128])
                            if eng == "act" else
                            E.tensor_copy(P.a(PTB + t * 128, 128), P.ps[pb2][:, 0:128])),
                            reads=(f"ps{pb2}",), writes=(f"pt.{t}",))
                    for t in range(ntile):
                        sc.add("pe", lambda E, t=t: E.matmul(
                            P.ps[3][:, 0:128], P.a(PTB + t * 128, 128),
                            P.a(vtm + kvh * 2048 + t * 128, 128),
                            start=(t == 0), stop=(t == ntile - 1)),
                            reads=(f"pt.{t}", f"vtm{kvh}"), writes=("ps3",))
                    sc.add("dve", lambda E: E.reciprocal(
                        P.a(CLB + 6, 1), P.a(CLB + 5, 1)),
                        reads=(KP("c13"),), writes=(KP("c14"),))
                    sc.add("dve", lambda E: E.tensor_mul(
                        P.a(CLB + 7, 1), P.a(CLB + 6, 1),
                        P.a(gtb + 3 * h + 1, 1)), reads=(KP("c14"), "gtb"),
                        writes=(KP("c15"),))
                    sc.add("dve", lambda E: E.scalar_tensor_tensor(
                        P.a(ao + g * 128, 128), P.ps[3][:, 0:128],
                        P.a(CLB + 7, 1), P.a(ao + g * 128, 128),
                        OP.mult, OP.add), reads=("ps3", KP("c15"), ak), writes=(ak,))
                    for t in range(3):
                        pb2 = 1 + (t % 2)
                        sc.add("pe", lambda E, t=t, pb2=pb2: E.transpose(
                            P.ps[pb2][:, 0:128], P.a(spb + t * 128, 128), P.ident[:]),
                            reads=("spb", "ident"), writes=(f"ps{pb2}",))
                        eng = "act" if t % 2 == 0 else "dve"
                        sc.add(eng, lambda E, t=t, pb2=pb2, eng=eng: (
                            E.copy(P.a(PTB + t * 128, 128), P.ps[pb2][:, 0:128])
                            if eng == "act" else
                            E.tensor_copy(P.a(PTB + t * 128, 128), P.ps[pb2][:, 0:128])),
                            reads=(f"ps{pb2}",), writes=(f"pt.{t}",))
                    for t in range(3):
                        sc.add("pe", lambda E, t=t: E.matmul(
                            P.ps[3][:, 0:128], P.a(PTB + t * 128, 128),
                            P.a(vtm + kvh * 2048 + soff + t * 128, 128),
                            start=(t == 0), stop=(t == 2)),
                            reads=(f"pt.{t}", f"vtm{kvh}"), writes=("ps3",))
                    sc.add("dve", lambda E: E.reciprocal(
                        P.a(CLB + 10, 1), P.a(CLB + 9, 1)),
                        reads=(KP("c17"),), writes=(KP("c18"),))
                    sc.add("dve", lambda E: E.tensor_mul(
                        P.a(CLB + 11, 1), P.a(CLB + 10, 1),
                        P.a(gtb + 3 * h + 2, 1)), reads=(KP("c18"), "gtb"),
                        writes=(KP("c19"),))
                    sc.add("dve", lambda E: E.scalar_tensor_tensor(
                        P.a(ao + g * 128, 128), P.ps[3][:, 0:128],
                        P.a(CLB + 11, 1), P.a(ao + g * 128, 128),
                        OP.mult, OP.add), reads=("ps3", KP("c19"), ak), writes=(ak,))
                    if last:
                        sc.add("qp", lambda E: E.dma_start(
                            out=acc_d[qt * 128:(qt + 1) * 128,
                                      kvh * 512:(kvh + 1) * 512], in_=P.a(ao, 512)),
                            reads=(ak,), writes=("acc_d",))

                if pend4[0] is not None:
                    pend4[0]()
                pend4[0] = stage2
    if pend4[0] is not None:
        pend4[0]()
        pend4[0] = None
    P.release()
    P.release()
    sc.fence()
    P.tap("attn_acc", lambda: acc_d[:], ("acc_d",))

    P.mark()
    atT = P.alloc(NCH * TOK)
    actl = P.alloc(D)
    wti = P.alloc(2 * 512)
    evb = P.alloc(4 * 512)
    for qt in range(TPC):
        sc.add("qs", lambda E, qt=qt: E.dma_start(
            out=P.a(actl, D), in_=acc_d[qt * 128:(qt + 1) * 128, :]),
            reads=("acc_d",), writes=("actl",))
        for c in range(NCH):
            pb = (c % 2)
            sc.add("pe", lambda E, c=c, pb=pb: E.transpose(
                P.ps[pb][:, 0:128], P.a(actl + c * 128, 128), P.ident[:]),
                reads=("actl", "ident"), writes=(f"ps{pb}",))
            eng = "act" if c % 2 == 0 else "dve"
            sc.add(eng, lambda E, c=c, qt=qt, pb=pb, eng=eng: (
                E.copy(P.a(atT + c * TOK + qt * 128, 128), P.ps[pb][:, 0:128])
                if eng == "act" else
                E.tensor_copy(P.a(atT + c * TOK + qt * 128, 128),
                              P.ps[pb][:, 0:128])),
                reads=(f"ps{pb}",), writes=(f"atT.{c}.{qt}",))
    atkeys = lambda c: tuple(f"atT.{c}.{q}" for q in range(TPC))

    def psq(qt):
        if qt < 4:
            return P.ps[qt][:, 0:512], f"ps{qt}"
        return P.psbig[:, (qt - 4) * 512:(qt - 3) * 512], f"psb{qt - 4}"

    for fc in range(8):
        for kc in range(NCH):
            i = fc * NCH + kc
            sc.add("qs", lambda E, kc=kc, fc=fc, i=i: E.dma_start(
                out=P.a(wti + (i % 2) * 512, 512), in_=d_["woTt"][kc, fc]),
                writes=(f"wt.{i % 2}",))
            for qt in range(TPC):
                ps, pk = psq(qt)
                sc.add("pe", lambda E, kc=kc, qt=qt, i=i, ps=ps: E.matmul(
                    ps, P.a(atT + kc * TOK + qt * 128, 128),
                    P.a(wti + (i % 2) * 512, 512),
                    start=(kc == 0), stop=(kc == NCH - 1)),
                    reads=(f"wt.{i % 2}", f"atT.{kc}.{qt}"),
                    writes=(pk,))
        for qt in range(TPC):
            ps, pk = psq(qt)
            ek = f"ev.{qt % 4}"
            sc.add("act", lambda E, qt=qt, ps=ps: E.copy(
                P.a(evb + (qt % 4) * 512, 512), ps),
                reads=(pk,), writes=(ek,))
            sc.add("qp", lambda E, qt=qt, fc=fc: E.dma_start(
                out=s0_d[qt * 128:(qt + 1) * 128, fc * 512:(fc + 1) * 512],
                in_=P.a(evb + (qt % 4) * 512, 512)),
                reads=(ek,), writes=("s0_d",))
    wti6 = P.alloc(D)
    b2 = P.alloc(D)
    kwb = P.alloc(D // 2)
    b1 = actl
    for kc in range(NCH):
        sc.add("qs", lambda E, kc=kc: E.dma_start(
            out=P.a(wti6, D), in_=d_["ffnWT"][kc]), writes=("w6",))
        sc.add("dve", lambda E: E.tensor_single_scalar(
            P.a(b1, D), P.a(wti6, D), P.swcol[:, 0:1], OP.mult),
            reads=("w6", "swcol"), writes=("actl",))
        sc.add("dve", lambda E: E.tensor_scalar(
            P.a(b2, D), P.a(b1, D), MAGIC, MAGIC, OP.add, OP.subtract),
            reads=("actl",), writes=("b2",))
        sc.add("dve", lambda E: E.tensor_copy(
            P.a(b1, D).bitcast(I32), P.a(b2, D)), reads=("b2",), writes=("actl",))
        sc.add("dve", lambda E: E.tensor_scalar(
            P.a(kwb, D // 2).bitcast(BF16), P.a(b1, D).bitcast(I32),
            -1.0, 1.0, OP.max, OP.min),
            reads=("actl",), writes=("kwb",))
        sc.add("qp", lambda E, kc=kc: E.dma_start(
            out=kwq_d[kc], in_=P.a(kwb, D // 2).bitcast(BF16)),
            reads=("kwb",), writes=("kwq_d",))
    P.release()
    sc.fence()
    P.tap("s0_tm", lambda: s0_d[:], ("s0_d",))

    xn_tm_d = env["xn_tm_d"]
    x_tm = env["x_tm"]
    x1_d = env["x1_d"]
    xn1_fm_d = env["xn1_fm_d"]
    q1_fm_d = env["q1_fm_d"]
    cc1_in = env["cc1_in"]
    cc1_out = env["cc1_out"]
    s1_d = env["s1_d"]
    out_d = env["out"]

    def psq(qt):
        if qt < 4:
            return P.ps[qt][:, 0:512], f"ps{qt}"
        return P.psbig[:, (qt - 4) * 512:(qt - 3) * 512], f"psb{qt - 4}"

    P.mark()
    kxT = P.alloc(NCH * TOK // 2)
    xnb = P.alloc(D)
    c1 = P.alloc(D)
    c2 = P.alloc(D)
    c3 = P.alloc(D)
    srows = P.alloc(TPC * 128)
    rcols = P.alloc(TPC)
    qcol = P.alloc(8)
    kwt = P.alloc(2 * 256)
    ffnbr = P.alloc(512)
    epi = P.alloc(4 * 512)

    def kxv(kc, qt):
        return P.a(kxT, NCH * TOK // 2).bitcast(BF16)[:, kc * TOK + qt * 128:
                                                      kc * TOK + qt * 128 + 128]

    def kwv(i):
        return P.a(kwt, 2 * 256).bitcast(BF16)[:, (i % 2) * 512:(i % 2) * 512 + 512]
    for qt in range(TPC):
        sc.add("qs", lambda E, qt=qt: E.dma_start(
            out=P.a(xnb, D), in_=xn_tm_d[qt * 128:(qt + 1) * 128, :]),
            reads=("xn_tm_d",), writes=("xnb",))
        sc.add("act", lambda E: E.activation(
            P.a(c1, D), P.a(xnb, D), AF.Square, accum_out=P.a(qcol, 1)),
            reads=("xnb",), writes=("c1", "q0"))
        sc.add("dve", lambda E: E.tensor_scalar(
            P.a(qcol + 1, 1), P.a(qcol, 1), 1.0 / D, EPS, OP.mult, OP.add),
            reads=("q0",), writes=("q1",))
        sc.add("dve", lambda E: E.reciprocal(P.a(qcol + 2, 1), P.a(qcol + 1, 1)),
               reads=("q1",), writes=("q2",))
        sc.add("act", lambda E: E.activation(
            P.a(qcol + 3, 1), P.a(qcol + 2, 1), AF.Sqrt),
            reads=("q2",), writes=("q3",))
        sc.add("dve", lambda E: E.tensor_single_scalar(
            P.a(c1, D), P.a(xnb, D), P.a(qcol + 3, 1), OP.mult),
            reads=("xnb", "q3"), writes=("c1",))
        sc.add("dve", lambda E: E.tensor_reduce(
            P.a(qcol + 4, 1), P.a(c1, D), axis=AX.X, op=OP.max,
            apply_absolute_value=True), reads=("c1",), writes=("q4",))
        sc.add("dve", lambda E: E.tensor_scalar_max(
            P.a(qcol + 5, 1), P.a(qcol + 4, 1), EPS), reads=("q4",), writes=("q5",))
        sc.add("dve", lambda E: E.reciprocal(P.a(qcol + 6, 1), P.a(qcol + 5, 1)),
               reads=("q5",), writes=("q6",))
        sc.add("dve", lambda E: E.tensor_scalar_mul(
            P.a(qcol + 6, 1), P.a(qcol + 6, 1), 127.0),
            reads=("q6",), writes=("q6",))
        sc.add("dve", lambda E: E.tensor_single_scalar(
            P.a(c2, D), P.a(c1, D), P.a(qcol + 6, 1), OP.mult),
            reads=("c1", "q6"), writes=("c2",))
        sc.add("dve", lambda E: E.tensor_scalar(
            P.a(c3, D), P.a(c2, D), MAGIC, MAGIC, OP.add, OP.subtract),
            reads=("c2",), writes=("c3",))
        sc.add("dve", lambda E: E.tensor_copy(
            P.a(c1, D).bitcast(I32), P.a(c3, D)), reads=("c3",), writes=("c1",))
        sc.add("dve", lambda E: E.tensor_copy(
            P.a(c2, D), P.a(c1, D).bitcast(I32)), reads=("c1",), writes=("c2",))
        sc.add("dve", lambda E: E.tensor_scalar(
            P.a(c3, D), P.a(c2, D), -128.0, 127.0, OP.max, OP.min),
            reads=("c2",), writes=("c3",))
        for c in range(NCH):
            pb = c % 2
            sc.add("pe", lambda E, c=c, pb=pb: E.transpose(
                P.ps[pb][:, 0:128], P.a(c3 + c * 128, 128), P.ident[:]),
                reads=("c3", "ident"), writes=(f"ps{pb}",))
            eng = "act" if c % 2 == 0 else "dve"
            sc.add(eng, lambda E, c=c, qt=qt, pb=pb, eng=eng: (
                E.copy(kxv(c, qt), P.ps[pb][:, 0:128])
                if eng == "act" else
                E.tensor_copy(kxv(c, qt), P.ps[pb][:, 0:128])),
                reads=(f"ps{pb}",), writes=(f"kxT.{c}.{qt}",))
        sc.add("dve", lambda E, qt=qt: E.tensor_tensor(
            P.a(qcol + 7, 1), P.a(qcol + 6, 1), P.swcol[:, 0:1], OP.mult),
            reads=("q6", "swcol"), writes=("q7",))
        sc.add("dve", lambda E, qt=qt: E.reciprocal(
            P.a(rcols + qt, 1), P.a(qcol + 7, 1)), reads=("q7",), writes=("rcols",))
        sc.add("pe", lambda E: E.transpose(
            P.ps[2][0:1, 0:128], P.a(qcol + 7, 1), P.ident[:]),
            reads=("q7", "ident"), writes=("ps2",))
        sc.add("dve", lambda E, qt=qt: E.tensor_copy(
            P.a(srows + qt * 128, 128, 0, 1), P.ps[2][0:1, 0:128]),
            reads=("ps2",), writes=("srows",))
    for fc in range(8):
        for kc in range(NCH):
            i = fc * NCH + kc
            sc.add("qs", lambda E, kc=kc, fc=fc, i=i: E.dma_start(
                out=kwv(i), in_=kwq_d[kc, :, fc * 512:(fc + 1) * 512]),
                reads=("kwq_d",), writes=(f"kw6.{i % 2}",))
            for qt in range(TPC):
                ps, pk = psq(qt)
                sc.add("pe", lambda E, kc=kc, qt=qt, i=i, ps=ps: E.matmul(
                    ps, kxv(kc, qt), kwv(i),
                    start=(kc == 0), stop=False),
                    reads=(f"kw6.{i % 2}", f"kxT.{kc}.{qt}"), writes=(pk,))
        sc.add("qs", lambda E, fc=fc: E.dma_start(
            out=P.a(ffnbr, 512, 0, 1),
            in_=d_["ffnb"][:, fc * 512:(fc + 1) * 512]), writes=("ffnbr",))
        for qt in range(TPC):
            ps, pk = psq(qt)
            sc.add("pe", lambda E, qt=qt, ps=ps: E.matmul(
                ps, P.a(srows + qt * 128, 128, 0, 1),
                P.a(ffnbr, 512, 0, 1), start=False, stop=True),
                reads=("srows", "ffnbr"), writes=(pk,))
            sc.add("act", lambda E, qt=qt, ps=ps: E.activation(
                P.a(epi, 512), ps, AF.Gelu, scale=P.a(rcols + qt, 1)),
                reads=(pk, "rcols"), writes=("e0",))
            sc.add("qs", lambda E, qt=qt, fc=fc: E.dma_start(
                out=P.a(epi + 512, 512),
                in_=xn_tm_d[qt * 128:(qt + 1) * 128, fc * 512:(fc + 1) * 512]),
                reads=("xn_tm_d",), writes=("e1",))
            sc.add("qs", lambda E, qt=qt, fc=fc: E.dma_start(
                out=P.a(epi + 1024, 512),
                in_=s0_d[qt * 128:(qt + 1) * 128, fc * 512:(fc + 1) * 512]),
                reads=("s0_d",), writes=("e2",))
            sc.add("qs", lambda E, qt=qt, fc=fc: E.dma_start(
                out=P.a(epi + 1536, 512),
                in_=x_tm[qt * 128:(qt + 1) * 128, fc * 512:(fc + 1) * 512]),
                writes=("e3",))
            sc.add("dve", lambda E: E.tensor_add(
                P.a(epi, 512), P.a(epi, 512), P.a(epi + 512, 512)),
                reads=("e0", "e1"), writes=("e0",))
            sc.add("act", lambda E: E.activation(
                P.a(epi + 512, 512), P.a(epi, 512), AF.Silu),
                reads=("e0",), writes=("e1",))
            sc.add("dve", lambda E: E.tensor_mul(
                P.a(epi, 512), P.a(epi + 512, 512), P.a(epi + 1024, 512)),
                reads=("e1", "e2"), writes=("e0",))
            sc.add("dve", lambda E: E.tensor_add(
                P.a(epi, 512), P.a(epi, 512), P.a(epi + 1536, 512)),
                reads=("e0", "e3"), writes=("e0",))
            sc.add("qp", lambda E, qt=qt, fc=fc: E.dma_start(
                out=x1_d[qt * 128:(qt + 1) * 128, fc * 512:(fc + 1) * 512],
                in_=P.a(epi, 512)), reads=("e0",), writes=("x1_d",))
    if "kwq" in P.taps:
        sc.add("qp", lambda E: E.dma_start(
            out=P.dram["tap_kwq"][:], in_=kwq_d[:]),
            reads=("kwq_d",), writes=("tap_kwq",))
    if "kx" in P.taps:
        sc.add("qp", lambda E: E.dma_start(
            out=P.dram["tap_kx"][:],
            in_=P.a(kxT, NCH * TOK // 2).bitcast(BF16)),
            reads=tuple(f"kxT.{c}.{q}" for c in range(NCH) for q in range(TPC)),
            writes=("tap_kx",))
    P.release()
    sc.fence()
    P.tap("x1_tm", lambda: x1_d[:], ("x1_d",))

    _ln_proj(P, dict(
        x_src=x1_d, wt=d_["w1catt"], nf=48, pfx="p7",
        xn_tm_d=None, xn_fm_d=xn1_fm_d, q_fm_d=q1_fm_d, q_chunks=32,
        cc_in=cc1_in, kv_chunks=(32, 40), bias=d_["b1cat"],
        wg=None))
    sc.fence()
    P.tap("xn1_fm", lambda: xn1_fm_d[:], ("xn1_fm_d",))

    if SIM_LOCAL_CC:
        for r in range(2):
            sc.add("qp", lambda E, r=r: E.dma_start(
                out=cc1_out[r], in_=cc1_in[:]),
                reads=("p7ccin",), writes=("cc1_out",))
    else:
        sc.add("pool", lambda E: E.collective_compute(
            "AllGather", OP.bypass, ins=[cc1_in[:]], outs=[cc1_out[:]],
            replica_groups=[[0, 1], [2, 3], [4, 5], [6, 7]]),
            reads=("p7ccin",), writes=("cc1_out",))
    sc.fence()
    P.mark()
    rk1 = P.alloc(KVH * 2048)
    v1tm = P.alloc(KVH * 2048)
    prm1 = P.alloc(128)
    cosf1 = P.alloc(2048)
    sinf1 = P.alloc(2048)
    vtmp1 = P.alloc(2048)
    w1b1 = P.alloc(2048)
    sc.add("qs", lambda E: E.dma_start(out=P.a(prm1, 128), in_=d_["permL1"][:]),
           writes=("prm1",))
    sc.add("qs", lambda E: E.dma_start(out=P.a(cosf1, 2048), in_=d_["cosL1"][:]),
           writes=("cosf1",))
    sc.add("qs", lambda E: E.dma_start(out=P.a(sinf1, 2048), in_=d_["sinL1"][:]),
           writes=("sinf1",))
    for kvh in range(KVH):
        rkk = f"rk1.{kvh}"
        for r in range(2):
            sc.add("qs", lambda E, r=r, kvh=kvh: E.dma_start(
                out=P.a(rk1 + kvh * 2048, 2048)
                    .rearrange("b (j r2 c) -> b r2 j c", r2=2, c=128)[:, r, :, :],
                in_=cc1_out[r, 0, kvh, :, :].rearrange("b (j c) -> b j c", c=128)),
                reads=("cc1_out",), writes=(rkk,))
            sc.add("qs", lambda E, r=r, kvh=kvh: E.dma_start(
                out=P.a(vtmp1, 2048)
                    .rearrange("b (j r2 c) -> b r2 j c", r2=2, c=128)[:, r, :, :],
                in_=cc1_out[r, 1, kvh, :, :].rearrange("b (j c) -> b j c", c=128)),
                reads=("cc1_out",), writes=("vtmp1",))
        for j in range(4):
            sc.add("pe", lambda E, kvh=kvh, j=j: E.matmul(
                P.psbig[:, j * 512:(j + 1) * 512], P.a(prm1, 128),
                P.a(rk1 + kvh * 2048 + j * 512, 512), start=True, stop=True),
                reads=(rkk, "prm1"), writes=(f"psb{j}",))
        sc.add("dve", lambda E: E.tensor_tensor(
            P.a(w1b1, 2048), P.psbig[:, 0:2048], P.a(sinf1, 2048), OP.mult),
            reads=("psb0", "psb1", "psb2", "psb3", "sinf1"), writes=("w1b1",))
        sc.add("dve", lambda E, kvh=kvh: E.tensor_tensor(
            P.a(rk1 + kvh * 2048, 2048), P.a(rk1 + kvh * 2048, 2048),
            P.a(cosf1, 2048), OP.mult), reads=(rkk, "cosf1"), writes=(rkk,))
        sc.add("dve", lambda E, kvh=kvh: E.tensor_tensor(
            P.a(rk1 + kvh * 2048, 2048), P.a(rk1 + kvh * 2048, 2048),
            P.a(w1b1, 2048), OP.add), reads=(rkk, "w1b1"), writes=(rkk,))
        for gt in range(16):
            pb = 2 + (gt % 2)
            sc.add("pe", lambda E, gt=gt, pb=pb: E.transpose(
                P.ps[pb][:, 0:128], P.a(vtmp1 + gt * 128, 128), P.ident[:]),
                reads=("vtmp1", "ident"), writes=(f"ps{pb}",))
            eng = "act" if gt % 2 == 0 else "dve"
            sc.add(eng, lambda E, kvh=kvh, gt=gt, pb=pb, eng=eng: (
                E.copy(P.a(v1tm + kvh * 2048 + gt * 128, 128), P.ps[pb][:, 0:128])
                if eng == "act" else
                E.tensor_copy(P.a(v1tm + kvh * 2048 + gt * 128, 128),
                              P.ps[pb][:, 0:128])),
                reads=(f"ps{pb}",), writes=(f"v1tm{kvh}",))

    clq1 = P.alloc(128)
    slq1 = P.alloc(128)
    qb9 = P.alloc(2 * 128)
    rq9 = P.alloc(2 * 128)
    pq9 = P.alloc(2 * 128)
    pb9 = P.alloc(2 * 2048)
    pt9 = P.alloc(2 * 2048)
    ob9 = P.alloc(2 * 128)
    cb9 = P.alloc(2 * 8)
    it9 = 0
    pend9 = [None]
    for qt in range(TPC):
        sc.add("qs", lambda E, qt=qt: E.dma_start(
            out=P.a(clq1, 128), in_=d_["coslq1"][:, qt * 128:(qt + 1) * 128]),
            writes=("clq1",))
        sc.add("qs", lambda E, qt=qt: E.dma_start(
            out=P.a(slq1, 128), in_=d_["sinlq1"][:, qt * 128:(qt + 1) * 128]),
            writes=("slq1",))
        for kvh in range(KVH):
            for g in range(G):
                h = kvh * G + g
                pr = it9 % 2
                it9 += 1
                QB, RQ, PQ, PB, PT, OB, CB_ = (qb9 + pr * 128, rq9 + pr * 128,
                                               pq9 + pr * 128, pb9 + pr * 2048,
                                               pt9 + pr * 2048, ob9 + pr * 128,
                                               cb9 + pr * 8)
                K_ = lambda nm, pr=pr: f"{nm}{pr}"
                PSP, PSA = 0, 3
                sc.add("qs", lambda E, h=h, qt=qt, QB=QB: E.dma_start(
                    out=P.a(QB, 128), in_=q1_fm_d[h, :, qt * 128:(qt + 1) * 128]),
                    reads=("p7qfm",), writes=(K_("qb9"),))
                sc.add("pe", lambda E, QB=QB, PSP=PSP: E.matmul(
                    P.ps[PSP][:, 128:256], P.a(prm1, 128), P.a(QB, 128),
                    start=True, stop=True), reads=(K_("qb9"), "prm1"),
                    writes=(f"ps{PSP}",))
                sc.add("dve", lambda E, PQ=PQ, PSP=PSP: E.tensor_copy(
                    P.a(PQ, 128), P.ps[PSP][:, 128:256]),
                    reads=(f"ps{PSP}",), writes=(K_("pq9"),))
                sc.add("dve", lambda E, PQ=PQ: E.tensor_tensor(
                    P.a(PQ, 128), P.a(PQ, 128), P.a(slq1, 128), OP.mult),
                    reads=(K_("pq9"), "slq1"), writes=(K_("pq9"),))
                sc.add("dve", lambda E, QB=QB, RQ=RQ: E.tensor_tensor(
                    P.a(RQ, 128), P.a(QB, 128), P.a(clq1, 128), OP.mult),
                    reads=(K_("qb9"), "clq1"), writes=(K_("rq9"),))
                sc.add("dve", lambda E, RQ=RQ, PQ=PQ: E.tensor_tensor(
                    P.a(RQ, 128), P.a(RQ, 128), P.a(PQ, 128), OP.add),
                    reads=(K_("rq9"), K_("pq9")), writes=(K_("rq9"),))
                for t in range(4):
                    sc.add("pe", lambda E, kvh=kvh, t=t, RQ=RQ: E.matmul(
                        P.psbig[:, t * 512:(t + 1) * 512], P.a(RQ, 128),
                        P.a(rk1 + kvh * 2048 + t * 512, 512),
                        start=True, stop=True),
                        reads=(K_("rq9"), f"rk1.{kvh}"), writes=(f"psb{t}",))
                psbks = ("psb0", "psb1", "psb2", "psb3")
                sc.add("dve", lambda E, CB_=CB_: E.tensor_reduce(
                    P.a(CB_, 1), P.psbig[:, 0:2048], axis=AX.X, op=OP.max,
                    negate=True), reads=psbks, writes=(K_("cb0"),))
                sc.add("dve", lambda E, CB_=CB_: E.tensor_scalar_mul(
                    P.a(CB_ + 1, 1), P.a(CB_, 1), SCALE),
                    reads=(K_("cb0"),), writes=(K_("cb1"),))
                sc.add("act", lambda E, CB_=CB_, PB=PB: E.activation(
                    P.a(PB, 2048), P.psbig[:, 0:2048], AF.Exp,
                    bias=P.a(CB_ + 1, 1), scale=SCALE,
                    accum_out=P.a(CB_ + 2, 1)),
                    reads=psbks + (K_("cb1"),), writes=(K_("pb9"), K_("cb2")))

                def stageB(kvh=kvh, qt=qt, h=h, PB=PB, PT=PT, OB=OB, CB_=CB_,
                           K_=K_, PSA=PSA):
                    for t in range(16):
                        pb = 1 + (t % 2)
                        sc.add("pe", lambda E, t=t, pb=pb, PB=PB: E.transpose(
                            P.ps[pb][:, 0:128], P.a(PB + t * 128, 128), P.ident[:]),
                            reads=(K_("pb9"), "ident"), writes=(f"ps{pb}",))
                        eng = "act" if t % 2 == 0 else "dve"
                        sc.add(eng, lambda E, t=t, pb=pb, eng=eng, PT=PT: (
                            E.copy(P.a(PT + t * 128, 128), P.ps[pb][:, 0:128])
                            if eng == "act" else
                            E.tensor_copy(P.a(PT + t * 128, 128), P.ps[pb][:, 0:128])),
                            reads=(f"ps{pb}",), writes=(K_(f"pt9.{t}"),))
                    for t in range(16):
                        sc.add("pe", lambda E, t=t, kvh=kvh, PT=PT, PSA=PSA: E.matmul(
                            P.ps[PSA][:, 0:128], P.a(PT + t * 128, 128),
                            P.a(v1tm + kvh * 2048 + t * 128, 128),
                            start=(t == 0), stop=(t == 15)),
                            reads=(K_(f"pt9.{t}"), f"v1tm{kvh}"), writes=(f"ps{PSA}",))
                    sc.add("dve", lambda E, CB_=CB_: E.reciprocal(
                        P.a(CB_ + 3, 1), P.a(CB_ + 2, 1)),
                        reads=(K_("cb2"),), writes=(K_("cb3"),))
                    sc.add("dve", lambda E, CB_=CB_, OB=OB, PSA=PSA: E.tensor_single_scalar(
                        P.a(OB, 128), P.ps[PSA][:, 0:128], P.a(CB_ + 3, 1), OP.mult),
                        reads=(f"ps{PSA}", K_("cb3")), writes=(K_("ob9"),))
                    sc.add("qp", lambda E, qt=qt, h=h, OB=OB: E.dma_start(
                        out=s1_d[qt * 128:(qt + 1) * 128, h * 128:(h + 1) * 128],
                        in_=P.a(OB, 128)), reads=(K_("ob9"),), writes=("s1_d",))

                if pend9[0] is not None:
                    pend9[0]()
                pend9[0] = stageB
    if pend9[0] is not None:
        pend9[0]()
        pend9[0] = None
    P.release()
    sc.fence()
    P.tap("s1_tm", lambda: s1_d[:], ("s1_d",))

    P.mark()
    xn1r = P.alloc(NCH * TOK)
    gwb = P.alloc(NCH * 28)
    acb = P.alloc(NCH * 224)
    hTb = P.alloc(TPC * 256)
    lgb = P.alloc(28)
    eb10 = P.alloc(28)
    selb = P.alloc(28)
    gtb10 = P.alloc(28)
    hb10 = P.alloc(224)
    mrep10 = P.alloc(8)
    cb10 = P.alloc(8)
    bct = P.alloc(1024)
    bwt = P.alloc(2 * 512)
    ep10 = P.alloc(4 * 512)
    sc.add("dve", lambda E: E.memset(P.a(mrep10, 8), 1.0e38), writes=("mrep10",))
    sc.add("qs", lambda E: E.dma_start(out=P.a(gwb, NCH * 28), in_=d_["gwTt"][:]),
           writes=("gwb",))
    sc.add("qs", lambda E: E.dma_start(out=P.a(acb, NCH * 224), in_=d_["AcatTt"][:]),
           writes=("acb",))
    for c in range(NCH):
        sc.add("qs", lambda E, c=c: E.dma_start(
            out=P.a(xn1r + c * TOK, TOK), in_=xn1_fm_d[c]),
            reads=("xn1_fm_d",), writes=(f"xn1.{c}",))
    for qt in range(TPC):
        for kc in range(NCH):
            sc.add("pe", lambda E, kc=kc, qt=qt: E.matmul(
                P.ps[0][:, 0:28], P.a(xn1r + kc * TOK + qt * 128, 128),
                P.a(gwb + kc * 28, 28), start=(kc == 0), stop=(kc == NCH - 1)),
                reads=(f"xn1.{kc}", "gwb"), writes=("ps0",))
        sc.add("dve", lambda E: E.tensor_copy(P.a(lgb, 28), P.ps[0][:, 0:28]),
               reads=("ps0",), writes=("lgb",))
        sc.add("dve", lambda E: E.tensor_reduce(
            P.a(cb10, 1), P.a(lgb, 28), axis=AX.X, op=OP.max, negate=True),
            reads=("lgb",), writes=("d0",))
        sc.add("dve", lambda E: E.tensor_scalar_mul(
            P.a(cb10 + 1, 1), P.a(cb10, 1), -1.0), reads=("d0",), writes=("d1",))
        sc.add("dve", lambda E: E.tensor_copy(P.a(eb10, 28), P.a(lgb, 28)),
               reads=("lgb",), writes=("eb10",))
        sc.add("dve", lambda E: E.tensor_copy(P.a(mrep10, 1), P.a(cb10 + 1, 1)),
               reads=("d1",), writes=("mrep10",))
        sc.add("dve", lambda E: E.match_replace(
            P.a(eb10, 28), P.a(mrep10, 8), P.a(eb10, 28), NEG),
            reads=("eb10", "mrep10"), writes=("eb10",))
        sc.add("dve", lambda E: E.tensor_reduce(
            P.a(cb10 + 2, 1), P.a(eb10, 28), axis=AX.X, op=OP.max),
            reads=("eb10",), writes=("d2",))
        sc.add("dve", lambda E: E.tensor_single_scalar(
            P.a(selb, 28), P.a(lgb, 28), P.a(cb10 + 2, 1), OP.is_ge),
            reads=("lgb", "d2"), writes=("selb",))
        sc.add("act", lambda E: E.activation(
            P.a(eb10, 28), P.a(lgb, 28), AF.Exp, bias=P.a(cb10, 1)),
            reads=("lgb", "d0"), writes=("eb10",))
        sc.add("dve", lambda E: E.tensor_tensor(
            P.a(eb10, 28), P.a(eb10, 28), P.a(selb, 28), OP.mult),
            reads=("eb10", "selb"), writes=("eb10",))
        sc.add("dve", lambda E: E.tensor_reduce(
            P.a(cb10 + 3, 1), P.a(eb10, 28), axis=AX.X, op=OP.add),
            reads=("eb10",), writes=("d3",))
        sc.add("dve", lambda E: E.reciprocal(P.a(cb10 + 4, 1), P.a(cb10 + 3, 1)),
               reads=("d3",), writes=("d4",))
        sc.add("dve", lambda E: E.tensor_scalar_mul(
            P.a(cb10 + 5, 1), P.a(cb10 + 4, 1), 2.0), reads=("d4",), writes=("d5",))
        sc.add("dve", lambda E: E.tensor_single_scalar(
            P.a(gtb10, 28), P.a(eb10, 28), P.a(cb10 + 5, 1), OP.mult),
            reads=("eb10", "d5"), writes=("gtb10",))
        for kc in range(NCH):
            sc.add("pe", lambda E, kc=kc, qt=qt: E.matmul(
                P.ps[1][:, 0:224], P.a(xn1r + kc * TOK + qt * 128, 128),
                P.a(acb + kc * 224, 224), start=(kc == 0), stop=(kc == NCH - 1)),
                reads=(f"xn1.{kc}", "acb"), writes=("ps1",))
        sc.add("dve", lambda E: E.tensor_copy(P.a(hb10, 224), P.ps[1][:, 0:224]),
               reads=("ps1",), writes=("hb10",))
        sc.add("dve", lambda E: E.tensor_tensor(
            P.a(hb10, 224).rearrange("b (e r) -> b e r", r=8),
            P.a(hb10, 224).rearrange("b (e r) -> b e r", r=8),
            P.a(gtb10, 28).unsqueeze(2).to_broadcast([128, 28, 8]), OP.mult),
            reads=("hb10", "gtb10"), writes=("hb10",))
        sc.add("pe", lambda E: E.transpose(
            P.ps[2][:, 0:128], P.a(hb10, 128), P.ident[:]),
            reads=("hb10", "ident"), writes=("ps2",))
        sc.add("dve", lambda E, qt=qt: E.tensor_copy(
            P.a(hTb + qt * 256, 128), P.ps[2][:, 0:128]),
            reads=("ps2",), writes=(f"hT.{qt}",))
        sc.add("pe", lambda E: E.transpose(
            P.ps[2][0:96, 0:128], P.a(hb10 + 128, 96), P.ident[:]),
            reads=("hb10", "ident"), writes=("ps2",))
        sc.add("dve", lambda E, qt=qt: E.tensor_copy(
            P.a(hTb + qt * 256 + 128, 128, 0, 96), P.ps[2][0:96, 0:128]),
            reads=("ps2",), writes=(f"hT.{qt}",))
    for fc in range(8):
        sc.add("qs", lambda E, fc=fc: E.dma_start(
            out=P.a(bct, 512), in_=d_["Bcat"][0:128, fc * 512:(fc + 1) * 512]),
            writes=("bct0",))
        sc.add("qs", lambda E, fc=fc: E.dma_start(
            out=P.a(bct + 512, 512, 0, 96),
            in_=d_["Bcat"][128:224, fc * 512:(fc + 1) * 512]),
            writes=("bct1",))
        for kc in range(NCH):
            i = fc * NCH + kc
            sc.add("qs", lambda E, kc=kc, fc=fc, i=i: E.dma_start(
                out=P.a(bwt + (i % 2) * 512, 512), in_=d_["baseTt"][kc, fc]),
                writes=(f"bw.{i % 2}",))
            for qt in range(TPC):
                ps, pk = psq(qt)
                sc.add("pe", lambda E, kc=kc, qt=qt, i=i, ps=ps: E.matmul(
                    ps, P.a(xn1r + kc * TOK + qt * 128, 128),
                    P.a(bwt + (i % 2) * 512, 512),
                    start=(kc == 0), stop=False),
                    reads=(f"bw.{i % 2}", f"xn1.{kc}"), writes=(pk,))
        for qt in range(TPC):
            ps, pk = psq(qt)
            sc.add("pe", lambda E, qt=qt, ps=ps: E.matmul(
                ps, P.a(hTb + qt * 256, 128), P.a(bct, 512),
                start=False, stop=False),
                reads=(f"hT.{qt}", "bct0"), writes=(pk,))
            sc.add("pe", lambda E, qt=qt, ps=ps: E.matmul(
                ps, P.a(hTb + qt * 256 + 128, 128, 0, 96),
                P.a(bct + 512, 512, 0, 96), start=False, stop=True),
                reads=(f"hT.{qt}", "bct1"), writes=(pk,))
            sc.add("act", lambda E, ps=ps: E.activation(
                P.a(ep10, 512), ps, AF.Silu), reads=(pk,), writes=("f0",))
            sc.add("qs", lambda E, qt=qt, fc=fc: E.dma_start(
                out=P.a(ep10 + 512, 512),
                in_=s1_d[qt * 128:(qt + 1) * 128, fc * 512:(fc + 1) * 512]),
                reads=("s1_d",), writes=("f1",))
            sc.add("qs", lambda E, qt=qt, fc=fc: E.dma_start(
                out=P.a(ep10 + 1024, 512),
                in_=x1_d[qt * 128:(qt + 1) * 128, fc * 512:(fc + 1) * 512]),
                reads=("x1_d",), writes=("f2",))
            sc.add("dve", lambda E: E.tensor_mul(
                P.a(ep10, 512), P.a(ep10, 512), P.a(ep10 + 512, 512)),
                reads=("f0", "f1"), writes=("f0",))
            sc.add("dve", lambda E: E.tensor_add(
                P.a(ep10, 512), P.a(ep10, 512), P.a(ep10 + 1024, 512)),
                reads=("f0", "f2"), writes=("f0",))
            sc.add("qp", lambda E, qt=qt, fc=fc: E.dma_start(
                out=out_d[qt * 128:(qt + 1) * 128, fc * 512:(fc + 1) * 512],
                in_=P.a(ep10, 512)), reads=("f0",), writes=("out",))
    P.release()


def _ln_proj(P, cfg):
    sc, nc = P.sc, P.nc
    pfx = cfg["pfx"]
    nf = cfg["nf"]
    qch = cfg["q_chunks"]
    kv0, v0 = cfg["kv_chunks"]
    wt = cfg["wt"]

    P.mark()
    xg = P.alloc(2 * D)
    xnt = P.alloc(D)
    sq = P.alloc(D)
    cols = P.alloc(2 * 8)
    xnfm = P.alloc(NCH * 512)
    wcol = P.alloc(2 * D)
    ev = P.alloc(4 * 512)
    has_g = cfg.get("wg") is not None
    if has_g:
        wgb = P.alloc(NCH * 96)
        gev = P.alloc(512)
        gtm = P.alloc(4 * 96)
        bgc = P.alloc(1)
        sc.add("qs", lambda E: E.dma_start(out=P.a(wgb, NCH * 96), in_=cfg["wg"][:]),
               writes=(pfx + "wg",))
        sc.add("qs", lambda E: E.dma_start(out=P.a(bgc, 1, 0, 96), in_=cfg["bg"][:]),
               writes=(pfx + "bg",))
    has_b = cfg.get("bias") is not None
    if has_b:
        bc = P.alloc(48)
        sc.add("qs", lambda E: E.dma_start(out=P.a(bc, 48), in_=cfg["bias"][:]),
               writes=(pfx + "bias",))

    for grp in range(2):
        for j2 in range(4):
            t = grp * 4 + j2
            xk = f"{pfx}xg.{t % 2}"
            ck = f"{pfx}cols.{t % 2}"
            o = xg + (t % 2) * D
            co = cols + (t % 2) * 8
            sc.add("qs", lambda E, t=t, o=o: E.dma_start(
                out=P.a(o, D), in_=cfg["x_src"][t * 128:(t + 1) * 128, :]),
                writes=(xk,))
            sc.add("act", lambda E, o=o, co=co: E.activation(
                P.a(sq, D), P.a(o, D), AF.Square, accum_out=P.a(co + 1, 1)),
                reads=(xk,), writes=(pfx + "sq", ck + "s"))
            sc.add("dve", lambda E, o=o, co=co: E.tensor_reduce(
                P.a(co, 1), P.a(o, D), axis=AX.X, op=OP.add),
                reads=(xk,), writes=(ck,))
            sc.add("dve", lambda E, co=co: E.tensor_scalar_mul(
                P.a(co + 2, 1), P.a(co, 1), 1.0 / D), reads=(ck,), writes=(ck,))
            sc.add("dve", lambda E, co=co: E.tensor_mul(
                P.a(co + 3, 1), P.a(co + 2, 1), P.a(co + 2, 1)),
                reads=(ck,), writes=(ck,))
            sc.add("dve", lambda E, co=co: E.scalar_tensor_tensor(
                P.a(co + 4, 1), P.a(co + 1, 1), 1.0 / D, P.a(co + 3, 1),
                OP.mult, OP.subtract), reads=(ck, ck + "s"), writes=(ck,))
            sc.add("dve", lambda E, co=co: E.tensor_scalar_add(
                P.a(co + 5, 1), P.a(co + 4, 1), EPS), reads=(ck,), writes=(ck,))
            sc.add("dve", lambda E, co=co: E.reciprocal(
                P.a(co + 6, 1), P.a(co + 5, 1)), reads=(ck,), writes=(ck,))
            sc.add("act", lambda E, co=co: E.activation(
                P.a(co + 7, 1), P.a(co + 6, 1), AF.Sqrt), reads=(ck,), writes=(ck + "r",))
            sc.add("dve", lambda E, o=o, co=co: E.scalar_tensor_tensor(
                P.a(xnt, D), P.a(o, D), P.a(co + 2, 1),
                P.a(co + 7, 1).to_broadcast([128, D]), OP.subtract, OP.mult),
                reads=(xk, ck, ck + "r"), writes=(pfx + "xnt",))
            if cfg.get("xn_tm_d") is not None:
                sc.add("qp", lambda E, t=t: E.dma_start(
                    out=cfg["xn_tm_d"][t * 128:(t + 1) * 128, :], in_=P.a(xnt, D)),
                    reads=(pfx + "xnt",), writes=("xn_tm_d",))
            for c in range(NCH):
                i = t * NCH + c
                pb = 2 + (i % 2)
                sc.add("pe", lambda E, c=c, pb=pb: E.transpose(
                    P.ps[pb][:, 0:128], P.a(xnt + c * 128, 128), P.ident[:]),
                    reads=(pfx + "xnt", "ident"), writes=(f"ps{pb}",))
                sc.add("act", lambda E, c=c, j2=j2, pb=pb: E.copy(
                    P.a(xnfm + c * 512 + j2 * 128, 128), P.ps[pb][:, 0:128]),
                    reads=(f"ps{pb}",), writes=(f"{pfx}fm.{c}.{j2}",))
            if cfg.get("xn_fm_d") is not None:
                sc.add("qp", lambda E, t=t, j2=j2: E.dma_start(
                    out=cfg["xn_fm_d"][:, :, t * 128:(t + 1) * 128]
                        .transpose([1, 0, 2]),
                    in_=P.a(xnfm, NCH * 512)
                        .rearrange("b (c n) -> b c n", n=512)
                        [:, :, j2 * 128:j2 * 128 + 128]),
                    reads=tuple(f"{pfx}fm.{c}.{j2}" for c in range(NCH)),
                    writes=("xn1_fm_d",))
        fmkeys = tuple(f"{pfx}fm.{c}.{j2}" for c in range(NCH) for j2 in range(4))
        for fc in range(nf):
            i = grp * nf + fc
            wk = f"{pfx}w.{i % 2}"
            ek = f"{pfx}ev.{i % 4}"
            pk = f"ps{i % 2}"
            wo = wcol + (i % 2) * D
            eo = ev + (i % 4) * 512
            sc.add("qs", lambda E, fc=fc, wo=wo: E.dma_start(
                out=P.a(wo, D), in_=wt[fc]), writes=(wk,))
            for kc in range(NCH):
                sc.add("pe", lambda E, kc=kc, wo=wo, pk=pk: E.matmul(
                    P.ps[int(pk[2])][:, 0:512], P.a(wo + kc * 128, 128),
                    P.a(xnfm + kc * 512, 512), start=(kc == 0), stop=(kc == NCH - 1)),
                    reads=(wk,) + (fmkeys if kc == 0 else
                                   (f"{pfx}fm.{kc}.0", f"{pfx}fm.{kc}.1",
                                    f"{pfx}fm.{kc}.2", f"{pfx}fm.{kc}.3")),
                    writes=(pk,))
            if has_b:
                sc.add("act", lambda E, fc=fc, eo=eo, pk=pk: E.activation(
                    P.a(eo, 512), P.ps[int(pk[2])][:, 0:512], AF.Identity,
                    bias=P.a(bc + fc, 1)),
                    reads=(pk, pfx + "bias"), writes=(ek,))
            else:
                sc.add("act", lambda E, eo=eo, pk=pk: E.copy(
                    P.a(eo, 512), P.ps[int(pk[2])][:, 0:512]),
                    reads=(pk,), writes=(ek,))
            if fc < qch:
                sc.add("qp", lambda E, fc=fc, grp=grp, eo=eo, dst=cfg["q_fm_d"]:
                       E.dma_start(out=dst[fc, :, grp * 512:(grp + 1) * 512],
                                   in_=P.a(eo, 512)),
                       reads=(ek,), writes=(pfx + "qfm",))
            else:
                kvsel = 0 if fc < v0 else 1
                kvh = fc - (kv0 if kvsel == 0 else v0)
                sc.add("qp", lambda E, kvsel=kvsel, kvh=kvh, grp=grp, eo=eo:
                       E.dma_start(out=cfg["cc_in"][kvsel, kvh, :,
                                                    grp * 512:(grp + 1) * 512],
                                   in_=P.a(eo, 512)),
                       reads=(ek,), writes=(pfx + "ccin",))
        if has_g:
            for kc in range(NCH):
                sc.add("pe", lambda E, kc=kc: E.matmul(
                    P.psbig[0:96, 0:512], P.a(wgb + kc * 96, 96, 0, 128),
                    P.a(xnfm + kc * 512, 512), start=(kc == 0), stop=(kc == NCH - 1)),
                    reads=(pfx + "wg",) + fmkeys, writes=("psg",))
            sc.add("act", lambda E: E.activation(
                P.a(gev, 512, 0, 96), P.psbig[0:96, 0:512], AF.Sigmoid,
                bias=P.a(bgc, 1, 0, 96)),
                reads=("psg", pfx + "bg"), writes=(pfx + "gev",))
            for j2 in range(4):
                sc.add("pe", lambda E, j2=j2: E.transpose(
                    P.psbig[0:128, 512:512 + 96],
                    P.a(gev + j2 * 128, 128, 0, 96), P.ident[0:96, 0:96]),
                    reads=(pfx + "gev", "ident"), writes=("psgt",))
                sc.add("dve", lambda E, j2=j2: E.tensor_copy(
                    P.a(gtm + j2 * 96, 96), P.psbig[0:128, 512:512 + 96]),
                    reads=("psgt",), writes=(f"{pfx}gtm.{j2}",))
                t = grp * 4 + j2
                sc.add("qp", lambda E, t=t, j2=j2: E.dma_start(
                    out=cfg["gates_d"][t * 128:(t + 1) * 128, :],
                    in_=P.a(gtm + j2 * 96, 96)),
                    reads=(f"{pfx}gtm.{j2}",), writes=("gates_d",))
    P.release()


def _run_raw(inputs, taps=(), trace=False):
    staged, tiles = [], []
    for c in range(8):
        d, tg = _stage(inputs, c)
        staged.append(d)
        tiles.append(tg)
    nc = bass.Bass("TRN2", target_bir_lowering=False)
    nc, P = build(nc, taps=taps)
    res = run_bass_kernel_spmd(nc, staged, core_ids=list(range(8)), trace=trace)
    return res, tiles


def kernel(**inputs):
    staged, tiles = [], []
    for c in range(8):
        d, tg = _stage(inputs, c)
        staged.append(d)
        tiles.append(tg)
    nc = bass.Bass("TRN2", target_bir_lowering=False)
    nc, P = build(nc, taps=())
    res = run_bass_kernel_spmd(nc, staged, core_ids=list(range(8)))
    out = np.zeros((B, S, D), dtype=np.float32)
    for c in range(8):
        b = c // 2
        o = np.asarray(res.results[c]["out"]).reshape(TPC, 128, D)
        for j, gt in enumerate(tiles[c]):
            out[b, gt * 128:(gt + 1) * 128, :] = o[j]
    return out
